# version 01
# baseline (speedup 1.0000x reference)
import contextlib
import os
import numpy as np
import ml_dtypes
import concourse.bass as bass
import concourse.mybir as mybir
from concourse.bass_utils import run_bass_kernel_spmd

F32 = mybir.dt.float32
BF16 = mybir.dt.bfloat16
AF = mybir.ActivationFunctionType
ALU = mybir.AluOpType
AX = mybir.AxisListType

P = 128
D = 2048
KC = D // P
TC = 1024
NT = TC // P
NCORE = 8
HF = 512
EPS = 1e-5

BF = ml_dtypes.bfloat16

_uid = [0]


def _nm(s):
    _uid[0] += 1
    return f"{s}_{_uid[0]}"


class Sem:

    def __init__(self, nc, name):
        self.h = nc.alloc_semaphore(name)
        self.n = 0

    def inc(self, instr, k=1):
        instr.then_inc(self.h, k)
        self.n += k
        return instr

    def wait(self, eng, val=None):
        eng.wait_ge(self.h, self.n if val is None else val)


class Ctx:
    def __init__(self, nc, block):
        self.nc = nc
        self.block = block
        self.mm = Sem(nc, "s_mm")
        self.ea = Sem(nc, "s_ea")
        self.ev = Sem(nc, "s_ev")
        self.ep = Sem(nc, "s_ep")
        self.ds = Sem(nc, "s_ds")
        self.dsc = Sem(nc, "s_dsc")
        self.dg = Sem(nc, "s_dg")
        self.wr = [Sem(nc, f"s_wr{i}") for i in range(3)]
        self.wring_free = [0, 0, 0]
        self.xr = [Sem(nc, f"s_xr{i}") for i in range(2)]
        self.cc = Sem(nc, "s_cc")
        self.barsem = Sem(nc, "s_bar")
        self.bufs = {}

    def vgap(self, vector):
        vector.tensor_copy(out=self.dummy[:, 8:16], in_=self.dummy[:, 0:8])
        vector.tensor_copy(out=self.dummy[:, 16:24], in_=self.dummy[:, 0:8])

    def bar(self):
        b = self.barsem
        target = b.n + 5
        blk = self.block

        @blk.sync
        def _(e):
            e.drain().then_inc(b.h, 1)
            e.wait_ge(b.h, target)

        @blk.tensor
        def _(e):
            e.drain().then_inc(b.h, 1)
            e.wait_ge(b.h, target)

        @blk.vector
        def _(e):
            e.drain().then_inc(b.h, 1)
            e.wait_ge(b.h, target)

        @blk.scalar
        def _(e):
            e.drain().then_inc(b.h, 1)
            e.wait_ge(b.h, target)

        @blk.gpsimd
        def _(e):
            e.drain().then_inc(b.h, 1)
            e.wait_ge(b.h, target)

        b.n = target


def _np_bf(x):
    return np.asarray(x, dtype=BF)



def emit_prologue(ctx, xin, consts):
    nc, blk = ctx.nc, ctx.block
    xT = ctx.xT
    with (
        nc.sbuf_tensor(_nm("xstage"), [P, 2 * D], F32) as xstage,
        nc.psum_tensor(_nm("ppro0"), [P, P], F32) as pt0,
        nc.psum_tensor(_nm("ppro1"), [P, P], F32) as pt1,
    ):
        pts = [pt0, pt1]
        mm0, ea0 = ctx.mm.n, ctx.ea.n

        @blk.sync
        def _(sync):
            for name in consts:
                ctx.ds.inc(sync.dma_start(out=ctx.bufs[name], in_=ctx.inl[name].ap()), 16)
            for t in range(NT):
                s = t % 2
                if t >= 2:
                    sync.wait_ge(ctx.mm.h, mm0 + KC * (t - 1))
                sync.dma_start(out=xstage[:, s * D:(s + 1) * D],
                               in_=xin[t * P:(t + 1) * P, :]).then_inc(ctx.xr[s].h, 16)
                ctx.xr[s].n += 16

        @blk.vector
        def _(vector):
            vector.memset(ctx.dummy[:, :], 0.0)

        @blk.tensor
        def _(tensor):
            ctx.ds.wait(tensor)
            for t in range(NT):
                s = t % 2
                tensor.wait_ge(ctx.xr[s].h, 16 * (t // 2 + 1))
                for q in range(KC):
                    j = t * KC + q
                    if j >= 2:
                        tensor.wait_ge(ctx.ea.h, ea0 + j - 1)
                    ctx.mm.inc(tensor.transpose(
                        pts[j % 2][:], xstage[:, s * D + q * P: s * D + (q + 1) * P],
                        ctx.bufs["ID_F32"]), 1)

        @blk.scalar
        def _(scalar):
            for t in range(NT):
                for q in range(KC):
                    j = t * KC + q
                    scalar.wait_ge(ctx.mm.h, mm0 + j + 1)
                    ctx.ea.inc(scalar.copy(
                        out=xT[:, q * TC + t * P: q * TC + (t + 1) * P],
                        in_=pts[j % 2][:]), 1)

        ctx.bar()


def emit_epilogue(ctx, xout):
    nc, blk = ctx.nc, ctx.block
    xT = ctx.xT
    with (
        nc.sbuf_tensor(_nm("ostage"), [P, 2 * D], F32) as ostage,
        nc.psum_tensor(_nm("pepi0"), [P, P], F32) as pt0,
        nc.psum_tensor(_nm("pepi1"), [P, P], F32) as pt1,
    ):
        pts = [pt0, pt1]
        mm0, ea0 = ctx.mm.n, ctx.ea.n
        st = Sem(nc, _nm("s_st"))

        @blk.tensor
        def _(tensor):
            for t in range(NT):
                for q in range(KC):
                    j = t * KC + q
                    if j >= 2:
                        tensor.wait_ge(ctx.ea.h, ea0 + j - 1)
                    ctx.mm.inc(tensor.transpose(
                        pts[j % 2][:], xT[:, q * TC + t * P: q * TC + (t + 1) * P],
                        ctx.bufs["ID_F32"]), 1)

        @blk.scalar
        def _(scalar):
            for t in range(NT):
                s = t % 2
                if t >= 2:
                    scalar.wait_ge(st.h, 16 * (t - 1))
                for q in range(KC):
                    j = t * KC + q
                    scalar.wait_ge(ctx.mm.h, mm0 + j + 1)
                    ctx.ea.inc(scalar.copy(
                        out=ostage[:, s * D + q * P: s * D + (q + 1) * P],
                        in_=pts[j % 2][:]), 1)

        @blk.sync
        def _(sync):
            for t in range(NT):
                s = t % 2
                sync.wait_ge(ctx.ea.h, ea0 + (t + 1) * KC)
                st.inc(sync.dma_start(out=xout[t * P:(t + 1) * P, :],
                                      in_=ostage[:, s * D:(s + 1) * D]), 16)
            st.wait(sync)



def emit_ln(ctx, src_q, dst_mode, xold_dram=None):
    nc, blk = ctx.nc, ctx.block
    with (
        nc.sbuf_tensor(_nm("ln_sq"), [P, 2 * TC], F32) as sq,
        nc.sbuf_tensor(_nm("ln_rows"), [1, 4 * TC], F32) as rows,
        nc.sbuf_tensor(_nm("ln_rstd"), [P, TC], F32) as rstd_s,
        nc.sbuf_tensor(_nm("ln_t"), [P, TC], F32) as tsc,
        nc.sbuf_tensor(_nm("ln_xold"), [P, 2 * TC], F32) as xoldb,
        nc.psum_tensor(_nm("ln_pm"), [1, TC], F32) as ps_m,
        nc.psum_tensor(_nm("ln_ps"), [1, TC], F32) as ps_s,
        nc.psum_tensor(_nm("ln_pb0"), [P, TC], F32) as ps_b0,
        nc.psum_tensor(_nm("ln_pb1"), [P, TC], F32) as ps_b1,
    ):
        mm0, ea0, ev0 = ctx.mm.n, ctx.ea.n, ctx.ev.n

        @blk.scalar
        def _(scalar):
            for q in range(KC):
                if q >= 2:
                    scalar.wait_ge(ctx.mm.h, mm0 + (q - 1))
                ctx.ea.inc(scalar.square(out=sq[:, (q % 2) * TC:(q % 2 + 1) * TC],
                                         in_=src_q(q)), 1)

        @blk.tensor
        def _(tensor):
            ones = ctx.bufs["ONES1"]
            for q in range(KC):
                tensor.wait_ge(ctx.ea.h, ea0 + q + 1)
                for h in range(2):
                    tensor.matmul(ps_m[0:1, h * HF:(h + 1) * HF], ones,
                                  src_q(q)[:, h * HF:(h + 1) * HF],
                                  start=(q == 0), stop=(q == KC - 1))
                for h in range(2):
                    m = tensor.matmul(ps_s[0:1, h * HF:(h + 1) * HF], ones,
                                      sq[:, (q % 2) * TC + h * HF:(q % 2) * TC + (h + 1) * HF],
                                      start=(q == 0), stop=(q == KC - 1))
                ctx.mm.inc(m, 1)

        m_row = rows[0:1, 0:TC]
        var_row = rows[0:1, TC:2 * TC]
        rstd_row = rows[0:1, 2 * TC:3 * TC]
        mrs_row = rows[0:1, 3 * TC:4 * TC]

        @blk.vector
        def _(vector):
            vector.wait_ge(ctx.mm.h, mm0 + KC)
            vector.tensor_scalar(out=m_row, in0=ps_m[0:1, :], scalar1=1.0 / D,
                                 scalar2=None, op0=ALU.mult)
            vector.tensor_scalar(out=var_row, in0=ps_s[0:1, :], scalar1=1.0 / D,
                                 scalar2=None, op0=ALU.mult)
            vector.tensor_tensor(out=rstd_row, in0=m_row, in1=m_row, op=ALU.mult)
            ctx.ev.inc(vector.tensor_tensor(out=var_row, in0=var_row, in1=rstd_row,
                                            op=ALU.subtract), 1)

        @blk.scalar
        def _(scalar):
            ctx.ev.wait(scalar)
            scalar.activation(out=rstd_row, in_=var_row, func=AF.Ln, bias=ctx.bufs["EPSCOL"][0:1, 0:1])
            ctx.ea.inc(scalar.activation(out=rstd_row, in_=rstd_row, func=AF.Exp,
                                         scale=-0.5), 1)

        @blk.vector
        def _(vector):
            ctx.ea.wait(vector)
            ctx.ev.inc(vector.tensor_tensor(out=mrs_row, in0=m_row, in1=rstd_row,
                                            op=ALU.mult), 1)

        @blk.tensor
        def _(tensor):
            ctx.ev.wait(tensor)
            onesrow = ctx.bufs["ONESROW"]
            for h in range(2):
                tensor.matmul(ps_b0[:, h * HF:(h + 1) * HF], onesrow,
                              rstd_row[:, h * HF:(h + 1) * HF], start=True, stop=True)
            for h in range(2):
                m = tensor.matmul(ps_b1[:, h * HF:(h + 1) * HF], onesrow,
                                  mrs_row[:, h * HF:(h + 1) * HF], start=True, stop=True)
            ctx.mm.inc(m, 1)

        @blk.scalar
        def _(scalar):
            ctx.mm.wait(scalar)
            ctx.ea.inc(scalar.copy(out=rstd_s[:], in_=ps_b0[:]), 1)

        ev1 = ctx.ev.n
        dsc0 = ctx.dsc.n
        if dst_mode == "residual":
            @blk.scalar
            def _(scalar):
                ctx.dg.wait(scalar)
                for q in range(KC):
                    s = q % 2
                    if q >= 2:
                        scalar.wait_ge(ctx.ev.h, ev1 + (q - 1))
                    ctx.dsc.inc(scalar.dma_start(
                        out=xoldb[:, s * TC:(s + 1) * TC],
                        in_=xold_dram.ap()[:, q * TC:(q + 1) * TC]), 16)

        @blk.vector
        def _(vector):
            ctx.ea.wait(vector)
            for q in range(KC):
                vector.tensor_tensor(out=tsc[:], in0=src_q(q), in1=rstd_s[:], op=ALU.mult)
                if dst_mode == "xln":
                    ctx.ev.inc(vector.tensor_tensor(
                        out=ctx.xln[:, q * TC:(q + 1) * TC],
                        in0=tsc[:], in1=ps_b1[:], op=ALU.subtract), 1)
                else:
                    s = q % 2
                    vector.wait_ge(ctx.dsc.h, dsc0 + 16 * (q + 1))
                    vector.tensor_tensor(out=tsc[:], in0=tsc[:], in1=ps_b1[:], op=ALU.subtract)
                    ctx.ev.inc(vector.tensor_tensor(
                        out=ctx.xT[:, q * TC:(q + 1) * TC],
                        in0=tsc[:], in1=xoldb[:, s * TC:(s + 1) * TC], op=ALU.add), 1)

    ctx.bar()


def emit_spill_x(ctx):
    blk = ctx.block

    @blk.gpsimd
    def _(g):
        ctx.dg.inc(g.dma_start(out=ctx.xspill.ap(), in_=ctx.xT[:, :]), 16)



def emit_l0_ffn(ctx, W):
    nc, blk = ctx.nc, ctx.block
    E, R = 8, 64
    DBG = int(os.environ.get("K_DBG_PHASES", "99"))

    esx = contextlib.ExitStack()
    ctx.xln = esx.enter_context(nc.sbuf_tensor(_nm("xln"), [P, KC * TC], BF16))
    xln = ctx.xln
    emit_ln(ctx, lambda q: ctx.xT[:, q * TC:(q + 1) * TC], "xln")
    emit_spill_x(ctx)

    with (
        nc.sbuf_tensor(_nm("f0_wrt"), [P, E * KC], BF16) as wrt,
        nc.sbuf_tensor(_nm("f0_rbb"), [P, E], F32) as rbb,
        nc.sbuf_tensor(_nm("f0_at"), [P, KC * 512], BF16) as at,
        nc.sbuf_tensor(_nm("f0_bt"), [P, 8 * 1024], BF16) as bt,
        nc.sbuf_tensor(_nm("f0_g0t"), [P, NT * P], BF16) as g0t,
        nc.sbuf_tensor(_nm("f0_selq"), [E, 4 * P], BF16) as selq,
        nc.sbuf_tensor(_nm("f0_hsb"), [P, 4 * TC], F32) as hsb,
        nc.sbuf_tensor(_nm("f0_hs"), [P, 4 * TC], BF16) as hs,
    ):
        if DBG < 2:
            return
        selq_np = np.zeros((E, 4 * P), dtype=np.float32)
        for qq in range(4):
            selq_np[2 * qq, qq * P:qq * P + 64] = 1.0
            selq_np[2 * qq + 1, qq * P + 64:(qq + 1) * P] = 1.0
        c_selq = nc.inline_tensor(_np_bf(selq_np), _nm("c_selq"))

        with (
            nc.sbuf_tensor(_nm("f0_stage"), [P, 4 * D], BF16) as stg,
            nc.sbuf_tensor(_nm("f0_wrst"), [E, D], F32) as wrst,
            nc.sbuf_tensor(_nm("f0_brrow"), [1, E], F32) as brrow,
            nc.psum_tensor(_nm("f0_psr0"), [P, E], F32) as ps_r0,
            nc.psum_tensor(_nm("f0_psr1"), [P, E], F32) as ps_r1,
            nc.psum_tensor(_nm("f0_psrb"), [P, E], F32) as ps_rb,
        ):
            ps_rp = [ps_r0, ps_r1]
            ds0, dg0, mm0, ea0 = ctx.ds.n, ctx.dg.n, ctx.mm.n, ctx.ea.n

            @blk.sync
            def _(sync):
                ctx.ds.inc(sync.dma_start(out=selq[:], in_=c_selq.ap()), 16)
                ctx.ds.inc(sync.dma_start(out=wrst[:], in_=W["L0_ffn_Wr"]), 16)
                ctx.ds.inc(sync.dma_start(out=brrow[0:1, :],
                                          in_=W["L0_ffn_br"].rearrange("(o e) -> o e", o=1)), 16)

            @blk.gpsimd
            def _(g):
                av = W["L0_ffn_A"].rearrange("e r d -> (e r) d")
                for a in range(4):
                    ctx.dg.inc(g.dma_start(out=stg[:, a * D:(a + 1) * D],
                                           in_=av[a * P:(a + 1) * P, :]), 16)

            @blk.tensor
            def _(tensor):
                ctx.ds.wait(tensor)
                for k in range(KC):
                    if k >= 2:
                        tensor.wait_ge(ctx.ea.h, ea0 + k - 1)
                    ctx.mm.inc(tensor.transpose(
                        ps_rp[k % 2][:],
                        wrst[:, k * P:(k + 1) * P],
                        ctx.bufs["ID_F32"][0:E, 0:E]), 1)
                m = tensor.matmul(ps_rb[:], ctx.bufs["ONESROW"],
                                  brrow[0:1, :], start=True, stop=True)
                ctx.mm.inc(m, 1)

            @blk.scalar
            def _(scalar):
                for k in range(KC):
                    scalar.wait_ge(ctx.mm.h, mm0 + k + 1)
                    ctx.ea.inc(scalar.copy(out=wrt[:, k * E:(k + 1) * E],
                                           in_=ps_rp[k % 2][:]), 1)
                scalar.wait_ge(ctx.mm.h, mm0 + KC + 1)
                ctx.ea.inc(scalar.copy(out=rbb[:], in_=ps_rb[:]), 1)

            @blk.sync
            def _(sync):
                ctx.dg.wait(sync)
                for a in range(4):
                    for k in range(KC):
                        ctx.ds.inc(sync.dma_start_transpose(
                            out=at[:, k * 512 + a * P: k * 512 + (a + 1) * P],
                            in_=stg[:, a * D + k * P: a * D + (k + 1) * P]), 16)

            ctx.bar()

            @blk.gpsimd
            def _(g):
                stgv = stg[:, 0:8192].rearrange("p (a e r) -> p a e r", a=KC, e=E, r=64)
                for e in range(E):
                    bv = W["L0_ffn_B"][e].rearrange("(a p) r -> p a r", p=P)
                    ctx.dg.inc(g.dma_start(out=stgv[:, :, e, :], in_=bv), 16)

            @blk.sync
            def _(sync):
                ctx.dg.wait(sync)
                for a in range(KC):
                    for e2 in range(4):
                        blk_i = a * 4 + e2
                        ctx.ds.inc(sync.dma_start_transpose(
                            out=bt[:, blk_i * P:(blk_i + 1) * P],
                            in_=stg[:, blk_i * P:(blk_i + 1) * P]), 16)

            ctx.bar()

        if DBG < 3:
            return
        with (
            nc.sbuf_tensor(_nm("f0_L"), [P, 3 * E], F32) as Lb,
            nc.sbuf_tensor(_nm("f0_cols"), [P, 4], F32) as cols,
            nc.sbuf_tensor(_nm("f0_gbf"), [P, P], BF16) as gbf,
            nc.psum_tensor(_nm("f0_pslg"), [P, E], F32) as ps_lg,
        ):
            L = Lb[:, 0:E]
            Ee = Lb[:, E:2 * E]
            mask = Lb[:, 2 * E:3 * E]
            nmx = cols[:, 0:1]
            zz = cols[:, 1:2]
            rz = cols[:, 2:3]
            ds_r0 = ctx.ds.n

            @blk.gpsimd
            def _(g):
                ctx.ep.inc(g.memset(gbf[:, :], 0.0), 1)

            for t in range(NT):
                @blk.tensor
                def _(tensor, t=t):
                    if t == 0:
                        ctx.ea.wait(tensor)
                    else:
                        tensor.wait_ge(ctx.ep.h, ctx.ep.n)
                    m = None
                    for k in range(KC):
                        m = tensor.matmul(ps_lg[:, :],
                                          xln[:, k * TC + t * P: k * TC + (t + 1) * P],
                                          wrt[:, k * E:(k + 1) * E],
                                          start=(k == 0), stop=(k == KC - 1))
                    ctx.mm.inc(m, 1)

                @blk.scalar
                def _(scalar, t=t):
                    ctx.mm.wait(scalar)
                    ctx.ea.inc(scalar.copy(out=L, in_=ps_lg[:, :]), 1)

                @blk.gpsimd
                def _(g, t=t):
                    ctx.ea.wait(g)
                    ctx.ep.inc(g.tensor_tensor(out=L, in0=L, in1=rbb[:], op=ALU.add), 1)

                @blk.vector
                def _(vector, t=t):
                    ctx.ep.wait(vector)
                    ctx.ev.inc(vector.reduce_max(out=nmx, in_=L, axis=AX.X,
                                                 negate=True), 1)

                @blk.scalar
                def _(scalar, t=t):
                    ctx.ev.wait(scalar)
                    ctx.ea.inc(scalar.activation(out=Ee, in_=L, func=AF.Exp, bias=nmx), 1)

                @blk.vector
                def _(vector, t=t):
                    ctx.ea.wait(vector)
                    ctx.ev.inc(vector.reduce_sum(out=zz, in_=Ee, axis=AX.X), 1)

                @blk.scalar
                def _(scalar, t=t):
                    ctx.ev.wait(scalar)
                    scalar.activation(out=rz, in_=zz, func=AF.Ln)
                    ctx.ea.inc(scalar.activation(out=rz, in_=rz, func=AF.Exp,
                                                 scale=-1.0), 1)

                @blk.gpsimd
                def _(g, t=t):
                    ctx.ea.wait(g)
                    if t >= 1:
                        g.wait_ge(ctx.ds.h, ds_r0 + 16 * t)
                    g.tensor_scalar(out=mask, in0=L, scalar1=nmx, scalar2=0.0,
                                    op0=ALU.add, op1=ALU.is_equal)
                    ctx.ep.inc(g.tensor_scalar(out=gbf[:, 0:E], in0=mask, scalar1=rz,
                                               scalar2=None, op0=ALU.mult), 1)

                @blk.sync
                def _(sync, t=t):
                    ctx.ep.wait(sync)
                    ctx.ds.inc(sync.dma_start_transpose(out=g0t[:, t * P:(t + 1) * P],
                                                        in_=gbf[:, :]), 16)

            ctx.bar()
            if os.environ.get("K_DBG_DUMP"):
                d_L = nc.dram_tensor("dbg_L", [P, 3 * E], F32, kind="ExternalOutput").ap()
                d_cols = nc.dram_tensor("dbg_cols", [P, 3], F32, kind="ExternalOutput").ap()
                d_gbf = nc.dram_tensor("dbg_gbf", [P, P], F32, kind="ExternalOutput").ap()
                with nc.sbuf_tensor(_nm("dbg2"), [P, P], F32) as dbg2:
                    @blk.vector
                    def _(v):
                        v.tensor_copy(out=dbg2[:, 0:P], in_=gbf[:, :])
                        ctx.ev.inc(v.tensor_copy(out=dbg2[:, 0:P], in_=gbf[:, :]), 1)

                    @blk.gpsimd
                    def _(g):
                        ctx.ev.wait(g)
                        ctx.dg.inc(g.dma_start(out=d_gbf, in_=dbg2[:, 0:P]), 16)
                        ctx.dg.inc(g.dma_start(out=d_L, in_=Lb[:, :]), 16)
                        ctx.dg.inc(g.dma_start(out=d_cols, in_=cols[:, 0:3]), 16)
                    ctx.bar()

        if DBG < 4:
            return
        with nc.psum_tensor(_nm("f0_ph0"), [P, TC], F32) as ph0, \
             nc.psum_tensor(_nm("f0_ph1"), [P, TC], F32) as ph1, \
             nc.psum_tensor(_nm("f0_ph2"), [P, TC], F32) as ph2, \
             nc.psum_tensor(_nm("f0_ph3"), [P, TC], F32) as ph3:
            ps_h = [ph0, ph1, ph2, ph3]
            mmh = ctx.mm.n

            @blk.tensor
            def _(tensor):
                ctx.ds.wait(tensor)
                for rc in range(4):
                    m = None
                    for k in range(KC):
                        for h in range(2):
                            m = tensor.matmul(ps_h[rc][:, h * HF:(h + 1) * HF],
                                              at[:, k * 512 + rc * P:k * 512 + (rc + 1) * P],
                                              xln[:, k * TC + h * HF:k * TC + (h + 1) * HF],
                                              start=(k == 0), stop=(k == KC - 1))
                    ctx.mm.inc(m, 1)

            @blk.scalar
            def _(scalar):
                for rc in range(4):
                    scalar.wait_ge(ctx.mm.h, mmh + rc + 1)
                    ctx.ea.inc(scalar.copy(out=hsb[:, rc * TC:(rc + 1) * TC],
                                           in_=ps_h[rc][:]), 1)

            eah = ctx.ea.n
            mmg = ctx.mm.n

            @blk.tensor
            def _(tensor):
                for rc in range(4):
                    tensor.wait_ge(ctx.ea.h, eah - 4 + rc + 1)
                    m = None
                    for h in range(2):
                        m = tensor.matmul(ps_h[rc][:, h * HF:(h + 1) * HF],
                                          selq[:, rc * P:(rc + 1) * P],
                                          g0t[0:E, h * HF:(h + 1) * HF], start=True, stop=True)
                    ctx.mm.inc(m, 1)

            @blk.vector
            def _(vector):
                for rc in range(4):
                    vector.wait_ge(ctx.mm.h, mmg + rc + 1)
                    ctx.ev.inc(vector.tensor_tensor(out=hs[:, rc * TC:(rc + 1) * TC],
                                                    in0=hsb[:, rc * TC:(rc + 1) * TC],
                                                    in1=ps_h[rc][:], op=ALU.mult), 1)

            ctx.bar()

        if DBG < 5:
            return
        with nc.psum_tensor(_nm("f0_py0"), [P, TC], F32) as py0, \
             nc.psum_tensor(_nm("f0_py1"), [P, TC], F32) as py1:
            ps_y = [py0, py1]
            mmy = ctx.mm.n
            eay = ctx.ea.n

            @blk.tensor
            def _(tensor):
                for oc in range(KC):
                    sl = oc % 2
                    if oc >= 2:
                        tensor.wait_ge(ctx.ea.h, eay + oc - 1)
                    m = None
                    for e2 in range(4):
                        blk_i = oc * 4 + e2
                        for h in range(2):
                            m = tensor.matmul(ps_y[sl][:, h * HF:(h + 1) * HF],
                                              bt[:, blk_i * P:(blk_i + 1) * P],
                                              hs[:, e2 * TC + h * HF:e2 * TC + (h + 1) * HF],
                                              start=(e2 == 0), stop=(e2 == 3))
                    ctx.mm.inc(m, 1)

            @blk.scalar
            def _(scalar):
                ctx.dg.wait(scalar)
                for oc in range(KC):
                    sl = oc % 2
                    scalar.wait_ge(ctx.mm.h, mmy + oc + 1)
                    ctx.ea.inc(scalar.activation(out=ctx.xT[:, oc * TC:(oc + 1) * TC],
                                                 in_=ps_y[sl][:], func=AF.Copy,
                                                 scale=1.0 / R), 1)

            ctx.bar()

        if os.environ.get("K_DBG_DUMP"):
            d_bt = nc.dram_tensor("dbg_bt", [P, 8 * 1024], F32, kind="ExternalOutput").ap()
            d_g0t = nc.dram_tensor("dbg_g0t", [P, NT * P], F32, kind="ExternalOutput").ap()
            d_hs = nc.dram_tensor("dbg_hs", [P, 4 * TC], F32, kind="ExternalOutput").ap()
            d_y = nc.dram_tensor("dbg_y", [P, KC * TC], F32, kind="ExternalOutput").ap()
            with nc.sbuf_tensor(_nm("dbgf"), [P, 8 * 1024], F32) as dbgf:
                @blk.vector
                def _(v):
                    ctx.ev.inc(v.tensor_copy(out=dbgf[:, 0:8 * 1024], in_=bt[:]), 1)

                @blk.gpsimd
                def _(g):
                    ctx.ev.wait(g)
                    ctx.dg.inc(g.dma_start(out=d_bt, in_=dbgf[:, 0:8 * 1024]), 16)
                ctx.bar()

                @blk.vector
                def _(v):
                    ctx.ev.inc(v.tensor_copy(out=dbgf[:, 0:NT * P], in_=g0t[:]), 1)

                @blk.gpsimd
                def _(g):
                    ctx.ev.wait(g)
                    ctx.dg.inc(g.dma_start(out=d_g0t, in_=dbgf[:, 0:NT * P]), 16)
                ctx.bar()

                @blk.vector
                def _(v):
                    ctx.ev.inc(v.tensor_copy(out=dbgf[:, 0:4 * TC], in_=hs[:]), 1)

                @blk.gpsimd
                def _(g):
                    ctx.ev.wait(g)
                    ctx.dg.inc(g.dma_start(out=d_hs, in_=dbgf[:, 0:4 * TC]), 16)
                ctx.bar()

                @blk.gpsimd
                def _(g):
                    ctx.dg.inc(g.dma_start(out=d_y, in_=ctx.xT[:, :]), 16)
                ctx.bar()

    esx.close()
    if DBG >= 6:
        emit_ln(ctx, lambda q: ctx.xT[:, q * TC:(q + 1) * TC], "residual",
                xold_dram=ctx.xspill)



BLOCK_INPUTS = {
    "l0a": ["L0_attn_log_decay", "L0_attn_Wq", "L0_attn_Wk", "L0_attn_Wv",
            "L0_attn_Wo", "L0_attn_Wg1", "L0_attn_Wg2", "L0_attn_norm_w"],
    "l0f": ["L0_ffn_A", "L0_ffn_B", "L0_ffn_Wr", "L0_ffn_br"],
    "l1a": ["L1_attn_w_hg", "L1_attn_w_out"],
    "l1f": ["L1_ffn_router_w", "L1_ffn_router_bias", "L1_ffn_Wu", "L1_ffn_Wg",
            "L1_ffn_Wd", "L1_ffn_Su", "L1_ffn_Sg", "L1_ffn_Sd"],
}

INPUT_SHAPES = {
    "L0_attn_log_decay": [16], "L0_attn_Wq": [D, D], "L0_attn_Wk": [D, D],
    "L0_attn_Wv": [D, D], "L0_attn_Wo": [D, D], "L0_attn_Wg1": [D, 128],
    "L0_attn_Wg2": [128, D], "L0_attn_norm_w": [D],
    "L0_ffn_A": [8, 64, D], "L0_ffn_B": [8, D, 64], "L0_ffn_Wr": [8, D], "L0_ffn_br": [8],
    "L1_attn_w_hg": [D, 6144], "L1_attn_w_out": [3072, D],
    "L1_ffn_router_w": [16, D], "L1_ffn_router_bias": [16],
    "L1_ffn_Wu": [16, D, 1024], "L1_ffn_Wg": [16, D, 1024], "L1_ffn_Wd": [16, 1024, D],
    "L1_ffn_Su": [1, D, D], "L1_ffn_Sg": [1, D, D], "L1_ffn_Sd": [1, D, D],
}

_PROGRAM_CACHE = {}


def build_program(blocks):
    key = tuple(blocks)
    if key in _PROGRAM_CACHE:
        return _PROGRAM_CACHE[key]
    nc = bass.Bass("TRN2", target_bir_lowering=False, detect_race_conditions=False)
    xin = nc.dram_tensor("xin", [TC, D], F32, kind="ExternalInput").ap()
    xout = nc.dram_tensor("xout", [TC, D], F32, kind="ExternalOutput").ap()
    W = {}
    names = []
    for b in blocks:
        for nmn in BLOCK_INPUTS[b]:
            W[nmn] = nc.dram_tensor(nmn, INPUT_SHAPES[nmn], F32, kind="ExternalInput").ap()
            names.append(nmn)
    if "l0a" in blocks:
        W["msel"] = nc.dram_tensor("msel", [8, 1], F32, kind="ExternalInput").ap()
        names.append("msel")
    if "l1a" in blocks:
        W["tmat"] = nc.dram_tensor("tmat", [8, 8], F32, kind="ExternalInput").ap()
        W["tbias"] = nc.dram_tensor("tbias", [8, 1], F32, kind="ExternalInput").ap()
        names += ["tmat", "tbias"]

    with nc.Block() as blk:
        ctx = Ctx(nc, blk)
        ctx.inl = {
            "ID_F32": nc.inline_tensor(np.eye(P, dtype=np.float32), "c_id_f32"),
            "ID_BF": nc.inline_tensor(_np_bf(np.eye(P)), "c_id_bf"),
            "ONES1": nc.inline_tensor(np.ones((P, 1), dtype=np.float32), "c_ones1"),
            "EPSCOL": nc.inline_tensor(np.full((P, 1), EPS, dtype=np.float32), "c_epscol"),
            "ONES1B": nc.inline_tensor(_np_bf(np.ones((P, 1))), "c_ones1b"),
            "ONESROW": nc.inline_tensor(np.ones((1, P), dtype=np.float32), "c_onesrow"),
        }
        consts = {
            "ID_F32": ([P, P], F32),
            "ID_BF": ([P, P], BF16),
            "ONES1": ([P, 1], F32),
            "EPSCOL": ([P, 1], F32),
            "ONES1B": ([P, 1], BF16),
            "ONESROW": ([1, P], F32),
        }
        for name, (shape, dtype) in consts.items():
            ctx.bufs[name] = nc.alloc_sbuf_tensor(_nm("sb_" + name), shape, dtype).ap()
        ctx.dummy = nc.alloc_sbuf_tensor("vgap_dummy", [P, 24], F32).ap()
        ctx.wbf = nc.alloc_sbuf_tensor("wbf", [P, 3 * 512], BF16).ap()
        ctx.xT = nc.alloc_sbuf_tensor("xT", [P, KC * TC], F32).ap()
        ctx.xspill = nc.dram_tensor("xspill", [P, KC * TC], F32)
        emit_prologue(ctx, xin, consts)
        for b in blocks:
            EMITTERS[b](ctx, W)
        emit_epilogue(ctx, xout)

    _PROGRAM_CACHE[key] = (nc, names)
    return _PROGRAM_CACHE[key]


def _percore_consts(core):
    r = core
    msel = np.array([[1024.0 * (r - 1 - j) if j < r else 1e9] for j in range(8)],
                    dtype=np.float32)
    tmat = np.zeros((8, 8), dtype=np.float32)
    for mrow in range(8):
        for j in range(8):
            if j < mrow < r:
                tmat[mrow, j] = 1.0
    tbias = np.array([[0.0 if j < r else -1e38] for j in range(8)], dtype=np.float32)
    return {"msel": msel, "tmat": tmat, "tbias": tbias}


def run_blocks(x_full, inputs, blocks):
    nc, names = build_program(blocks)
    in_maps = []
    for c in range(NCORE):
        m = {"xin": np.ascontiguousarray(x_full[c * TC:(c + 1) * TC])}
        pc = _percore_consts(c)
        for nmn in names:
            m[nmn] = pc[nmn] if nmn in pc else inputs[nmn]
        in_maps.append(m)
    res = run_bass_kernel_spmd(nc, in_maps, core_ids=list(range(NCORE)))
    return np.concatenate([res.results[c]["xout"] for c in range(NCORE)], axis=0)


def kernel(**inputs):
    inputs = {k: np.asarray(v, dtype=np.float32) for k, v in inputs.items()}
    x = np.ascontiguousarray(inputs["L0_attn_x"].reshape(8192, D))
    for blocks in (["l0a"], ["l0f"], ["l1a"], ["l1f"]):
        x = run_blocks(x, inputs, blocks)
    return x.reshape(1, 8192, D)



def wstream(ctx, pieces, wtile, on_piece, pe_prewait=None):
    nc, blk = ctx.nc, ctx.block
    NS = 3
    mm0 = ctx.mm.n
    wr0 = [sm.n for sm in ctx.wr]

    @blk.gpsimd
    def _(g):
        for n, ap in enumerate(pieces):
            sl = n % NS
            if ctx.wring_free[sl] > 0:
                g.wait_ge(ctx.mm.h, ctx.wring_free[sl])
            ctx.wring_free[sl] = mm0 + n + 1
            g.dma_start(out=ctx.wbf[:, sl * 512:sl * 512 + wtile], in_=ap).then_inc(
                ctx.wr[sl].h, 16)
            ctx.wr[sl].n += 16

    @blk.tensor
    def _(tensor):
        for n in range(len(pieces)):
            sl = n % NS
            tensor.wait_ge(ctx.wr[sl].h, wr0[sl] + 16 * (n // NS + 1))
            if pe_prewait is not None:
                pe_prewait(tensor, n)
            last = on_piece(tensor, n, ctx.wbf[:, sl * 512:sl * 512 + wtile])
            ctx.mm.inc(last, 1)


def emit_l1_ffn(ctx, W):
    nc, blk = ctx.nc, ctx.block
    E = 16
    NEXP = int(os.environ.get("K_DBG_L1F_NEXP", "16"))
    NOSHARED = bool(os.environ.get("K_DBG_L1F_NOSHARED", ""))
    NOPOSTLN = bool(os.environ.get("K_DBG_L1F_NOPOSTLN", ""))
    Wu, Wg, Wd = W["L1_ffn_Wu"], W["L1_ffn_Wg"], W["L1_ffn_Wd"]
    Su, Sg, Sd = W["L1_ffn_Su"], W["L1_ffn_Sg"], W["L1_ffn_Sd"]

    esx = contextlib.ExitStack()
    ctx.xln = esx.enter_context(nc.sbuf_tensor(_nm("xln"), [P, KC * TC], BF16))
    xln = ctx.xln
    emit_ln(ctx, lambda q: ctx.xT[:, q * TC:(q + 1) * TC], "xln")
    emit_spill_x(ctx)

    onehot_np = np.zeros((E, E * P), dtype=np.float32)
    for e in range(E):
        onehot_np[e, e * P:(e + 1) * P] = 1.0
    c_oh = nc.inline_tensor(_np_bf(onehot_np), _nm("c_oh16"))

    with (
        nc.sbuf_tensor(_nm("f1_rwt"), [P, E * KC], BF16) as rwt,
        nc.sbuf_tensor(_nm("f1_rbb"), [P, E], F32) as rbb,
        nc.sbuf_tensor(_nm("f1_oh"), [E, E * P], BF16) as oh,
        nc.sbuf_tensor(_nm("f1_gt"), [P, NT * P], BF16) as gt,
        nc.sbuf_tensor(_nm("f1_usb"), [P, NT * TC], BF16) as usb,
        nc.sbuf_tensor(_nm("f1_gsb"), [P, NT * TC], BF16) as gsb,
        nc.sbuf_tensor(_nm("f1_gbs"), [P, TC], BF16) as gbs,
        nc.sbuf_tensor(_nm("f1_ssb"), [P, KC * TC], BF16) as ssb,
    ):
        with (
            nc.sbuf_tensor(_nm("f1_rwst"), [E, D], F32) as rwst,
            nc.sbuf_tensor(_nm("f1_brrow"), [1, E], F32) as brrow,
            nc.psum_tensor(_nm("f1_psr0"), [P, E], F32) as ps_r0,
            nc.psum_tensor(_nm("f1_psr1"), [P, E], F32) as ps_r1,
            nc.psum_tensor(_nm("f1_psrb"), [P, E], F32) as ps_rb,
        ):
            ps_rp = [ps_r0, ps_r1]
            mm0, ea0 = ctx.mm.n, ctx.ea.n

            @blk.sync
            def _(sync):
                ctx.ds.inc(sync.dma_start(out=oh[:], in_=c_oh.ap()), 16)
                ctx.ds.inc(sync.dma_start(out=rwst[:], in_=W["L1_ffn_router_w"]), 16)
                ctx.ds.inc(sync.dma_start(
                    out=brrow[0:1, :],
                    in_=W["L1_ffn_router_bias"].rearrange("(o e) -> o e", o=1)), 16)

            @blk.tensor
            def _(tensor):
                ctx.ds.wait(tensor)
                for k in range(KC):
                    if k >= 2:
                        tensor.wait_ge(ctx.ea.h, ea0 + k - 1)
                    ctx.mm.inc(tensor.transpose(
                        ps_rp[k % 2][:], rwst[:, k * P:(k + 1) * P],
                        ctx.bufs["ID_F32"][0:E, 0:E]), 1)
                m = tensor.matmul(ps_rb[:], ctx.bufs["ONESROW"], brrow[0:1, :],
                                  start=True, stop=True)
                ctx.mm.inc(m, 1)

            @blk.scalar
            def _(scalar):
                for k in range(KC):
                    scalar.wait_ge(ctx.mm.h, mm0 + k + 1)
                    ctx.ea.inc(scalar.copy(out=rwt[:, k * E:(k + 1) * E],
                                           in_=ps_rp[k % 2][:]), 1)
                scalar.wait_ge(ctx.mm.h, mm0 + KC + 1)
                ctx.ea.inc(scalar.copy(out=rbb[:], in_=ps_rb[:]), 1)

            ctx.bar()

        with (
            nc.sbuf_tensor(_nm("f1_L"), [P, 8 * E], F32) as Lb,
            nc.sbuf_tensor(_nm("f1_cols"), [P, 8], F32) as cols,
            nc.sbuf_tensor(_nm("f1_cbf"), [P, P], BF16) as cbf,
            nc.psum_tensor(_nm("f1_pslg"), [P, E], F32) as ps_lg,
        ):
            L = Lb[:, 0:E]
            LB = Lb[:, E:2 * E]
            m1 = Lb[:, 2 * E:3 * E]
            m2 = Lb[:, 3 * E:4 * E]
            Ex = Lb[:, 4 * E:5 * E]
            wrk = Lb[:, 5 * E:6 * E]
            w1b = Lb[:, 6 * E:7 * E]
            w2b = Lb[:, 7 * E:8 * E]
            nm1 = cols[:, 0:1]
            nm2 = cols[:, 1:2]
            nmL = cols[:, 2:3]
            aa = cols[:, 3:4]
            bb2 = cols[:, 4:5]
            rs = cols[:, 5:6]
            t1c = cols[:, 6:7]
            t2c = cols[:, 7:8]
            ds_r0 = ctx.ds.n

            @blk.gpsimd
            def _(g):
                ctx.ep.inc(g.memset(cbf[:, :], 0.0), 1)

            for t in range(NT):
                @blk.tensor
                def _(tensor, t=t):
                    if t == 0:
                        ctx.ea.wait(tensor)
                    else:
                        tensor.wait_ge(ctx.ep.h, ctx.ep.n)
                    m = None
                    for k in range(KC):
                        m = tensor.matmul(ps_lg[:, :],
                                          xln[:, k * TC + t * P:k * TC + (t + 1) * P],
                                          rwt[:, k * E:(k + 1) * E],
                                          start=(k == 0), stop=(k == KC - 1))
                    ctx.mm.inc(m, 1)

                @blk.scalar
                def _(scalar, t=t):
                    ctx.mm.wait(scalar)
                    ctx.ea.inc(scalar.copy(out=L, in_=ps_lg[:, :]), 1)

                @blk.gpsimd
                def _(g, t=t):
                    ctx.ea.wait(g)
                    ctx.ep.inc(g.tensor_tensor(out=LB, in0=L, in1=rbb[:], op=ALU.add), 1)

                @blk.vector
                def _(vector, t=t):
                    ctx.ep.wait(vector)
                    vector.reduce_max(out=nm1, in_=LB, axis=AX.X, negate=True)
                    ctx.ev.inc(vector.reduce_max(out=nmL, in_=L, axis=AX.X,
                                                 negate=True), 1)

                @blk.gpsimd
                def _(g, t=t):
                    ctx.ev.wait(g)
                    g.tensor_scalar(out=m1, in0=LB, scalar1=nm1, scalar2=0.0,
                                    op0=ALU.add, op1=ALU.is_equal)
                    g.tensor_scalar(out=wrk, in0=m1, scalar1=-1e30, scalar2=None,
                                    op0=ALU.mult)
                    ctx.ep.inc(g.tensor_tensor(out=wrk, in0=wrk, in1=LB, op=ALU.add), 1)

                @blk.vector
                def _(vector, t=t):
                    ctx.ep.wait(vector)
                    ctx.ev.inc(vector.reduce_max(out=nm2, in_=wrk, axis=AX.X,
                                                 negate=True), 1)

                @blk.scalar
                def _(scalar, t=t):
                    ctx.ev.wait(scalar)
                    ctx.ea.inc(scalar.activation(out=Ex, in_=L, func=AF.Exp,
                                                 bias=nmL), 1)

                @blk.gpsimd
                def _(g, t=t):
                    ctx.ea.wait(g)
                    g.tensor_scalar(out=m2, in0=wrk, scalar1=nm2, scalar2=0.0,
                                    op0=ALU.add, op1=ALU.is_equal)
                    g.tensor_tensor(out=w1b, in0=m1, in1=Ex, op=ALU.mult)
                    ctx.ep.inc(g.tensor_tensor(out=w2b, in0=m2, in1=Ex, op=ALU.mult), 1)

                @blk.vector
                def _(vector, t=t):
                    ctx.ep.wait(vector)
                    vector.reduce_sum(out=aa, in_=w1b, axis=AX.X)
                    ctx.ev.inc(vector.reduce_sum(out=bb2, in_=w2b, axis=AX.X), 1)

                @blk.gpsimd
                def _(g, t=t):
                    ctx.ev.wait(g)
                    ctx.ep.inc(g.tensor_tensor(out=rs, in0=aa, in1=bb2, op=ALU.add), 1)

                @blk.scalar
                def _(scalar, t=t):
                    ctx.ep.wait(scalar)
                    scalar.activation(out=rs, in_=rs, func=AF.Ln)
                    ctx.ea.inc(scalar.activation(out=rs, in_=rs, func=AF.Exp,
                                                 scale=-1.0), 1)

                @blk.gpsimd
                def _(g, t=t):
                    ctx.ea.wait(g)
                    if t >= 1:
                        g.wait_ge(ctx.ds.h, ds_r0 + 16 * t)
                    g.tensor_tensor(out=t1c, in0=aa, in1=rs, op=ALU.mult)
                    g.tensor_tensor(out=t2c, in0=bb2, in1=rs, op=ALU.mult)
                    g.tensor_scalar(out=wrk, in0=m1, scalar1=t1c, scalar2=None,
                                    op0=ALU.mult)
                    g.tensor_scalar(out=w2b, in0=m2, scalar1=t2c, scalar2=None,
                                    op0=ALU.mult)
                    ctx.ep.inc(g.tensor_tensor(out=cbf[:, 0:E], in0=w2b, in1=wrk,
                                               op=ALU.add), 1)

                @blk.sync
                def _(sync, t=t):
                    ctx.ep.wait(sync)
                    ctx.ds.inc(sync.dma_start_transpose(out=gt[:, t * P:(t + 1) * P],
                                                        in_=cbf[:, :]), 16)

            ctx.bar()
            if os.environ.get("K_DBG_DUMP"):
                d_gt = nc.dram_tensor("dbg_gt", [P, NT * P], F32, kind="ExternalOutput").ap()
                d_L = nc.dram_tensor("dbg_L", [P, 6 * E], F32, kind="ExternalOutput").ap()
                d_cols = nc.dram_tensor("dbg_cols", [P, 8], F32, kind="ExternalOutput").ap()
                with nc.sbuf_tensor(_nm("dbg3"), [P, NT * P], F32) as dbg3:
                    @blk.vector
                    def _(v):
                        ctx.ev.inc(v.tensor_copy(out=dbg3[:, :], in_=gt[:, :]), 1)

                    @blk.gpsimd
                    def _(g):
                        ctx.ev.wait(g)
                        ctx.dg.inc(g.dma_start(out=d_gt, in_=dbg3[:, :]), 16)
                        ctx.dg.inc(g.dma_start(out=d_L, in_=Lb[:, :]), 16)
                        ctx.dg.inc(g.dma_start(out=d_cols, in_=cols[:, :]), 16)
                    ctx.bar()

        with (
            nc.psum_tensor(_nm("f1_pa"), [P, TC], F32) as pA,
            nc.psum_tensor(_nm("f1_pb"), [P, TC], F32) as pB,
            nc.psum_tensor(_nm("f1_pc"), [P, TC], F32) as pC,
            nc.psum_tensor(_nm("f1_pd"), [P, TC], F32) as pD,
        ):
            ps4 = [pA, pB, pC, pD]

            def u_or_g(mat_ap, dst_sb, act_fn):
                eaE = [ctx.ea.n]

                def prewait(tensor, n):
                    og = n // KC
                    if n == 0:
                        tensor.wait_ge(ctx.ea.h, eaE[0])
                    if n % KC == 0 and og >= 1:
                        tensor.wait_ge(ctx.ea.h, eaE[0] + og * 4)

                def onp(tensor, n, slot):
                    og = n // KC
                    k = n % KC
                    m = None
                    for oc in range(4):
                        for h in range(2):
                            m = tensor.matmul(
                                ps4[oc][:, h * HF:(h + 1) * HF],
                                slot[:, oc * P:(oc + 1) * P],
                                xln[:, k * TC + h * HF:k * TC + (h + 1) * HF],
                                start=(k == 0), stop=(k == KC - 1))
                    return m

                pieces = []
                for og in range(2):
                    for k in range(KC):
                        pieces.append(mat_ap[k * P:(k + 1) * P, og * 512:(og + 1) * 512])
                mmE = ctx.mm.n
                wstream(ctx, pieces, 512, onp, prewait)

                @blk.scalar
                def _(scalar):
                    for og in range(2):
                        scalar.wait_ge(ctx.mm.h, mmE + (og + 1) * KC)
                        for oc in range(4):
                            ctx.ea.inc(scalar.activation(
                                out=dst_sb[:, (og * 4 + oc) * TC:(og * 4 + oc + 1) * TC],
                                in_=ps4[oc][:], func=act_fn), 1)

            for e in range(NEXP):
                mmG = ctx.mm.n

                @blk.tensor
                def _(tensor, e=e):
                    if e == 0:
                        ctx.ea.wait(tensor)
                    else:
                        ctx.ev.wait(tensor)
                    m = None
                    for h in range(2):
                        m = tensor.matmul(pA[:, h * HF:(h + 1) * HF],
                                          oh[:, e * P:(e + 1) * P],
                                          gt[0:E, h * HF:(h + 1) * HF],
                                          start=True, stop=True)
                    ctx.mm.inc(m, 1)

                @blk.scalar
                def _(scalar, e=e):
                    ctx.mm.wait(scalar)
                    ctx.ea.inc(scalar.copy(out=gbs[:], in_=pA[:]), 1)

                u_or_g(Wu[e], usb, AF.Silu)
                u_or_g(Wg[e], gsb, AF.Copy)

                @blk.vector
                def _(vector, e=e):
                    ctx.ea.wait(vector)
                    if os.environ.get("K_DBG_L1F_NOH"):
                        ctx.ev.inc(vector.tensor_copy(out=ctx.dummy[:, 3:4],
                                                      in_=ctx.dummy[:, 0:1]), 1)
                        return
                    for rc in range(NT):
                        vector.tensor_tensor(out=usb[:, rc * TC:(rc + 1) * TC],
                                             in0=usb[:, rc * TC:(rc + 1) * TC],
                                             in1=gsb[:, rc * TC:(rc + 1) * TC],
                                             op=ALU.mult)
                        ctx.ev.inc(vector.tensor_tensor(
                            out=usb[:, rc * TC:(rc + 1) * TC],
                            in0=usb[:, rc * TC:(rc + 1) * TC],
                            in1=gbs[:], op=ALU.mult), 1)

                evY0 = ctx.ev.n

                def prewaity(tensor, n, evY0=evY0):
                    og2 = n // NT
                    if n % NT == 0:
                        if og2 == 0:
                            tensor.wait_ge(ctx.ev.h, evY0)
                        else:
                            tensor.wait_ge(ctx.ev.h, evY0 + og2 * 4)

                def onpy(tensor, n, slot):
                    og2 = n // NT
                    rk = n % NT
                    m = None
                    for oc in range(4):
                        for h in range(2):
                            m = tensor.matmul(
                                ps4[oc][:, h * HF:(h + 1) * HF],
                                slot[:, oc * P:(oc + 1) * P],
                                usb[:, rk * TC + h * HF:rk * TC + (h + 1) * HF],
                                start=(rk == 0), stop=(rk == NT - 1))
                    return m

                piecesy = []
                for og2 in range(4):
                    for rk in range(NT):
                        piecesy.append(Wd[e][rk * P:(rk + 1) * P,
                                             og2 * 512:(og2 + 1) * 512])
                mmY = ctx.mm.n
                wstream(ctx, piecesy, 512, onpy, prewaity)

                @blk.vector
                def _(vector, e=e, mmY=mmY):
                    for og2 in range(4):
                        vector.wait_ge(ctx.mm.h, mmY + (og2 + 1) * NT)
                        for oc in range(4):
                            dst = ctx.xT[:, (og2 * 4 + oc) * TC:(og2 * 4 + oc + 1) * TC]
                            if e == 0:
                                ctx.ev.inc(vector.tensor_copy(out=dst,
                                                              in_=ps4[oc][:]), 1)
                            else:
                                ctx.ev.inc(vector.tensor_tensor(
                                    out=dst, in0=dst, in1=ps4[oc][:], op=ALU.add), 1)

            def s_u_or_g(mat_ap, act_fn, second):
                eaE = [ctx.ea.n]
                ev_last = ctx.ev.n

                def prewait(tensor, n):
                    og = n // KC
                    if n == 0:
                        tensor.wait_ge(ctx.ea.h, eaE[0])
                        if not second:
                            tensor.wait_ge(ctx.ev.h, ev_last)
                    if n % KC == 0 and og >= 1:
                        tensor.wait_ge(ctx.ea.h, eaE[0] + og * 4)

                def onp(tensor, n, slot):
                    og = n // KC
                    k = n % KC
                    m = None
                    for oc in range(4):
                        for h in range(2):
                            m = tensor.matmul(
                                ps4[oc][:, h * HF:(h + 1) * HF],
                                slot[:, oc * P:(oc + 1) * P],
                                xln[:, k * TC + h * HF:k * TC + (h + 1) * HF],
                                start=(k == 0), stop=(k == KC - 1))
                    return m

                pieces = []
                for og in range(4):
                    for k in range(KC):
                        pieces.append(mat_ap[k * P:(k + 1) * P, og * 512:(og + 1) * 512])
                mmE = ctx.mm.n
                wstream(ctx, pieces, 512, onp, prewait)

                @blk.scalar
                def _(scalar):
                    for og in range(4):
                        scalar.wait_ge(ctx.mm.h, mmE + (og + 1) * KC)
                        for oc in range(4):
                            occ = og * 4 + oc
                            if second:
                                dst = (usb if occ < NT else gsb)[
                                    :, (occ % NT) * TC:(occ % NT + 1) * TC]
                            else:
                                dst = ssb[:, occ * TC:(occ + 1) * TC]
                            ctx.ea.inc(scalar.activation(out=dst, in_=ps4[oc][:],
                                                         func=act_fn), 1)

            if not NOSHARED:
                s_u_or_g(Su[0], AF.Silu, False)
                s_u_or_g(Sg[0], AF.Copy, True)

            if not NOSHARED:
                @blk.vector
                def _(vector):
                    ctx.ea.wait(vector)
                    for rc in range(KC):
                        src2 = (usb if rc < NT else gsb)[:, (rc % NT) * TC:(rc % NT + 1) * TC]
                        ctx.ev.inc(vector.tensor_tensor(
                            out=ssb[:, rc * TC:(rc + 1) * TC],
                            in0=ssb[:, rc * TC:(rc + 1) * TC],
                            in1=src2, op=ALU.mult), 1)

            evY0s = ctx.ev.n

            def prewaitsy(tensor, n):
                og2 = n // KC
                if n % KC == 0:
                    if og2 == 0:
                        tensor.wait_ge(ctx.ev.h, evY0s)
                    else:
                        tensor.wait_ge(ctx.ev.h, evY0s + og2 * 4)

            def onpsy(tensor, n, slot):
                og2 = n // KC
                rk = n % KC
                m = None
                for oc in range(4):
                    for h in range(2):
                        m = tensor.matmul(
                            ps4[oc][:, h * HF:(h + 1) * HF],
                            slot[:, oc * P:(oc + 1) * P],
                            ssb[:, rk * TC + h * HF:rk * TC + (h + 1) * HF],
                            start=(rk == 0), stop=(rk == KC - 1))
                return m

            if not NOSHARED:
                piecessy = []
                for og2 in range(4):
                    for rk in range(KC):
                        piecessy.append(Sd[0][rk * P:(rk + 1) * P, og2 * 512:(og2 + 1) * 512])
                mmSY = ctx.mm.n
                wstream(ctx, piecessy, 512, onpsy, prewaitsy)

                @blk.vector
                def _(vector, mmSY=mmSY):
                    for og2 in range(4):
                        vector.wait_ge(ctx.mm.h, mmSY + (og2 + 1) * KC)
                        for oc in range(4):
                            dst = ctx.xT[:, (og2 * 4 + oc) * TC:(og2 * 4 + oc + 1) * TC]
                            ctx.ev.inc(vector.tensor_tensor(out=dst, in0=dst,
                                                            in1=ps4[oc][:], op=ALU.add), 1)

            ctx.bar()

    emit_ln(ctx, lambda q: ctx.xT[:, q * TC:(q + 1) * TC], "residual",
            xold_dram=ctx.xspill)


def emit_l1_attn(ctx, W):
    nc, blk = ctx.nc, ctx.block
    whg = W["L1_attn_w_hg"]
    wout = W["L1_attn_w_out"]
    NQ = 24

    esx = contextlib.ExitStack()
    ctx.xln = esx.enter_context(nc.sbuf_tensor(_nm("xln"), [P, KC * TC], BF16))
    xln = ctx.xln
    emit_ln(ctx, lambda q: ctx.xT[:, q * TC:(q + 1) * TC], "xln")
    emit_spill_x(ctx)

    pspill = nc.dram_tensor(_nm("pspill"), [P, NQ * TC], BF16)
    ccin = nc.dram_tensor(_nm("cc_h_in"), [2, 3072], F32)
    ccout = nc.dram_tensor(_nm("cc_h_out"), [16, 3072], F32)

    with (
        nc.sbuf_tensor(_nm("a1_h"), [P, NQ * TC], BF16) as hbuf,
        nc.sbuf_tensor(_nm("a1_pst"), [P, TC], BF16) as pst,
        nc.sbuf_tensor(_nm("a1_pst2"), [P, 2 * TC], BF16) as pst2,
        nc.sbuf_tensor(_nm("a1_s1"), [P, TC], F32) as s1,
        nc.sbuf_tensor(_nm("a1_s2"), [P, TC], F32) as s2,
        nc.sbuf_tensor(_nm("a1_az"), [P, 2 * TC], F32) as az,
        nc.sbuf_tensor(_nm("a1_zero"), [P, TC], F32) as zeros,
        nc.sbuf_tensor(_nm("a1_stg"), [P, 48], F32) as stg,
        nc.sbuf_tensor(_nm("a1_stgT"), [48, P], F32) as stgT,
        nc.sbuf_tensor(_nm("a1_he"), [8, 3072], BF16) as he,
        nc.sbuf_tensor(_nm("a1_pt"), [8, 3072], BF16) as pt8,
        nc.sbuf_tensor(_nm("a1_w6"), [8, 3072], BF16) as w6,
        nc.sbuf_tensor(_nm("a1_tm"), [8, 8], BF16) as tm,
        nc.sbuf_tensor(_nm("a1_tb"), [8, 1], F32) as tb,
        nc.sbuf_tensor(_nm("a1_hc"), [P, NQ], F32) as hcols,
        nc.psum_tensor(_nm("a1_p0"), [P, TC], F32) as pA,
        nc.psum_tensor(_nm("a1_p1"), [P, TC], F32) as pB,
        nc.psum_tensor(_nm("a1_p2"), [P, TC], F32) as pC,
        nc.psum_tensor(_nm("a1_p3"), [P, TC], F32) as pD,
    ):
        ps4 = [pA, pB, pC, pD]
        a_sb = az[:, 0:TC]

        @blk.vector
        def _(vector):
            ctx.ev.inc(vector.memset(zeros[:, :], 0.0), 1)

        @blk.gpsimd
        def _(g):
            ctx.dg.inc(g.dma_start(out=tm[:, :], in_=W["tmat"]), 16)

        @blk.sync
        def _(sync):
            ctx.ds.inc(sync.dma_start(out=tb[:, :], in_=W["tbias"]), 16)

        dg_spill0 = ctx.dg.n
        for pg in range(12):
            q0 = 2 * pg
            pieces = []
            for k in range(KC):
                pieces.append(whg[k * P:(k + 1) * P, q0 * P:(q0 + 2) * P])
                pieces.append(whg[k * P:(k + 1) * P, (24 + q0) * P:(24 + q0 + 2) * P])
            eaB = ctx.ea.n
            evB = ctx.ev.n

            def prewait(tensor, n, eaB=eaB, evB=evB):
                if n == 0:
                    tensor.wait_ge(ctx.ea.h, eaB)
                    tensor.wait_ge(ctx.ev.h, evB)

            def onp(tensor, n, slot, pg=pg):
                k = n // 2
                hid = (n % 2 == 0)
                m = None
                for j in range(2):
                    tgt = ps4[j] if hid else ps4[2 + j]
                    for h in range(2):
                        m = tensor.matmul(
                            tgt[:, h * HF:(h + 1) * HF],
                            slot[:, j * P:(j + 1) * P],
                            xln[:, k * TC + h * HF:k * TC + (h + 1) * HF],
                            start=(k == 0), stop=(k == KC - 1))
                return m

            wstream(ctx, pieces, 256, onp, prewait)

            for j in range(2):
                q = q0 + j
                hidp, gatep = ps4[j], ps4[2 + j]
                mmB = ctx.mm.n

                @blk.scalar
                def _(scalar, q=q, hidp=hidp, gatep=gatep, mmB=mmB, j=j):
                    scalar.wait_ge(ctx.mm.h, mmB)
                    scalar.wait_ge(ctx.ev.h, ctx.ev.n)
                    scalar.activation(out=az[:, 0:TC], in_=gatep[:], func=AF.Sigmoid,
                                      scale=-1.0)
                    scalar.activation(out=az[:, TC:2 * TC], in_=gatep[:],
                                      func=AF.Sigmoid)
                    ctx.ea.inc(scalar.activation(out=s1[:], in_=hidp[:],
                                                 func=AF.Relu), 1)

                @blk.vector
                def _(vector, q=q, hidp=hidp):
                    ctx.ea.wait(vector)
                    ctx.ev.inc(vector.tensor_scalar(out=s2[:], in0=hidp[:], scalar1=0.0,
                                                    scalar2=None, op0=ALU.min), 1)

                @blk.scalar
                def _(scalar, q=q):
                    ctx.ev.wait(scalar)
                    ctx.ea.inc(scalar.activation(out=s2[:], in_=s2[:],
                                                 func=AF.Sigmoid), 1)

                @blk.vector
                def _(vector, q=q):
                    ctx.ea.wait(vector)
                    vector.tensor_tensor(out=s1[:], in0=s1[:], in1=s2[:], op=ALU.add)
                    vector.tensor_tensor(out=s2[:], in0=az[:, TC:2 * TC], in1=s1[:],
                                         op=ALU.mult)
                    ctx.vgap(vector)
                    vector.tensor_tensor_scan(out=hbuf[:, q * TC:(q + 1) * TC],
                                              data0=a_sb, data1=s2[:], initial=0.0,
                                              op0=ALU.mult, op1=ALU.add)
                    ctx.vgap(vector)
                    vector.wait_ge(ctx.dg.h, ctx.dg.n)
                    vector.tensor_tensor_scan(out=pst[:, :], data0=a_sb, data1=zeros[:],
                                              initial=1.0, op0=ALU.mult, op1=ALU.add)
                    ctx.vgap(vector)
                    vector.tensor_copy(out=stg[:, q:q + 1],
                                       in_=hbuf[:, q * TC + TC - 1:q * TC + TC])
                    ctx.ev.inc(vector.tensor_copy(out=stg[:, 24 + q:25 + q],
                                                  in_=pst[:, TC - 1:TC]), 1)

                @blk.gpsimd
                def _(g, q=q):
                    ctx.ev.wait(g)
                    ctx.dg.inc(g.dma_start(out=pspill.ap()[:, q * TC:(q + 1) * TC],
                                           in_=pst[:, :]), 16)

        mmX = ctx.mm.n

        @blk.tensor
        def _(tensor):
            ctx.ev.wait(tensor)
            ctx.mm.inc(tensor.transpose(pA[0:48, 0:P], stg[:, :],
                                        ctx.bufs["ID_F32"]), 1)

        @blk.scalar
        def _(scalar):
            ctx.mm.wait(scalar)
            ctx.ea.inc(scalar.copy(out=stgT[:, :], in_=pA[0:48, 0:P]), 1)

        @blk.sync
        def _(sync):
            ctx.ea.wait(sync)
            ctx.ds.inc(sync.dma_start(
                out=ccin.ap()[0:1, :].rearrange("o (q p) -> (o q) p", p=P),
                in_=stgT[0:24, :]), 16)
            ctx.ds.inc(sync.dma_start(
                out=ccin.ap()[1:2, :].rearrange("o (q p) -> (o q) p", p=P),
                in_=stgT[24:48, :]), 16)

        @blk.gpsimd
        def _(g):
            ctx.ds.wait(g)
            ctx.cc.inc(g.collective_compute(
                "AllGather", ALU.bypass,
                replica_groups=[[0, 1, 2, 3, 4, 5, 6, 7]],
                ins=[ccin.ap()], outs=[ccout.ap()]), 1)

        @blk.gpsimd
        def _(g):
            ctx.cc.wait(g)
            cov = ccout.ap().rearrange("(j two) c -> two j c", two=2)
            ctx.dg.inc(g.dma_start(out=he[:, :], in_=cov[0]), 16)
            ctx.dg.inc(g.dma_start(out=pt8[:, :], in_=cov[1]), 16)

        @blk.gpsimd
        def _(g):
            ctx.dg.wait(g)
            ctx.ep.inc(g.tensor_scalar(out=pt8[:, :], in0=pt8[:, :], scalar1=1e-38,
                                       scalar2=None, op0=ALU.max), 1)

        @blk.scalar
        def _(scalar):
            ctx.ep.wait(scalar)
            ctx.ea.inc(scalar.activation(out=pt8[:, :], in_=pt8[:, :], func=AF.Ln), 1)

        mmW = ctx.mm.n
        eaW0 = ctx.ea.n

        @blk.tensor
        def _(tensor):
            ctx.ea.wait(tensor)
            for sl6 in range(6):
                if sl6 >= 2:
                    tensor.wait_ge(ctx.ea.h, eaW0 + sl6 - 1)
                ctx.mm.inc(tensor.matmul(pB[0:8, (sl6 % 2) * HF:(sl6 % 2 + 1) * HF],
                                         tm[:, :], pt8[:, sl6 * HF:(sl6 + 1) * HF],
                                         start=True, stop=True), 1)

        @blk.scalar
        def _(scalar):
            for sl6 in range(6):
                scalar.wait_ge(ctx.mm.h, mmW + sl6 + 1)
                ctx.ea.inc(scalar.activation(out=w6[:, sl6 * HF:(sl6 + 1) * HF],
                                             in_=pB[0:8, (sl6 % 2) * HF:(sl6 % 2 + 1) * HF],
                                             func=AF.Exp, bias=tb[:, 0:1]), 1)

        @blk.gpsimd
        def _(g):
            ctx.ea.wait(g)
            ctx.ep.inc(g.tensor_tensor(out=w6[:, :], in0=w6[:, :], in1=he[:, :],
                                       op=ALU.mult), 1)

        mmH = ctx.mm.n

        @blk.tensor
        def _(tensor):
            ctx.ep.wait(tensor)
            for q in range(NQ):
                ctx.mm.inc(tensor.matmul(pC[:, q:q + 1],
                                         w6[:, q * P:(q + 1) * P],
                                         ctx.bufs["ONES1B"][0:8, 0:1],
                                         start=True, stop=True), 1)

        @blk.scalar
        def _(scalar):
            scalar.wait_ge(ctx.mm.h, mmH + NQ)
            ctx.ea.inc(scalar.copy(out=hcols[:, :], in_=pC[:, 0:NQ]), 1)

        dsc1 = ctx.dsc.n
        ev1 = ctx.ev.n

        @blk.scalar
        def _(scalar):
            ctx.dg.wait(scalar)
            for q in range(NQ):
                sl = q % 2
                if q >= 2:
                    scalar.wait_ge(ctx.ev.h, ev1 + q - 1)
                ctx.dsc.inc(scalar.dma_start(
                    out=pst2[:, sl * TC:(sl + 1) * TC],
                    in_=pspill.ap()[:, q * TC:(q + 1) * TC]), 16)

        @blk.vector
        def _(vector):
            ctx.ea.wait(vector)
            for q in range(NQ):
                sl = q % 2
                vector.wait_ge(ctx.dsc.h, dsc1 + 16 * (q + 1))
                ctx.ev.inc(vector.scalar_tensor_tensor(
                    out=hbuf[:, q * TC:(q + 1) * TC], in0=pst2[:, sl * TC:(sl + 1) * TC],
                    scalar=hcols[:, q:q + 1],
                    in1=hbuf[:, q * TC:(q + 1) * TC],
                    op0=ALU.mult, op1=ALU.add), 1)

        ctx.bar()

        for og2 in range(4):
            pieces = [wout[rk * P:(rk + 1) * P, og2 * 512:(og2 + 1) * 512]
                      for rk in range(NQ)]
            eaW = ctx.ea.n

            def prewait(tensor, n, eaW=eaW):
                if n == 0:
                    tensor.wait_ge(ctx.ea.h, eaW)

            def onp(tensor, n, slot):
                rk = n
                m = None
                for oc in range(4):
                    for h in range(2):
                        m = tensor.matmul(
                            ps4[oc][:, h * HF:(h + 1) * HF],
                            slot[:, oc * P:(oc + 1) * P],
                            hbuf[:, rk * TC + h * HF:rk * TC + (h + 1) * HF],
                            start=(rk == 0), stop=(rk == NQ - 1))
                return m

            mmWo = ctx.mm.n
            wstream(ctx, pieces, 512, onp, prewait)

            @blk.scalar
            def _(scalar, og2=og2, mmWo=mmWo):
                scalar.wait_ge(ctx.mm.h, mmWo + NQ)
                for oc in range(4):
                    ctx.ea.inc(scalar.copy(
                        out=ctx.xT[:, (og2 * 4 + oc) * TC:(og2 * 4 + oc + 1) * TC],
                        in_=ps4[oc][:]), 1)

        ctx.bar()

    esx.close()
    emit_ln(ctx, lambda q: ctx.xT[:, q * TC:(q + 1) * TC], "residual",
            xold_dram=ctx.xspill)


def emit_l0_attn(ctx, W):
    nc, blk = ctx.nc, ctx.block
    NH = 16

    qspill = nc.dram_tensor(_nm("qspill"), [D, TC], BF16)
    ospill = nc.dram_tensor(_nm("ospill"), [D, TC], BF16)
    kspill = nc.dram_tensor(_nm("kspill"), [D, TC], BF16)
    ccs_in = nc.dram_tensor(_nm("ccs_in"), [P, D], F32)
    ccs_out = nc.dram_tensor(_nm("ccs_out"), [NCORE * P, D], F32)

    cs_np = np.zeros((P, P), dtype=np.float32)
    for s_ in range(P):
        for c_ in range(P):
            cs_np[s_, c_] = (c_ - s_) if c_ >= s_ else 1e9
    c_cs = nc.inline_tensor(cs_np, _nm("c_cs"))
    c127_np = (127.0 - np.arange(P, dtype=np.float32))[:, None]
    c_127 = nc.inline_tensor(c127_np, _nm("c_127"))
    cqp_np = np.zeros((P, 8), dtype=np.float32)
    ckp_np = np.zeros((P, 8), dtype=np.float32)
    for i_ in range(8):
        cqp_np[:, i_] = i_ * P + np.arange(P) + 1.0
        ckp_np[:, i_] = 1023.0 - i_ * P - np.arange(P)
    c_qp = nc.inline_tensor(cqp_np, _nm("c_qp"))
    c_kp = nc.inline_tensor(ckp_np, _nm("c_kp"))
    ej_np = np.zeros((8, 8 * P), dtype=np.float32)
    for j_ in range(8):
        ej_np[j_, j_ * P:(j_ + 1) * P] = 1.0
    c_ej = nc.inline_tensor(ej_np, _nm("c_ej"))

    with (
        nc.sbuf_tensor(_nm("a0_v"), [P, NT * D], BF16) as V,
        nc.sbuf_tensor(_nm("a0_g1"), [P, TC], BF16) as g1,
        nc.sbuf_tensor(_nm("a0_wg2"), [P, D], BF16) as wg2,
        nc.sbuf_tensor(_nm("a0_ldr"), [1, 16], F32) as ldr,
        nc.sbuf_tensor(_nm("a0_msel"), [8, 1], F32) as msel,
        nc.sbuf_tensor(_nm("a0_ej"), [8, 8 * P], F32) as ejs,
        nc.sbuf_tensor(_nm("a0_nwr"), [1, D], F32) as nwr,
    ):
        with (
            nc.sbuf_tensor(_nm("xln"), [P, KC * TC], BF16) as xln,
            nc.sbuf_tensor(_nm("a0_stg"), [P, 2 * TC], BF16) as stg,
        ):
            ctx.xln = xln
            emit_ln(ctx, lambda q: ctx.xT[:, q * TC:(q + 1) * TC], "xln")
            emit_spill_x(ctx)
            esp = contextlib.ExitStack()
            ps4 = [esp.enter_context(nc.psum_tensor(_nm(f"a0_pp{i}"), [P, TC], F32))
                   for i in range(4)]
            pp0 = ps4[0]

            @blk.gpsimd
            def _(g):
                ctx.dg.inc(g.dma_start(out=wg2[:, :], in_=W["L0_attn_Wg2"]), 16)

            @blk.sync
            def _(sync):
                ctx.ds.inc(sync.dma_start(
                    out=ldr[0:1, :],
                    in_=W["L0_attn_log_decay"].rearrange("(o e) -> o e", o=1)), 16)
                ctx.ds.inc(sync.dma_start(out=msel[:, :], in_=W["msel"]), 16)
                ctx.ds.inc(sync.dma_start(out=ejs[:, :], in_=c_ej.ap()), 16)
                ctx.ds.inc(sync.dma_start(
                    out=nwr[0:1, :],
                    in_=W["L0_attn_norm_w"].rearrange("(o e) -> o e", o=1)), 16)

            def qk_proj(mat_ap, spill):
                eaP = [ctx.ea.n]
                dscP = [ctx.dsc.n]

                def prewait(tensor, n, eaP=eaP):
                    og = n // KC
                    if n == 0:
                        tensor.wait_ge(ctx.ea.h, eaP[0])
                    if n % KC == 0 and og >= 1:
                        tensor.wait_ge(ctx.ea.h, eaP[0] + og * 4)

                def onp(tensor, n, slot):
                    k = n % KC
                    m = None
                    for oc in range(4):
                        for h in range(2):
                            m = tensor.matmul(
                                ps4[oc][:, h * HF:(h + 1) * HF],
                                slot[:, oc * P:(oc + 1) * P],
                                xln[:, k * TC + h * HF:k * TC + (h + 1) * HF],
                                start=(k == 0), stop=(k == KC - 1))
                    return m

                pieces = []
                for og in range(4):
                    for k in range(KC):
                        pieces.append(mat_ap[k * P:(k + 1) * P, og * 512:(og + 1) * 512])
                mmP = ctx.mm.n
                wstream(ctx, pieces, 512, onp, prewait)

                eaQ0 = [0]

                @blk.scalar
                def _(scalar):
                    nd0 = dscP[0]
                    eaQ0[0] = ctx.ea.n
                    for og in range(4):
                        scalar.wait_ge(ctx.mm.h, mmP + (og + 1) * KC)
                        for oc in range(4):
                            hh = og * 4 + oc
                            sl = hh % 2
                            if hh >= 2:
                                scalar.wait_ge(ctx.dsc.h, nd0 + 16 * (hh - 1))
                            ctx.ea.inc(scalar.activation(
                                out=stg[:, sl * TC:(sl + 1) * TC],
                                in_=ps4[oc][:], func=AF.Silu), 1)

                @blk.sync
                def _(sync):
                    for hh in range(16):
                        sl = hh % 2
                        sync.wait_ge(ctx.ea.h, eaQ0[0] + hh + 1)
                        ctx.dsc.inc(sync.dma_start(
                            out=spill.ap()[hh * P:(hh + 1) * P, :],
                            in_=stg[:, sl * TC:(sl + 1) * TC]), 16)

            qk_proj(W["L0_attn_Wq"], qspill)
            qk_proj(W["L0_attn_Wk"], kspill)

            eaV = [ctx.ea.n]

            def prewaitv(tensor, n):
                og = n // KC
                if n == 0:
                    tensor.wait_ge(ctx.ea.h, eaV[0])
                if n % KC == 0 and og >= 1:
                    tensor.wait_ge(ctx.ea.h, eaV[0] + og * 8)

            def onpv(tensor, n, slot):
                k = n % KC
                m = None
                for t in range(NT):
                    tgt = ps4[t % 4][:, (t // 4) * HF:(t // 4 + 1) * HF]
                    m = tensor.matmul(tgt,
                                      xln[:, k * TC + t * P:k * TC + (t + 1) * P],
                                      slot[:, :],
                                      start=(k == 0), stop=(k == KC - 1))
                return m

            piecesv = []
            for og in range(4):
                for k in range(KC):
                    piecesv.append(W["L0_attn_Wv"][k * P:(k + 1) * P,
                                                   og * 512:(og + 1) * 512])
            mmV = ctx.mm.n
            wstream(ctx, piecesv, 512, onpv, prewaitv)

            @blk.scalar
            def _(scalar):
                for og in range(4):
                    scalar.wait_ge(ctx.mm.h, mmV + (og + 1) * KC)
                    for t in range(NT):
                        ctx.ea.inc(scalar.activation(
                            out=V[:, t * D + og * 512:t * D + (og + 1) * 512],
                            in_=ps4[t % 4][:, (t // 4) * HF:(t // 4 + 1) * HF],
                            func=AF.Silu), 1)

            eaG = [ctx.ea.n]

            def prewaitg(tensor, n):
                if n == 0:
                    tensor.wait_ge(ctx.ea.h, eaG[0])

            def onpg(tensor, n, slot):
                k = n
                m = None
                for h in range(2):
                    m = tensor.matmul(pp0[:, h * HF:(h + 1) * HF], slot[:, 0:P],
                                      xln[:, k * TC + h * HF:k * TC + (h + 1) * HF],
                                      start=(k == 0), stop=(k == KC - 1))
                return m

            piecesg = [W["L0_attn_Wg1"][k * P:(k + 1) * P, :] for k in range(KC)]
            mmG = ctx.mm.n
            wstream(ctx, piecesg, 128, onpg, prewaitg)

            @blk.scalar
            def _(scalar):
                scalar.wait_ge(ctx.mm.h, mmG + KC)
                ctx.ea.inc(scalar.copy(out=g1[:, :], in_=pp0[:]), 1)

            ctx.bar()
            esp.close()

        es = contextlib.ExitStack()
        csb = es.enter_context(nc.sbuf_tensor(_nm("a0_csb"), [P, P], F32))
        cpp = es.enter_context(nc.sbuf_tensor(_nm("a0_cpp"), [P, 17], F32))
        DT = es.enter_context(nc.sbuf_tensor(_nm("a0_dt"), [P, NH * P], BF16))
        ldbc = es.enter_context(nc.sbuf_tensor(_nm("a0_ldbc"), [P, 16], F32))
        KDG = es.enter_context(nc.sbuf_tensor(_nm("a0_kdg"), [P, 8 * 16], F32))
        QDG = es.enter_context(nc.sbuf_tensor(_nm("a0_qdg"), [P, 8 * 16], F32))
        KDL = es.enter_context(nc.sbuf_tensor(_nm("a0_kdl"), [P, 16], F32))
        CD = es.enter_context(nc.sbuf_tensor(_nm("a0_cd"), [P, 16], F32))
        coef8 = es.enter_context(nc.sbuf_tensor(_nm("a0_cf8"), [8, 16], F32))
        COEFB = es.enter_context(nc.sbuf_tensor(_nm("a0_coefb"), [P, 8 * 16], F32))
        NW = es.enter_context(nc.sbuf_tensor(_nm("a0_nw"), [P, D], BF16))
        S_f32 = es.enter_context(nc.sbuf_tensor(_nm("a0_sf"), [P, D], F32))
        S_bf = es.enter_context(nc.sbuf_tensor(_nm("a0_sb"), [P, D], BF16))
        og = es.enter_context(nc.sbuf_tensor(_nm("a0_og"), [P, D], F32))
        gs = es.enter_context(nc.sbuf_tensor(_nm("a0_gs"), [P, D], F32))
        ks2 = es.enter_context(nc.sbuf_tensor(_nm("a0_ks"), [P, 2 * D], BF16))
        st2 = es.enter_context(nc.sbuf_tensor(_nm("a0_st2"), [P, 512], BF16))
        rmsb = es.enter_context(nc.sbuf_tensor(_nm("a0_rms"), [P, 33], F32))
        sqtr = es.enter_context(nc.sbuf_tensor(_nm("a0_sqtr"), [P, P], F32))
        og_bf = es.enter_context(nc.sbuf_tensor(_nm("a0_ogbf"), [P, D], BF16))
        ktok = es.enter_context(nc.sbuf_tensor(_nm("a0_ktok"), [P, 2 * D], BF16))
        qti = es.enter_context(nc.sbuf_tensor(_nm("a0_qti"), [P, 2 * D], BF16))
        kti = es.enter_context(nc.sbuf_tensor(_nm("a0_kti"), [P, 2 * D], BF16))
        onTi = es.enter_context(nc.sbuf_tensor(_nm("a0_onti"), [P, D], BF16))
        if True:
            with (
                nc.psum_tensor(_nm("a0_pc0"), [P, TC], F32) as pc0,
                nc.psum_tensor(_nm("a0_pc1"), [P, TC], F32) as pc1,
            ):
                mmC = ctx.mm.n

                @blk.sync
                def _(sync):
                    ctx.ds.inc(sync.dma_start(out=csb[:, :], in_=c_cs.ap()), 16)
                    ctx.ds.inc(sync.dma_start(out=cpp[:, 0:1], in_=c_127.ap()), 16)
                    ctx.ds.inc(sync.dma_start(out=cpp[:, 1:9], in_=c_qp.ap()), 16)
                    ctx.ds.inc(sync.dma_start(out=cpp[:, 9:17], in_=c_kp.ap()), 16)

                eaC = ctx.ea.n

                @blk.tensor
                def _(tensor):
                    ctx.ds.wait(tensor)
                    ctx.mm.inc(tensor.matmul(pc0[:, 0:16], ctx.bufs["ONESROW"],
                                             ldr[0:1, :], start=True, stop=True), 1)
                    for r in range(2):
                        if r == 1:
                            tensor.wait_ge(ctx.ea.h, eaC + 2)
                        for h2 in range(2):
                            h = 2 * r + h2
                            ctx.mm.inc(tensor.matmul(pc1[:, h2 * HF:(h2 + 1) * HF],
                                                     ctx.bufs["ONESROW"],
                                                     nwr[0:1, h * HF:(h + 1) * HF],
                                                     start=True, stop=True), 1)

                @blk.scalar
                def _(scalar):
                    scalar.wait_ge(ctx.mm.h, mmC + 1)
                    ctx.ea.inc(scalar.copy(out=ldbc[:, :], in_=pc0[:, 0:16]), 1)
                    for r in range(2):
                        scalar.wait_ge(ctx.mm.h, mmC + 1 + (r + 1) * 2)
                        ctx.ea.inc(scalar.copy(out=NW[:, r * TC:(r + 1) * TC],
                                               in_=pc1[:]), 1)

                @blk.vector
                def _(vector):
                    ctx.ea.wait(vector)
                    for h in range(NH):
                        vector.tensor_scalar(out=gs[:, h * P:(h + 1) * P], in0=csb[:, :],
                                             scalar1=ldbc[:, h:h + 1], scalar2=None,
                                             op0=ALU.mult)
                    vector.tensor_scalar(out=KDL[:, :], in0=ldbc[:, :],
                                         scalar1=cpp[:, 0:1], scalar2=None, op0=ALU.mult)
                    vector.tensor_scalar(out=CD[:, :], in0=ldbc[:, :], scalar1=128.0,
                                         scalar2=None, op0=ALU.mult)
                    for i in range(8):
                        vector.tensor_scalar(out=QDG[:, i * 16:(i + 1) * 16],
                                             in0=ldbc[:, :], scalar1=cpp[:, 1 + i:2 + i],
                                             scalar2=None, op0=ALU.mult)
                        vector.tensor_scalar(out=KDG[:, i * 16:(i + 1) * 16],
                                             in0=ldbc[:, :], scalar1=cpp[:, 9 + i:10 + i],
                                             scalar2=None, op0=ALU.mult)
                    ctx.ev.inc(vector.tensor_scalar(out=coef8[:, :], in0=ldbc[0:8, :],
                                                    scalar1=msel[:, 0:1], scalar2=None,
                                                    op0=ALU.mult), 1)

                @blk.scalar
                def _(scalar):
                    ctx.ev.wait(scalar)
                    scalar.activation(out=DT[:, :], in_=gs[:, :], func=AF.Exp)
                    scalar.activation(out=KDL[:, :], in_=KDL[:, :], func=AF.Exp)
                    scalar.activation(out=CD[:, :], in_=CD[:, :], func=AF.Exp)
                    scalar.activation(out=QDG[:, :], in_=QDG[:, :], func=AF.Exp)
                    scalar.activation(out=KDG[:, :], in_=KDG[:, :], func=AF.Exp)
                    ctx.ea.inc(scalar.activation(out=coef8[:, :], in_=coef8[:, :],
                                                 func=AF.Exp), 1)

                mmE = ctx.mm.n

                @blk.tensor
                def _(tensor):
                    ctx.ea.wait(tensor)
                    eaEJ = ctx.ea.n
                    for j in range(8):
                        if j >= 2:
                            tensor.wait_ge(ctx.ea.h, eaEJ + j - 1)
                        ctx.mm.inc(tensor.matmul(
                            pc0[:, (j % 2) * 16:(j % 2) * 16 + 16],
                            ejs[:, j * P:(j + 1) * P], coef8[:, :],
                            start=True, stop=True), 1)

                @blk.scalar
                def _(scalar):
                    for j in range(8):
                        scalar.wait_ge(ctx.mm.h, mmE + j + 1)
                        ctx.ea.inc(scalar.copy(out=COEFB[:, j * 16:(j + 1) * 16],
                                               in_=pc0[:, (j % 2) * 16:(j % 2) * 16 + 16]), 1)

                ctx.bar()

            with nc.psum_tensor(_nm("a0_psA"), [P, D], F32) as psS:
                trs = [Sem(nc, _nm("s_tr0")), Sem(nc, _nm("s_tr1"))]
                eaA0 = ctx.ea.n
                mmA0 = ctx.mm.n
                for i in range(NT):
                    sl = i % 2

                    @blk.sync
                    def _(sync, i=i, sl=sl):
                        if i >= 2:
                            sync.wait_ge(ctx.ea.h, eaA0 + (i - 1) * NH)
                        trs[sl].inc(sync.dma_start_transpose(
                            out=ktok[:, sl * D:(sl + 1) * D],
                            in_=kspill.ap()[:, i * P:(i + 1) * P]), 16)

                    @blk.scalar
                    def _(scalar, i=i, sl=sl):
                        trs[sl].wait(scalar)
                        if i >= 2:
                            scalar.wait_ge(ctx.mm.h, mmA0 + i - 1)
                        for h in range(NH):
                            ctx.ea.inc(scalar.activation(
                                out=ks2[:, sl * D + h * P:sl * D + (h + 1) * P],
                                in_=ktok[:, sl * D + h * P:sl * D + (h + 1) * P],
                                func=AF.Copy, scale=KDG[:, i * 16 + h:i * 16 + h + 1]), 1)

                    @blk.tensor
                    def _(tensor, i=i, sl=sl):
                        ctx.ea.wait(tensor)
                        m = None
                        for h in range(NH):
                            m = tensor.matmul(psS[:, h * P:(h + 1) * P],
                                              ks2[:, sl * D + h * P:sl * D + (h + 1) * P],
                                              V[:, i * D + h * P:i * D + (h + 1) * P],
                                              start=(i == 0 and h % 4 == 0),
                                              stop=(i == NT - 1 and h % 4 == 3))
                        ctx.mm.inc(m, 1)

                @blk.scalar
                def _(scalar):
                    ctx.mm.wait(scalar)
                    for h in range(4):
                        ctx.ea.inc(scalar.copy(out=og[:, h * HF:(h + 1) * HF],
                                               in_=psS[:, h * HF:(h + 1) * HF]), 1)

                @blk.sync
                def _(sync):
                    ctx.ea.wait(sync)
                    ctx.ds.inc(sync.dma_start(out=ccs_in.ap(), in_=og[:, :]), 16)

                @blk.gpsimd
                def _(g):
                    ctx.ds.wait(g)
                    ctx.cc.inc(g.collective_compute(
                        "AllGather", ALU.bypass,
                        replica_groups=[[0, 1, 2, 3, 4, 5, 6, 7]],
                        ins=[ccs_in.ap()], outs=[ccs_out.ap()]), 1)

                ctx.bar()

            @blk.vector
            def _(vector):
                ctx.ev.inc(vector.memset(S_f32[:, :], 0.0), 1)

            dscS = ctx.dsc.n
            evS0 = ctx.ev.n
            with nc.sbuf_tensor(_nm("a0_sj"), [P, D], F32) as sj:
                @blk.scalar
                def _(scalar):
                    ctx.cc.wait(scalar)
                    for j in range(8):
                        if j >= 1:
                            scalar.wait_ge(ctx.ev.h, evS0 + j * NH)
                        ctx.dsc.inc(scalar.dma_start(
                            out=sj[:, :],
                            in_=ccs_out.ap()[j * P:(j + 1) * P, :]), 16)

                @blk.vector
                def _(vector):
                    for j in range(8):
                        vector.wait_ge(ctx.dsc.h, dscS + 16 * (j + 1))
                        for h in range(NH):
                            ctx.ev.inc(vector.scalar_tensor_tensor(
                                out=S_f32[:, h * P:(h + 1) * P],
                                in0=sj[:, h * P:(h + 1) * P],
                                scalar=COEFB[:, j * 16 + h:j * 16 + h + 1],
                                in1=S_f32[:, h * P:(h + 1) * P],
                                op0=ALU.mult, op1=ALU.add), 1)

                @blk.scalar
                def _(scalar):
                    ctx.ev.wait(scalar)
                    ctx.ea.inc(scalar.copy(out=S_bf[:, :], in_=S_f32[:, :]), 1)

                ctx.bar()

            with (
                nc.psum_tensor(_nm("a0_pO"), [P, D], F32) as pO,
                nc.psum_tensor(_nm("a0_pst"), [P, 512], F32) as pST,
                nc.psum_tensor(_nm("a0_px"), [P, 512], F32) as pX,
                nc.psum_tensor(_nm("a0_ps2"), [P, 512], F32) as pS2,
            ):
                qrs = [Sem(nc, _nm("s_qr0")), Sem(nc, _nm("s_qr1"))]
                krs = [Sem(nc, _nm("s_kr0")), Sem(nc, _nm("s_kr1"))]
                trb = [Sem(nc, _nm("s_tb0")), Sem(nc, _nm("s_tb1"))]
                for i in range(NT):
                    sl = i % 2
                    mm0 = ctx.mm.n
                    ea0 = ctx.ea.n
                    ev0 = ctx.ev.n

                    @blk.sync
                    def _(sync, i=i, sl=sl):
                        qrs[sl].inc(sync.dma_start(
                            out=qti[:, sl * D:(sl + 1) * D],
                            in_=qspill.ap()
                            .rearrange("(f p) t -> p f t", p=P)[:, :, i * P:(i + 1) * P]), 16)
                        krs[sl].inc(sync.dma_start(
                            out=kti[:, sl * D:(sl + 1) * D],
                            in_=kspill.ap()
                            .rearrange("(f p) t -> p f t", p=P)[:, :, i * P:(i + 1) * P]), 16)
                        trb[sl].inc(sync.dma_start_transpose(
                            out=ktok[:, sl * D:(sl + 1) * D],
                            in_=kspill.ap()[:, i * P:(i + 1) * P]), 16)

                    for gI in range(4):
                        hs0 = gI * 4

                        @blk.tensor
                        def _(tensor, i=i, sl=sl, gI=gI, hs0=hs0, mm0=mm0, ev0=ev0, ea0=ea0):
                            if gI == 0:
                                tensor.wait_ge(qrs[sl].h, qrs[sl].n)
                                tensor.wait_ge(krs[sl].h, krs[sl].n)
                            else:
                                tensor.wait_ge(ctx.ev.h, ctx.ev.n)
                            m = None
                            for hh in range(4):
                                h = hs0 + hh
                                m = tensor.matmul(pST[:, hh * P:(hh + 1) * P],
                                                  kti[:, sl * D + h * P:sl * D + (h + 1) * P],
                                                  qti[:, sl * D + h * P:sl * D + (h + 1) * P],
                                                  start=True, stop=True)
                            ctx.mm.inc(m, 1)

                        @blk.vector
                        def _(vector, i=i, gI=gI, hs0=hs0):
                            ctx.mm.wait(vector)
                            m = None
                            for hh in range(4):
                                h = hs0 + hh
                                m = vector.tensor_tensor(
                                    out=st2[:, hh * P:(hh + 1) * P],
                                    in0=pST[:, hh * P:(hh + 1) * P],
                                    in1=DT[:, h * P:(h + 1) * P], op=ALU.mult)
                            ctx.ev.inc(m, 1)

                        @blk.tensor
                        def _(tensor, i=i, sl=sl, gI=gI, hs0=hs0):
                            ctx.ev.wait(tensor)
                            ctx.ea.wait(tensor)
                            m = None
                            for hh in range(4):
                                h = hs0 + hh
                                tensor.matmul(pO[:, h * P:(h + 1) * P],
                                              st2[:, hh * P:(hh + 1) * P],
                                              V[:, i * D + h * P:i * D + (h + 1) * P],
                                              start=True, stop=True)
                                m = tensor.matmul(pX[:, hh * P:(hh + 1) * P],
                                                  qti[:, sl * D + h * P:sl * D + (h + 1) * P],
                                                  S_bf[:, h * P:(h + 1) * P],
                                                  start=True, stop=True)
                            ctx.mm.inc(m, 1)

                        @blk.scalar
                        def _(scalar, i=i, sl=sl, gI=gI, hs0=hs0):
                            ctx.mm.wait(scalar)
                            if gI == 0:
                                trb[sl].wait(scalar)
                            m = None
                            for hh in range(4):
                                h = hs0 + hh
                                scalar.activation(
                                    out=og[:, h * P:(h + 1) * P],
                                    in_=pX[:, hh * P:(hh + 1) * P], func=AF.Copy,
                                    scale=QDG[:, h:h + 1])
                                m = scalar.activation(
                                    out=ks2[:, sl * D + h * P:sl * D + (h + 1) * P],
                                    in_=ktok[:, sl * D + h * P:sl * D + (h + 1) * P],
                                    func=AF.Copy, scale=KDL[:, h:h + 1])
                            ctx.ea.inc(m, 1)

                        @blk.tensor
                        def _(tensor, i=i, sl=sl, gI=gI, hs0=hs0):
                            ctx.ea.wait(tensor)
                            ctx.ev.wait(tensor)
                            m = None
                            for hh in range(4):
                                h = hs0 + hh
                                m = tensor.matmul(pS2[:, hh * P:(hh + 1) * P],
                                                  ks2[:, sl * D + h * P:sl * D + (h + 1) * P],
                                                  V[:, i * D + h * P:i * D + (h + 1) * P],
                                                  start=True, stop=True)
                            ctx.mm.inc(m, 1)

                        @blk.vector
                        def _(vector, gI=gI, hs0=hs0):
                            ctx.mm.wait(vector)
                            m = None
                            for hh in range(4):
                                h = hs0 + hh
                                m = vector.scalar_tensor_tensor(
                                    out=S_f32[:, h * P:(h + 1) * P],
                                    in0=S_f32[:, h * P:(h + 1) * P],
                                    scalar=CD[:, h:h + 1],
                                    in1=pS2[:, hh * P:(hh + 1) * P],
                                    op0=ALU.mult, op1=ALU.add)
                            ctx.ev.inc(m, 1)

                        @blk.scalar
                        def _(scalar, gI=gI, hs0=hs0):
                            ctx.ev.wait(scalar)
                            ctx.ea.inc(scalar.copy(
                                out=S_bf[:, hs0 * P:(hs0 + 4) * P],
                                in_=S_f32[:, hs0 * P:(hs0 + 4) * P]), 1)

                    @blk.vector
                    def _(vector, i=i):
                        ctx.ea.wait(vector)
                        m = None
                        for h4 in range(4):
                            m = vector.tensor_tensor(
                                out=og[:, h4 * HF:(h4 + 1) * HF],
                                in0=og[:, h4 * HF:(h4 + 1) * HF],
                                in1=pO[:, h4 * HF:(h4 + 1) * HF], op=ALU.add)
                        ctx.ev.inc(m, 1)

                    @blk.tensor
                    def _(tensor, i=i):
                        ctx.ev.wait(tensor)
                        m = None
                        for h4 in range(4):
                            m = tensor.matmul(pO[:, h4 * HF:(h4 + 1) * HF],
                                              g1[:, i * P:(i + 1) * P],
                                              wg2[:, h4 * HF:(h4 + 1) * HF],
                                              start=True, stop=True)
                        ctx.mm.inc(m, 1)

                    @blk.scalar
                    def _(scalar, i=i):
                        ctx.mm.wait(scalar)
                        ctx.ea.inc(scalar.activation(out=gs[:, :], in_=pO[:, :],
                                                     func=AF.Sigmoid), 1)

                    @blk.vector
                    def _(vector, i=i):
                        ctx.ea.wait(vector)
                        ctx.ev.inc(vector.tensor_tensor(out=og[:, :], in0=og[:, :],
                                                        in1=gs[:, :], op=ALU.mult), 1)

                    @blk.scalar
                    def _(scalar, i=i):
                        ctx.ev.wait(scalar)
                        for h in range(NH):
                            scalar.activation(out=sqtr[:, :], in_=og[:, h * P:(h + 1) * P],
                                              func=AF.Square,
                                              accum_out=rmsb[:, h:h + 1])
                        scalar.activation(out=rmsb[:, 16:32], in_=rmsb[:, 0:16],
                                          func=AF.Ln, scale=1.0 / P,
                                          bias=ctx.bufs["EPSCOL"][:, 0:1])
                        ctx.ea.inc(scalar.activation(out=rmsb[:, 16:32],
                                                     in_=rmsb[:, 16:32],
                                                     func=AF.Exp, scale=-0.5), 1)

                    @blk.vector
                    def _(vector, i=i):
                        ctx.ea.wait(vector)
                        for h in range(NH):
                            vector.tensor_scalar(out=og[:, h * P:(h + 1) * P],
                                                 in0=og[:, h * P:(h + 1) * P],
                                                 scalar1=rmsb[:, 16 + h:17 + h],
                                                 scalar2=None, op0=ALU.mult)
                        m = None
                        for h4 in range(4):
                            m = vector.tensor_tensor(out=og[:, h4 * HF:(h4 + 1) * HF],
                                                     in0=og[:, h4 * HF:(h4 + 1) * HF],
                                                     in1=NW[:, h4 * HF:(h4 + 1) * HF],
                                                     op=ALU.mult)
                        ctx.ev.inc(m, 1)

                    @blk.scalar
                    def _(scalar, i=i):
                        ctx.ev.wait(scalar)
                        ctx.ea.inc(scalar.copy(out=og_bf[:, :], in_=og[:, :]), 1)

                    @blk.sync
                    def _(sync, i=i):
                        ctx.ea.wait(sync)
                        for f in range(NH):
                            ctx.ds.inc(sync.dma_start_transpose(
                                out=onTi[:, f * P:(f + 1) * P],
                                in_=og_bf[:, f * P:(f + 1) * P]), 16)

                    @blk.gpsimd
                    def _(g, i=i):
                        ctx.ds.wait(g)
                        ctx.dg.inc(g.dma_start(
                            out=ospill.ap().rearrange("(k p) t -> p k t", p=P)
                            [:, :, i * P:(i + 1) * P],
                            in_=onTi[:, :].rearrange("p (k c) -> p k c", c=P)), 16)

                    ctx.bar()

            onk = es.enter_context(nc.sbuf_tensor(_nm("a0_onk"), [P, 2 * TC], BF16))
            oks = [Sem(nc, _nm("s_ok0")), Sem(nc, _nm("s_ok1"))]
            for og2 in range(4):
                pieces = [W["L0_attn_Wo"][k * P:(k + 1) * P, og2 * 512:(og2 + 1) * 512]
                          for k in range(KC)]
                eaWo = ctx.ea.n
                mmWb = ctx.mm.n

                @blk.sync
                def _(sync, og2=og2, mmWb=mmWb):
                    if og2 == 0:
                        ctx.dg.wait(sync)
                    for k in range(KC):
                        sl = k % 2
                        if k >= 2:
                            sync.wait_ge(ctx.mm.h, mmWb + k - 1)
                        oks[sl].inc(sync.dma_start(
                            out=onk[:, sl * TC:(sl + 1) * TC],
                            in_=ospill.ap()[k * P:(k + 1) * P, :]), 16)

                def prewait(tensor, n, eaWo=eaWo, og2=og2):
                    sl = n % 2
                    tensor.wait_ge(oks[sl].h, oks[sl].n - 16 * (KC // 2 - 1 - n // 2))
                    if n == 0:
                        tensor.wait_ge(ctx.ea.h, eaWo)

                def onp(tensor, n, slot):
                    k = n
                    sl = k % 2
                    m = None
                    for oc in range(4):
                        for h in range(2):
                            m = tensor.matmul(
                                psW4[oc][:, h * HF:(h + 1) * HF],
                                slot[:, oc * P:(oc + 1) * P],
                                onk[:, sl * TC + h * HF:sl * TC + (h + 1) * HF],
                                start=(k == 0), stop=(k == KC - 1))
                    return m

                with (
                    nc.psum_tensor(_nm("a0_pw0"), [P, TC], F32) as pw0,
                    nc.psum_tensor(_nm("a0_pw1"), [P, TC], F32) as pw1,
                    nc.psum_tensor(_nm("a0_pw2"), [P, TC], F32) as pw2,
                    nc.psum_tensor(_nm("a0_pw3"), [P, TC], F32) as pw3,
                ):
                    psW4 = [pw0, pw1, pw2, pw3]
                    mmWo = ctx.mm.n
                    wstream(ctx, pieces, 512, onp, prewait)

                    @blk.scalar
                    def _(scalar, og2=og2, mmWo=mmWo):
                        ctx.dg.wait(scalar)
                        scalar.wait_ge(ctx.mm.h, mmWo + KC)
                        for oc in range(4):
                            ctx.ea.inc(scalar.copy(
                                out=ctx.xT[:, (og2 * 4 + oc) * TC:(og2 * 4 + oc + 1) * TC],
                                in_=psW4[oc][:]), 1)

                    ctx.bar()

            es.close()

    if os.environ.get("K_DBG_DUMP"):
        d_os = nc.dram_tensor("dbg_ospill", [D, TC], BF16, kind="ExternalOutput").ap()
        d_qs = nc.dram_tensor("dbg_qspill", [D, TC], BF16, kind="ExternalOutput").ap()
        d_cc = nc.dram_tensor("dbg_ccsin", [P, D], F32, kind="ExternalOutput").ap()
        d_cco = nc.dram_tensor("dbg_ccsout", [NCORE * P, D], F32, kind="ExternalOutput").ap()
        d_y = nc.dram_tensor("dbg_y", [P, KC * TC], F32, kind="ExternalOutput").ap()

        @blk.gpsimd
        def _(g):
            ctx.dg.inc(g.dma_start(out=d_os, in_=ospill.ap()), 16)
            ctx.dg.inc(g.dma_start(out=d_qs, in_=qspill.ap()), 16)
            ctx.dg.inc(g.dma_start(out=d_cc, in_=ccs_in.ap()), 16)
            ctx.dg.inc(g.dma_start(out=d_cco, in_=ccs_out.ap()), 16)
            ctx.dg.inc(g.dma_start(out=d_y, in_=ctx.xT[:, :]), 16)
        ctx.bar()

    emit_ln(ctx, lambda q: ctx.xT[:, q * TC:(q + 1) * TC], "residual",
            xold_dram=ctx.xspill)


EMITTERS = {"l0a": emit_l0_attn, "l0f": emit_l0_ffn, "l1a": emit_l1_attn,
            "l1f": emit_l1_ffn}


# version 02
# speedup vs baseline: 1.0244x; 1.0244x over previous
import contextlib
import os
import numpy as np
import ml_dtypes
import concourse.bass as bass
import concourse.mybir as mybir
from concourse.bass_utils import run_bass_kernel_spmd

F32 = mybir.dt.float32
BF16 = mybir.dt.bfloat16
AF = mybir.ActivationFunctionType
ALU = mybir.AluOpType
AX = mybir.AxisListType

P = 128
D = 2048
KC = D // P
TC = 1024
NT = TC // P
NCORE = 8
HF = 512
EPS = 1e-5

BF = ml_dtypes.bfloat16

_uid = [0]


def _nm(s):
    _uid[0] += 1
    return f"{s}_{_uid[0]}"


class Sem:

    def __init__(self, nc, name):
        self.h = nc.alloc_semaphore(name)
        self.n = 0

    def inc(self, instr, k=1):
        instr.then_inc(self.h, k)
        self.n += k
        return instr

    def wait(self, eng, val=None):
        eng.wait_ge(self.h, self.n if val is None else val)


class Ctx:
    def __init__(self, nc, block):
        self.nc = nc
        self.block = block
        self.mm = Sem(nc, "s_mm")
        self.ea = Sem(nc, "s_ea")
        self.ev = Sem(nc, "s_ev")
        self.ep = Sem(nc, "s_ep")
        self.ds = Sem(nc, "s_ds")
        self.dsc = Sem(nc, "s_dsc")
        self.dg = Sem(nc, "s_dg")
        self.wr = [Sem(nc, f"s_wr{i}") for i in range(3)]
        self.wring_free = [0, 0, 0]
        self.xr = [Sem(nc, f"s_xr{i}") for i in range(2)]
        self.cc = Sem(nc, "s_cc")
        self.barsem = Sem(nc, "s_bar")
        self.bufs = {}

    def fresh_dma_sems(self, tag):
        nc = self.nc
        self.ds = Sem(nc, _nm(f"ds_{tag}"))
        self.dsc = Sem(nc, _nm(f"dsc_{tag}"))
        self.dg = Sem(nc, _nm(f"dg_{tag}"))
        self.wr = [Sem(nc, _nm(f"wr{i}_{tag}")) for i in range(3)]
        self.wring_free = [0, 0, 0]
        self.cc = Sem(nc, _nm(f"cc_{tag}"))

    def vgap(self, vector):
        vector.tensor_copy(out=self.dummy[:, 8:16], in_=self.dummy[:, 0:8])
        vector.tensor_copy(out=self.dummy[:, 16:24], in_=self.dummy[:, 0:8])

    def bar(self):
        b = self.barsem
        target = b.n + 5
        blk = self.block

        @blk.sync
        def _(e):
            e.drain().then_inc(b.h, 1)
            e.wait_ge(b.h, target)

        @blk.tensor
        def _(e):
            e.drain().then_inc(b.h, 1)
            e.wait_ge(b.h, target)

        @blk.vector
        def _(e):
            e.drain().then_inc(b.h, 1)
            e.wait_ge(b.h, target)

        @blk.scalar
        def _(e):
            e.drain().then_inc(b.h, 1)
            e.wait_ge(b.h, target)

        @blk.gpsimd
        def _(e):
            e.drain().then_inc(b.h, 1)
            e.wait_ge(b.h, target)

        b.n = target


def _np_bf(x):
    return np.asarray(x, dtype=BF)



def emit_prologue(ctx, xin, consts):
    nc, blk = ctx.nc, ctx.block
    xT = ctx.xT
    with (
        nc.sbuf_tensor(_nm("xstage"), [P, 2 * D], F32) as xstage,
        nc.psum_tensor(_nm("ppro0"), [P, P], F32) as pt0,
        nc.psum_tensor(_nm("ppro1"), [P, P], F32) as pt1,
    ):
        pts = [pt0, pt1]
        mm0, ea0 = ctx.mm.n, ctx.ea.n

        @blk.sync
        def _(sync):
            for name in consts:
                ctx.ds.inc(sync.dma_start(out=ctx.bufs[name], in_=ctx.inl[name].ap()), 16)
            for t in range(NT):
                s = t % 2
                if t >= 2:
                    sync.wait_ge(ctx.mm.h, mm0 + KC * (t - 1))
                sync.dma_start(out=xstage[:, s * D:(s + 1) * D],
                               in_=xin[t * P:(t + 1) * P, :]).then_inc(ctx.xr[s].h, 16)
                ctx.xr[s].n += 16

        @blk.vector
        def _(vector):
            vector.memset(ctx.dummy[:, :], 0.0)

        @blk.tensor
        def _(tensor):
            ctx.ds.wait(tensor)
            for t in range(NT):
                s = t % 2
                tensor.wait_ge(ctx.xr[s].h, 16 * (t // 2 + 1))
                for q in range(KC):
                    j = t * KC + q
                    if j >= 2:
                        tensor.wait_ge(ctx.ea.h, ea0 + j - 1)
                    ctx.mm.inc(tensor.transpose(
                        pts[j % 2][:], xstage[:, s * D + q * P: s * D + (q + 1) * P],
                        ctx.bufs["ID_F32"]), 1)

        @blk.scalar
        def _(scalar):
            for t in range(NT):
                for q in range(KC):
                    j = t * KC + q
                    scalar.wait_ge(ctx.mm.h, mm0 + j + 1)
                    ctx.ea.inc(scalar.copy(
                        out=xT[:, q * TC + t * P: q * TC + (t + 1) * P],
                        in_=pts[j % 2][:]), 1)

        ctx.bar()


def emit_epilogue(ctx, xout):
    nc, blk = ctx.nc, ctx.block
    xT = ctx.xT
    with (
        nc.sbuf_tensor(_nm("ostage"), [P, 2 * D], F32) as ostage,
        nc.psum_tensor(_nm("pepi0"), [P, P], F32) as pt0,
        nc.psum_tensor(_nm("pepi1"), [P, P], F32) as pt1,
    ):
        pts = [pt0, pt1]
        mm0, ea0 = ctx.mm.n, ctx.ea.n
        st = Sem(nc, _nm("s_st"))

        @blk.tensor
        def _(tensor):
            for t in range(NT):
                for q in range(KC):
                    j = t * KC + q
                    if j >= 2:
                        tensor.wait_ge(ctx.ea.h, ea0 + j - 1)
                    ctx.mm.inc(tensor.transpose(
                        pts[j % 2][:], xT[:, q * TC + t * P: q * TC + (t + 1) * P],
                        ctx.bufs["ID_F32"]), 1)

        @blk.scalar
        def _(scalar):
            for t in range(NT):
                s = t % 2
                if t >= 2:
                    scalar.wait_ge(st.h, 16 * (t - 1))
                for q in range(KC):
                    j = t * KC + q
                    scalar.wait_ge(ctx.mm.h, mm0 + j + 1)
                    ctx.ea.inc(scalar.copy(
                        out=ostage[:, s * D + q * P: s * D + (q + 1) * P],
                        in_=pts[j % 2][:]), 1)

        @blk.sync
        def _(sync):
            for t in range(NT):
                s = t % 2
                sync.wait_ge(ctx.ea.h, ea0 + (t + 1) * KC)
                st.inc(sync.dma_start(out=xout[t * P:(t + 1) * P, :],
                                      in_=ostage[:, s * D:(s + 1) * D]), 16)
            st.wait(sync)



def emit_ln(ctx, src_q, dst_mode, xold_dram=None):
    nc, blk = ctx.nc, ctx.block
    with (
        nc.sbuf_tensor(_nm("ln_sq"), [P, 2 * TC], F32) as sq,
        nc.sbuf_tensor(_nm("ln_rows"), [1, 4 * TC], F32) as rows,
        nc.sbuf_tensor(_nm("ln_rstd"), [P, TC], F32) as rstd_s,
        nc.sbuf_tensor(_nm("ln_t"), [P, TC], F32) as tsc,
        nc.sbuf_tensor(_nm("ln_xold"), [P, 2 * TC], F32) as xoldb,
        nc.psum_tensor(_nm("ln_pm"), [1, TC], F32) as ps_m,
        nc.psum_tensor(_nm("ln_ps"), [1, TC], F32) as ps_s,
        nc.psum_tensor(_nm("ln_pb0"), [P, TC], F32) as ps_b0,
        nc.psum_tensor(_nm("ln_pb1"), [P, TC], F32) as ps_b1,
    ):
        mm0, ea0, ev0 = ctx.mm.n, ctx.ea.n, ctx.ev.n

        @blk.scalar
        def _(scalar):
            for q in range(KC):
                if q >= 2:
                    scalar.wait_ge(ctx.mm.h, mm0 + (q - 1))
                ctx.ea.inc(scalar.square(out=sq[:, (q % 2) * TC:(q % 2 + 1) * TC],
                                         in_=src_q(q)), 1)

        @blk.tensor
        def _(tensor):
            ones = ctx.bufs["ONES1"]
            for q in range(KC):
                tensor.wait_ge(ctx.ea.h, ea0 + q + 1)
                for h in range(2):
                    tensor.matmul(ps_m[0:1, h * HF:(h + 1) * HF], ones,
                                  src_q(q)[:, h * HF:(h + 1) * HF],
                                  start=(q == 0), stop=(q == KC - 1))
                for h in range(2):
                    m = tensor.matmul(ps_s[0:1, h * HF:(h + 1) * HF], ones,
                                      sq[:, (q % 2) * TC + h * HF:(q % 2) * TC + (h + 1) * HF],
                                      start=(q == 0), stop=(q == KC - 1))
                ctx.mm.inc(m, 1)

        m_row = rows[0:1, 0:TC]
        var_row = rows[0:1, TC:2 * TC]
        rstd_row = rows[0:1, 2 * TC:3 * TC]
        mrs_row = rows[0:1, 3 * TC:4 * TC]

        @blk.vector
        def _(vector):
            vector.wait_ge(ctx.mm.h, mm0 + KC)
            vector.tensor_scalar(out=m_row, in0=ps_m[0:1, :], scalar1=1.0 / D,
                                 scalar2=None, op0=ALU.mult)
            vector.tensor_scalar(out=var_row, in0=ps_s[0:1, :], scalar1=1.0 / D,
                                 scalar2=None, op0=ALU.mult)
            vector.tensor_tensor(out=rstd_row, in0=m_row, in1=m_row, op=ALU.mult)
            ctx.ev.inc(vector.tensor_tensor(out=var_row, in0=var_row, in1=rstd_row,
                                            op=ALU.subtract), 1)

        @blk.scalar
        def _(scalar):
            ctx.ev.wait(scalar)
            scalar.activation(out=rstd_row, in_=var_row, func=AF.Ln, bias=ctx.bufs["EPSCOL"][0:1, 0:1])
            ctx.ea.inc(scalar.activation(out=rstd_row, in_=rstd_row, func=AF.Exp,
                                         scale=-0.5), 1)

        @blk.vector
        def _(vector):
            ctx.ea.wait(vector)
            ctx.ev.inc(vector.tensor_tensor(out=mrs_row, in0=m_row, in1=rstd_row,
                                            op=ALU.mult), 1)

        @blk.tensor
        def _(tensor):
            ctx.ev.wait(tensor)
            onesrow = ctx.bufs["ONESROW"]
            for h in range(2):
                tensor.matmul(ps_b0[:, h * HF:(h + 1) * HF], onesrow,
                              rstd_row[:, h * HF:(h + 1) * HF], start=True, stop=True)
            for h in range(2):
                m = tensor.matmul(ps_b1[:, h * HF:(h + 1) * HF], onesrow,
                                  mrs_row[:, h * HF:(h + 1) * HF], start=True, stop=True)
            ctx.mm.inc(m, 1)

        @blk.scalar
        def _(scalar):
            ctx.mm.wait(scalar)
            ctx.ea.inc(scalar.copy(out=rstd_s[:], in_=ps_b0[:]), 1)

        ev1 = ctx.ev.n
        dsc0 = ctx.dsc.n
        if dst_mode == "residual":
            @blk.scalar
            def _(scalar):
                ctx.dg.wait(scalar)
                for q in range(KC):
                    s = q % 2
                    if q >= 2:
                        scalar.wait_ge(ctx.ev.h, ev1 + (q - 1))
                    ctx.dsc.inc(scalar.dma_start(
                        out=xoldb[:, s * TC:(s + 1) * TC],
                        in_=xold_dram.ap()[:, q * TC:(q + 1) * TC]), 16)

        @blk.vector
        def _(vector):
            ctx.ea.wait(vector)
            for q in range(KC):
                vector.tensor_tensor(out=tsc[:], in0=src_q(q), in1=rstd_s[:], op=ALU.mult)
                if dst_mode == "xln":
                    ctx.ev.inc(vector.tensor_tensor(
                        out=ctx.xln[:, q * TC:(q + 1) * TC],
                        in0=tsc[:], in1=ps_b1[:], op=ALU.subtract), 1)
                else:
                    s = q % 2
                    vector.wait_ge(ctx.dsc.h, dsc0 + 16 * (q + 1))
                    vector.tensor_tensor(out=tsc[:], in0=tsc[:], in1=ps_b1[:], op=ALU.subtract)
                    ctx.ev.inc(vector.tensor_tensor(
                        out=ctx.xT[:, q * TC:(q + 1) * TC],
                        in0=tsc[:], in1=xoldb[:, s * TC:(s + 1) * TC], op=ALU.add), 1)

    ctx.bar()


def emit_spill_x(ctx):
    blk = ctx.block

    @blk.gpsimd
    def _(g):
        ctx.dg.inc(g.dma_start(out=ctx.xspill.ap(), in_=ctx.xT[:, :]), 16)



def emit_l0_ffn(ctx, W):
    nc, blk = ctx.nc, ctx.block
    ctx.fresh_dma_sems("l0f")
    E, R = 8, 64
    DBG = int(os.environ.get("K_DBG_PHASES", "99"))

    esx = contextlib.ExitStack()
    ctx.xln = esx.enter_context(nc.sbuf_tensor(_nm("xln"), [P, KC * TC], BF16))
    xln = ctx.xln
    emit_ln(ctx, lambda q: ctx.xT[:, q * TC:(q + 1) * TC], "xln")
    emit_spill_x(ctx)

    with (
        nc.sbuf_tensor(_nm("f0_wrt"), [P, E * KC], BF16) as wrt,
        nc.sbuf_tensor(_nm("f0_rbb"), [P, E], F32) as rbb,
        nc.sbuf_tensor(_nm("f0_at"), [P, KC * 512], BF16) as at,
        nc.sbuf_tensor(_nm("f0_bt"), [P, 8 * 1024], BF16) as bt,
        nc.sbuf_tensor(_nm("f0_g0t"), [P, NT * P], BF16) as g0t,
        nc.sbuf_tensor(_nm("f0_selq"), [E, 4 * P], BF16) as selq,
        nc.sbuf_tensor(_nm("f0_hsb"), [P, 4 * TC], F32) as hsb,
        nc.sbuf_tensor(_nm("f0_hs"), [P, 4 * TC], BF16) as hs,
    ):
        if DBG < 2:
            return
        selq_np = np.zeros((E, 4 * P), dtype=np.float32)
        for qq in range(4):
            selq_np[2 * qq, qq * P:qq * P + 64] = 1.0
            selq_np[2 * qq + 1, qq * P + 64:(qq + 1) * P] = 1.0
        c_selq = nc.inline_tensor(_np_bf(selq_np), _nm("c_selq"))

        with (
            nc.sbuf_tensor(_nm("f0_stage"), [P, 4 * D], BF16) as stg,
            nc.sbuf_tensor(_nm("f0_wrst"), [E, D], F32) as wrst,
            nc.sbuf_tensor(_nm("f0_brrow"), [1, E], F32) as brrow,
            nc.psum_tensor(_nm("f0_psr0"), [P, E], F32) as ps_r0,
            nc.psum_tensor(_nm("f0_psr1"), [P, E], F32) as ps_r1,
            nc.psum_tensor(_nm("f0_psrb"), [P, E], F32) as ps_rb,
        ):
            ps_rp = [ps_r0, ps_r1]
            ds0, dg0, mm0, ea0 = ctx.ds.n, ctx.dg.n, ctx.mm.n, ctx.ea.n

            @blk.sync
            def _(sync):
                ctx.ds.inc(sync.dma_start(out=selq[:], in_=c_selq.ap()), 16)
                ctx.ds.inc(sync.dma_start(out=wrst[:], in_=W["L0_ffn_Wr"]), 16)
                ctx.ds.inc(sync.dma_start(out=brrow[0:1, :],
                                          in_=W["L0_ffn_br"].rearrange("(o e) -> o e", o=1)), 16)

            @blk.gpsimd
            def _(g):
                av = W["L0_ffn_A"].rearrange("e r d -> (e r) d")
                for a in range(4):
                    ctx.dg.inc(g.dma_start(out=stg[:, a * D:(a + 1) * D],
                                           in_=av[a * P:(a + 1) * P, :]), 16)

            @blk.tensor
            def _(tensor):
                ctx.ds.wait(tensor)
                for k in range(KC):
                    if k >= 2:
                        tensor.wait_ge(ctx.ea.h, ea0 + k - 1)
                    ctx.mm.inc(tensor.transpose(
                        ps_rp[k % 2][:],
                        wrst[:, k * P:(k + 1) * P],
                        ctx.bufs["ID_F32"][0:E, 0:E]), 1)
                m = tensor.matmul(ps_rb[:], ctx.bufs["ONESROW"],
                                  brrow[0:1, :], start=True, stop=True)
                ctx.mm.inc(m, 1)

            @blk.scalar
            def _(scalar):
                for k in range(KC):
                    scalar.wait_ge(ctx.mm.h, mm0 + k + 1)
                    ctx.ea.inc(scalar.copy(out=wrt[:, k * E:(k + 1) * E],
                                           in_=ps_rp[k % 2][:]), 1)
                scalar.wait_ge(ctx.mm.h, mm0 + KC + 1)
                ctx.ea.inc(scalar.copy(out=rbb[:], in_=ps_rb[:]), 1)

            @blk.sync
            def _(sync):
                ctx.dg.wait(sync)
                for a in range(4):
                    for k in range(KC):
                        ctx.ds.inc(sync.dma_start_transpose(
                            out=at[:, k * 512 + a * P: k * 512 + (a + 1) * P],
                            in_=stg[:, a * D + k * P: a * D + (k + 1) * P]), 16)

            ctx.bar()

            @blk.gpsimd
            def _(g):
                stgv = stg[:, 0:8192].rearrange("p (a e r) -> p a e r", a=KC, e=E, r=64)
                for e in range(E):
                    bv = W["L0_ffn_B"][e].rearrange("(a p) r -> p a r", p=P)
                    ctx.dg.inc(g.dma_start(out=stgv[:, :, e, :], in_=bv), 16)

            @blk.sync
            def _(sync):
                ctx.dg.wait(sync)
                for a in range(KC):
                    for e2 in range(4):
                        blk_i = a * 4 + e2
                        ctx.ds.inc(sync.dma_start_transpose(
                            out=bt[:, blk_i * P:(blk_i + 1) * P],
                            in_=stg[:, blk_i * P:(blk_i + 1) * P]), 16)

            ctx.bar()

        if DBG < 3:
            return
        with (
            nc.sbuf_tensor(_nm("f0_L"), [P, 3 * E], F32) as Lb,
            nc.sbuf_tensor(_nm("f0_cols"), [P, 4], F32) as cols,
            nc.sbuf_tensor(_nm("f0_gbf"), [P, P], BF16) as gbf,
            nc.psum_tensor(_nm("f0_pslg"), [P, E], F32) as ps_lg,
        ):
            L = Lb[:, 0:E]
            Ee = Lb[:, E:2 * E]
            mask = Lb[:, 2 * E:3 * E]
            nmx = cols[:, 0:1]
            zz = cols[:, 1:2]
            rz = cols[:, 2:3]
            ds_r0 = ctx.ds.n

            @blk.gpsimd
            def _(g):
                ctx.ep.inc(g.memset(gbf[:, :], 0.0), 1)

            for t in range(NT):
                @blk.tensor
                def _(tensor, t=t):
                    if t == 0:
                        ctx.ea.wait(tensor)
                    else:
                        tensor.wait_ge(ctx.ep.h, ctx.ep.n)
                    m = None
                    for k in range(KC):
                        m = tensor.matmul(ps_lg[:, :],
                                          xln[:, k * TC + t * P: k * TC + (t + 1) * P],
                                          wrt[:, k * E:(k + 1) * E],
                                          start=(k == 0), stop=(k == KC - 1))
                    ctx.mm.inc(m, 1)

                @blk.scalar
                def _(scalar, t=t):
                    ctx.mm.wait(scalar)
                    ctx.ea.inc(scalar.copy(out=L, in_=ps_lg[:, :]), 1)

                @blk.gpsimd
                def _(g, t=t):
                    ctx.ea.wait(g)
                    ctx.ep.inc(g.tensor_tensor(out=L, in0=L, in1=rbb[:], op=ALU.add), 1)

                @blk.vector
                def _(vector, t=t):
                    ctx.ep.wait(vector)
                    ctx.ev.inc(vector.reduce_max(out=nmx, in_=L, axis=AX.X,
                                                 negate=True), 1)

                @blk.scalar
                def _(scalar, t=t):
                    ctx.ev.wait(scalar)
                    ctx.ea.inc(scalar.activation(out=Ee, in_=L, func=AF.Exp, bias=nmx), 1)

                @blk.vector
                def _(vector, t=t):
                    ctx.ea.wait(vector)
                    ctx.ev.inc(vector.reduce_sum(out=zz, in_=Ee, axis=AX.X), 1)

                @blk.scalar
                def _(scalar, t=t):
                    ctx.ev.wait(scalar)
                    scalar.activation(out=rz, in_=zz, func=AF.Ln)
                    ctx.ea.inc(scalar.activation(out=rz, in_=rz, func=AF.Exp,
                                                 scale=-1.0), 1)

                @blk.gpsimd
                def _(g, t=t):
                    ctx.ea.wait(g)
                    if t >= 1:
                        g.wait_ge(ctx.ds.h, ds_r0 + 16 * t)
                    g.tensor_scalar(out=mask, in0=L, scalar1=nmx, scalar2=0.0,
                                    op0=ALU.add, op1=ALU.is_equal)
                    ctx.ep.inc(g.tensor_scalar(out=gbf[:, 0:E], in0=mask, scalar1=rz,
                                               scalar2=None, op0=ALU.mult), 1)

                @blk.sync
                def _(sync, t=t):
                    ctx.ep.wait(sync)
                    ctx.ds.inc(sync.dma_start_transpose(out=g0t[:, t * P:(t + 1) * P],
                                                        in_=gbf[:, :]), 16)

            ctx.bar()
            if os.environ.get("K_DBG_DUMP"):
                d_L = nc.dram_tensor("dbg_L", [P, 3 * E], F32, kind="ExternalOutput").ap()
                d_cols = nc.dram_tensor("dbg_cols", [P, 3], F32, kind="ExternalOutput").ap()
                d_gbf = nc.dram_tensor("dbg_gbf", [P, P], F32, kind="ExternalOutput").ap()
                with nc.sbuf_tensor(_nm("dbg2"), [P, P], F32) as dbg2:
                    @blk.vector
                    def _(v):
                        v.tensor_copy(out=dbg2[:, 0:P], in_=gbf[:, :])
                        ctx.ev.inc(v.tensor_copy(out=dbg2[:, 0:P], in_=gbf[:, :]), 1)

                    @blk.gpsimd
                    def _(g):
                        ctx.ev.wait(g)
                        ctx.dg.inc(g.dma_start(out=d_gbf, in_=dbg2[:, 0:P]), 16)
                        ctx.dg.inc(g.dma_start(out=d_L, in_=Lb[:, :]), 16)
                        ctx.dg.inc(g.dma_start(out=d_cols, in_=cols[:, 0:3]), 16)
                    ctx.bar()

        if DBG < 4:
            return
        with nc.psum_tensor(_nm("f0_ph0"), [P, TC], F32) as ph0, \
             nc.psum_tensor(_nm("f0_ph1"), [P, TC], F32) as ph1, \
             nc.psum_tensor(_nm("f0_ph2"), [P, TC], F32) as ph2, \
             nc.psum_tensor(_nm("f0_ph3"), [P, TC], F32) as ph3:
            ps_h = [ph0, ph1, ph2, ph3]
            mmh = ctx.mm.n

            @blk.tensor
            def _(tensor):
                ctx.ds.wait(tensor)
                for rc in range(4):
                    m = None
                    for k in range(KC):
                        for h in range(2):
                            m = tensor.matmul(ps_h[rc][:, h * HF:(h + 1) * HF],
                                              at[:, k * 512 + rc * P:k * 512 + (rc + 1) * P],
                                              xln[:, k * TC + h * HF:k * TC + (h + 1) * HF],
                                              start=(k == 0), stop=(k == KC - 1))
                    ctx.mm.inc(m, 1)

            @blk.scalar
            def _(scalar):
                for rc in range(4):
                    scalar.wait_ge(ctx.mm.h, mmh + rc + 1)
                    ctx.ea.inc(scalar.copy(out=hsb[:, rc * TC:(rc + 1) * TC],
                                           in_=ps_h[rc][:]), 1)

            eah = ctx.ea.n
            mmg = ctx.mm.n

            @blk.tensor
            def _(tensor):
                for rc in range(4):
                    tensor.wait_ge(ctx.ea.h, eah - 4 + rc + 1)
                    m = None
                    for h in range(2):
                        m = tensor.matmul(ps_h[rc][:, h * HF:(h + 1) * HF],
                                          selq[:, rc * P:(rc + 1) * P],
                                          g0t[0:E, h * HF:(h + 1) * HF], start=True, stop=True)
                    ctx.mm.inc(m, 1)

            @blk.vector
            def _(vector):
                for rc in range(4):
                    vector.wait_ge(ctx.mm.h, mmg + rc + 1)
                    ctx.ev.inc(vector.tensor_tensor(out=hs[:, rc * TC:(rc + 1) * TC],
                                                    in0=hsb[:, rc * TC:(rc + 1) * TC],
                                                    in1=ps_h[rc][:], op=ALU.mult), 1)

            ctx.bar()

        if DBG < 5:
            return
        with nc.psum_tensor(_nm("f0_py0"), [P, TC], F32) as py0, \
             nc.psum_tensor(_nm("f0_py1"), [P, TC], F32) as py1:
            ps_y = [py0, py1]
            mmy = ctx.mm.n
            eay = ctx.ea.n

            @blk.tensor
            def _(tensor):
                for oc in range(KC):
                    sl = oc % 2
                    if oc >= 2:
                        tensor.wait_ge(ctx.ea.h, eay + oc - 1)
                    m = None
                    for e2 in range(4):
                        blk_i = oc * 4 + e2
                        for h in range(2):
                            m = tensor.matmul(ps_y[sl][:, h * HF:(h + 1) * HF],
                                              bt[:, blk_i * P:(blk_i + 1) * P],
                                              hs[:, e2 * TC + h * HF:e2 * TC + (h + 1) * HF],
                                              start=(e2 == 0), stop=(e2 == 3))
                    ctx.mm.inc(m, 1)

            @blk.scalar
            def _(scalar):
                ctx.dg.wait(scalar)
                for oc in range(KC):
                    sl = oc % 2
                    scalar.wait_ge(ctx.mm.h, mmy + oc + 1)
                    ctx.ea.inc(scalar.activation(out=ctx.xT[:, oc * TC:(oc + 1) * TC],
                                                 in_=ps_y[sl][:], func=AF.Copy,
                                                 scale=1.0 / R), 1)

            ctx.bar()

        if os.environ.get("K_DBG_DUMP"):
            d_bt = nc.dram_tensor("dbg_bt", [P, 8 * 1024], F32, kind="ExternalOutput").ap()
            d_g0t = nc.dram_tensor("dbg_g0t", [P, NT * P], F32, kind="ExternalOutput").ap()
            d_hs = nc.dram_tensor("dbg_hs", [P, 4 * TC], F32, kind="ExternalOutput").ap()
            d_y = nc.dram_tensor("dbg_y", [P, KC * TC], F32, kind="ExternalOutput").ap()
            with nc.sbuf_tensor(_nm("dbgf"), [P, 8 * 1024], F32) as dbgf:
                @blk.vector
                def _(v):
                    ctx.ev.inc(v.tensor_copy(out=dbgf[:, 0:8 * 1024], in_=bt[:]), 1)

                @blk.gpsimd
                def _(g):
                    ctx.ev.wait(g)
                    ctx.dg.inc(g.dma_start(out=d_bt, in_=dbgf[:, 0:8 * 1024]), 16)
                ctx.bar()

                @blk.vector
                def _(v):
                    ctx.ev.inc(v.tensor_copy(out=dbgf[:, 0:NT * P], in_=g0t[:]), 1)

                @blk.gpsimd
                def _(g):
                    ctx.ev.wait(g)
                    ctx.dg.inc(g.dma_start(out=d_g0t, in_=dbgf[:, 0:NT * P]), 16)
                ctx.bar()

                @blk.vector
                def _(v):
                    ctx.ev.inc(v.tensor_copy(out=dbgf[:, 0:4 * TC], in_=hs[:]), 1)

                @blk.gpsimd
                def _(g):
                    ctx.ev.wait(g)
                    ctx.dg.inc(g.dma_start(out=d_hs, in_=dbgf[:, 0:4 * TC]), 16)
                ctx.bar()

                @blk.gpsimd
                def _(g):
                    ctx.dg.inc(g.dma_start(out=d_y, in_=ctx.xT[:, :]), 16)
                ctx.bar()

    esx.close()
    if DBG >= 6:
        emit_ln(ctx, lambda q: ctx.xT[:, q * TC:(q + 1) * TC], "residual",
                xold_dram=ctx.xspill)



BLOCK_INPUTS = {
    "l0a": ["L0_attn_log_decay", "L0_attn_Wq", "L0_attn_Wk", "L0_attn_Wv",
            "L0_attn_Wo", "L0_attn_Wg1", "L0_attn_Wg2", "L0_attn_norm_w"],
    "l0f": ["L0_ffn_A", "L0_ffn_B", "L0_ffn_Wr", "L0_ffn_br"],
    "l1a": ["L1_attn_w_hg", "L1_attn_w_out"],
    "l1f": ["L1_ffn_router_w", "L1_ffn_router_bias", "L1_ffn_Wu", "L1_ffn_Wg",
            "L1_ffn_Wd", "L1_ffn_Su", "L1_ffn_Sg", "L1_ffn_Sd"],
}

INPUT_SHAPES = {
    "L0_attn_log_decay": [16], "L0_attn_Wq": [D, D], "L0_attn_Wk": [D, D],
    "L0_attn_Wv": [D, D], "L0_attn_Wo": [D, D], "L0_attn_Wg1": [D, 128],
    "L0_attn_Wg2": [128, D], "L0_attn_norm_w": [D],
    "L0_ffn_A": [8, 64, D], "L0_ffn_B": [8, D, 64], "L0_ffn_Wr": [8, D], "L0_ffn_br": [8],
    "L1_attn_w_hg": [D, 6144], "L1_attn_w_out": [3072, D],
    "L1_ffn_router_w": [16, D], "L1_ffn_router_bias": [16],
    "L1_ffn_Wu": [16, D, 1024], "L1_ffn_Wg": [16, D, 1024], "L1_ffn_Wd": [16, 1024, D],
    "L1_ffn_Su": [1, D, D], "L1_ffn_Sg": [1, D, D], "L1_ffn_Sd": [1, D, D],
}

_PROGRAM_CACHE = {}


def build_program(blocks):
    key = tuple(blocks)
    if key in _PROGRAM_CACHE:
        return _PROGRAM_CACHE[key]
    nc = bass.Bass("TRN2", target_bir_lowering=False, detect_race_conditions=False)
    xin = nc.dram_tensor("xin", [TC, D], F32, kind="ExternalInput").ap()
    xout = nc.dram_tensor("xout", [TC, D], F32, kind="ExternalOutput").ap()
    W = {}
    names = []
    for b in blocks:
        for nmn in BLOCK_INPUTS[b]:
            W[nmn] = nc.dram_tensor(nmn, INPUT_SHAPES[nmn], F32, kind="ExternalInput").ap()
            names.append(nmn)
    if "l0a" in blocks:
        W["msel"] = nc.dram_tensor("msel", [8, 1], F32, kind="ExternalInput").ap()
        names.append("msel")
    if "l1a" in blocks:
        W["tmat"] = nc.dram_tensor("tmat", [8, 8], F32, kind="ExternalInput").ap()
        W["tbias"] = nc.dram_tensor("tbias", [8, 1], F32, kind="ExternalInput").ap()
        names += ["tmat", "tbias"]

    with nc.Block() as blk:
        ctx = Ctx(nc, blk)
        ctx.inl = {
            "ID_F32": nc.inline_tensor(np.eye(P, dtype=np.float32), "c_id_f32"),
            "ID_BF": nc.inline_tensor(_np_bf(np.eye(P)), "c_id_bf"),
            "ONES1": nc.inline_tensor(np.ones((P, 1), dtype=np.float32), "c_ones1"),
            "EPSCOL": nc.inline_tensor(np.full((P, 1), EPS, dtype=np.float32), "c_epscol"),
            "ONES1B": nc.inline_tensor(_np_bf(np.ones((P, 1))), "c_ones1b"),
            "ONESROW": nc.inline_tensor(np.ones((1, P), dtype=np.float32), "c_onesrow"),
        }
        consts = {
            "ID_F32": ([P, P], F32),
            "ID_BF": ([P, P], BF16),
            "ONES1": ([P, 1], F32),
            "EPSCOL": ([P, 1], F32),
            "ONES1B": ([P, 1], BF16),
            "ONESROW": ([1, P], F32),
        }
        for name, (shape, dtype) in consts.items():
            ctx.bufs[name] = nc.alloc_sbuf_tensor(_nm("sb_" + name), shape, dtype).ap()
        ctx.dummy = nc.alloc_sbuf_tensor("vgap_dummy", [P, 24], F32).ap()
        ctx.wbf = nc.alloc_sbuf_tensor("wbf", [P, 3 * 512], BF16).ap()
        ctx.xT = nc.alloc_sbuf_tensor("xT", [P, KC * TC], F32).ap()
        ctx.xspill = nc.dram_tensor("xspill", [P, KC * TC], F32)
        emit_prologue(ctx, xin, consts)
        for b in blocks:
            EMITTERS[b](ctx, W)
        emit_epilogue(ctx, xout)

    _PROGRAM_CACHE[key] = (nc, names)
    return _PROGRAM_CACHE[key]


def _percore_consts(core):
    r = core
    msel = np.array([[1024.0 * (r - 1 - j) if j < r else 1e9] for j in range(8)],
                    dtype=np.float32)
    tmat = np.zeros((8, 8), dtype=np.float32)
    for mrow in range(8):
        for j in range(8):
            if j < mrow < r:
                tmat[mrow, j] = 1.0
    tbias = np.array([[0.0 if j < r else -1e38] for j in range(8)], dtype=np.float32)
    return {"msel": msel, "tmat": tmat, "tbias": tbias}


def run_blocks(x_full, inputs, blocks):
    nc, names = build_program(blocks)
    in_maps = []
    for c in range(NCORE):
        m = {"xin": np.ascontiguousarray(x_full[c * TC:(c + 1) * TC])}
        pc = _percore_consts(c)
        for nmn in names:
            m[nmn] = pc[nmn] if nmn in pc else inputs[nmn]
        in_maps.append(m)
    res = run_bass_kernel_spmd(nc, in_maps, core_ids=list(range(NCORE)))
    return np.concatenate([res.results[c]["xout"] for c in range(NCORE)], axis=0)


ALL_BLOCKS = ["l0a", "l0f", "l1a", "l1f"]


def kernel(**inputs):
    inputs = {k: np.asarray(v, dtype=np.float32) for k, v in inputs.items()}
    x = np.ascontiguousarray(inputs["L0_attn_x"].reshape(8192, D))
    if os.environ.get("K_UNFUSED"):
        for blocks in (["l0a"], ["l0f"], ["l1a"], ["l1f"]):
            x = run_blocks(x, inputs, blocks)
    else:
        x = run_blocks(x, inputs, ALL_BLOCKS)
    return x.reshape(1, 8192, D)



def wstream(ctx, pieces, wtile, on_piece, pe_prewait=None):
    nc, blk = ctx.nc, ctx.block
    NS = 3
    mm0 = ctx.mm.n
    wr0 = [sm.n for sm in ctx.wr]

    @blk.gpsimd
    def _(g):
        for n, ap in enumerate(pieces):
            sl = n % NS
            if ctx.wring_free[sl] > 0:
                g.wait_ge(ctx.mm.h, ctx.wring_free[sl])
            ctx.wring_free[sl] = mm0 + n + 1
            g.dma_start(out=ctx.wbf[:, sl * 512:sl * 512 + wtile], in_=ap).then_inc(
                ctx.wr[sl].h, 16)
            ctx.wr[sl].n += 16

    @blk.tensor
    def _(tensor):
        for n in range(len(pieces)):
            sl = n % NS
            tensor.wait_ge(ctx.wr[sl].h, wr0[sl] + 16 * (n // NS + 1))
            if pe_prewait is not None:
                pe_prewait(tensor, n)
            last = on_piece(tensor, n, ctx.wbf[:, sl * 512:sl * 512 + wtile])
            ctx.mm.inc(last, 1)


def emit_l1_ffn(ctx, W):
    nc, blk = ctx.nc, ctx.block
    ctx.fresh_dma_sems("l1f")
    E = 16
    NEXP = int(os.environ.get("K_DBG_L1F_NEXP", "16"))
    NOSHARED = bool(os.environ.get("K_DBG_L1F_NOSHARED", ""))
    NOPOSTLN = bool(os.environ.get("K_DBG_L1F_NOPOSTLN", ""))
    Wu, Wg, Wd = W["L1_ffn_Wu"], W["L1_ffn_Wg"], W["L1_ffn_Wd"]
    Su, Sg, Sd = W["L1_ffn_Su"], W["L1_ffn_Sg"], W["L1_ffn_Sd"]

    esx = contextlib.ExitStack()
    ctx.xln = esx.enter_context(nc.sbuf_tensor(_nm("xln"), [P, KC * TC], BF16))
    xln = ctx.xln
    emit_ln(ctx, lambda q: ctx.xT[:, q * TC:(q + 1) * TC], "xln")
    emit_spill_x(ctx)

    onehot_np = np.zeros((E, E * P), dtype=np.float32)
    for e in range(E):
        onehot_np[e, e * P:(e + 1) * P] = 1.0
    c_oh = nc.inline_tensor(_np_bf(onehot_np), _nm("c_oh16"))

    with (
        nc.sbuf_tensor(_nm("f1_rwt"), [P, E * KC], BF16) as rwt,
        nc.sbuf_tensor(_nm("f1_rbb"), [P, E], F32) as rbb,
        nc.sbuf_tensor(_nm("f1_oh"), [E, E * P], BF16) as oh,
        nc.sbuf_tensor(_nm("f1_gt"), [P, NT * P], BF16) as gt,
        nc.sbuf_tensor(_nm("f1_usb"), [P, NT * TC], BF16) as usb,
        nc.sbuf_tensor(_nm("f1_gsb"), [P, NT * TC], BF16) as gsb,
        nc.sbuf_tensor(_nm("f1_gbs"), [P, TC], BF16) as gbs,
        nc.sbuf_tensor(_nm("f1_ssb"), [P, KC * TC], BF16) as ssb,
    ):
        with (
            nc.sbuf_tensor(_nm("f1_rwst"), [E, D], F32) as rwst,
            nc.sbuf_tensor(_nm("f1_brrow"), [1, E], F32) as brrow,
            nc.psum_tensor(_nm("f1_psr0"), [P, E], F32) as ps_r0,
            nc.psum_tensor(_nm("f1_psr1"), [P, E], F32) as ps_r1,
            nc.psum_tensor(_nm("f1_psrb"), [P, E], F32) as ps_rb,
        ):
            ps_rp = [ps_r0, ps_r1]
            mm0, ea0 = ctx.mm.n, ctx.ea.n

            @blk.sync
            def _(sync):
                ctx.ds.inc(sync.dma_start(out=oh[:], in_=c_oh.ap()), 16)
                ctx.ds.inc(sync.dma_start(out=rwst[:], in_=W["L1_ffn_router_w"]), 16)
                ctx.ds.inc(sync.dma_start(
                    out=brrow[0:1, :],
                    in_=W["L1_ffn_router_bias"].rearrange("(o e) -> o e", o=1)), 16)

            @blk.tensor
            def _(tensor):
                ctx.ds.wait(tensor)
                for k in range(KC):
                    if k >= 2:
                        tensor.wait_ge(ctx.ea.h, ea0 + k - 1)
                    ctx.mm.inc(tensor.transpose(
                        ps_rp[k % 2][:], rwst[:, k * P:(k + 1) * P],
                        ctx.bufs["ID_F32"][0:E, 0:E]), 1)
                m = tensor.matmul(ps_rb[:], ctx.bufs["ONESROW"], brrow[0:1, :],
                                  start=True, stop=True)
                ctx.mm.inc(m, 1)

            @blk.scalar
            def _(scalar):
                for k in range(KC):
                    scalar.wait_ge(ctx.mm.h, mm0 + k + 1)
                    ctx.ea.inc(scalar.copy(out=rwt[:, k * E:(k + 1) * E],
                                           in_=ps_rp[k % 2][:]), 1)
                scalar.wait_ge(ctx.mm.h, mm0 + KC + 1)
                ctx.ea.inc(scalar.copy(out=rbb[:], in_=ps_rb[:]), 1)

            ctx.bar()

        with (
            nc.sbuf_tensor(_nm("f1_L"), [P, 8 * E], F32) as Lb,
            nc.sbuf_tensor(_nm("f1_cols"), [P, 8], F32) as cols,
            nc.sbuf_tensor(_nm("f1_cbf"), [P, P], BF16) as cbf,
            nc.psum_tensor(_nm("f1_pslg"), [P, E], F32) as ps_lg,
        ):
            L = Lb[:, 0:E]
            LB = Lb[:, E:2 * E]
            m1 = Lb[:, 2 * E:3 * E]
            m2 = Lb[:, 3 * E:4 * E]
            Ex = Lb[:, 4 * E:5 * E]
            wrk = Lb[:, 5 * E:6 * E]
            w1b = Lb[:, 6 * E:7 * E]
            w2b = Lb[:, 7 * E:8 * E]
            nm1 = cols[:, 0:1]
            nm2 = cols[:, 1:2]
            nmL = cols[:, 2:3]
            aa = cols[:, 3:4]
            bb2 = cols[:, 4:5]
            rs = cols[:, 5:6]
            t1c = cols[:, 6:7]
            t2c = cols[:, 7:8]
            ds_r0 = ctx.ds.n

            @blk.gpsimd
            def _(g):
                ctx.ep.inc(g.memset(cbf[:, :], 0.0), 1)

            for t in range(NT):
                @blk.tensor
                def _(tensor, t=t):
                    if t == 0:
                        ctx.ea.wait(tensor)
                    else:
                        tensor.wait_ge(ctx.ep.h, ctx.ep.n)
                    m = None
                    for k in range(KC):
                        m = tensor.matmul(ps_lg[:, :],
                                          xln[:, k * TC + t * P:k * TC + (t + 1) * P],
                                          rwt[:, k * E:(k + 1) * E],
                                          start=(k == 0), stop=(k == KC - 1))
                    ctx.mm.inc(m, 1)

                @blk.scalar
                def _(scalar, t=t):
                    ctx.mm.wait(scalar)
                    ctx.ea.inc(scalar.copy(out=L, in_=ps_lg[:, :]), 1)

                @blk.gpsimd
                def _(g, t=t):
                    ctx.ea.wait(g)
                    ctx.ep.inc(g.tensor_tensor(out=LB, in0=L, in1=rbb[:], op=ALU.add), 1)

                @blk.vector
                def _(vector, t=t):
                    ctx.ep.wait(vector)
                    vector.reduce_max(out=nm1, in_=LB, axis=AX.X, negate=True)
                    ctx.ev.inc(vector.reduce_max(out=nmL, in_=L, axis=AX.X,
                                                 negate=True), 1)

                @blk.gpsimd
                def _(g, t=t):
                    ctx.ev.wait(g)
                    g.tensor_scalar(out=m1, in0=LB, scalar1=nm1, scalar2=0.0,
                                    op0=ALU.add, op1=ALU.is_equal)
                    g.tensor_scalar(out=wrk, in0=m1, scalar1=-1e30, scalar2=None,
                                    op0=ALU.mult)
                    ctx.ep.inc(g.tensor_tensor(out=wrk, in0=wrk, in1=LB, op=ALU.add), 1)

                @blk.vector
                def _(vector, t=t):
                    ctx.ep.wait(vector)
                    ctx.ev.inc(vector.reduce_max(out=nm2, in_=wrk, axis=AX.X,
                                                 negate=True), 1)

                @blk.scalar
                def _(scalar, t=t):
                    ctx.ev.wait(scalar)
                    ctx.ea.inc(scalar.activation(out=Ex, in_=L, func=AF.Exp,
                                                 bias=nmL), 1)

                @blk.gpsimd
                def _(g, t=t):
                    ctx.ea.wait(g)
                    g.tensor_scalar(out=m2, in0=wrk, scalar1=nm2, scalar2=0.0,
                                    op0=ALU.add, op1=ALU.is_equal)
                    g.tensor_tensor(out=w1b, in0=m1, in1=Ex, op=ALU.mult)
                    ctx.ep.inc(g.tensor_tensor(out=w2b, in0=m2, in1=Ex, op=ALU.mult), 1)

                @blk.vector
                def _(vector, t=t):
                    ctx.ep.wait(vector)
                    vector.reduce_sum(out=aa, in_=w1b, axis=AX.X)
                    ctx.ev.inc(vector.reduce_sum(out=bb2, in_=w2b, axis=AX.X), 1)

                @blk.gpsimd
                def _(g, t=t):
                    ctx.ev.wait(g)
                    ctx.ep.inc(g.tensor_tensor(out=rs, in0=aa, in1=bb2, op=ALU.add), 1)

                @blk.scalar
                def _(scalar, t=t):
                    ctx.ep.wait(scalar)
                    scalar.activation(out=rs, in_=rs, func=AF.Ln)
                    ctx.ea.inc(scalar.activation(out=rs, in_=rs, func=AF.Exp,
                                                 scale=-1.0), 1)

                @blk.gpsimd
                def _(g, t=t):
                    ctx.ea.wait(g)
                    if t >= 1:
                        g.wait_ge(ctx.ds.h, ds_r0 + 16 * t)
                    g.tensor_tensor(out=t1c, in0=aa, in1=rs, op=ALU.mult)
                    g.tensor_tensor(out=t2c, in0=bb2, in1=rs, op=ALU.mult)
                    g.tensor_scalar(out=wrk, in0=m1, scalar1=t1c, scalar2=None,
                                    op0=ALU.mult)
                    g.tensor_scalar(out=w2b, in0=m2, scalar1=t2c, scalar2=None,
                                    op0=ALU.mult)
                    ctx.ep.inc(g.tensor_tensor(out=cbf[:, 0:E], in0=w2b, in1=wrk,
                                               op=ALU.add), 1)

                @blk.sync
                def _(sync, t=t):
                    ctx.ep.wait(sync)
                    ctx.ds.inc(sync.dma_start_transpose(out=gt[:, t * P:(t + 1) * P],
                                                        in_=cbf[:, :]), 16)

            ctx.bar()
            if os.environ.get("K_DBG_DUMP"):
                d_gt = nc.dram_tensor("dbg_gt", [P, NT * P], F32, kind="ExternalOutput").ap()
                d_L = nc.dram_tensor("dbg_L", [P, 6 * E], F32, kind="ExternalOutput").ap()
                d_cols = nc.dram_tensor("dbg_cols", [P, 8], F32, kind="ExternalOutput").ap()
                with nc.sbuf_tensor(_nm("dbg3"), [P, NT * P], F32) as dbg3:
                    @blk.vector
                    def _(v):
                        ctx.ev.inc(v.tensor_copy(out=dbg3[:, :], in_=gt[:, :]), 1)

                    @blk.gpsimd
                    def _(g):
                        ctx.ev.wait(g)
                        ctx.dg.inc(g.dma_start(out=d_gt, in_=dbg3[:, :]), 16)
                        ctx.dg.inc(g.dma_start(out=d_L, in_=Lb[:, :]), 16)
                        ctx.dg.inc(g.dma_start(out=d_cols, in_=cols[:, :]), 16)
                    ctx.bar()

        with (
            nc.psum_tensor(_nm("f1_pa"), [P, TC], F32) as pA,
            nc.psum_tensor(_nm("f1_pb"), [P, TC], F32) as pB,
            nc.psum_tensor(_nm("f1_pc"), [P, TC], F32) as pC,
            nc.psum_tensor(_nm("f1_pd"), [P, TC], F32) as pD,
        ):
            ps4 = [pA, pB, pC, pD]

            def u_or_g(mat_ap, dst_sb, act_fn):
                eaE = [ctx.ea.n]

                def prewait(tensor, n):
                    og = n // KC
                    if n == 0:
                        tensor.wait_ge(ctx.ea.h, eaE[0])
                    if n % KC == 0 and og >= 1:
                        tensor.wait_ge(ctx.ea.h, eaE[0] + og * 4)

                def onp(tensor, n, slot):
                    og = n // KC
                    k = n % KC
                    m = None
                    for oc in range(4):
                        for h in range(2):
                            m = tensor.matmul(
                                ps4[oc][:, h * HF:(h + 1) * HF],
                                slot[:, oc * P:(oc + 1) * P],
                                xln[:, k * TC + h * HF:k * TC + (h + 1) * HF],
                                start=(k == 0), stop=(k == KC - 1))
                    return m

                pieces = []
                for og in range(2):
                    for k in range(KC):
                        pieces.append(mat_ap[k * P:(k + 1) * P, og * 512:(og + 1) * 512])
                mmE = ctx.mm.n
                wstream(ctx, pieces, 512, onp, prewait)

                @blk.scalar
                def _(scalar):
                    for og in range(2):
                        scalar.wait_ge(ctx.mm.h, mmE + (og + 1) * KC)
                        for oc in range(4):
                            ctx.ea.inc(scalar.activation(
                                out=dst_sb[:, (og * 4 + oc) * TC:(og * 4 + oc + 1) * TC],
                                in_=ps4[oc][:], func=act_fn), 1)

            for e in range(NEXP):
                mmG = ctx.mm.n

                @blk.tensor
                def _(tensor, e=e):
                    if e == 0:
                        ctx.ea.wait(tensor)
                    else:
                        ctx.ev.wait(tensor)
                    m = None
                    for h in range(2):
                        m = tensor.matmul(pA[:, h * HF:(h + 1) * HF],
                                          oh[:, e * P:(e + 1) * P],
                                          gt[0:E, h * HF:(h + 1) * HF],
                                          start=True, stop=True)
                    ctx.mm.inc(m, 1)

                @blk.scalar
                def _(scalar, e=e):
                    ctx.mm.wait(scalar)
                    ctx.ea.inc(scalar.copy(out=gbs[:], in_=pA[:]), 1)

                u_or_g(Wu[e], usb, AF.Silu)
                u_or_g(Wg[e], gsb, AF.Copy)

                @blk.vector
                def _(vector, e=e):
                    ctx.ea.wait(vector)
                    if os.environ.get("K_DBG_L1F_NOH"):
                        ctx.ev.inc(vector.tensor_copy(out=ctx.dummy[:, 3:4],
                                                      in_=ctx.dummy[:, 0:1]), 1)
                        return
                    for rc in range(NT):
                        vector.tensor_tensor(out=usb[:, rc * TC:(rc + 1) * TC],
                                             in0=usb[:, rc * TC:(rc + 1) * TC],
                                             in1=gsb[:, rc * TC:(rc + 1) * TC],
                                             op=ALU.mult)
                        ctx.ev.inc(vector.tensor_tensor(
                            out=usb[:, rc * TC:(rc + 1) * TC],
                            in0=usb[:, rc * TC:(rc + 1) * TC],
                            in1=gbs[:], op=ALU.mult), 1)

                evY0 = ctx.ev.n

                def prewaity(tensor, n, evY0=evY0):
                    og2 = n // NT
                    if n % NT == 0:
                        if og2 == 0:
                            tensor.wait_ge(ctx.ev.h, evY0)
                        else:
                            tensor.wait_ge(ctx.ev.h, evY0 + og2 * 4)

                def onpy(tensor, n, slot):
                    og2 = n // NT
                    rk = n % NT
                    m = None
                    for oc in range(4):
                        for h in range(2):
                            m = tensor.matmul(
                                ps4[oc][:, h * HF:(h + 1) * HF],
                                slot[:, oc * P:(oc + 1) * P],
                                usb[:, rk * TC + h * HF:rk * TC + (h + 1) * HF],
                                start=(rk == 0), stop=(rk == NT - 1))
                    return m

                piecesy = []
                for og2 in range(4):
                    for rk in range(NT):
                        piecesy.append(Wd[e][rk * P:(rk + 1) * P,
                                             og2 * 512:(og2 + 1) * 512])
                mmY = ctx.mm.n
                wstream(ctx, piecesy, 512, onpy, prewaity)

                @blk.vector
                def _(vector, e=e, mmY=mmY):
                    for og2 in range(4):
                        vector.wait_ge(ctx.mm.h, mmY + (og2 + 1) * NT)
                        for oc in range(4):
                            dst = ctx.xT[:, (og2 * 4 + oc) * TC:(og2 * 4 + oc + 1) * TC]
                            if e == 0:
                                ctx.ev.inc(vector.tensor_copy(out=dst,
                                                              in_=ps4[oc][:]), 1)
                            else:
                                ctx.ev.inc(vector.tensor_tensor(
                                    out=dst, in0=dst, in1=ps4[oc][:], op=ALU.add), 1)

            def s_u_or_g(mat_ap, act_fn, second):
                eaE = [ctx.ea.n]
                ev_last = ctx.ev.n

                def prewait(tensor, n):
                    og = n // KC
                    if n == 0:
                        tensor.wait_ge(ctx.ea.h, eaE[0])
                        if not second:
                            tensor.wait_ge(ctx.ev.h, ev_last)
                    if n % KC == 0 and og >= 1:
                        tensor.wait_ge(ctx.ea.h, eaE[0] + og * 4)

                def onp(tensor, n, slot):
                    og = n // KC
                    k = n % KC
                    m = None
                    for oc in range(4):
                        for h in range(2):
                            m = tensor.matmul(
                                ps4[oc][:, h * HF:(h + 1) * HF],
                                slot[:, oc * P:(oc + 1) * P],
                                xln[:, k * TC + h * HF:k * TC + (h + 1) * HF],
                                start=(k == 0), stop=(k == KC - 1))
                    return m

                pieces = []
                for og in range(4):
                    for k in range(KC):
                        pieces.append(mat_ap[k * P:(k + 1) * P, og * 512:(og + 1) * 512])
                mmE = ctx.mm.n
                wstream(ctx, pieces, 512, onp, prewait)

                @blk.scalar
                def _(scalar):
                    for og in range(4):
                        scalar.wait_ge(ctx.mm.h, mmE + (og + 1) * KC)
                        for oc in range(4):
                            occ = og * 4 + oc
                            if second:
                                dst = (usb if occ < NT else gsb)[
                                    :, (occ % NT) * TC:(occ % NT + 1) * TC]
                            else:
                                dst = ssb[:, occ * TC:(occ + 1) * TC]
                            ctx.ea.inc(scalar.activation(out=dst, in_=ps4[oc][:],
                                                         func=act_fn), 1)

            if not NOSHARED:
                s_u_or_g(Su[0], AF.Silu, False)
                s_u_or_g(Sg[0], AF.Copy, True)

            if not NOSHARED:
                @blk.vector
                def _(vector):
                    ctx.ea.wait(vector)
                    for rc in range(KC):
                        src2 = (usb if rc < NT else gsb)[:, (rc % NT) * TC:(rc % NT + 1) * TC]
                        ctx.ev.inc(vector.tensor_tensor(
                            out=ssb[:, rc * TC:(rc + 1) * TC],
                            in0=ssb[:, rc * TC:(rc + 1) * TC],
                            in1=src2, op=ALU.mult), 1)

            evY0s = ctx.ev.n

            def prewaitsy(tensor, n):
                og2 = n // KC
                if n % KC == 0:
                    if og2 == 0:
                        tensor.wait_ge(ctx.ev.h, evY0s)
                    else:
                        tensor.wait_ge(ctx.ev.h, evY0s + og2 * 4)

            def onpsy(tensor, n, slot):
                og2 = n // KC
                rk = n % KC
                m = None
                for oc in range(4):
                    for h in range(2):
                        m = tensor.matmul(
                            ps4[oc][:, h * HF:(h + 1) * HF],
                            slot[:, oc * P:(oc + 1) * P],
                            ssb[:, rk * TC + h * HF:rk * TC + (h + 1) * HF],
                            start=(rk == 0), stop=(rk == KC - 1))
                return m

            if not NOSHARED:
                piecessy = []
                for og2 in range(4):
                    for rk in range(KC):
                        piecessy.append(Sd[0][rk * P:(rk + 1) * P, og2 * 512:(og2 + 1) * 512])
                mmSY = ctx.mm.n
                wstream(ctx, piecessy, 512, onpsy, prewaitsy)

                @blk.vector
                def _(vector, mmSY=mmSY):
                    for og2 in range(4):
                        vector.wait_ge(ctx.mm.h, mmSY + (og2 + 1) * KC)
                        for oc in range(4):
                            dst = ctx.xT[:, (og2 * 4 + oc) * TC:(og2 * 4 + oc + 1) * TC]
                            ctx.ev.inc(vector.tensor_tensor(out=dst, in0=dst,
                                                            in1=ps4[oc][:], op=ALU.add), 1)

            ctx.bar()

    emit_ln(ctx, lambda q: ctx.xT[:, q * TC:(q + 1) * TC], "residual",
            xold_dram=ctx.xspill)


def emit_l1_attn(ctx, W):
    nc, blk = ctx.nc, ctx.block
    ctx.fresh_dma_sems("l1a")
    whg = W["L1_attn_w_hg"]
    wout = W["L1_attn_w_out"]
    NQ = 24

    esx = contextlib.ExitStack()
    ctx.xln = esx.enter_context(nc.sbuf_tensor(_nm("xln"), [P, KC * TC], BF16))
    xln = ctx.xln
    emit_ln(ctx, lambda q: ctx.xT[:, q * TC:(q + 1) * TC], "xln")
    emit_spill_x(ctx)

    pspill = nc.dram_tensor(_nm("pspill"), [P, NQ * TC], BF16)
    ccin = nc.dram_tensor(_nm("cc_h_in"), [2, 3072], F32)
    ccout = nc.dram_tensor(_nm("cc_h_out"), [16, 3072], F32)

    with (
        nc.sbuf_tensor(_nm("a1_h"), [P, NQ * TC], BF16) as hbuf,
        nc.sbuf_tensor(_nm("a1_pst"), [P, TC], BF16) as pst,
        nc.sbuf_tensor(_nm("a1_pst2"), [P, 2 * TC], BF16) as pst2,
        nc.sbuf_tensor(_nm("a1_s1"), [P, TC], F32) as s1,
        nc.sbuf_tensor(_nm("a1_s2"), [P, TC], F32) as s2,
        nc.sbuf_tensor(_nm("a1_az"), [P, 2 * TC], F32) as az,
        nc.sbuf_tensor(_nm("a1_zero"), [P, TC], F32) as zeros,
        nc.sbuf_tensor(_nm("a1_stg"), [P, 48], F32) as stg,
        nc.sbuf_tensor(_nm("a1_stgT"), [48, P], F32) as stgT,
        nc.sbuf_tensor(_nm("a1_he"), [8, 3072], BF16) as he,
        nc.sbuf_tensor(_nm("a1_pt"), [8, 3072], BF16) as pt8,
        nc.sbuf_tensor(_nm("a1_w6"), [8, 3072], BF16) as w6,
        nc.sbuf_tensor(_nm("a1_tm"), [8, 8], BF16) as tm,
        nc.sbuf_tensor(_nm("a1_tb"), [8, 1], F32) as tb,
        nc.sbuf_tensor(_nm("a1_hc"), [P, NQ], F32) as hcols,
        nc.psum_tensor(_nm("a1_p0"), [P, TC], F32) as pA,
        nc.psum_tensor(_nm("a1_p1"), [P, TC], F32) as pB,
        nc.psum_tensor(_nm("a1_p2"), [P, TC], F32) as pC,
        nc.psum_tensor(_nm("a1_p3"), [P, TC], F32) as pD,
    ):
        ps4 = [pA, pB, pC, pD]
        a_sb = az[:, 0:TC]

        @blk.vector
        def _(vector):
            ctx.ev.inc(vector.memset(zeros[:, :], 0.0), 1)

        @blk.gpsimd
        def _(g):
            ctx.dg.inc(g.dma_start(out=tm[:, :], in_=W["tmat"]), 16)

        @blk.sync
        def _(sync):
            ctx.ds.inc(sync.dma_start(out=tb[:, :], in_=W["tbias"]), 16)

        dg_spill0 = ctx.dg.n
        for pg in range(12):
            q0 = 2 * pg
            pieces = []
            for k in range(KC):
                pieces.append(whg[k * P:(k + 1) * P, q0 * P:(q0 + 2) * P])
                pieces.append(whg[k * P:(k + 1) * P, (24 + q0) * P:(24 + q0 + 2) * P])
            eaB = ctx.ea.n
            evB = ctx.ev.n

            def prewait(tensor, n, eaB=eaB, evB=evB):
                if n == 0:
                    tensor.wait_ge(ctx.ea.h, eaB)
                    tensor.wait_ge(ctx.ev.h, evB)

            def onp(tensor, n, slot, pg=pg):
                k = n // 2
                hid = (n % 2 == 0)
                m = None
                for j in range(2):
                    tgt = ps4[j] if hid else ps4[2 + j]
                    for h in range(2):
                        m = tensor.matmul(
                            tgt[:, h * HF:(h + 1) * HF],
                            slot[:, j * P:(j + 1) * P],
                            xln[:, k * TC + h * HF:k * TC + (h + 1) * HF],
                            start=(k == 0), stop=(k == KC - 1))
                return m

            wstream(ctx, pieces, 256, onp, prewait)

            for j in range(2):
                q = q0 + j
                hidp, gatep = ps4[j], ps4[2 + j]
                mmB = ctx.mm.n

                @blk.scalar
                def _(scalar, q=q, hidp=hidp, gatep=gatep, mmB=mmB, j=j):
                    scalar.wait_ge(ctx.mm.h, mmB)
                    scalar.wait_ge(ctx.ev.h, ctx.ev.n)
                    scalar.activation(out=az[:, 0:TC], in_=gatep[:], func=AF.Sigmoid,
                                      scale=-1.0)
                    scalar.activation(out=az[:, TC:2 * TC], in_=gatep[:],
                                      func=AF.Sigmoid)
                    ctx.ea.inc(scalar.activation(out=s1[:], in_=hidp[:],
                                                 func=AF.Relu), 1)

                @blk.vector
                def _(vector, q=q, hidp=hidp):
                    ctx.ea.wait(vector)
                    ctx.ev.inc(vector.tensor_scalar(out=s2[:], in0=hidp[:], scalar1=0.0,
                                                    scalar2=None, op0=ALU.min), 1)

                @blk.scalar
                def _(scalar, q=q):
                    ctx.ev.wait(scalar)
                    ctx.ea.inc(scalar.activation(out=s2[:], in_=s2[:],
                                                 func=AF.Sigmoid), 1)

                @blk.vector
                def _(vector, q=q):
                    ctx.ea.wait(vector)
                    vector.tensor_tensor(out=s1[:], in0=s1[:], in1=s2[:], op=ALU.add)
                    vector.tensor_tensor(out=s2[:], in0=az[:, TC:2 * TC], in1=s1[:],
                                         op=ALU.mult)
                    ctx.vgap(vector)
                    vector.tensor_tensor_scan(out=hbuf[:, q * TC:(q + 1) * TC],
                                              data0=a_sb, data1=s2[:], initial=0.0,
                                              op0=ALU.mult, op1=ALU.add)
                    ctx.vgap(vector)
                    vector.wait_ge(ctx.dg.h, ctx.dg.n)
                    vector.tensor_tensor_scan(out=pst[:, :], data0=a_sb, data1=zeros[:],
                                              initial=1.0, op0=ALU.mult, op1=ALU.add)
                    ctx.vgap(vector)
                    vector.tensor_copy(out=stg[:, q:q + 1],
                                       in_=hbuf[:, q * TC + TC - 1:q * TC + TC])
                    ctx.ev.inc(vector.tensor_copy(out=stg[:, 24 + q:25 + q],
                                                  in_=pst[:, TC - 1:TC]), 1)

                @blk.gpsimd
                def _(g, q=q):
                    ctx.ev.wait(g)
                    ctx.dg.inc(g.dma_start(out=pspill.ap()[:, q * TC:(q + 1) * TC],
                                           in_=pst[:, :]), 16)

        mmX = ctx.mm.n

        @blk.tensor
        def _(tensor):
            ctx.ev.wait(tensor)
            ctx.mm.inc(tensor.transpose(pA[0:48, 0:P], stg[:, :],
                                        ctx.bufs["ID_F32"]), 1)

        @blk.scalar
        def _(scalar):
            ctx.mm.wait(scalar)
            ctx.ea.inc(scalar.copy(out=stgT[:, :], in_=pA[0:48, 0:P]), 1)

        @blk.sync
        def _(sync):
            ctx.ea.wait(sync)
            ctx.ds.inc(sync.dma_start(
                out=ccin.ap()[0:1, :].rearrange("o (q p) -> (o q) p", p=P),
                in_=stgT[0:24, :]), 16)
            ctx.ds.inc(sync.dma_start(
                out=ccin.ap()[1:2, :].rearrange("o (q p) -> (o q) p", p=P),
                in_=stgT[24:48, :]), 16)

        @blk.gpsimd
        def _(g):
            ctx.ds.wait(g)
            ctx.cc.inc(g.collective_compute(
                "AllGather", ALU.bypass,
                replica_groups=[[0, 1, 2, 3, 4, 5, 6, 7]],
                ins=[ccin.ap()], outs=[ccout.ap()]), 1)

        @blk.gpsimd
        def _(g):
            ctx.cc.wait(g)
            cov = ccout.ap().rearrange("(j two) c -> two j c", two=2)
            ctx.dg.inc(g.dma_start(out=he[:, :], in_=cov[0]), 16)
            ctx.dg.inc(g.dma_start(out=pt8[:, :], in_=cov[1]), 16)

        @blk.gpsimd
        def _(g):
            ctx.dg.wait(g)
            ctx.ep.inc(g.tensor_scalar(out=pt8[:, :], in0=pt8[:, :], scalar1=1e-38,
                                       scalar2=None, op0=ALU.max), 1)

        @blk.scalar
        def _(scalar):
            ctx.ep.wait(scalar)
            ctx.ea.inc(scalar.activation(out=pt8[:, :], in_=pt8[:, :], func=AF.Ln), 1)

        mmW = ctx.mm.n
        eaW0 = ctx.ea.n

        @blk.tensor
        def _(tensor):
            ctx.ea.wait(tensor)
            for sl6 in range(6):
                if sl6 >= 2:
                    tensor.wait_ge(ctx.ea.h, eaW0 + sl6 - 1)
                ctx.mm.inc(tensor.matmul(pB[0:8, (sl6 % 2) * HF:(sl6 % 2 + 1) * HF],
                                         tm[:, :], pt8[:, sl6 * HF:(sl6 + 1) * HF],
                                         start=True, stop=True), 1)

        @blk.scalar
        def _(scalar):
            for sl6 in range(6):
                scalar.wait_ge(ctx.mm.h, mmW + sl6 + 1)
                ctx.ea.inc(scalar.activation(out=w6[:, sl6 * HF:(sl6 + 1) * HF],
                                             in_=pB[0:8, (sl6 % 2) * HF:(sl6 % 2 + 1) * HF],
                                             func=AF.Exp, bias=tb[:, 0:1]), 1)

        @blk.gpsimd
        def _(g):
            ctx.ea.wait(g)
            ctx.ep.inc(g.tensor_tensor(out=w6[:, :], in0=w6[:, :], in1=he[:, :],
                                       op=ALU.mult), 1)

        mmH = ctx.mm.n

        @blk.tensor
        def _(tensor):
            ctx.ep.wait(tensor)
            for q in range(NQ):
                ctx.mm.inc(tensor.matmul(pC[:, q:q + 1],
                                         w6[:, q * P:(q + 1) * P],
                                         ctx.bufs["ONES1B"][0:8, 0:1],
                                         start=True, stop=True), 1)

        @blk.scalar
        def _(scalar):
            scalar.wait_ge(ctx.mm.h, mmH + NQ)
            ctx.ea.inc(scalar.copy(out=hcols[:, :], in_=pC[:, 0:NQ]), 1)

        dsc1 = ctx.dsc.n
        ev1 = ctx.ev.n

        @blk.scalar
        def _(scalar):
            ctx.dg.wait(scalar)
            for q in range(NQ):
                sl = q % 2
                if q >= 2:
                    scalar.wait_ge(ctx.ev.h, ev1 + q - 1)
                ctx.dsc.inc(scalar.dma_start(
                    out=pst2[:, sl * TC:(sl + 1) * TC],
                    in_=pspill.ap()[:, q * TC:(q + 1) * TC]), 16)

        @blk.vector
        def _(vector):
            ctx.ea.wait(vector)
            for q in range(NQ):
                sl = q % 2
                vector.wait_ge(ctx.dsc.h, dsc1 + 16 * (q + 1))
                ctx.ev.inc(vector.scalar_tensor_tensor(
                    out=hbuf[:, q * TC:(q + 1) * TC], in0=pst2[:, sl * TC:(sl + 1) * TC],
                    scalar=hcols[:, q:q + 1],
                    in1=hbuf[:, q * TC:(q + 1) * TC],
                    op0=ALU.mult, op1=ALU.add), 1)

        ctx.bar()

        for og2 in range(4):
            pieces = [wout[rk * P:(rk + 1) * P, og2 * 512:(og2 + 1) * 512]
                      for rk in range(NQ)]
            eaW = ctx.ea.n

            def prewait(tensor, n, eaW=eaW):
                if n == 0:
                    tensor.wait_ge(ctx.ea.h, eaW)

            def onp(tensor, n, slot):
                rk = n
                m = None
                for oc in range(4):
                    for h in range(2):
                        m = tensor.matmul(
                            ps4[oc][:, h * HF:(h + 1) * HF],
                            slot[:, oc * P:(oc + 1) * P],
                            hbuf[:, rk * TC + h * HF:rk * TC + (h + 1) * HF],
                            start=(rk == 0), stop=(rk == NQ - 1))
                return m

            mmWo = ctx.mm.n
            wstream(ctx, pieces, 512, onp, prewait)

            @blk.scalar
            def _(scalar, og2=og2, mmWo=mmWo):
                scalar.wait_ge(ctx.mm.h, mmWo + NQ)
                for oc in range(4):
                    ctx.ea.inc(scalar.copy(
                        out=ctx.xT[:, (og2 * 4 + oc) * TC:(og2 * 4 + oc + 1) * TC],
                        in_=ps4[oc][:]), 1)

        ctx.bar()

    esx.close()
    emit_ln(ctx, lambda q: ctx.xT[:, q * TC:(q + 1) * TC], "residual",
            xold_dram=ctx.xspill)


def emit_l0_attn(ctx, W):
    nc, blk = ctx.nc, ctx.block
    ctx.fresh_dma_sems("l0a")
    NH = 16

    qspill = nc.dram_tensor(_nm("qspill"), [D, TC], BF16)
    ospill = nc.dram_tensor(_nm("ospill"), [D, TC], BF16)
    kspill = nc.dram_tensor(_nm("kspill"), [D, TC], BF16)
    ccs_in = nc.dram_tensor(_nm("ccs_in"), [P, D], F32)
    ccs_out = nc.dram_tensor(_nm("ccs_out"), [NCORE * P, D], F32)

    cs_np = np.zeros((P, P), dtype=np.float32)
    for s_ in range(P):
        for c_ in range(P):
            cs_np[s_, c_] = (c_ - s_) if c_ >= s_ else 1e9
    c_cs = nc.inline_tensor(cs_np, _nm("c_cs"))
    c127_np = (127.0 - np.arange(P, dtype=np.float32))[:, None]
    c_127 = nc.inline_tensor(c127_np, _nm("c_127"))
    cqp_np = np.zeros((P, 8), dtype=np.float32)
    ckp_np = np.zeros((P, 8), dtype=np.float32)
    for i_ in range(8):
        cqp_np[:, i_] = i_ * P + np.arange(P) + 1.0
        ckp_np[:, i_] = 1023.0 - i_ * P - np.arange(P)
    c_qp = nc.inline_tensor(cqp_np, _nm("c_qp"))
    c_kp = nc.inline_tensor(ckp_np, _nm("c_kp"))
    ej_np = np.zeros((8, 8 * P), dtype=np.float32)
    for j_ in range(8):
        ej_np[j_, j_ * P:(j_ + 1) * P] = 1.0
    c_ej = nc.inline_tensor(ej_np, _nm("c_ej"))

    with (
        nc.sbuf_tensor(_nm("a0_v"), [P, NT * D], BF16) as V,
        nc.sbuf_tensor(_nm("a0_g1"), [P, TC], BF16) as g1,
        nc.sbuf_tensor(_nm("a0_wg2"), [P, D], BF16) as wg2,
        nc.sbuf_tensor(_nm("a0_ldr"), [1, 16], F32) as ldr,
        nc.sbuf_tensor(_nm("a0_msel"), [8, 1], F32) as msel,
        nc.sbuf_tensor(_nm("a0_ej"), [8, 8 * P], F32) as ejs,
        nc.sbuf_tensor(_nm("a0_nwr"), [1, D], F32) as nwr,
    ):
        with (
            nc.sbuf_tensor(_nm("xln"), [P, KC * TC], BF16) as xln,
            nc.sbuf_tensor(_nm("a0_stg"), [P, 2 * TC], BF16) as stg,
        ):
            ctx.xln = xln
            emit_ln(ctx, lambda q: ctx.xT[:, q * TC:(q + 1) * TC], "xln")
            emit_spill_x(ctx)
            esp = contextlib.ExitStack()
            ps4 = [esp.enter_context(nc.psum_tensor(_nm(f"a0_pp{i}"), [P, TC], F32))
                   for i in range(4)]
            pp0 = ps4[0]

            @blk.gpsimd
            def _(g):
                ctx.dg.inc(g.dma_start(out=wg2[:, :], in_=W["L0_attn_Wg2"]), 16)

            @blk.sync
            def _(sync):
                ctx.ds.inc(sync.dma_start(
                    out=ldr[0:1, :],
                    in_=W["L0_attn_log_decay"].rearrange("(o e) -> o e", o=1)), 16)
                ctx.ds.inc(sync.dma_start(out=msel[:, :], in_=W["msel"]), 16)
                ctx.ds.inc(sync.dma_start(out=ejs[:, :], in_=c_ej.ap()), 16)
                ctx.ds.inc(sync.dma_start(
                    out=nwr[0:1, :],
                    in_=W["L0_attn_norm_w"].rearrange("(o e) -> o e", o=1)), 16)

            def qk_proj(mat_ap, spill):
                eaP = [ctx.ea.n]
                dscP = [ctx.dsc.n]

                def prewait(tensor, n, eaP=eaP):
                    og = n // KC
                    if n == 0:
                        tensor.wait_ge(ctx.ea.h, eaP[0])
                    if n % KC == 0 and og >= 1:
                        tensor.wait_ge(ctx.ea.h, eaP[0] + og * 4)

                def onp(tensor, n, slot):
                    k = n % KC
                    m = None
                    for oc in range(4):
                        for h in range(2):
                            m = tensor.matmul(
                                ps4[oc][:, h * HF:(h + 1) * HF],
                                slot[:, oc * P:(oc + 1) * P],
                                xln[:, k * TC + h * HF:k * TC + (h + 1) * HF],
                                start=(k == 0), stop=(k == KC - 1))
                    return m

                pieces = []
                for og in range(4):
                    for k in range(KC):
                        pieces.append(mat_ap[k * P:(k + 1) * P, og * 512:(og + 1) * 512])
                mmP = ctx.mm.n
                wstream(ctx, pieces, 512, onp, prewait)

                eaQ0 = [0]

                @blk.scalar
                def _(scalar):
                    nd0 = dscP[0]
                    eaQ0[0] = ctx.ea.n
                    for og in range(4):
                        scalar.wait_ge(ctx.mm.h, mmP + (og + 1) * KC)
                        for oc in range(4):
                            hh = og * 4 + oc
                            sl = hh % 2
                            if hh >= 2:
                                scalar.wait_ge(ctx.dsc.h, nd0 + 16 * (hh - 1))
                            ctx.ea.inc(scalar.activation(
                                out=stg[:, sl * TC:(sl + 1) * TC],
                                in_=ps4[oc][:], func=AF.Silu), 1)

                @blk.sync
                def _(sync):
                    for hh in range(16):
                        sl = hh % 2
                        sync.wait_ge(ctx.ea.h, eaQ0[0] + hh + 1)
                        ctx.dsc.inc(sync.dma_start(
                            out=spill.ap()[hh * P:(hh + 1) * P, :],
                            in_=stg[:, sl * TC:(sl + 1) * TC]), 16)

            qk_proj(W["L0_attn_Wq"], qspill)
            qk_proj(W["L0_attn_Wk"], kspill)

            eaV = [ctx.ea.n]

            def prewaitv(tensor, n):
                og = n // KC
                if n == 0:
                    tensor.wait_ge(ctx.ea.h, eaV[0])
                if n % KC == 0 and og >= 1:
                    tensor.wait_ge(ctx.ea.h, eaV[0] + og * 8)

            def onpv(tensor, n, slot):
                k = n % KC
                m = None
                for t in range(NT):
                    tgt = ps4[t % 4][:, (t // 4) * HF:(t // 4 + 1) * HF]
                    m = tensor.matmul(tgt,
                                      xln[:, k * TC + t * P:k * TC + (t + 1) * P],
                                      slot[:, :],
                                      start=(k == 0), stop=(k == KC - 1))
                return m

            piecesv = []
            for og in range(4):
                for k in range(KC):
                    piecesv.append(W["L0_attn_Wv"][k * P:(k + 1) * P,
                                                   og * 512:(og + 1) * 512])
            mmV = ctx.mm.n
            wstream(ctx, piecesv, 512, onpv, prewaitv)

            @blk.scalar
            def _(scalar):
                for og in range(4):
                    scalar.wait_ge(ctx.mm.h, mmV + (og + 1) * KC)
                    for t in range(NT):
                        ctx.ea.inc(scalar.activation(
                            out=V[:, t * D + og * 512:t * D + (og + 1) * 512],
                            in_=ps4[t % 4][:, (t // 4) * HF:(t // 4 + 1) * HF],
                            func=AF.Silu), 1)

            eaG = [ctx.ea.n]

            def prewaitg(tensor, n):
                if n == 0:
                    tensor.wait_ge(ctx.ea.h, eaG[0])

            def onpg(tensor, n, slot):
                k = n
                m = None
                for h in range(2):
                    m = tensor.matmul(pp0[:, h * HF:(h + 1) * HF], slot[:, 0:P],
                                      xln[:, k * TC + h * HF:k * TC + (h + 1) * HF],
                                      start=(k == 0), stop=(k == KC - 1))
                return m

            piecesg = [W["L0_attn_Wg1"][k * P:(k + 1) * P, :] for k in range(KC)]
            mmG = ctx.mm.n
            wstream(ctx, piecesg, 128, onpg, prewaitg)

            @blk.scalar
            def _(scalar):
                scalar.wait_ge(ctx.mm.h, mmG + KC)
                ctx.ea.inc(scalar.copy(out=g1[:, :], in_=pp0[:]), 1)

            ctx.bar()
            esp.close()

        es = contextlib.ExitStack()
        csb = es.enter_context(nc.sbuf_tensor(_nm("a0_csb"), [P, P], F32))
        cpp = es.enter_context(nc.sbuf_tensor(_nm("a0_cpp"), [P, 17], F32))
        DT = es.enter_context(nc.sbuf_tensor(_nm("a0_dt"), [P, NH * P], BF16))
        ldbc = es.enter_context(nc.sbuf_tensor(_nm("a0_ldbc"), [P, 16], F32))
        KDG = es.enter_context(nc.sbuf_tensor(_nm("a0_kdg"), [P, 8 * 16], F32))
        QDG = es.enter_context(nc.sbuf_tensor(_nm("a0_qdg"), [P, 8 * 16], F32))
        KDL = es.enter_context(nc.sbuf_tensor(_nm("a0_kdl"), [P, 16], F32))
        CD = es.enter_context(nc.sbuf_tensor(_nm("a0_cd"), [P, 16], F32))
        coef8 = es.enter_context(nc.sbuf_tensor(_nm("a0_cf8"), [8, 16], F32))
        COEFB = es.enter_context(nc.sbuf_tensor(_nm("a0_coefb"), [P, 8 * 16], F32))
        NW = es.enter_context(nc.sbuf_tensor(_nm("a0_nw"), [P, D], BF16))
        S_f32 = es.enter_context(nc.sbuf_tensor(_nm("a0_sf"), [P, D], F32))
        S_bf = es.enter_context(nc.sbuf_tensor(_nm("a0_sb"), [P, D], BF16))
        og = es.enter_context(nc.sbuf_tensor(_nm("a0_og"), [P, D], F32))
        gs = es.enter_context(nc.sbuf_tensor(_nm("a0_gs"), [P, D], F32))
        ks2 = es.enter_context(nc.sbuf_tensor(_nm("a0_ks"), [P, 2 * D], BF16))
        st2 = es.enter_context(nc.sbuf_tensor(_nm("a0_st2"), [P, 512], BF16))
        rmsb = es.enter_context(nc.sbuf_tensor(_nm("a0_rms"), [P, 33], F32))
        sqtr = es.enter_context(nc.sbuf_tensor(_nm("a0_sqtr"), [P, P], F32))
        og_bf = es.enter_context(nc.sbuf_tensor(_nm("a0_ogbf"), [P, D], BF16))
        ktok = es.enter_context(nc.sbuf_tensor(_nm("a0_ktok"), [P, 2 * D], BF16))
        qti = es.enter_context(nc.sbuf_tensor(_nm("a0_qti"), [P, 2 * D], BF16))
        kti = es.enter_context(nc.sbuf_tensor(_nm("a0_kti"), [P, 2 * D], BF16))
        onTi = es.enter_context(nc.sbuf_tensor(_nm("a0_onti"), [P, D], BF16))
        if True:
            with (
                nc.psum_tensor(_nm("a0_pc0"), [P, TC], F32) as pc0,
                nc.psum_tensor(_nm("a0_pc1"), [P, TC], F32) as pc1,
            ):
                mmC = ctx.mm.n

                @blk.sync
                def _(sync):
                    ctx.ds.inc(sync.dma_start(out=csb[:, :], in_=c_cs.ap()), 16)
                    ctx.ds.inc(sync.dma_start(out=cpp[:, 0:1], in_=c_127.ap()), 16)
                    ctx.ds.inc(sync.dma_start(out=cpp[:, 1:9], in_=c_qp.ap()), 16)
                    ctx.ds.inc(sync.dma_start(out=cpp[:, 9:17], in_=c_kp.ap()), 16)

                eaC = ctx.ea.n

                @blk.tensor
                def _(tensor):
                    ctx.ds.wait(tensor)
                    ctx.mm.inc(tensor.matmul(pc0[:, 0:16], ctx.bufs["ONESROW"],
                                             ldr[0:1, :], start=True, stop=True), 1)
                    for r in range(2):
                        if r == 1:
                            tensor.wait_ge(ctx.ea.h, eaC + 2)
                        for h2 in range(2):
                            h = 2 * r + h2
                            ctx.mm.inc(tensor.matmul(pc1[:, h2 * HF:(h2 + 1) * HF],
                                                     ctx.bufs["ONESROW"],
                                                     nwr[0:1, h * HF:(h + 1) * HF],
                                                     start=True, stop=True), 1)

                @blk.scalar
                def _(scalar):
                    scalar.wait_ge(ctx.mm.h, mmC + 1)
                    ctx.ea.inc(scalar.copy(out=ldbc[:, :], in_=pc0[:, 0:16]), 1)
                    for r in range(2):
                        scalar.wait_ge(ctx.mm.h, mmC + 1 + (r + 1) * 2)
                        ctx.ea.inc(scalar.copy(out=NW[:, r * TC:(r + 1) * TC],
                                               in_=pc1[:]), 1)

                @blk.vector
                def _(vector):
                    ctx.ea.wait(vector)
                    for h in range(NH):
                        vector.tensor_scalar(out=gs[:, h * P:(h + 1) * P], in0=csb[:, :],
                                             scalar1=ldbc[:, h:h + 1], scalar2=None,
                                             op0=ALU.mult)
                    vector.tensor_scalar(out=KDL[:, :], in0=ldbc[:, :],
                                         scalar1=cpp[:, 0:1], scalar2=None, op0=ALU.mult)
                    vector.tensor_scalar(out=CD[:, :], in0=ldbc[:, :], scalar1=128.0,
                                         scalar2=None, op0=ALU.mult)
                    for i in range(8):
                        vector.tensor_scalar(out=QDG[:, i * 16:(i + 1) * 16],
                                             in0=ldbc[:, :], scalar1=cpp[:, 1 + i:2 + i],
                                             scalar2=None, op0=ALU.mult)
                        vector.tensor_scalar(out=KDG[:, i * 16:(i + 1) * 16],
                                             in0=ldbc[:, :], scalar1=cpp[:, 9 + i:10 + i],
                                             scalar2=None, op0=ALU.mult)
                    ctx.ev.inc(vector.tensor_scalar(out=coef8[:, :], in0=ldbc[0:8, :],
                                                    scalar1=msel[:, 0:1], scalar2=None,
                                                    op0=ALU.mult), 1)

                @blk.scalar
                def _(scalar):
                    ctx.ev.wait(scalar)
                    scalar.activation(out=DT[:, :], in_=gs[:, :], func=AF.Exp)
                    scalar.activation(out=KDL[:, :], in_=KDL[:, :], func=AF.Exp)
                    scalar.activation(out=CD[:, :], in_=CD[:, :], func=AF.Exp)
                    scalar.activation(out=QDG[:, :], in_=QDG[:, :], func=AF.Exp)
                    scalar.activation(out=KDG[:, :], in_=KDG[:, :], func=AF.Exp)
                    ctx.ea.inc(scalar.activation(out=coef8[:, :], in_=coef8[:, :],
                                                 func=AF.Exp), 1)

                mmE = ctx.mm.n

                @blk.tensor
                def _(tensor):
                    ctx.ea.wait(tensor)
                    eaEJ = ctx.ea.n
                    for j in range(8):
                        if j >= 2:
                            tensor.wait_ge(ctx.ea.h, eaEJ + j - 1)
                        ctx.mm.inc(tensor.matmul(
                            pc0[:, (j % 2) * 16:(j % 2) * 16 + 16],
                            ejs[:, j * P:(j + 1) * P], coef8[:, :],
                            start=True, stop=True), 1)

                @blk.scalar
                def _(scalar):
                    for j in range(8):
                        scalar.wait_ge(ctx.mm.h, mmE + j + 1)
                        ctx.ea.inc(scalar.copy(out=COEFB[:, j * 16:(j + 1) * 16],
                                               in_=pc0[:, (j % 2) * 16:(j % 2) * 16 + 16]), 1)

                ctx.bar()

            with nc.psum_tensor(_nm("a0_psA"), [P, D], F32) as psS:
                trs = [Sem(nc, _nm("s_tr0")), Sem(nc, _nm("s_tr1"))]
                eaA0 = ctx.ea.n
                mmA0 = ctx.mm.n
                for i in range(NT):
                    sl = i % 2

                    @blk.sync
                    def _(sync, i=i, sl=sl):
                        if i >= 2:
                            sync.wait_ge(ctx.ea.h, eaA0 + (i - 1) * NH)
                        trs[sl].inc(sync.dma_start_transpose(
                            out=ktok[:, sl * D:(sl + 1) * D],
                            in_=kspill.ap()[:, i * P:(i + 1) * P]), 16)

                    @blk.scalar
                    def _(scalar, i=i, sl=sl):
                        trs[sl].wait(scalar)
                        if i >= 2:
                            scalar.wait_ge(ctx.mm.h, mmA0 + i - 1)
                        for h in range(NH):
                            ctx.ea.inc(scalar.activation(
                                out=ks2[:, sl * D + h * P:sl * D + (h + 1) * P],
                                in_=ktok[:, sl * D + h * P:sl * D + (h + 1) * P],
                                func=AF.Copy, scale=KDG[:, i * 16 + h:i * 16 + h + 1]), 1)

                    @blk.tensor
                    def _(tensor, i=i, sl=sl):
                        ctx.ea.wait(tensor)
                        m = None
                        for h in range(NH):
                            m = tensor.matmul(psS[:, h * P:(h + 1) * P],
                                              ks2[:, sl * D + h * P:sl * D + (h + 1) * P],
                                              V[:, i * D + h * P:i * D + (h + 1) * P],
                                              start=(i == 0 and h % 4 == 0),
                                              stop=(i == NT - 1 and h % 4 == 3))
                        ctx.mm.inc(m, 1)

                @blk.scalar
                def _(scalar):
                    ctx.mm.wait(scalar)
                    for h in range(4):
                        ctx.ea.inc(scalar.copy(out=og[:, h * HF:(h + 1) * HF],
                                               in_=psS[:, h * HF:(h + 1) * HF]), 1)

                @blk.sync
                def _(sync):
                    ctx.ea.wait(sync)
                    ctx.ds.inc(sync.dma_start(out=ccs_in.ap(), in_=og[:, :]), 16)

                @blk.gpsimd
                def _(g):
                    ctx.ds.wait(g)
                    ctx.cc.inc(g.collective_compute(
                        "AllGather", ALU.bypass,
                        replica_groups=[[0, 1, 2, 3, 4, 5, 6, 7]],
                        ins=[ccs_in.ap()], outs=[ccs_out.ap()]), 1)

                ctx.bar()

            @blk.vector
            def _(vector):
                ctx.ev.inc(vector.memset(S_f32[:, :], 0.0), 1)

            dscS = ctx.dsc.n
            evS0 = ctx.ev.n
            with nc.sbuf_tensor(_nm("a0_sj"), [P, D], F32) as sj:
                @blk.scalar
                def _(scalar):
                    ctx.cc.wait(scalar)
                    for j in range(8):
                        if j >= 1:
                            scalar.wait_ge(ctx.ev.h, evS0 + j * NH)
                        ctx.dsc.inc(scalar.dma_start(
                            out=sj[:, :],
                            in_=ccs_out.ap()[j * P:(j + 1) * P, :]), 16)

                @blk.vector
                def _(vector):
                    for j in range(8):
                        vector.wait_ge(ctx.dsc.h, dscS + 16 * (j + 1))
                        for h in range(NH):
                            ctx.ev.inc(vector.scalar_tensor_tensor(
                                out=S_f32[:, h * P:(h + 1) * P],
                                in0=sj[:, h * P:(h + 1) * P],
                                scalar=COEFB[:, j * 16 + h:j * 16 + h + 1],
                                in1=S_f32[:, h * P:(h + 1) * P],
                                op0=ALU.mult, op1=ALU.add), 1)

                @blk.scalar
                def _(scalar):
                    ctx.ev.wait(scalar)
                    ctx.ea.inc(scalar.copy(out=S_bf[:, :], in_=S_f32[:, :]), 1)

                ctx.bar()

            with (
                nc.psum_tensor(_nm("a0_pO"), [P, D], F32) as pO,
                nc.psum_tensor(_nm("a0_pst"), [P, 512], F32) as pST,
                nc.psum_tensor(_nm("a0_px"), [P, 512], F32) as pX,
                nc.psum_tensor(_nm("a0_ps2"), [P, 512], F32) as pS2,
            ):
                qrs = [Sem(nc, _nm("s_qr0")), Sem(nc, _nm("s_qr1"))]
                krs = [Sem(nc, _nm("s_kr0")), Sem(nc, _nm("s_kr1"))]
                trb = [Sem(nc, _nm("s_tb0")), Sem(nc, _nm("s_tb1"))]
                for i in range(NT):
                    sl = i % 2
                    mm0 = ctx.mm.n
                    ea0 = ctx.ea.n
                    ev0 = ctx.ev.n

                    @blk.sync
                    def _(sync, i=i, sl=sl):
                        qrs[sl].inc(sync.dma_start(
                            out=qti[:, sl * D:(sl + 1) * D],
                            in_=qspill.ap()
                            .rearrange("(f p) t -> p f t", p=P)[:, :, i * P:(i + 1) * P]), 16)
                        krs[sl].inc(sync.dma_start(
                            out=kti[:, sl * D:(sl + 1) * D],
                            in_=kspill.ap()
                            .rearrange("(f p) t -> p f t", p=P)[:, :, i * P:(i + 1) * P]), 16)
                        trb[sl].inc(sync.dma_start_transpose(
                            out=ktok[:, sl * D:(sl + 1) * D],
                            in_=kspill.ap()[:, i * P:(i + 1) * P]), 16)

                    for gI in range(4):
                        hs0 = gI * 4

                        @blk.tensor
                        def _(tensor, i=i, sl=sl, gI=gI, hs0=hs0, mm0=mm0, ev0=ev0, ea0=ea0):
                            if gI == 0:
                                tensor.wait_ge(qrs[sl].h, qrs[sl].n)
                                tensor.wait_ge(krs[sl].h, krs[sl].n)
                            else:
                                tensor.wait_ge(ctx.ev.h, ctx.ev.n)
                            m = None
                            for hh in range(4):
                                h = hs0 + hh
                                m = tensor.matmul(pST[:, hh * P:(hh + 1) * P],
                                                  kti[:, sl * D + h * P:sl * D + (h + 1) * P],
                                                  qti[:, sl * D + h * P:sl * D + (h + 1) * P],
                                                  start=True, stop=True)
                            ctx.mm.inc(m, 1)

                        @blk.vector
                        def _(vector, i=i, gI=gI, hs0=hs0):
                            ctx.mm.wait(vector)
                            m = None
                            for hh in range(4):
                                h = hs0 + hh
                                m = vector.tensor_tensor(
                                    out=st2[:, hh * P:(hh + 1) * P],
                                    in0=pST[:, hh * P:(hh + 1) * P],
                                    in1=DT[:, h * P:(h + 1) * P], op=ALU.mult)
                            ctx.ev.inc(m, 1)

                        @blk.tensor
                        def _(tensor, i=i, sl=sl, gI=gI, hs0=hs0):
                            ctx.ev.wait(tensor)
                            ctx.ea.wait(tensor)
                            m = None
                            for hh in range(4):
                                h = hs0 + hh
                                tensor.matmul(pO[:, h * P:(h + 1) * P],
                                              st2[:, hh * P:(hh + 1) * P],
                                              V[:, i * D + h * P:i * D + (h + 1) * P],
                                              start=True, stop=True)
                                m = tensor.matmul(pX[:, hh * P:(hh + 1) * P],
                                                  qti[:, sl * D + h * P:sl * D + (h + 1) * P],
                                                  S_bf[:, h * P:(h + 1) * P],
                                                  start=True, stop=True)
                            ctx.mm.inc(m, 1)

                        @blk.scalar
                        def _(scalar, i=i, sl=sl, gI=gI, hs0=hs0):
                            ctx.mm.wait(scalar)
                            if gI == 0:
                                trb[sl].wait(scalar)
                            m = None
                            for hh in range(4):
                                h = hs0 + hh
                                scalar.activation(
                                    out=og[:, h * P:(h + 1) * P],
                                    in_=pX[:, hh * P:(hh + 1) * P], func=AF.Copy,
                                    scale=QDG[:, h:h + 1])
                                m = scalar.activation(
                                    out=ks2[:, sl * D + h * P:sl * D + (h + 1) * P],
                                    in_=ktok[:, sl * D + h * P:sl * D + (h + 1) * P],
                                    func=AF.Copy, scale=KDL[:, h:h + 1])
                            ctx.ea.inc(m, 1)

                        @blk.tensor
                        def _(tensor, i=i, sl=sl, gI=gI, hs0=hs0):
                            ctx.ea.wait(tensor)
                            ctx.ev.wait(tensor)
                            m = None
                            for hh in range(4):
                                h = hs0 + hh
                                m = tensor.matmul(pS2[:, hh * P:(hh + 1) * P],
                                                  ks2[:, sl * D + h * P:sl * D + (h + 1) * P],
                                                  V[:, i * D + h * P:i * D + (h + 1) * P],
                                                  start=True, stop=True)
                            ctx.mm.inc(m, 1)

                        @blk.vector
                        def _(vector, gI=gI, hs0=hs0):
                            ctx.mm.wait(vector)
                            m = None
                            for hh in range(4):
                                h = hs0 + hh
                                m = vector.scalar_tensor_tensor(
                                    out=S_f32[:, h * P:(h + 1) * P],
                                    in0=S_f32[:, h * P:(h + 1) * P],
                                    scalar=CD[:, h:h + 1],
                                    in1=pS2[:, hh * P:(hh + 1) * P],
                                    op0=ALU.mult, op1=ALU.add)
                            ctx.ev.inc(m, 1)

                        @blk.scalar
                        def _(scalar, gI=gI, hs0=hs0):
                            ctx.ev.wait(scalar)
                            ctx.ea.inc(scalar.copy(
                                out=S_bf[:, hs0 * P:(hs0 + 4) * P],
                                in_=S_f32[:, hs0 * P:(hs0 + 4) * P]), 1)

                    @blk.vector
                    def _(vector, i=i):
                        ctx.ea.wait(vector)
                        m = None
                        for h4 in range(4):
                            m = vector.tensor_tensor(
                                out=og[:, h4 * HF:(h4 + 1) * HF],
                                in0=og[:, h4 * HF:(h4 + 1) * HF],
                                in1=pO[:, h4 * HF:(h4 + 1) * HF], op=ALU.add)
                        ctx.ev.inc(m, 1)

                    @blk.tensor
                    def _(tensor, i=i):
                        ctx.ev.wait(tensor)
                        m = None
                        for h4 in range(4):
                            m = tensor.matmul(pO[:, h4 * HF:(h4 + 1) * HF],
                                              g1[:, i * P:(i + 1) * P],
                                              wg2[:, h4 * HF:(h4 + 1) * HF],
                                              start=True, stop=True)
                        ctx.mm.inc(m, 1)

                    @blk.scalar
                    def _(scalar, i=i):
                        ctx.mm.wait(scalar)
                        ctx.ea.inc(scalar.activation(out=gs[:, :], in_=pO[:, :],
                                                     func=AF.Sigmoid), 1)

                    @blk.vector
                    def _(vector, i=i):
                        ctx.ea.wait(vector)
                        ctx.ev.inc(vector.tensor_tensor(out=og[:, :], in0=og[:, :],
                                                        in1=gs[:, :], op=ALU.mult), 1)

                    @blk.scalar
                    def _(scalar, i=i):
                        ctx.ev.wait(scalar)
                        for h in range(NH):
                            scalar.activation(out=sqtr[:, :], in_=og[:, h * P:(h + 1) * P],
                                              func=AF.Square,
                                              accum_out=rmsb[:, h:h + 1])
                        scalar.activation(out=rmsb[:, 16:32], in_=rmsb[:, 0:16],
                                          func=AF.Ln, scale=1.0 / P,
                                          bias=ctx.bufs["EPSCOL"][:, 0:1])
                        ctx.ea.inc(scalar.activation(out=rmsb[:, 16:32],
                                                     in_=rmsb[:, 16:32],
                                                     func=AF.Exp, scale=-0.5), 1)

                    @blk.vector
                    def _(vector, i=i):
                        ctx.ea.wait(vector)
                        for h in range(NH):
                            vector.tensor_scalar(out=og[:, h * P:(h + 1) * P],
                                                 in0=og[:, h * P:(h + 1) * P],
                                                 scalar1=rmsb[:, 16 + h:17 + h],
                                                 scalar2=None, op0=ALU.mult)
                        m = None
                        for h4 in range(4):
                            m = vector.tensor_tensor(out=og[:, h4 * HF:(h4 + 1) * HF],
                                                     in0=og[:, h4 * HF:(h4 + 1) * HF],
                                                     in1=NW[:, h4 * HF:(h4 + 1) * HF],
                                                     op=ALU.mult)
                        ctx.ev.inc(m, 1)

                    @blk.scalar
                    def _(scalar, i=i):
                        ctx.ev.wait(scalar)
                        ctx.ea.inc(scalar.copy(out=og_bf[:, :], in_=og[:, :]), 1)

                    @blk.sync
                    def _(sync, i=i):
                        ctx.ea.wait(sync)
                        for f in range(NH):
                            ctx.ds.inc(sync.dma_start_transpose(
                                out=onTi[:, f * P:(f + 1) * P],
                                in_=og_bf[:, f * P:(f + 1) * P]), 16)

                    @blk.gpsimd
                    def _(g, i=i):
                        ctx.ds.wait(g)
                        ctx.dg.inc(g.dma_start(
                            out=ospill.ap().rearrange("(k p) t -> p k t", p=P)
                            [:, :, i * P:(i + 1) * P],
                            in_=onTi[:, :].rearrange("p (k c) -> p k c", c=P)), 16)

                    ctx.bar()

            onk = es.enter_context(nc.sbuf_tensor(_nm("a0_onk"), [P, 2 * TC], BF16))
            oks = [Sem(nc, _nm("s_ok0")), Sem(nc, _nm("s_ok1"))]
            for og2 in range(4):
                pieces = [W["L0_attn_Wo"][k * P:(k + 1) * P, og2 * 512:(og2 + 1) * 512]
                          for k in range(KC)]
                eaWo = ctx.ea.n
                mmWb = ctx.mm.n

                @blk.sync
                def _(sync, og2=og2, mmWb=mmWb):
                    if og2 == 0:
                        ctx.dg.wait(sync)
                    for k in range(KC):
                        sl = k % 2
                        if k >= 2:
                            sync.wait_ge(ctx.mm.h, mmWb + k - 1)
                        oks[sl].inc(sync.dma_start(
                            out=onk[:, sl * TC:(sl + 1) * TC],
                            in_=ospill.ap()[k * P:(k + 1) * P, :]), 16)

                def prewait(tensor, n, eaWo=eaWo, og2=og2):
                    sl = n % 2
                    tensor.wait_ge(oks[sl].h, oks[sl].n - 16 * (KC // 2 - 1 - n // 2))
                    if n == 0:
                        tensor.wait_ge(ctx.ea.h, eaWo)

                def onp(tensor, n, slot):
                    k = n
                    sl = k % 2
                    m = None
                    for oc in range(4):
                        for h in range(2):
                            m = tensor.matmul(
                                psW4[oc][:, h * HF:(h + 1) * HF],
                                slot[:, oc * P:(oc + 1) * P],
                                onk[:, sl * TC + h * HF:sl * TC + (h + 1) * HF],
                                start=(k == 0), stop=(k == KC - 1))
                    return m

                with (
                    nc.psum_tensor(_nm("a0_pw0"), [P, TC], F32) as pw0,
                    nc.psum_tensor(_nm("a0_pw1"), [P, TC], F32) as pw1,
                    nc.psum_tensor(_nm("a0_pw2"), [P, TC], F32) as pw2,
                    nc.psum_tensor(_nm("a0_pw3"), [P, TC], F32) as pw3,
                ):
                    psW4 = [pw0, pw1, pw2, pw3]
                    mmWo = ctx.mm.n
                    wstream(ctx, pieces, 512, onp, prewait)

                    @blk.scalar
                    def _(scalar, og2=og2, mmWo=mmWo):
                        ctx.dg.wait(scalar)
                        scalar.wait_ge(ctx.mm.h, mmWo + KC)
                        for oc in range(4):
                            ctx.ea.inc(scalar.copy(
                                out=ctx.xT[:, (og2 * 4 + oc) * TC:(og2 * 4 + oc + 1) * TC],
                                in_=psW4[oc][:]), 1)

                    ctx.bar()

            es.close()

    if os.environ.get("K_DBG_DUMP"):
        d_os = nc.dram_tensor("dbg_ospill", [D, TC], BF16, kind="ExternalOutput").ap()
        d_qs = nc.dram_tensor("dbg_qspill", [D, TC], BF16, kind="ExternalOutput").ap()
        d_cc = nc.dram_tensor("dbg_ccsin", [P, D], F32, kind="ExternalOutput").ap()
        d_cco = nc.dram_tensor("dbg_ccsout", [NCORE * P, D], F32, kind="ExternalOutput").ap()
        d_y = nc.dram_tensor("dbg_y", [P, KC * TC], F32, kind="ExternalOutput").ap()

        @blk.gpsimd
        def _(g):
            ctx.dg.inc(g.dma_start(out=d_os, in_=ospill.ap()), 16)
            ctx.dg.inc(g.dma_start(out=d_qs, in_=qspill.ap()), 16)
            ctx.dg.inc(g.dma_start(out=d_cc, in_=ccs_in.ap()), 16)
            ctx.dg.inc(g.dma_start(out=d_cco, in_=ccs_out.ap()), 16)
            ctx.dg.inc(g.dma_start(out=d_y, in_=ctx.xT[:, :]), 16)
        ctx.bar()

    emit_ln(ctx, lambda q: ctx.xT[:, q * TC:(q + 1) * TC], "residual",
            xold_dram=ctx.xspill)


EMITTERS = {"l0a": emit_l0_attn, "l0f": emit_l0_ffn, "l1a": emit_l1_attn,
            "l1f": emit_l1_ffn}


# version 03
# speedup vs baseline: 1.0478x; 1.0228x over previous
import contextlib
import os
import numpy as np
import ml_dtypes
import concourse.bass as bass
import concourse.mybir as mybir
from concourse.bass_utils import run_bass_kernel_spmd

F32 = mybir.dt.float32
BF16 = mybir.dt.bfloat16
AF = mybir.ActivationFunctionType
ALU = mybir.AluOpType
AX = mybir.AxisListType

P = 128
D = 2048
KC = D // P
TC = 1024
NT = TC // P
NCORE = 8
HF = 512
EPS = 1e-5

BF = ml_dtypes.bfloat16

_uid = [0]


def _nm(s):
    _uid[0] += 1
    return f"{s}_{_uid[0]}"


class Sem:

    def __init__(self, nc, name):
        self.h = nc.alloc_semaphore(name)
        self.n = 0

    def inc(self, instr, k=1):
        instr.then_inc(self.h, k)
        self.n += k
        return instr

    def wait(self, eng, val=None):
        eng.wait_ge(self.h, self.n if val is None else val)


class Ctx:
    def __init__(self, nc, block):
        self.nc = nc
        self.block = block
        self.mm = Sem(nc, "s_mm")
        self.ea = Sem(nc, "s_ea")
        self.ev = Sem(nc, "s_ev")
        self.ep = Sem(nc, "s_ep")
        self.ds = Sem(nc, "s_ds")
        self.dsc = Sem(nc, "s_dsc")
        self.dg = Sem(nc, "s_dg")
        self.wr = [Sem(nc, f"s_wr{i}") for i in range(3)]
        self.wring_free = [0, 0, 0]
        self.xr = [Sem(nc, f"s_xr{i}") for i in range(2)]
        self.cc = Sem(nc, "s_cc")
        self.barsem = Sem(nc, "s_bar")
        self.bufs = {}
        self.carry_ready = False
        self.carry_next = True

    def fresh_dma_sems(self, tag):
        nc = self.nc
        self.ds = Sem(nc, _nm(f"ds_{tag}"))
        self.dsc = Sem(nc, _nm(f"dsc_{tag}"))
        self.dg = Sem(nc, _nm(f"dg_{tag}"))
        self.wr = [Sem(nc, _nm(f"wr{i}_{tag}")) for i in range(3)]
        self.wring_free = [0, 0, 0]
        self.cc = Sem(nc, _nm(f"cc_{tag}"))

    def vgap(self, vector):
        vector.tensor_copy(out=self.dummy[:, 8:16], in_=self.dummy[:, 0:8])
        vector.tensor_copy(out=self.dummy[:, 16:24], in_=self.dummy[:, 0:8])

    def bar(self):
        b = self.barsem
        target = b.n + 5
        blk = self.block

        @blk.sync
        def _(e):
            e.drain().then_inc(b.h, 1)
            e.wait_ge(b.h, target)

        @blk.tensor
        def _(e):
            e.drain().then_inc(b.h, 1)
            e.wait_ge(b.h, target)

        @blk.vector
        def _(e):
            e.drain().then_inc(b.h, 1)
            e.wait_ge(b.h, target)

        @blk.scalar
        def _(e):
            e.drain().then_inc(b.h, 1)
            e.wait_ge(b.h, target)

        @blk.gpsimd
        def _(e):
            e.drain().then_inc(b.h, 1)
            e.wait_ge(b.h, target)

        b.n = target


def _np_bf(x):
    return np.asarray(x, dtype=BF)



def emit_prologue(ctx, xin, consts):
    nc, blk = ctx.nc, ctx.block
    xT = ctx.xT
    with (
        nc.sbuf_tensor(_nm("xstage"), [P, 2 * D], F32) as xstage,
        nc.sbuf_tensor(_nm("pro_sq"), [P, 2 * TC], F32) as sq,
        nc.psum_tensor(_nm("ppro0"), [P, P], F32) as pt0,
        nc.psum_tensor(_nm("ppro1"), [P, P], F32) as pt1,
        nc.psum_tensor(_nm("ppro_m"), [1, TC], F32) as ps_m,
        nc.psum_tensor(_nm("ppro_s"), [1, TC], F32) as ps_s,
    ):
        pts = [pt0, pt1]
        mm0, ea0 = ctx.mm.n, ctx.ea.n

        @blk.sync
        def _(sync):
            for name in consts:
                ctx.ds.inc(sync.dma_start(out=ctx.bufs[name], in_=ctx.inl[name].ap()), 16)
            for t in range(NT):
                s = t % 2
                if t >= 2:
                    sync.wait_ge(ctx.mm.h, mm0 + KC * (t - 1))
                sync.dma_start(out=xstage[:, s * D:(s + 1) * D],
                               in_=xin[t * P:(t + 1) * P, :]).then_inc(ctx.xr[s].h, 16)
                ctx.xr[s].n += 16

        @blk.vector
        def _(vector):
            vector.memset(ctx.dummy[:, :], 0.0)

        @blk.tensor
        def _(tensor):
            ctx.ds.wait(tensor)
            for t in range(NT):
                s = t % 2
                tensor.wait_ge(ctx.xr[s].h, 16 * (t // 2 + 1))
                for q in range(KC):
                    j = t * KC + q
                    if j >= 2:
                        tensor.wait_ge(ctx.ea.h, ea0 + j - 1)
                    ctx.mm.inc(tensor.transpose(
                        pts[j % 2][:], xstage[:, s * D + q * P: s * D + (q + 1) * P],
                        ctx.bufs["ID_F32"]), 1)

        @blk.scalar
        def _(scalar):
            for t in range(NT):
                for q in range(KC):
                    j = t * KC + q
                    scalar.wait_ge(ctx.mm.h, mm0 + j + 1)
                    ctx.ea.inc(scalar.copy(
                        out=xT[:, q * TC + t * P: q * TC + (t + 1) * P],
                        in_=pts[j % 2][:]), 1)

        mmS0, eaS0 = ctx.mm.n, ctx.ea.n

        @blk.scalar
        def _(scalar):
            for q in range(KC):
                if q >= 2:
                    scalar.wait_ge(ctx.mm.h, mmS0 + (q - 1))
                ctx.ea.inc(scalar.square(out=sq[:, (q % 2) * TC:(q % 2 + 1) * TC],
                                         in_=xT[:, q * TC:(q + 1) * TC]), 1)

        @blk.tensor
        def _(tensor):
            ones = ctx.bufs["ONES1"]
            for q in range(KC):
                tensor.wait_ge(ctx.ea.h, eaS0 + q + 1)
                for h in range(2):
                    tensor.matmul(ps_m[0:1, h * HF:(h + 1) * HF], ones,
                                  xT[:, q * TC + h * HF:q * TC + (h + 1) * HF],
                                  start=(q == 0), stop=(q == KC - 1))
                for h in range(2):
                    m = tensor.matmul(ps_s[0:1, h * HF:(h + 1) * HF], ones,
                                      sq[:, (q % 2) * TC + h * HF:(q % 2) * TC + (h + 1) * HF],
                                      start=(q == 0), stop=(q == KC - 1))
                ctx.mm.inc(m, 1)

        @blk.scalar
        def _(scalar):
            scalar.wait_ge(ctx.mm.h, mmS0 + KC)
            scalar.copy(out=sq[0:1, 0:TC], in_=ps_m[0:1, :])
            ctx.ea.inc(scalar.copy(out=sq[0:1, TC:2 * TC], in_=ps_s[0:1, :]), 1)

        @blk.sync
        def _(sync):
            ctx.ea.wait(sync)
            ctx.ds.inc(sync.dma_start(out=ctx.carryd.ap(), in_=sq[0:1, 0:2 * TC]), 16)
            ctx.ds.wait(sync)

        ctx.carry_ready = True
        ctx.bar()


def emit_epilogue(ctx, xout):
    nc, blk = ctx.nc, ctx.block
    xT = ctx.xT
    with (
        nc.sbuf_tensor(_nm("ostage"), [P, 2 * D], F32) as ostage,
        nc.psum_tensor(_nm("pepi0"), [P, P], F32) as pt0,
        nc.psum_tensor(_nm("pepi1"), [P, P], F32) as pt1,
    ):
        pts = [pt0, pt1]
        mm0, ea0 = ctx.mm.n, ctx.ea.n
        st = Sem(nc, _nm("s_st"))

        @blk.tensor
        def _(tensor):
            for t in range(NT):
                for q in range(KC):
                    j = t * KC + q
                    if j >= 2:
                        tensor.wait_ge(ctx.ea.h, ea0 + j - 1)
                    ctx.mm.inc(tensor.transpose(
                        pts[j % 2][:], xT[:, q * TC + t * P: q * TC + (t + 1) * P],
                        ctx.bufs["ID_F32"]), 1)

        @blk.scalar
        def _(scalar):
            for t in range(NT):
                s = t % 2
                if t >= 2:
                    scalar.wait_ge(st.h, 16 * (t - 1))
                for q in range(KC):
                    j = t * KC + q
                    scalar.wait_ge(ctx.mm.h, mm0 + j + 1)
                    ctx.ea.inc(scalar.copy(
                        out=ostage[:, s * D + q * P: s * D + (q + 1) * P],
                        in_=pts[j % 2][:]), 1)

        @blk.sync
        def _(sync):
            for t in range(NT):
                s = t % 2
                sync.wait_ge(ctx.ea.h, ea0 + (t + 1) * KC)
                st.inc(sync.dma_start(out=xout[t * P:(t + 1) * P, :],
                                      in_=ostage[:, s * D:(s + 1) * D]), 16)
            st.wait(sync)



LN_SPLIT = 10


def emit_ln(ctx, src_q, dst_mode, xold_dram=None):
    nc, blk = ctx.nc, ctx.block
    carry_in = dst_mode == "xln" and ctx.carry_ready
    carry_out = dst_mode == "residual" and ctx.carry_next
    SPL = LN_SPLIT
    import contextlib
    with contextlib.ExitStack() as _es:
        need_sq = (not carry_in) or carry_out
        if need_sq:
            sq = _es.enter_context(nc.sbuf_tensor(_nm("ln_sq"), [P, 2 * TC], F32))
        rows = _es.enter_context(nc.sbuf_tensor(_nm("ln_rows"), [1, 4 * TC], F32))
        rstd_s = _es.enter_context(nc.sbuf_tensor(_nm("ln_rstd"), [P, TC], F32))
        mrs_s = _es.enter_context(nc.sbuf_tensor(_nm("ln_mrs"), [P, TC], F32))
        tsc = _es.enter_context(nc.sbuf_tensor(_nm("ln_t"), [P, TC], F32))
        tsc2 = _es.enter_context(nc.sbuf_tensor(_nm("ln_t2"), [P, TC], F32))
        if dst_mode == "residual":
            xoldb = _es.enter_context(nc.sbuf_tensor(_nm("ln_xold"), [P, 4 * TC], F32))
        if not carry_in or carry_out:
            ps_m = _es.enter_context(nc.psum_tensor(_nm("ln_pm"), [1, TC], F32))
            ps_s = _es.enter_context(nc.psum_tensor(_nm("ln_ps"), [1, TC], F32))
        ps_b0 = _es.enter_context(nc.psum_tensor(_nm("ln_pb0"), [P, TC], F32))
        ps_b1 = _es.enter_context(nc.psum_tensor(_nm("ln_pb1"), [P, TC], F32))
        mm0, ea0, ev0 = ctx.mm.n, ctx.ea.n, ctx.ev.n
        dsl0 = ctx.ds.n
        if carry_in:
            @blk.sync
            def _(sync):
                ctx.ds.inc(sync.dma_start(out=rows[0:1, 2 * TC:4 * TC],
                                          in_=ctx.carryd.ap()), 16)

        if not carry_in:
            @blk.scalar
            def _(scalar):
                for q in range(KC):
                    if q >= 2:
                        scalar.wait_ge(ctx.mm.h, mm0 + (q - 1))
                    ctx.ea.inc(scalar.square(out=sq[:, (q % 2) * TC:(q % 2 + 1) * TC],
                                             in_=src_q(q)), 1)

            @blk.tensor
            def _(tensor):
                ones = ctx.bufs["ONES1"]
                for q in range(KC):
                    tensor.wait_ge(ctx.ea.h, ea0 + q + 1)
                    for h in range(2):
                        tensor.matmul(ps_m[0:1, h * HF:(h + 1) * HF], ones,
                                      src_q(q)[:, h * HF:(h + 1) * HF],
                                      start=(q == 0), stop=(q == KC - 1))
                    for h in range(2):
                        m = tensor.matmul(ps_s[0:1, h * HF:(h + 1) * HF], ones,
                                          sq[:, (q % 2) * TC + h * HF:(q % 2) * TC + (h + 1) * HF],
                                          start=(q == 0), stop=(q == KC - 1))
                    ctx.mm.inc(m, 1)

        m_row = rows[0:1, 0:TC]
        var_row = rows[0:1, TC:2 * TC]
        rstd_row = rows[0:1, 2 * TC:3 * TC]
        mrs_row = rows[0:1, 3 * TC:4 * TC]
        sum_m = rstd_row if carry_in else ps_m[0:1, :]
        sum_s = mrs_row if carry_in else ps_s[0:1, :]

        @blk.vector
        def _(vector):
            if carry_in:
                vector.wait_ge(ctx.ds.h, dsl0 + 16)
            else:
                vector.wait_ge(ctx.mm.h, mm0 + KC)
            vector.tensor_scalar(out=m_row, in0=sum_m, scalar1=1.0 / D,
                                 scalar2=None, op0=ALU.mult)
            vector.tensor_scalar(out=var_row, in0=sum_s, scalar1=1.0 / D,
                                 scalar2=None, op0=ALU.mult)
            vector.tensor_tensor(out=rstd_row, in0=m_row, in1=m_row, op=ALU.mult)
            ctx.ev.inc(vector.tensor_tensor(out=var_row, in0=var_row, in1=rstd_row,
                                            op=ALU.subtract), 1)

        @blk.scalar
        def _(scalar):
            ctx.ev.wait(scalar)
            scalar.activation(out=rstd_row, in_=var_row, func=AF.Ln, bias=ctx.bufs["EPSCOL"][0:1, 0:1])
            ctx.ea.inc(scalar.activation(out=rstd_row, in_=rstd_row, func=AF.Exp,
                                         scale=-0.5), 1)

        @blk.vector
        def _(vector):
            ctx.ea.wait(vector)
            ctx.ev.inc(vector.tensor_tensor(out=mrs_row, in0=m_row, in1=rstd_row,
                                            op=ALU.mult), 1)

        @blk.tensor
        def _(tensor):
            ctx.ev.wait(tensor)
            onesrow = ctx.bufs["ONESROW"]
            for h in range(2):
                tensor.matmul(ps_b0[:, h * HF:(h + 1) * HF], onesrow,
                              rstd_row[:, h * HF:(h + 1) * HF], start=True, stop=True)
            for h in range(2):
                m = tensor.matmul(ps_b1[:, h * HF:(h + 1) * HF], onesrow,
                                  mrs_row[:, h * HF:(h + 1) * HF], start=True, stop=True)
            ctx.mm.inc(m, 1)

        @blk.scalar
        def _(scalar):
            ctx.mm.wait(scalar)
            scalar.copy(out=rstd_s[:], in_=ps_b0[:])
            ctx.ea.inc(scalar.copy(out=mrs_s[:], in_=ps_b1[:]), 1)

        evD0 = ctx.ev.n
        epG0 = ctx.ep.n
        dsc0 = ctx.dsc.n
        ds0 = ctx.ds.n
        if dst_mode == "residual":
            @blk.scalar
            def _(scalar):
                ctx.dg.wait(scalar)
                for q in range(SPL):
                    if q >= 2:
                        scalar.wait_ge(ctx.ev.h, evD0 + (q - 1))
                    ctx.dsc.inc(scalar.dma_start(
                        out=xoldb[:, (q % 2) * TC:(q % 2 + 1) * TC],
                        in_=xold_dram.ap()[:, q * TC:(q + 1) * TC]), 16)

            @blk.sync
            def _(sync):
                ctx.dg.wait(sync)
                for q in range(SPL, KC):
                    j = q - SPL
                    if j >= 2:
                        sync.wait_ge(ctx.ep.h, epG0 + (j - 1))
                    ctx.ds.inc(sync.dma_start(
                        out=xoldb[:, (2 + q % 2) * TC:(2 + q % 2 + 1) * TC],
                        in_=xold_dram.ap()[:, q * TC:(q + 1) * TC]), 16)

        @blk.vector
        def _(vector):
            ctx.ea.wait(vector)
            for q in range(SPL):
                if dst_mode == "xln":
                    vector.tensor_tensor(out=tsc[:], in0=src_q(q), in1=rstd_s[:],
                                         op=ALU.mult)
                    ctx.ev.inc(vector.tensor_tensor(
                        out=ctx.xln[:, q * TC:(q + 1) * TC],
                        in0=tsc[:], in1=mrs_s[:], op=ALU.subtract), 1)
                else:
                    vector.wait_ge(ctx.dsc.h, dsc0 + 16 * (q + 1))
                    vector.tensor_tensor(out=tsc[:], in0=src_q(q), in1=rstd_s[:],
                                         op=ALU.mult)
                    vector.tensor_tensor(out=tsc[:], in0=tsc[:], in1=mrs_s[:],
                                         op=ALU.subtract)
                    ctx.ev.inc(vector.tensor_tensor(
                        out=ctx.xT[:, q * TC:(q + 1) * TC],
                        in0=tsc[:], in1=xoldb[:, (q % 2) * TC:(q % 2 + 1) * TC],
                        op=ALU.add), 1)

        @blk.gpsimd
        def _(g):
            ctx.ea.wait(g)
            for q in range(SPL, KC):
                j = q - SPL
                if dst_mode == "xln":
                    g.tensor_tensor(out=tsc2[:], in0=src_q(q), in1=rstd_s[:],
                                    op=ALU.mult)
                    ctx.ep.inc(g.tensor_tensor(
                        out=ctx.xln[:, q * TC:(q + 1) * TC],
                        in0=tsc2[:], in1=mrs_s[:], op=ALU.subtract), 1)
                else:
                    g.wait_ge(ctx.ds.h, ds0 + 16 * (j + 1))
                    g.tensor_tensor(out=tsc2[:], in0=src_q(q), in1=rstd_s[:],
                                    op=ALU.mult)
                    g.tensor_tensor(out=tsc2[:], in0=tsc2[:], in1=mrs_s[:],
                                    op=ALU.subtract)
                    ctx.ep.inc(g.tensor_tensor(
                        out=ctx.xT[:, q * TC:(q + 1) * TC],
                        in0=tsc2[:], in1=xoldb[:, (2 + q % 2) * TC:(2 + q % 2 + 1) * TC],
                        op=ALU.add), 1)

        if carry_out:
            mmC0, eaC0 = ctx.mm.n, ctx.ea.n

            @blk.scalar
            def _(scalar):
                for q in range(KC):
                    if q >= 2:
                        scalar.wait_ge(ctx.mm.h, mmC0 + (q - 1))
                    if q < SPL:
                        scalar.wait_ge(ctx.ev.h, evD0 + q + 1)
                    else:
                        scalar.wait_ge(ctx.ep.h, epG0 + (q - SPL) + 1)
                    ctx.ea.inc(scalar.square(out=sq[:, (q % 2) * TC:(q % 2 + 1) * TC],
                                             in_=ctx.xT[:, q * TC:(q + 1) * TC]), 1)

            @blk.tensor
            def _(tensor):
                ones = ctx.bufs["ONES1"]
                for q in range(KC):
                    tensor.wait_ge(ctx.ea.h, eaC0 + q + 1)
                    for h in range(2):
                        tensor.matmul(ps_m[0:1, h * HF:(h + 1) * HF], ones,
                                      ctx.xT[:, q * TC + h * HF:q * TC + (h + 1) * HF],
                                      start=(q == 0), stop=(q == KC - 1))
                    for h in range(2):
                        m = tensor.matmul(ps_s[0:1, h * HF:(h + 1) * HF], ones,
                                          sq[:, (q % 2) * TC + h * HF:(q % 2) * TC + (h + 1) * HF],
                                          start=(q == 0), stop=(q == KC - 1))
                    ctx.mm.inc(m, 1)

            @blk.scalar
            def _(scalar):
                scalar.wait_ge(ctx.mm.h, mmC0 + KC)
                scalar.copy(out=m_row, in_=ps_m[0:1, :])
                ctx.ea.inc(scalar.copy(out=var_row, in_=ps_s[0:1, :]), 1)

            @blk.sync
            def _(sync):
                ctx.ea.wait(sync)
                ctx.ds.inc(sync.dma_start(out=ctx.carryd.ap(),
                                          in_=rows[0:1, 0:2 * TC]), 16)
                ctx.ds.wait(sync)

    if dst_mode == "residual":
        ctx.carry_ready = carry_out
    ctx.bar()


def emit_spill_x(ctx):
    blk = ctx.block

    @blk.gpsimd
    def _(g):
        ctx.dg.inc(g.dma_start(out=ctx.xspill.ap(), in_=ctx.xT[:, :]), 16)



def emit_l0_ffn(ctx, W):
    nc, blk = ctx.nc, ctx.block
    ctx.fresh_dma_sems("l0f")
    E, R = 8, 64
    DBG = int(os.environ.get("K_DBG_PHASES", "99"))

    esx = contextlib.ExitStack()
    ctx.xln = esx.enter_context(nc.sbuf_tensor(_nm("xln"), [P, KC * TC], BF16))
    xln = ctx.xln
    emit_ln(ctx, lambda q: ctx.xT[:, q * TC:(q + 1) * TC], "xln")
    emit_spill_x(ctx)

    with (
        nc.sbuf_tensor(_nm("f0_wrt"), [P, E * KC], BF16) as wrt,
        nc.sbuf_tensor(_nm("f0_rbb"), [P, E], F32) as rbb,
        nc.sbuf_tensor(_nm("f0_at"), [P, KC * 512], BF16) as at,
        nc.sbuf_tensor(_nm("f0_bt"), [P, 8 * 1024], BF16) as bt,
        nc.sbuf_tensor(_nm("f0_g0t"), [P, NT * P], BF16) as g0t,
        nc.sbuf_tensor(_nm("f0_selq"), [E, 4 * P], BF16) as selq,
        nc.sbuf_tensor(_nm("f0_hsb"), [P, 4 * TC], F32) as hsb,
        nc.sbuf_tensor(_nm("f0_hs"), [P, 4 * TC], BF16) as hs,
    ):
        if DBG < 2:
            return
        selq_np = np.zeros((E, 4 * P), dtype=np.float32)
        for qq in range(4):
            selq_np[2 * qq, qq * P:qq * P + 64] = 1.0
            selq_np[2 * qq + 1, qq * P + 64:(qq + 1) * P] = 1.0
        c_selq = nc.inline_tensor(_np_bf(selq_np), _nm("c_selq"))

        with (
            nc.sbuf_tensor(_nm("f0_stage"), [P, 4 * D], BF16) as stg,
            nc.sbuf_tensor(_nm("f0_wrst"), [E, D], F32) as wrst,
            nc.sbuf_tensor(_nm("f0_brrow"), [1, E], F32) as brrow,
            nc.psum_tensor(_nm("f0_psr0"), [P, E], F32) as ps_r0,
            nc.psum_tensor(_nm("f0_psr1"), [P, E], F32) as ps_r1,
            nc.psum_tensor(_nm("f0_psrb"), [P, E], F32) as ps_rb,
        ):
            ps_rp = [ps_r0, ps_r1]
            ds0, dg0, mm0, ea0 = ctx.ds.n, ctx.dg.n, ctx.mm.n, ctx.ea.n

            @blk.sync
            def _(sync):
                ctx.ds.inc(sync.dma_start(out=selq[:], in_=c_selq.ap()), 16)
                ctx.ds.inc(sync.dma_start(out=wrst[:], in_=W["L0_ffn_Wr"]), 16)
                ctx.ds.inc(sync.dma_start(out=brrow[0:1, :],
                                          in_=W["L0_ffn_br"].rearrange("(o e) -> o e", o=1)), 16)

            @blk.gpsimd
            def _(g):
                av = W["L0_ffn_A"].rearrange("e r d -> (e r) d")
                for a in range(4):
                    ctx.dg.inc(g.dma_start(out=stg[:, a * D:(a + 1) * D],
                                           in_=av[a * P:(a + 1) * P, :]), 16)

            @blk.tensor
            def _(tensor):
                ctx.ds.wait(tensor)
                for k in range(KC):
                    if k >= 2:
                        tensor.wait_ge(ctx.ea.h, ea0 + k - 1)
                    ctx.mm.inc(tensor.transpose(
                        ps_rp[k % 2][:],
                        wrst[:, k * P:(k + 1) * P],
                        ctx.bufs["ID_F32"][0:E, 0:E]), 1)
                m = tensor.matmul(ps_rb[:], ctx.bufs["ONESROW"],
                                  brrow[0:1, :], start=True, stop=True)
                ctx.mm.inc(m, 1)

            @blk.scalar
            def _(scalar):
                for k in range(KC):
                    scalar.wait_ge(ctx.mm.h, mm0 + k + 1)
                    ctx.ea.inc(scalar.copy(out=wrt[:, k * E:(k + 1) * E],
                                           in_=ps_rp[k % 2][:]), 1)
                scalar.wait_ge(ctx.mm.h, mm0 + KC + 1)
                ctx.ea.inc(scalar.copy(out=rbb[:], in_=ps_rb[:]), 1)

            @blk.sync
            def _(sync):
                ctx.dg.wait(sync)
                for a in range(4):
                    for k in range(KC):
                        ctx.ds.inc(sync.dma_start_transpose(
                            out=at[:, k * 512 + a * P: k * 512 + (a + 1) * P],
                            in_=stg[:, a * D + k * P: a * D + (k + 1) * P]), 16)

            ctx.bar()

            @blk.gpsimd
            def _(g):
                stgv = stg[:, 0:8192].rearrange("p (a e r) -> p a e r", a=KC, e=E, r=64)
                for e in range(E):
                    bv = W["L0_ffn_B"][e].rearrange("(a p) r -> p a r", p=P)
                    ctx.dg.inc(g.dma_start(out=stgv[:, :, e, :], in_=bv), 16)

            @blk.sync
            def _(sync):
                ctx.dg.wait(sync)
                for a in range(KC):
                    for e2 in range(4):
                        blk_i = a * 4 + e2
                        ctx.ds.inc(sync.dma_start_transpose(
                            out=bt[:, blk_i * P:(blk_i + 1) * P],
                            in_=stg[:, blk_i * P:(blk_i + 1) * P]), 16)

            ctx.bar()

        if DBG < 3:
            return
        with (
            nc.sbuf_tensor(_nm("f0_L"), [P, 3 * E], F32) as Lb,
            nc.sbuf_tensor(_nm("f0_cols"), [P, 4], F32) as cols,
            nc.sbuf_tensor(_nm("f0_gbf"), [P, P], BF16) as gbf,
            nc.psum_tensor(_nm("f0_pslg"), [P, E], F32) as ps_lg,
        ):
            L = Lb[:, 0:E]
            Ee = Lb[:, E:2 * E]
            mask = Lb[:, 2 * E:3 * E]
            nmx = cols[:, 0:1]
            zz = cols[:, 1:2]
            rz = cols[:, 2:3]
            ds_r0 = ctx.ds.n

            @blk.gpsimd
            def _(g):
                ctx.ep.inc(g.memset(gbf[:, :], 0.0), 1)

            for t in range(NT):
                @blk.tensor
                def _(tensor, t=t):
                    if t == 0:
                        ctx.ea.wait(tensor)
                    else:
                        tensor.wait_ge(ctx.ep.h, ctx.ep.n)
                    m = None
                    for k in range(KC):
                        m = tensor.matmul(ps_lg[:, :],
                                          xln[:, k * TC + t * P: k * TC + (t + 1) * P],
                                          wrt[:, k * E:(k + 1) * E],
                                          start=(k == 0), stop=(k == KC - 1))
                    ctx.mm.inc(m, 1)

                @blk.scalar
                def _(scalar, t=t):
                    ctx.mm.wait(scalar)
                    ctx.ea.inc(scalar.copy(out=L, in_=ps_lg[:, :]), 1)

                @blk.gpsimd
                def _(g, t=t):
                    ctx.ea.wait(g)
                    ctx.ep.inc(g.tensor_tensor(out=L, in0=L, in1=rbb[:], op=ALU.add), 1)

                @blk.vector
                def _(vector, t=t):
                    ctx.ep.wait(vector)
                    ctx.ev.inc(vector.reduce_max(out=nmx, in_=L, axis=AX.X,
                                                 negate=True), 1)

                @blk.scalar
                def _(scalar, t=t):
                    ctx.ev.wait(scalar)
                    ctx.ea.inc(scalar.activation(out=Ee, in_=L, func=AF.Exp, bias=nmx), 1)

                @blk.vector
                def _(vector, t=t):
                    ctx.ea.wait(vector)
                    ctx.ev.inc(vector.reduce_sum(out=zz, in_=Ee, axis=AX.X), 1)

                @blk.scalar
                def _(scalar, t=t):
                    ctx.ev.wait(scalar)
                    scalar.activation(out=rz, in_=zz, func=AF.Ln)
                    ctx.ea.inc(scalar.activation(out=rz, in_=rz, func=AF.Exp,
                                                 scale=-1.0), 1)

                @blk.gpsimd
                def _(g, t=t):
                    ctx.ea.wait(g)
                    if t >= 1:
                        g.wait_ge(ctx.ds.h, ds_r0 + 16 * t)
                    g.tensor_scalar(out=mask, in0=L, scalar1=nmx, scalar2=0.0,
                                    op0=ALU.add, op1=ALU.is_equal)
                    ctx.ep.inc(g.tensor_scalar(out=gbf[:, 0:E], in0=mask, scalar1=rz,
                                               scalar2=None, op0=ALU.mult), 1)

                @blk.sync
                def _(sync, t=t):
                    ctx.ep.wait(sync)
                    ctx.ds.inc(sync.dma_start_transpose(out=g0t[:, t * P:(t + 1) * P],
                                                        in_=gbf[:, :]), 16)

            ctx.bar()
            if os.environ.get("K_DBG_DUMP"):
                d_L = nc.dram_tensor("dbg_L", [P, 3 * E], F32, kind="ExternalOutput").ap()
                d_cols = nc.dram_tensor("dbg_cols", [P, 3], F32, kind="ExternalOutput").ap()
                d_gbf = nc.dram_tensor("dbg_gbf", [P, P], F32, kind="ExternalOutput").ap()
                with nc.sbuf_tensor(_nm("dbg2"), [P, P], F32) as dbg2:
                    @blk.vector
                    def _(v):
                        v.tensor_copy(out=dbg2[:, 0:P], in_=gbf[:, :])
                        ctx.ev.inc(v.tensor_copy(out=dbg2[:, 0:P], in_=gbf[:, :]), 1)

                    @blk.gpsimd
                    def _(g):
                        ctx.ev.wait(g)
                        ctx.dg.inc(g.dma_start(out=d_gbf, in_=dbg2[:, 0:P]), 16)
                        ctx.dg.inc(g.dma_start(out=d_L, in_=Lb[:, :]), 16)
                        ctx.dg.inc(g.dma_start(out=d_cols, in_=cols[:, 0:3]), 16)
                    ctx.bar()

        if DBG < 4:
            return
        with nc.psum_tensor(_nm("f0_ph0"), [P, TC], F32) as ph0, \
             nc.psum_tensor(_nm("f0_ph1"), [P, TC], F32) as ph1, \
             nc.psum_tensor(_nm("f0_ph2"), [P, TC], F32) as ph2, \
             nc.psum_tensor(_nm("f0_ph3"), [P, TC], F32) as ph3:
            ps_h = [ph0, ph1, ph2, ph3]
            mmh = ctx.mm.n

            @blk.tensor
            def _(tensor):
                ctx.ds.wait(tensor)
                for rc in range(4):
                    m = None
                    for k in range(KC):
                        for h in range(2):
                            m = tensor.matmul(ps_h[rc][:, h * HF:(h + 1) * HF],
                                              at[:, k * 512 + rc * P:k * 512 + (rc + 1) * P],
                                              xln[:, k * TC + h * HF:k * TC + (h + 1) * HF],
                                              start=(k == 0), stop=(k == KC - 1))
                    ctx.mm.inc(m, 1)

            @blk.scalar
            def _(scalar):
                for rc in range(4):
                    scalar.wait_ge(ctx.mm.h, mmh + rc + 1)
                    ctx.ea.inc(scalar.copy(out=hsb[:, rc * TC:(rc + 1) * TC],
                                           in_=ps_h[rc][:]), 1)

            eah = ctx.ea.n
            mmg = ctx.mm.n

            @blk.tensor
            def _(tensor):
                for rc in range(4):
                    tensor.wait_ge(ctx.ea.h, eah - 4 + rc + 1)
                    m = None
                    for h in range(2):
                        m = tensor.matmul(ps_h[rc][:, h * HF:(h + 1) * HF],
                                          selq[:, rc * P:(rc + 1) * P],
                                          g0t[0:E, h * HF:(h + 1) * HF], start=True, stop=True)
                    ctx.mm.inc(m, 1)

            @blk.vector
            def _(vector):
                for rc in range(4):
                    vector.wait_ge(ctx.mm.h, mmg + rc + 1)
                    ctx.ev.inc(vector.tensor_tensor(out=hs[:, rc * TC:(rc + 1) * TC],
                                                    in0=hsb[:, rc * TC:(rc + 1) * TC],
                                                    in1=ps_h[rc][:], op=ALU.mult), 1)

            ctx.bar()

        if DBG < 5:
            return
        with nc.psum_tensor(_nm("f0_py0"), [P, TC], F32) as py0, \
             nc.psum_tensor(_nm("f0_py1"), [P, TC], F32) as py1:
            ps_y = [py0, py1]
            mmy = ctx.mm.n
            eay = ctx.ea.n

            @blk.tensor
            def _(tensor):
                for oc in range(KC):
                    sl = oc % 2
                    if oc >= 2:
                        tensor.wait_ge(ctx.ea.h, eay + oc - 1)
                    m = None
                    for e2 in range(4):
                        blk_i = oc * 4 + e2
                        for h in range(2):
                            m = tensor.matmul(ps_y[sl][:, h * HF:(h + 1) * HF],
                                              bt[:, blk_i * P:(blk_i + 1) * P],
                                              hs[:, e2 * TC + h * HF:e2 * TC + (h + 1) * HF],
                                              start=(e2 == 0), stop=(e2 == 3))
                    ctx.mm.inc(m, 1)

            @blk.scalar
            def _(scalar):
                ctx.dg.wait(scalar)
                for oc in range(KC):
                    sl = oc % 2
                    scalar.wait_ge(ctx.mm.h, mmy + oc + 1)
                    ctx.ea.inc(scalar.activation(out=ctx.xT[:, oc * TC:(oc + 1) * TC],
                                                 in_=ps_y[sl][:], func=AF.Copy,
                                                 scale=1.0 / R), 1)

            ctx.bar()

        if os.environ.get("K_DBG_DUMP"):
            d_bt = nc.dram_tensor("dbg_bt", [P, 8 * 1024], F32, kind="ExternalOutput").ap()
            d_g0t = nc.dram_tensor("dbg_g0t", [P, NT * P], F32, kind="ExternalOutput").ap()
            d_hs = nc.dram_tensor("dbg_hs", [P, 4 * TC], F32, kind="ExternalOutput").ap()
            d_y = nc.dram_tensor("dbg_y", [P, KC * TC], F32, kind="ExternalOutput").ap()
            with nc.sbuf_tensor(_nm("dbgf"), [P, 8 * 1024], F32) as dbgf:
                @blk.vector
                def _(v):
                    ctx.ev.inc(v.tensor_copy(out=dbgf[:, 0:8 * 1024], in_=bt[:]), 1)

                @blk.gpsimd
                def _(g):
                    ctx.ev.wait(g)
                    ctx.dg.inc(g.dma_start(out=d_bt, in_=dbgf[:, 0:8 * 1024]), 16)
                ctx.bar()

                @blk.vector
                def _(v):
                    ctx.ev.inc(v.tensor_copy(out=dbgf[:, 0:NT * P], in_=g0t[:]), 1)

                @blk.gpsimd
                def _(g):
                    ctx.ev.wait(g)
                    ctx.dg.inc(g.dma_start(out=d_g0t, in_=dbgf[:, 0:NT * P]), 16)
                ctx.bar()

                @blk.vector
                def _(v):
                    ctx.ev.inc(v.tensor_copy(out=dbgf[:, 0:4 * TC], in_=hs[:]), 1)

                @blk.gpsimd
                def _(g):
                    ctx.ev.wait(g)
                    ctx.dg.inc(g.dma_start(out=d_hs, in_=dbgf[:, 0:4 * TC]), 16)
                ctx.bar()

                @blk.gpsimd
                def _(g):
                    ctx.dg.inc(g.dma_start(out=d_y, in_=ctx.xT[:, :]), 16)
                ctx.bar()

    esx.close()
    if DBG >= 6:
        emit_ln(ctx, lambda q: ctx.xT[:, q * TC:(q + 1) * TC], "residual",
                xold_dram=ctx.xspill)



BLOCK_INPUTS = {
    "l0a": ["L0_attn_log_decay", "L0_attn_Wq", "L0_attn_Wk", "L0_attn_Wv",
            "L0_attn_Wo", "L0_attn_Wg1", "L0_attn_Wg2", "L0_attn_norm_w"],
    "l0f": ["L0_ffn_A", "L0_ffn_B", "L0_ffn_Wr", "L0_ffn_br"],
    "l1a": ["L1_attn_w_hg", "L1_attn_w_out"],
    "l1f": ["L1_ffn_router_w", "L1_ffn_router_bias", "L1_ffn_Wu", "L1_ffn_Wg",
            "L1_ffn_Wd", "L1_ffn_Su", "L1_ffn_Sg", "L1_ffn_Sd"],
}

INPUT_SHAPES = {
    "L0_attn_log_decay": [16], "L0_attn_Wq": [D, D], "L0_attn_Wk": [D, D],
    "L0_attn_Wv": [D, D], "L0_attn_Wo": [D, D], "L0_attn_Wg1": [D, 128],
    "L0_attn_Wg2": [128, D], "L0_attn_norm_w": [D],
    "L0_ffn_A": [8, 64, D], "L0_ffn_B": [8, D, 64], "L0_ffn_Wr": [8, D], "L0_ffn_br": [8],
    "L1_attn_w_hg": [D, 6144], "L1_attn_w_out": [3072, D],
    "L1_ffn_router_w": [16, D], "L1_ffn_router_bias": [16],
    "L1_ffn_Wu": [16, D, 1024], "L1_ffn_Wg": [16, D, 1024], "L1_ffn_Wd": [16, 1024, D],
    "L1_ffn_Su": [1, D, D], "L1_ffn_Sg": [1, D, D], "L1_ffn_Sd": [1, D, D],
}

_PROGRAM_CACHE = {}


def build_program(blocks):
    key = tuple(blocks)
    if key in _PROGRAM_CACHE:
        return _PROGRAM_CACHE[key]
    nc = bass.Bass("TRN2", target_bir_lowering=False, detect_race_conditions=False)
    xin = nc.dram_tensor("xin", [TC, D], F32, kind="ExternalInput").ap()
    xout = nc.dram_tensor("xout", [TC, D], F32, kind="ExternalOutput").ap()
    W = {}
    names = []
    for b in blocks:
        for nmn in BLOCK_INPUTS[b]:
            W[nmn] = nc.dram_tensor(nmn, INPUT_SHAPES[nmn], F32, kind="ExternalInput").ap()
            names.append(nmn)
    if "l0a" in blocks:
        W["msel"] = nc.dram_tensor("msel", [8, 1], F32, kind="ExternalInput").ap()
        names.append("msel")
    if "l1a" in blocks:
        W["tmat"] = nc.dram_tensor("tmat", [8, 8], F32, kind="ExternalInput").ap()
        W["tbias"] = nc.dram_tensor("tbias", [8, 1], F32, kind="ExternalInput").ap()
        names += ["tmat", "tbias"]

    with nc.Block() as blk:
        ctx = Ctx(nc, blk)
        ctx.inl = {
            "ID_F32": nc.inline_tensor(np.eye(P, dtype=np.float32), "c_id_f32"),
            "ID_BF": nc.inline_tensor(_np_bf(np.eye(P)), "c_id_bf"),
            "ONES1": nc.inline_tensor(np.ones((P, 1), dtype=np.float32), "c_ones1"),
            "EPSCOL": nc.inline_tensor(np.full((P, 1), EPS, dtype=np.float32), "c_epscol"),
            "ONES1B": nc.inline_tensor(_np_bf(np.ones((P, 1))), "c_ones1b"),
            "ONESROW": nc.inline_tensor(np.ones((1, P), dtype=np.float32), "c_onesrow"),
        }
        consts = {
            "ID_F32": ([P, P], F32),
            "ID_BF": ([P, P], BF16),
            "ONES1": ([P, 1], F32),
            "EPSCOL": ([P, 1], F32),
            "ONES1B": ([P, 1], BF16),
            "ONESROW": ([1, P], F32),
        }
        for name, (shape, dtype) in consts.items():
            ctx.bufs[name] = nc.alloc_sbuf_tensor(_nm("sb_" + name), shape, dtype).ap()
        ctx.dummy = nc.alloc_sbuf_tensor("vgap_dummy", [P, 24], F32).ap()
        ctx.wbf = nc.alloc_sbuf_tensor("wbf", [P, 3 * 512], BF16).ap()
        ctx.carryd = nc.dram_tensor("ln_carry", [1, 2 * TC], F32)
        ctx.xT = nc.alloc_sbuf_tensor("xT", [P, KC * TC], F32).ap()
        ctx.xspill = nc.dram_tensor("xspill", [P, KC * TC], F32)
        emit_prologue(ctx, xin, consts)
        for i, b in enumerate(blocks):
            ctx.carry_next = i < len(blocks) - 1
            EMITTERS[b](ctx, W)
        emit_epilogue(ctx, xout)

    _PROGRAM_CACHE[key] = (nc, names)
    return _PROGRAM_CACHE[key]


def _percore_consts(core):
    r = core
    msel = np.array([[1024.0 * (r - 1 - j) if j < r else 1e9] for j in range(8)],
                    dtype=np.float32)
    tmat = np.zeros((8, 8), dtype=np.float32)
    for mrow in range(8):
        for j in range(8):
            if j < mrow < r:
                tmat[mrow, j] = 1.0
    tbias = np.array([[0.0 if j < r else -1e38] for j in range(8)], dtype=np.float32)
    return {"msel": msel, "tmat": tmat, "tbias": tbias}


def run_blocks(x_full, inputs, blocks):
    nc, names = build_program(blocks)
    in_maps = []
    for c in range(NCORE):
        m = {"xin": np.ascontiguousarray(x_full[c * TC:(c + 1) * TC])}
        pc = _percore_consts(c)
        for nmn in names:
            m[nmn] = pc[nmn] if nmn in pc else inputs[nmn]
        in_maps.append(m)
    res = run_bass_kernel_spmd(nc, in_maps, core_ids=list(range(NCORE)))
    return np.concatenate([res.results[c]["xout"] for c in range(NCORE)], axis=0)


ALL_BLOCKS = ["l0a", "l0f", "l1a", "l1f"]


def kernel(**inputs):
    inputs = {k: np.asarray(v, dtype=np.float32) for k, v in inputs.items()}
    x = np.ascontiguousarray(inputs["L0_attn_x"].reshape(8192, D))
    if os.environ.get("K_UNFUSED"):
        for blocks in (["l0a"], ["l0f"], ["l1a"], ["l1f"]):
            x = run_blocks(x, inputs, blocks)
    else:
        x = run_blocks(x, inputs, ALL_BLOCKS)
    return x.reshape(1, 8192, D)



def wstream(ctx, pieces, wtile, on_piece, pe_prewait=None, queue="gp"):
    nc, blk = ctx.nc, ctx.block
    NS = 3
    mm0 = ctx.mm.n
    wr0 = [sm.n for sm in ctx.wr]

    def emit_dmas(eng):
        for n, ap in enumerate(pieces):
            sl = n % NS
            if ctx.wring_free[sl] > 0:
                eng.wait_ge(ctx.mm.h, ctx.wring_free[sl])
            ctx.wring_free[sl] = mm0 + n + 1
            eng.dma_start(out=ctx.wbf[:, sl * 512:sl * 512 + wtile], in_=ap).then_inc(
                ctx.wr[sl].h, 16)
            ctx.wr[sl].n += 16

    if queue == "gp":
        @blk.gpsimd
        def _(g):
            emit_dmas(g)
    else:
        @blk.sync
        def _(sy):
            emit_dmas(sy)

    @blk.tensor
    def _(tensor):
        for n in range(len(pieces)):
            sl = n % NS
            tensor.wait_ge(ctx.wr[sl].h, wr0[sl] + 16 * (n // NS + 1))
            if pe_prewait is not None:
                pe_prewait(tensor, n)
            last = on_piece(tensor, n, ctx.wbf[:, sl * 512:sl * 512 + wtile])
            ctx.mm.inc(last, 1)


def emit_l1_ffn(ctx, W):
    nc, blk = ctx.nc, ctx.block
    ctx.fresh_dma_sems("l1f")
    E = 16
    NEXP = int(os.environ.get("K_DBG_L1F_NEXP", "16"))
    NOSHARED = bool(os.environ.get("K_DBG_L1F_NOSHARED", ""))
    NOPOSTLN = bool(os.environ.get("K_DBG_L1F_NOPOSTLN", ""))
    Wu, Wg, Wd = W["L1_ffn_Wu"], W["L1_ffn_Wg"], W["L1_ffn_Wd"]
    Su, Sg, Sd = W["L1_ffn_Su"], W["L1_ffn_Sg"], W["L1_ffn_Sd"]

    esx = contextlib.ExitStack()
    ctx.xln = esx.enter_context(nc.sbuf_tensor(_nm("xln"), [P, KC * TC], BF16))
    xln = ctx.xln
    emit_ln(ctx, lambda q: ctx.xT[:, q * TC:(q + 1) * TC], "xln")
    emit_spill_x(ctx)

    onehot_np = np.zeros((E, E * P), dtype=np.float32)
    for e in range(E):
        onehot_np[e, e * P:(e + 1) * P] = 1.0
    c_oh = nc.inline_tensor(_np_bf(onehot_np), _nm("c_oh16"))

    with (
        nc.sbuf_tensor(_nm("f1_rwt"), [P, E * KC], BF16) as rwt,
        nc.sbuf_tensor(_nm("f1_rbb"), [P, E], F32) as rbb,
        nc.sbuf_tensor(_nm("f1_oh"), [E, E * P], BF16) as oh,
        nc.sbuf_tensor(_nm("f1_gt"), [P, NT * P], BF16) as gt,
        nc.sbuf_tensor(_nm("f1_usb"), [P, NT * TC], BF16) as usb,
        nc.sbuf_tensor(_nm("f1_gsb"), [P, NT * TC], BF16) as gsb,
        nc.sbuf_tensor(_nm("f1_gbs"), [P, TC], BF16) as gbs,
        nc.sbuf_tensor(_nm("f1_ssb"), [P, KC * TC], BF16) as ssb,
    ):
        with (
            nc.sbuf_tensor(_nm("f1_rwst"), [E, D], F32) as rwst,
            nc.sbuf_tensor(_nm("f1_brrow"), [1, E], F32) as brrow,
            nc.psum_tensor(_nm("f1_psr0"), [P, E], F32) as ps_r0,
            nc.psum_tensor(_nm("f1_psr1"), [P, E], F32) as ps_r1,
            nc.psum_tensor(_nm("f1_psrb"), [P, E], F32) as ps_rb,
        ):
            ps_rp = [ps_r0, ps_r1]
            mm0, ea0 = ctx.mm.n, ctx.ea.n

            @blk.sync
            def _(sync):
                ctx.ds.inc(sync.dma_start(out=oh[:], in_=c_oh.ap()), 16)
                ctx.ds.inc(sync.dma_start(out=rwst[:], in_=W["L1_ffn_router_w"]), 16)
                ctx.ds.inc(sync.dma_start(
                    out=brrow[0:1, :],
                    in_=W["L1_ffn_router_bias"].rearrange("(o e) -> o e", o=1)), 16)

            @blk.tensor
            def _(tensor):
                ctx.ds.wait(tensor)
                for k in range(KC):
                    if k >= 2:
                        tensor.wait_ge(ctx.ea.h, ea0 + k - 1)
                    ctx.mm.inc(tensor.transpose(
                        ps_rp[k % 2][:], rwst[:, k * P:(k + 1) * P],
                        ctx.bufs["ID_F32"][0:E, 0:E]), 1)
                m = tensor.matmul(ps_rb[:], ctx.bufs["ONESROW"], brrow[0:1, :],
                                  start=True, stop=True)
                ctx.mm.inc(m, 1)

            @blk.scalar
            def _(scalar):
                for k in range(KC):
                    scalar.wait_ge(ctx.mm.h, mm0 + k + 1)
                    ctx.ea.inc(scalar.copy(out=rwt[:, k * E:(k + 1) * E],
                                           in_=ps_rp[k % 2][:]), 1)
                scalar.wait_ge(ctx.mm.h, mm0 + KC + 1)
                ctx.ea.inc(scalar.copy(out=rbb[:], in_=ps_rb[:]), 1)

            ctx.bar()

        with (
            nc.sbuf_tensor(_nm("f1_L"), [P, 8 * E], F32) as Lb,
            nc.sbuf_tensor(_nm("f1_cols"), [P, 8], F32) as cols,
            nc.sbuf_tensor(_nm("f1_cbf"), [P, P], BF16) as cbf,
            nc.psum_tensor(_nm("f1_pslg"), [P, E], F32) as ps_lg,
        ):
            L = Lb[:, 0:E]
            LB = Lb[:, E:2 * E]
            m1 = Lb[:, 2 * E:3 * E]
            m2 = Lb[:, 3 * E:4 * E]
            Ex = Lb[:, 4 * E:5 * E]
            wrk = Lb[:, 5 * E:6 * E]
            w1b = Lb[:, 6 * E:7 * E]
            w2b = Lb[:, 7 * E:8 * E]
            nm1 = cols[:, 0:1]
            nm2 = cols[:, 1:2]
            nmL = cols[:, 2:3]
            aa = cols[:, 3:4]
            bb2 = cols[:, 4:5]
            rs = cols[:, 5:6]
            t1c = cols[:, 6:7]
            t2c = cols[:, 7:8]
            ds_r0 = ctx.ds.n

            @blk.gpsimd
            def _(g):
                ctx.ep.inc(g.memset(cbf[:, :], 0.0), 1)

            for t in range(NT):
                @blk.tensor
                def _(tensor, t=t):
                    if t == 0:
                        ctx.ea.wait(tensor)
                    else:
                        tensor.wait_ge(ctx.ep.h, ctx.ep.n)
                    m = None
                    for k in range(KC):
                        m = tensor.matmul(ps_lg[:, :],
                                          xln[:, k * TC + t * P:k * TC + (t + 1) * P],
                                          rwt[:, k * E:(k + 1) * E],
                                          start=(k == 0), stop=(k == KC - 1))
                    ctx.mm.inc(m, 1)

                @blk.scalar
                def _(scalar, t=t):
                    ctx.mm.wait(scalar)
                    ctx.ea.inc(scalar.copy(out=L, in_=ps_lg[:, :]), 1)

                @blk.gpsimd
                def _(g, t=t):
                    ctx.ea.wait(g)
                    ctx.ep.inc(g.tensor_tensor(out=LB, in0=L, in1=rbb[:], op=ALU.add), 1)

                @blk.vector
                def _(vector, t=t):
                    ctx.ep.wait(vector)
                    vector.reduce_max(out=nm1, in_=LB, axis=AX.X, negate=True)
                    ctx.ev.inc(vector.reduce_max(out=nmL, in_=L, axis=AX.X,
                                                 negate=True), 1)

                @blk.gpsimd
                def _(g, t=t):
                    ctx.ev.wait(g)
                    g.tensor_scalar(out=m1, in0=LB, scalar1=nm1, scalar2=0.0,
                                    op0=ALU.add, op1=ALU.is_equal)
                    g.tensor_scalar(out=wrk, in0=m1, scalar1=-1e30, scalar2=None,
                                    op0=ALU.mult)
                    ctx.ep.inc(g.tensor_tensor(out=wrk, in0=wrk, in1=LB, op=ALU.add), 1)

                @blk.vector
                def _(vector, t=t):
                    ctx.ep.wait(vector)
                    ctx.ev.inc(vector.reduce_max(out=nm2, in_=wrk, axis=AX.X,
                                                 negate=True), 1)

                @blk.scalar
                def _(scalar, t=t):
                    ctx.ev.wait(scalar)
                    ctx.ea.inc(scalar.activation(out=Ex, in_=L, func=AF.Exp,
                                                 bias=nmL), 1)

                @blk.gpsimd
                def _(g, t=t):
                    ctx.ea.wait(g)
                    g.tensor_scalar(out=m2, in0=wrk, scalar1=nm2, scalar2=0.0,
                                    op0=ALU.add, op1=ALU.is_equal)
                    g.tensor_tensor(out=w1b, in0=m1, in1=Ex, op=ALU.mult)
                    ctx.ep.inc(g.tensor_tensor(out=w2b, in0=m2, in1=Ex, op=ALU.mult), 1)

                @blk.vector
                def _(vector, t=t):
                    ctx.ep.wait(vector)
                    vector.reduce_sum(out=aa, in_=w1b, axis=AX.X)
                    ctx.ev.inc(vector.reduce_sum(out=bb2, in_=w2b, axis=AX.X), 1)

                @blk.gpsimd
                def _(g, t=t):
                    ctx.ev.wait(g)
                    ctx.ep.inc(g.tensor_tensor(out=rs, in0=aa, in1=bb2, op=ALU.add), 1)

                @blk.scalar
                def _(scalar, t=t):
                    ctx.ep.wait(scalar)
                    scalar.activation(out=rs, in_=rs, func=AF.Ln)
                    ctx.ea.inc(scalar.activation(out=rs, in_=rs, func=AF.Exp,
                                                 scale=-1.0), 1)

                @blk.gpsimd
                def _(g, t=t):
                    ctx.ea.wait(g)
                    if t >= 1:
                        g.wait_ge(ctx.ds.h, ds_r0 + 16 * t)
                    g.tensor_tensor(out=t1c, in0=aa, in1=rs, op=ALU.mult)
                    g.tensor_tensor(out=t2c, in0=bb2, in1=rs, op=ALU.mult)
                    g.tensor_scalar(out=wrk, in0=m1, scalar1=t1c, scalar2=None,
                                    op0=ALU.mult)
                    g.tensor_scalar(out=w2b, in0=m2, scalar1=t2c, scalar2=None,
                                    op0=ALU.mult)
                    ctx.ep.inc(g.tensor_tensor(out=cbf[:, 0:E], in0=w2b, in1=wrk,
                                               op=ALU.add), 1)

                @blk.sync
                def _(sync, t=t):
                    ctx.ep.wait(sync)
                    ctx.ds.inc(sync.dma_start_transpose(out=gt[:, t * P:(t + 1) * P],
                                                        in_=cbf[:, :]), 16)

            ctx.bar()
            if os.environ.get("K_DBG_DUMP"):
                d_gt = nc.dram_tensor("dbg_gt", [P, NT * P], F32, kind="ExternalOutput").ap()
                d_L = nc.dram_tensor("dbg_L", [P, 6 * E], F32, kind="ExternalOutput").ap()
                d_cols = nc.dram_tensor("dbg_cols", [P, 8], F32, kind="ExternalOutput").ap()
                with nc.sbuf_tensor(_nm("dbg3"), [P, NT * P], F32) as dbg3:
                    @blk.vector
                    def _(v):
                        ctx.ev.inc(v.tensor_copy(out=dbg3[:, :], in_=gt[:, :]), 1)

                    @blk.gpsimd
                    def _(g):
                        ctx.ev.wait(g)
                        ctx.dg.inc(g.dma_start(out=d_gt, in_=dbg3[:, :]), 16)
                        ctx.dg.inc(g.dma_start(out=d_L, in_=Lb[:, :]), 16)
                        ctx.dg.inc(g.dma_start(out=d_cols, in_=cols[:, :]), 16)
                    ctx.bar()

        with (
            nc.psum_tensor(_nm("f1_pa"), [P, TC], F32) as pA,
            nc.psum_tensor(_nm("f1_pb"), [P, TC], F32) as pB,
            nc.psum_tensor(_nm("f1_pc"), [P, TC], F32) as pC,
            nc.psum_tensor(_nm("f1_pd"), [P, TC], F32) as pD,
        ):
            ps4 = [pA, pB, pC, pD]

            def u_or_g(mat_ap, dst_sb, act_fn):
                eaE = [ctx.ea.n]

                def prewait(tensor, n):
                    if n == 0:
                        tensor.wait_ge(ctx.ea.h, eaE[0])

                def onp(tensor, n, slot):
                    og = n // KC
                    k = n % KC
                    m = None
                    for oc in range(4):
                        if k == 0 and og >= 1:
                            tensor.wait_ge(ctx.ea.h, eaE[0] + (og - 1) * 4 + oc + 1)
                        for h in range(2):
                            m = tensor.matmul(
                                ps4[oc][:, h * HF:(h + 1) * HF],
                                slot[:, oc * P:(oc + 1) * P],
                                xln[:, k * TC + h * HF:k * TC + (h + 1) * HF],
                                start=(k == 0), stop=(k == KC - 1))
                    return m

                pieces = []
                for og in range(2):
                    for k in range(KC):
                        pieces.append(mat_ap[k * P:(k + 1) * P, og * 512:(og + 1) * 512])
                mmE = ctx.mm.n
                wstream(ctx, pieces, 512, onp, prewait)

                @blk.scalar
                def _(scalar):
                    for og in range(2):
                        scalar.wait_ge(ctx.mm.h, mmE + (og + 1) * KC)
                        for oc in range(4):
                            ctx.ea.inc(scalar.activation(
                                out=dst_sb[:, (og * 4 + oc) * TC:(og * 4 + oc + 1) * TC],
                                in_=ps4[oc][:], func=act_fn), 1)

            for e in range(NEXP):
                mmG = ctx.mm.n

                @blk.tensor
                def _(tensor, e=e):
                    if e == 0:
                        ctx.ea.wait(tensor)
                    else:
                        ctx.ev.wait(tensor)
                    m = None
                    for h in range(2):
                        m = tensor.matmul(pA[:, h * HF:(h + 1) * HF],
                                          oh[:, e * P:(e + 1) * P],
                                          gt[0:E, h * HF:(h + 1) * HF],
                                          start=True, stop=True)
                    ctx.mm.inc(m, 1)

                @blk.scalar
                def _(scalar, e=e):
                    ctx.mm.wait(scalar)
                    ctx.ea.inc(scalar.copy(out=gbs[:], in_=pA[:]), 1)

                u_or_g(Wu[e], usb, AF.Silu)
                u_or_g(Wg[e], gsb, AF.Copy)

                noh = bool(os.environ.get("K_DBG_L1F_NOH"))
                evh0 = ctx.ev.n

                @blk.vector
                def _(vector, e=e):
                    ctx.ea.wait(vector)
                    if noh:
                        ctx.ev.inc(vector.tensor_copy(out=ctx.dummy[:, 3:4],
                                                      in_=ctx.dummy[:, 0:1]), 1)
                        return
                    for rc in range(NT):
                        vector.tensor_tensor(out=usb[:, rc * TC:(rc + 1) * TC],
                                             in0=usb[:, rc * TC:(rc + 1) * TC],
                                             in1=gsb[:, rc * TC:(rc + 1) * TC],
                                             op=ALU.mult)
                        ctx.ev.inc(vector.tensor_tensor(
                            out=usb[:, rc * TC:(rc + 1) * TC],
                            in0=usb[:, rc * TC:(rc + 1) * TC],
                            in1=gbs[:], op=ALU.mult), 1)

                evY0 = ctx.ev.n

                def prewaity(tensor, n, evh0=evh0, evY0=evY0):
                    og2 = n // NT
                    rk = n % NT
                    if og2 == 0:
                        tensor.wait_ge(ctx.ev.h,
                                       evY0 if noh else evh0 + rk + 1)

                def onpy(tensor, n, slot, evY0=evY0):
                    og2 = n // NT
                    rk = n % NT
                    m = None
                    for oc in range(4):
                        if rk == 0 and og2 >= 1:
                            tensor.wait_ge(ctx.ev.h, evY0 + (og2 - 1) * 4 + oc + 1)
                        for h in range(2):
                            m = tensor.matmul(
                                ps4[oc][:, h * HF:(h + 1) * HF],
                                slot[:, oc * P:(oc + 1) * P],
                                usb[:, rk * TC + h * HF:rk * TC + (h + 1) * HF],
                                start=(rk == 0), stop=(rk == NT - 1))
                    return m

                piecesy = []
                for og2 in range(4):
                    for rk in range(NT):
                        piecesy.append(Wd[e][rk * P:(rk + 1) * P,
                                             og2 * 512:(og2 + 1) * 512])
                mmY = ctx.mm.n
                wstream(ctx, piecesy, 512, onpy, prewaity)

                @blk.vector
                def _(vector, e=e, mmY=mmY):
                    for og2 in range(4):
                        vector.wait_ge(ctx.mm.h, mmY + (og2 + 1) * NT)
                        for oc in range(4):
                            dst = ctx.xT[:, (og2 * 4 + oc) * TC:(og2 * 4 + oc + 1) * TC]
                            if e == 0:
                                ctx.ev.inc(vector.tensor_copy(out=dst,
                                                              in_=ps4[oc][:]), 1)
                            else:
                                ctx.ev.inc(vector.tensor_tensor(
                                    out=dst, in0=dst, in1=ps4[oc][:], op=ALU.add), 1)

            def s_u_or_g(mat_ap, act_fn, second):
                eaE = [ctx.ea.n]
                ev_last = ctx.ev.n

                def prewait(tensor, n):
                    if n == 0:
                        tensor.wait_ge(ctx.ea.h, eaE[0])
                        if not second:
                            tensor.wait_ge(ctx.ev.h, ev_last)
                def onp(tensor, n, slot):
                    og = n // KC
                    k = n % KC
                    m = None
                    for oc in range(4):
                        if k == 0 and og >= 1:
                            tensor.wait_ge(ctx.ea.h, eaE[0] + (og - 1) * 4 + oc + 1)
                        for h in range(2):
                            m = tensor.matmul(
                                ps4[oc][:, h * HF:(h + 1) * HF],
                                slot[:, oc * P:(oc + 1) * P],
                                xln[:, k * TC + h * HF:k * TC + (h + 1) * HF],
                                start=(k == 0), stop=(k == KC - 1))
                    return m

                pieces = []
                for og in range(4):
                    for k in range(KC):
                        pieces.append(mat_ap[k * P:(k + 1) * P, og * 512:(og + 1) * 512])
                mmE = ctx.mm.n
                wstream(ctx, pieces, 512, onp, prewait)

                @blk.scalar
                def _(scalar):
                    for og in range(4):
                        scalar.wait_ge(ctx.mm.h, mmE + (og + 1) * KC)
                        for oc in range(4):
                            occ = og * 4 + oc
                            if second:
                                dst = (usb if occ < NT else gsb)[
                                    :, (occ % NT) * TC:(occ % NT + 1) * TC]
                            else:
                                dst = ssb[:, occ * TC:(occ + 1) * TC]
                            ctx.ea.inc(scalar.activation(out=dst, in_=ps4[oc][:],
                                                         func=act_fn), 1)

            if not NOSHARED:
                s_u_or_g(Su[0], AF.Silu, False)
                s_u_or_g(Sg[0], AF.Copy, True)

            evSS0 = ctx.ev.n
            if not NOSHARED:
                @blk.vector
                def _(vector):
                    ctx.ea.wait(vector)
                    for rc in range(KC):
                        src2 = (usb if rc < NT else gsb)[:, (rc % NT) * TC:(rc % NT + 1) * TC]
                        ctx.ev.inc(vector.tensor_tensor(
                            out=ssb[:, rc * TC:(rc + 1) * TC],
                            in0=ssb[:, rc * TC:(rc + 1) * TC],
                            in1=src2, op=ALU.mult), 1)

            evY0s = ctx.ev.n

            def prewaitsy(tensor, n):
                og2 = n // KC
                rk = n % KC
                if og2 == 0:
                    tensor.wait_ge(ctx.ev.h, evSS0 + rk + 1)

            def onpsy(tensor, n, slot):
                og2 = n // KC
                rk = n % KC
                m = None
                for oc in range(4):
                    if rk == 0 and og2 >= 1:
                        tensor.wait_ge(ctx.ev.h, evY0s + (og2 - 1) * 4 + oc + 1)
                    for h in range(2):
                        m = tensor.matmul(
                            ps4[oc][:, h * HF:(h + 1) * HF],
                            slot[:, oc * P:(oc + 1) * P],
                            ssb[:, rk * TC + h * HF:rk * TC + (h + 1) * HF],
                            start=(rk == 0), stop=(rk == KC - 1))
                return m

            if not NOSHARED:
                piecessy = []
                for og2 in range(4):
                    for rk in range(KC):
                        piecessy.append(Sd[0][rk * P:(rk + 1) * P, og2 * 512:(og2 + 1) * 512])
                mmSY = ctx.mm.n
                wstream(ctx, piecessy, 512, onpsy, prewaitsy)

                @blk.vector
                def _(vector, mmSY=mmSY):
                    for og2 in range(4):
                        vector.wait_ge(ctx.mm.h, mmSY + (og2 + 1) * KC)
                        for oc in range(4):
                            dst = ctx.xT[:, (og2 * 4 + oc) * TC:(og2 * 4 + oc + 1) * TC]
                            ctx.ev.inc(vector.tensor_tensor(out=dst, in0=dst,
                                                            in1=ps4[oc][:], op=ALU.add), 1)

            ctx.bar()

    emit_ln(ctx, lambda q: ctx.xT[:, q * TC:(q + 1) * TC], "residual",
            xold_dram=ctx.xspill)


def emit_l1_attn(ctx, W):
    nc, blk = ctx.nc, ctx.block
    ctx.fresh_dma_sems("l1a")
    whg = W["L1_attn_w_hg"]
    wout = W["L1_attn_w_out"]
    NQ = 24

    esx = contextlib.ExitStack()
    ctx.xln = esx.enter_context(nc.sbuf_tensor(_nm("xln"), [P, KC * TC], BF16))
    xln = ctx.xln
    emit_ln(ctx, lambda q: ctx.xT[:, q * TC:(q + 1) * TC], "xln")
    emit_spill_x(ctx)

    pspill = nc.dram_tensor(_nm("pspill"), [P, NQ * TC], BF16)
    ccin = nc.dram_tensor(_nm("cc_h_in"), [2, 3072], F32)
    ccout = nc.dram_tensor(_nm("cc_h_out"), [16, 3072], F32)

    with (
        nc.sbuf_tensor(_nm("a1_h"), [P, NQ * TC], BF16) as hbuf,
        nc.sbuf_tensor(_nm("a1_pst"), [P, TC], BF16) as pst,
        nc.sbuf_tensor(_nm("a1_pst2"), [P, 2 * TC], BF16) as pst2,
        nc.sbuf_tensor(_nm("a1_s1"), [P, TC], F32) as s1,
        nc.sbuf_tensor(_nm("a1_s2"), [P, TC], F32) as s2,
        nc.sbuf_tensor(_nm("a1_az"), [P, 2 * TC], F32) as az,
        nc.sbuf_tensor(_nm("a1_zero"), [P, TC], F32) as zeros,
        nc.sbuf_tensor(_nm("a1_stg"), [P, 48], F32) as stg,
        nc.sbuf_tensor(_nm("a1_stgT"), [48, P], F32) as stgT,
        nc.sbuf_tensor(_nm("a1_he"), [8, 3072], BF16) as he,
        nc.sbuf_tensor(_nm("a1_pt"), [8, 3072], BF16) as pt8,
        nc.sbuf_tensor(_nm("a1_w6"), [8, 3072], BF16) as w6,
        nc.sbuf_tensor(_nm("a1_tm"), [8, 8], BF16) as tm,
        nc.sbuf_tensor(_nm("a1_tb"), [8, 1], F32) as tb,
        nc.sbuf_tensor(_nm("a1_hc"), [P, NQ], F32) as hcols,
        nc.psum_tensor(_nm("a1_p0"), [P, TC], F32) as pA,
        nc.psum_tensor(_nm("a1_p1"), [P, TC], F32) as pB,
        nc.psum_tensor(_nm("a1_p2"), [P, TC], F32) as pC,
        nc.psum_tensor(_nm("a1_p3"), [P, TC], F32) as pD,
    ):
        ps4 = [pA, pB, pC, pD]
        a_sb = az[:, 0:TC]

        @blk.vector
        def _(vector):
            ctx.ev.inc(vector.memset(zeros[:, :], 0.0), 1)

        @blk.gpsimd
        def _(g):
            ctx.dg.inc(g.dma_start(out=tm[:, :], in_=W["tmat"]), 16)

        @blk.sync
        def _(sync):
            ctx.ds.inc(sync.dma_start(out=tb[:, :], in_=W["tbias"]), 16)

        dg_spill0 = ctx.dg.n
        for pg in range(12):
            q0 = 2 * pg
            pieces = []
            for k in range(KC):
                pieces.append(whg[k * P:(k + 1) * P, q0 * P:(q0 + 2) * P])
                pieces.append(whg[k * P:(k + 1) * P, (24 + q0) * P:(24 + q0 + 2) * P])
            eaB = ctx.ea.n
            evB = ctx.ev.n

            def prewait(tensor, n, eaB=eaB, evB=evB):
                if n == 0:
                    tensor.wait_ge(ctx.ea.h, eaB)
                    tensor.wait_ge(ctx.ev.h, evB)

            def onp(tensor, n, slot, pg=pg):
                k = n // 2
                hid = (n % 2 == 0)
                m = None
                for j in range(2):
                    tgt = ps4[j] if hid else ps4[2 + j]
                    for h in range(2):
                        m = tensor.matmul(
                            tgt[:, h * HF:(h + 1) * HF],
                            slot[:, j * P:(j + 1) * P],
                            xln[:, k * TC + h * HF:k * TC + (h + 1) * HF],
                            start=(k == 0), stop=(k == KC - 1))
                return m

            wstream(ctx, pieces, 256, onp, prewait)

            for j in range(2):
                q = q0 + j
                hidp, gatep = ps4[j], ps4[2 + j]
                mmB = ctx.mm.n

                @blk.scalar
                def _(scalar, q=q, hidp=hidp, gatep=gatep, mmB=mmB, j=j):
                    scalar.wait_ge(ctx.mm.h, mmB)
                    scalar.wait_ge(ctx.ev.h, ctx.ev.n)
                    scalar.activation(out=az[:, 0:TC], in_=gatep[:], func=AF.Sigmoid,
                                      scale=-1.0)
                    scalar.activation(out=az[:, TC:2 * TC], in_=gatep[:],
                                      func=AF.Sigmoid)
                    ctx.ea.inc(scalar.activation(out=s1[:], in_=hidp[:],
                                                 func=AF.Relu), 1)

                @blk.vector
                def _(vector, q=q, hidp=hidp):
                    ctx.ea.wait(vector)
                    ctx.ev.inc(vector.tensor_scalar(out=s2[:], in0=hidp[:], scalar1=0.0,
                                                    scalar2=None, op0=ALU.min), 1)

                @blk.scalar
                def _(scalar, q=q):
                    ctx.ev.wait(scalar)
                    ctx.ea.inc(scalar.activation(out=s2[:], in_=s2[:],
                                                 func=AF.Sigmoid), 1)

                @blk.vector
                def _(vector, q=q):
                    ctx.ea.wait(vector)
                    vector.tensor_tensor(out=s1[:], in0=s1[:], in1=s2[:], op=ALU.add)
                    vector.tensor_tensor(out=s2[:], in0=az[:, TC:2 * TC], in1=s1[:],
                                         op=ALU.mult)
                    ctx.vgap(vector)
                    vector.tensor_tensor_scan(out=hbuf[:, q * TC:(q + 1) * TC],
                                              data0=a_sb, data1=s2[:], initial=0.0,
                                              op0=ALU.mult, op1=ALU.add)
                    ctx.vgap(vector)
                    vector.wait_ge(ctx.dg.h, ctx.dg.n)
                    vector.tensor_tensor_scan(out=pst[:, :], data0=a_sb, data1=zeros[:],
                                              initial=1.0, op0=ALU.mult, op1=ALU.add)
                    ctx.vgap(vector)
                    vector.tensor_copy(out=stg[:, q:q + 1],
                                       in_=hbuf[:, q * TC + TC - 1:q * TC + TC])
                    ctx.ev.inc(vector.tensor_copy(out=stg[:, 24 + q:25 + q],
                                                  in_=pst[:, TC - 1:TC]), 1)

                @blk.gpsimd
                def _(g, q=q):
                    ctx.ev.wait(g)
                    ctx.dg.inc(g.dma_start(out=pspill.ap()[:, q * TC:(q + 1) * TC],
                                           in_=pst[:, :]), 16)

        mmX = ctx.mm.n

        @blk.tensor
        def _(tensor):
            ctx.ev.wait(tensor)
            ctx.mm.inc(tensor.transpose(pA[0:48, 0:P], stg[:, :],
                                        ctx.bufs["ID_F32"]), 1)

        @blk.scalar
        def _(scalar):
            ctx.mm.wait(scalar)
            ctx.ea.inc(scalar.copy(out=stgT[:, :], in_=pA[0:48, 0:P]), 1)

        @blk.sync
        def _(sync):
            ctx.ea.wait(sync)
            ctx.ds.inc(sync.dma_start(
                out=ccin.ap()[0:1, :].rearrange("o (q p) -> (o q) p", p=P),
                in_=stgT[0:24, :]), 16)
            ctx.ds.inc(sync.dma_start(
                out=ccin.ap()[1:2, :].rearrange("o (q p) -> (o q) p", p=P),
                in_=stgT[24:48, :]), 16)

        @blk.gpsimd
        def _(g):
            ctx.ds.wait(g)
            ctx.cc.inc(g.collective_compute(
                "AllGather", ALU.bypass,
                replica_groups=[[0, 1, 2, 3, 4, 5, 6, 7]],
                ins=[ccin.ap()], outs=[ccout.ap()]), 1)

        @blk.gpsimd
        def _(g):
            ctx.cc.wait(g)
            cov = ccout.ap().rearrange("(j two) c -> two j c", two=2)
            ctx.dg.inc(g.dma_start(out=he[:, :], in_=cov[0]), 16)
            ctx.dg.inc(g.dma_start(out=pt8[:, :], in_=cov[1]), 16)

        @blk.gpsimd
        def _(g):
            ctx.dg.wait(g)
            ctx.ep.inc(g.tensor_scalar(out=pt8[:, :], in0=pt8[:, :], scalar1=1e-38,
                                       scalar2=None, op0=ALU.max), 1)

        @blk.scalar
        def _(scalar):
            ctx.ep.wait(scalar)
            ctx.ea.inc(scalar.activation(out=pt8[:, :], in_=pt8[:, :], func=AF.Ln), 1)

        mmW = ctx.mm.n
        eaW0 = ctx.ea.n

        @blk.tensor
        def _(tensor):
            ctx.ea.wait(tensor)
            for sl6 in range(6):
                if sl6 >= 2:
                    tensor.wait_ge(ctx.ea.h, eaW0 + sl6 - 1)
                ctx.mm.inc(tensor.matmul(pB[0:8, (sl6 % 2) * HF:(sl6 % 2 + 1) * HF],
                                         tm[:, :], pt8[:, sl6 * HF:(sl6 + 1) * HF],
                                         start=True, stop=True), 1)

        @blk.scalar
        def _(scalar):
            for sl6 in range(6):
                scalar.wait_ge(ctx.mm.h, mmW + sl6 + 1)
                ctx.ea.inc(scalar.activation(out=w6[:, sl6 * HF:(sl6 + 1) * HF],
                                             in_=pB[0:8, (sl6 % 2) * HF:(sl6 % 2 + 1) * HF],
                                             func=AF.Exp, bias=tb[:, 0:1]), 1)

        @blk.gpsimd
        def _(g):
            ctx.ea.wait(g)
            ctx.ep.inc(g.tensor_tensor(out=w6[:, :], in0=w6[:, :], in1=he[:, :],
                                       op=ALU.mult), 1)

        mmH = ctx.mm.n

        @blk.tensor
        def _(tensor):
            ctx.ep.wait(tensor)
            for q in range(NQ):
                ctx.mm.inc(tensor.matmul(pC[:, q:q + 1],
                                         w6[:, q * P:(q + 1) * P],
                                         ctx.bufs["ONES1B"][0:8, 0:1],
                                         start=True, stop=True), 1)

        @blk.scalar
        def _(scalar):
            scalar.wait_ge(ctx.mm.h, mmH + NQ)
            ctx.ea.inc(scalar.copy(out=hcols[:, :], in_=pC[:, 0:NQ]), 1)

        dsc1 = ctx.dsc.n
        ev1 = ctx.ev.n

        @blk.scalar
        def _(scalar):
            ctx.dg.wait(scalar)
            for q in range(NQ):
                sl = q % 2
                if q >= 2:
                    scalar.wait_ge(ctx.ev.h, ev1 + q - 1)
                ctx.dsc.inc(scalar.dma_start(
                    out=pst2[:, sl * TC:(sl + 1) * TC],
                    in_=pspill.ap()[:, q * TC:(q + 1) * TC]), 16)

        @blk.vector
        def _(vector):
            ctx.ea.wait(vector)
            for q in range(NQ):
                sl = q % 2
                vector.wait_ge(ctx.dsc.h, dsc1 + 16 * (q + 1))
                ctx.ev.inc(vector.scalar_tensor_tensor(
                    out=hbuf[:, q * TC:(q + 1) * TC], in0=pst2[:, sl * TC:(sl + 1) * TC],
                    scalar=hcols[:, q:q + 1],
                    in1=hbuf[:, q * TC:(q + 1) * TC],
                    op0=ALU.mult, op1=ALU.add), 1)

        ctx.bar()

        for og2 in range(4):
            pieces = [wout[rk * P:(rk + 1) * P, og2 * 512:(og2 + 1) * 512]
                      for rk in range(NQ)]
            eaW = ctx.ea.n

            def prewait(tensor, n, eaW=eaW):
                if n == 0:
                    tensor.wait_ge(ctx.ea.h, eaW)

            def onp(tensor, n, slot):
                rk = n
                m = None
                for oc in range(4):
                    for h in range(2):
                        m = tensor.matmul(
                            ps4[oc][:, h * HF:(h + 1) * HF],
                            slot[:, oc * P:(oc + 1) * P],
                            hbuf[:, rk * TC + h * HF:rk * TC + (h + 1) * HF],
                            start=(rk == 0), stop=(rk == NQ - 1))
                return m

            mmWo = ctx.mm.n
            wstream(ctx, pieces, 512, onp, prewait)

            @blk.scalar
            def _(scalar, og2=og2, mmWo=mmWo):
                scalar.wait_ge(ctx.mm.h, mmWo + NQ)
                for oc in range(4):
                    ctx.ea.inc(scalar.copy(
                        out=ctx.xT[:, (og2 * 4 + oc) * TC:(og2 * 4 + oc + 1) * TC],
                        in_=ps4[oc][:]), 1)

        ctx.bar()

    esx.close()
    emit_ln(ctx, lambda q: ctx.xT[:, q * TC:(q + 1) * TC], "residual",
            xold_dram=ctx.xspill)


def emit_l0_attn(ctx, W):
    nc, blk = ctx.nc, ctx.block
    ctx.fresh_dma_sems("l0a")
    NH = 16

    qspill = nc.dram_tensor(_nm("qspill"), [D, TC], BF16)
    ospill = nc.dram_tensor(_nm("ospill"), [D, TC], BF16)
    kspill = nc.dram_tensor(_nm("kspill"), [D, TC], BF16)
    ccs_in = nc.dram_tensor(_nm("ccs_in"), [P, D], F32)
    ccs_out = nc.dram_tensor(_nm("ccs_out"), [NCORE * P, D], F32)

    cs_np = np.zeros((P, P), dtype=np.float32)
    for s_ in range(P):
        for c_ in range(P):
            cs_np[s_, c_] = (c_ - s_) if c_ >= s_ else 1e9
    c_cs = nc.inline_tensor(cs_np, _nm("c_cs"))
    c127_np = (127.0 - np.arange(P, dtype=np.float32))[:, None]
    c_127 = nc.inline_tensor(c127_np, _nm("c_127"))
    cqp_np = np.zeros((P, 8), dtype=np.float32)
    ckp_np = np.zeros((P, 8), dtype=np.float32)
    for i_ in range(8):
        cqp_np[:, i_] = i_ * P + np.arange(P) + 1.0
        ckp_np[:, i_] = 1023.0 - i_ * P - np.arange(P)
    c_qp = nc.inline_tensor(cqp_np, _nm("c_qp"))
    c_kp = nc.inline_tensor(ckp_np, _nm("c_kp"))
    ej_np = np.zeros((8, 8 * P), dtype=np.float32)
    for j_ in range(8):
        ej_np[j_, j_ * P:(j_ + 1) * P] = 1.0
    c_ej = nc.inline_tensor(ej_np, _nm("c_ej"))

    with (
        nc.sbuf_tensor(_nm("a0_v"), [P, NT * D], BF16) as V,
        nc.sbuf_tensor(_nm("a0_g1"), [P, TC], BF16) as g1,
        nc.sbuf_tensor(_nm("a0_wg2"), [P, D], BF16) as wg2,
        nc.sbuf_tensor(_nm("a0_ldr"), [1, 16], F32) as ldr,
        nc.sbuf_tensor(_nm("a0_msel"), [8, 1], F32) as msel,
        nc.sbuf_tensor(_nm("a0_ej"), [8, 8 * P], F32) as ejs,
        nc.sbuf_tensor(_nm("a0_nwr"), [1, D], F32) as nwr,
    ):
        with (
            nc.sbuf_tensor(_nm("xln"), [P, KC * TC], BF16) as xln,
            nc.sbuf_tensor(_nm("a0_stg"), [P, 2 * TC], BF16) as stg,
        ):
            ctx.xln = xln
            emit_ln(ctx, lambda q: ctx.xT[:, q * TC:(q + 1) * TC], "xln")
            emit_spill_x(ctx)
            esp = contextlib.ExitStack()
            ps4 = [esp.enter_context(nc.psum_tensor(_nm(f"a0_pp{i}"), [P, TC], F32))
                   for i in range(4)]
            pp0 = ps4[0]

            @blk.gpsimd
            def _(g):
                ctx.dg.inc(g.dma_start(out=wg2[:, :], in_=W["L0_attn_Wg2"]), 16)

            @blk.sync
            def _(sync):
                ctx.ds.inc(sync.dma_start(
                    out=ldr[0:1, :],
                    in_=W["L0_attn_log_decay"].rearrange("(o e) -> o e", o=1)), 16)
                ctx.ds.inc(sync.dma_start(out=msel[:, :], in_=W["msel"]), 16)
                ctx.ds.inc(sync.dma_start(out=ejs[:, :], in_=c_ej.ap()), 16)
                ctx.ds.inc(sync.dma_start(
                    out=nwr[0:1, :],
                    in_=W["L0_attn_norm_w"].rearrange("(o e) -> o e", o=1)), 16)

            def qk_proj(mat_ap, spill):
                eaP = [ctx.ea.n]
                dscP = [ctx.dsc.n]

                def prewait(tensor, n, eaP=eaP):
                    og = n // KC
                    if n == 0:
                        tensor.wait_ge(ctx.ea.h, eaP[0])
                    if n % KC == 0 and og >= 1:
                        tensor.wait_ge(ctx.ea.h, eaP[0] + og * 4)

                def onp(tensor, n, slot):
                    k = n % KC
                    m = None
                    for oc in range(4):
                        for h in range(2):
                            m = tensor.matmul(
                                ps4[oc][:, h * HF:(h + 1) * HF],
                                slot[:, oc * P:(oc + 1) * P],
                                xln[:, k * TC + h * HF:k * TC + (h + 1) * HF],
                                start=(k == 0), stop=(k == KC - 1))
                    return m

                pieces = []
                for og in range(4):
                    for k in range(KC):
                        pieces.append(mat_ap[k * P:(k + 1) * P, og * 512:(og + 1) * 512])
                mmP = ctx.mm.n
                wstream(ctx, pieces, 512, onp, prewait)

                eaQ0 = [0]

                @blk.scalar
                def _(scalar):
                    nd0 = dscP[0]
                    eaQ0[0] = ctx.ea.n
                    for og in range(4):
                        scalar.wait_ge(ctx.mm.h, mmP + (og + 1) * KC)
                        for oc in range(4):
                            hh = og * 4 + oc
                            sl = hh % 2
                            if hh >= 2:
                                scalar.wait_ge(ctx.dsc.h, nd0 + 16 * (hh - 1))
                            ctx.ea.inc(scalar.activation(
                                out=stg[:, sl * TC:(sl + 1) * TC],
                                in_=ps4[oc][:], func=AF.Silu), 1)

                @blk.sync
                def _(sync):
                    for hh in range(16):
                        sl = hh % 2
                        sync.wait_ge(ctx.ea.h, eaQ0[0] + hh + 1)
                        ctx.dsc.inc(sync.dma_start(
                            out=spill.ap()[hh * P:(hh + 1) * P, :],
                            in_=stg[:, sl * TC:(sl + 1) * TC]), 16)

            qk_proj(W["L0_attn_Wq"], qspill)
            qk_proj(W["L0_attn_Wk"], kspill)

            eaV = [ctx.ea.n]

            def prewaitv(tensor, n):
                og = n // KC
                if n == 0:
                    tensor.wait_ge(ctx.ea.h, eaV[0])
                if n % KC == 0 and og >= 1:
                    tensor.wait_ge(ctx.ea.h, eaV[0] + og * 8)

            def onpv(tensor, n, slot):
                k = n % KC
                m = None
                for t in range(NT):
                    tgt = ps4[t % 4][:, (t // 4) * HF:(t // 4 + 1) * HF]
                    m = tensor.matmul(tgt,
                                      xln[:, k * TC + t * P:k * TC + (t + 1) * P],
                                      slot[:, :],
                                      start=(k == 0), stop=(k == KC - 1))
                return m

            piecesv = []
            for og in range(4):
                for k in range(KC):
                    piecesv.append(W["L0_attn_Wv"][k * P:(k + 1) * P,
                                                   og * 512:(og + 1) * 512])
            mmV = ctx.mm.n
            wstream(ctx, piecesv, 512, onpv, prewaitv)

            @blk.scalar
            def _(scalar):
                for og in range(4):
                    scalar.wait_ge(ctx.mm.h, mmV + (og + 1) * KC)
                    for t in range(NT):
                        ctx.ea.inc(scalar.activation(
                            out=V[:, t * D + og * 512:t * D + (og + 1) * 512],
                            in_=ps4[t % 4][:, (t // 4) * HF:(t // 4 + 1) * HF],
                            func=AF.Silu), 1)

            eaG = [ctx.ea.n]

            def prewaitg(tensor, n):
                if n == 0:
                    tensor.wait_ge(ctx.ea.h, eaG[0])

            def onpg(tensor, n, slot):
                k = n
                m = None
                for h in range(2):
                    m = tensor.matmul(pp0[:, h * HF:(h + 1) * HF], slot[:, 0:P],
                                      xln[:, k * TC + h * HF:k * TC + (h + 1) * HF],
                                      start=(k == 0), stop=(k == KC - 1))
                return m

            piecesg = [W["L0_attn_Wg1"][k * P:(k + 1) * P, :] for k in range(KC)]
            mmG = ctx.mm.n
            wstream(ctx, piecesg, 128, onpg, prewaitg)

            @blk.scalar
            def _(scalar):
                scalar.wait_ge(ctx.mm.h, mmG + KC)
                ctx.ea.inc(scalar.copy(out=g1[:, :], in_=pp0[:]), 1)

            ctx.bar()
            esp.close()

        es = contextlib.ExitStack()
        csb = es.enter_context(nc.sbuf_tensor(_nm("a0_csb"), [P, P], F32))
        cpp = es.enter_context(nc.sbuf_tensor(_nm("a0_cpp"), [P, 17], F32))
        DT = es.enter_context(nc.sbuf_tensor(_nm("a0_dt"), [P, NH * P], BF16))
        ldbc = es.enter_context(nc.sbuf_tensor(_nm("a0_ldbc"), [P, 16], F32))
        KDG = es.enter_context(nc.sbuf_tensor(_nm("a0_kdg"), [P, 8 * 16], F32))
        QDG = es.enter_context(nc.sbuf_tensor(_nm("a0_qdg"), [P, 8 * 16], F32))
        KDL = es.enter_context(nc.sbuf_tensor(_nm("a0_kdl"), [P, 16], F32))
        CD = es.enter_context(nc.sbuf_tensor(_nm("a0_cd"), [P, 16], F32))
        coef8 = es.enter_context(nc.sbuf_tensor(_nm("a0_cf8"), [8, 16], F32))
        COEFB = es.enter_context(nc.sbuf_tensor(_nm("a0_coefb"), [P, 8 * 16], F32))
        NW = es.enter_context(nc.sbuf_tensor(_nm("a0_nw"), [P, D], BF16))
        S_f32 = es.enter_context(nc.sbuf_tensor(_nm("a0_sf"), [P, D], F32))
        S_bf = es.enter_context(nc.sbuf_tensor(_nm("a0_sb"), [P, D], BF16))
        og = es.enter_context(nc.sbuf_tensor(_nm("a0_og"), [P, D], F32))
        gs = es.enter_context(nc.sbuf_tensor(_nm("a0_gs"), [P, D], F32))
        ks2 = es.enter_context(nc.sbuf_tensor(_nm("a0_ks"), [P, 2 * D], BF16))
        st2 = es.enter_context(nc.sbuf_tensor(_nm("a0_st2"), [P, 512], BF16))
        rmsb = es.enter_context(nc.sbuf_tensor(_nm("a0_rms"), [P, 33], F32))
        sqtr = es.enter_context(nc.sbuf_tensor(_nm("a0_sqtr"), [P, P], F32))
        og_bf = es.enter_context(nc.sbuf_tensor(_nm("a0_ogbf"), [P, D], BF16))
        ktok = es.enter_context(nc.sbuf_tensor(_nm("a0_ktok"), [P, 2 * D], BF16))
        qti = es.enter_context(nc.sbuf_tensor(_nm("a0_qti"), [P, 2 * D], BF16))
        kti = es.enter_context(nc.sbuf_tensor(_nm("a0_kti"), [P, 2 * D], BF16))
        onTi = es.enter_context(nc.sbuf_tensor(_nm("a0_onti"), [P, D], BF16))
        if True:
            with (
                nc.psum_tensor(_nm("a0_pc0"), [P, TC], F32) as pc0,
                nc.psum_tensor(_nm("a0_pc1"), [P, TC], F32) as pc1,
            ):
                mmC = ctx.mm.n

                @blk.sync
                def _(sync):
                    ctx.ds.inc(sync.dma_start(out=csb[:, :], in_=c_cs.ap()), 16)
                    ctx.ds.inc(sync.dma_start(out=cpp[:, 0:1], in_=c_127.ap()), 16)
                    ctx.ds.inc(sync.dma_start(out=cpp[:, 1:9], in_=c_qp.ap()), 16)
                    ctx.ds.inc(sync.dma_start(out=cpp[:, 9:17], in_=c_kp.ap()), 16)

                eaC = ctx.ea.n

                @blk.tensor
                def _(tensor):
                    ctx.ds.wait(tensor)
                    ctx.mm.inc(tensor.matmul(pc0[:, 0:16], ctx.bufs["ONESROW"],
                                             ldr[0:1, :], start=True, stop=True), 1)
                    for r in range(2):
                        if r == 1:
                            tensor.wait_ge(ctx.ea.h, eaC + 2)
                        for h2 in range(2):
                            h = 2 * r + h2
                            ctx.mm.inc(tensor.matmul(pc1[:, h2 * HF:(h2 + 1) * HF],
                                                     ctx.bufs["ONESROW"],
                                                     nwr[0:1, h * HF:(h + 1) * HF],
                                                     start=True, stop=True), 1)

                @blk.scalar
                def _(scalar):
                    scalar.wait_ge(ctx.mm.h, mmC + 1)
                    ctx.ea.inc(scalar.copy(out=ldbc[:, :], in_=pc0[:, 0:16]), 1)
                    for r in range(2):
                        scalar.wait_ge(ctx.mm.h, mmC + 1 + (r + 1) * 2)
                        ctx.ea.inc(scalar.copy(out=NW[:, r * TC:(r + 1) * TC],
                                               in_=pc1[:]), 1)

                @blk.vector
                def _(vector):
                    ctx.ea.wait(vector)
                    for h in range(NH):
                        vector.tensor_scalar(out=gs[:, h * P:(h + 1) * P], in0=csb[:, :],
                                             scalar1=ldbc[:, h:h + 1], scalar2=None,
                                             op0=ALU.mult)
                    vector.tensor_scalar(out=KDL[:, :], in0=ldbc[:, :],
                                         scalar1=cpp[:, 0:1], scalar2=None, op0=ALU.mult)
                    vector.tensor_scalar(out=CD[:, :], in0=ldbc[:, :], scalar1=128.0,
                                         scalar2=None, op0=ALU.mult)
                    for i in range(8):
                        vector.tensor_scalar(out=QDG[:, i * 16:(i + 1) * 16],
                                             in0=ldbc[:, :], scalar1=cpp[:, 1 + i:2 + i],
                                             scalar2=None, op0=ALU.mult)
                        vector.tensor_scalar(out=KDG[:, i * 16:(i + 1) * 16],
                                             in0=ldbc[:, :], scalar1=cpp[:, 9 + i:10 + i],
                                             scalar2=None, op0=ALU.mult)
                    ctx.ev.inc(vector.tensor_scalar(out=coef8[:, :], in0=ldbc[0:8, :],
                                                    scalar1=msel[:, 0:1], scalar2=None,
                                                    op0=ALU.mult), 1)

                @blk.scalar
                def _(scalar):
                    ctx.ev.wait(scalar)
                    scalar.activation(out=DT[:, :], in_=gs[:, :], func=AF.Exp)
                    scalar.activation(out=KDL[:, :], in_=KDL[:, :], func=AF.Exp)
                    scalar.activation(out=CD[:, :], in_=CD[:, :], func=AF.Exp)
                    scalar.activation(out=QDG[:, :], in_=QDG[:, :], func=AF.Exp)
                    scalar.activation(out=KDG[:, :], in_=KDG[:, :], func=AF.Exp)
                    ctx.ea.inc(scalar.activation(out=coef8[:, :], in_=coef8[:, :],
                                                 func=AF.Exp), 1)

                mmE = ctx.mm.n

                @blk.tensor
                def _(tensor):
                    ctx.ea.wait(tensor)
                    eaEJ = ctx.ea.n
                    for j in range(8):
                        if j >= 2:
                            tensor.wait_ge(ctx.ea.h, eaEJ + j - 1)
                        ctx.mm.inc(tensor.matmul(
                            pc0[:, (j % 2) * 16:(j % 2) * 16 + 16],
                            ejs[:, j * P:(j + 1) * P], coef8[:, :],
                            start=True, stop=True), 1)

                @blk.scalar
                def _(scalar):
                    for j in range(8):
                        scalar.wait_ge(ctx.mm.h, mmE + j + 1)
                        ctx.ea.inc(scalar.copy(out=COEFB[:, j * 16:(j + 1) * 16],
                                               in_=pc0[:, (j % 2) * 16:(j % 2) * 16 + 16]), 1)

                ctx.bar()

            with nc.psum_tensor(_nm("a0_psA"), [P, D], F32) as psS:
                trs = [Sem(nc, _nm("s_tr0")), Sem(nc, _nm("s_tr1"))]
                eaA0 = ctx.ea.n
                mmA0 = ctx.mm.n
                for i in range(NT):
                    sl = i % 2

                    @blk.sync
                    def _(sync, i=i, sl=sl):
                        if i >= 2:
                            sync.wait_ge(ctx.ea.h, eaA0 + (i - 1) * NH)
                        trs[sl].inc(sync.dma_start_transpose(
                            out=ktok[:, sl * D:(sl + 1) * D],
                            in_=kspill.ap()[:, i * P:(i + 1) * P]), 16)

                    @blk.scalar
                    def _(scalar, i=i, sl=sl):
                        trs[sl].wait(scalar)
                        if i >= 2:
                            scalar.wait_ge(ctx.mm.h, mmA0 + i - 1)
                        for h in range(NH):
                            ctx.ea.inc(scalar.activation(
                                out=ks2[:, sl * D + h * P:sl * D + (h + 1) * P],
                                in_=ktok[:, sl * D + h * P:sl * D + (h + 1) * P],
                                func=AF.Copy, scale=KDG[:, i * 16 + h:i * 16 + h + 1]), 1)

                    @blk.tensor
                    def _(tensor, i=i, sl=sl):
                        ctx.ea.wait(tensor)
                        m = None
                        for h in range(NH):
                            m = tensor.matmul(psS[:, h * P:(h + 1) * P],
                                              ks2[:, sl * D + h * P:sl * D + (h + 1) * P],
                                              V[:, i * D + h * P:i * D + (h + 1) * P],
                                              start=(i == 0 and h % 4 == 0),
                                              stop=(i == NT - 1 and h % 4 == 3))
                        ctx.mm.inc(m, 1)

                @blk.scalar
                def _(scalar):
                    ctx.mm.wait(scalar)
                    for h in range(4):
                        ctx.ea.inc(scalar.copy(out=og[:, h * HF:(h + 1) * HF],
                                               in_=psS[:, h * HF:(h + 1) * HF]), 1)

                @blk.sync
                def _(sync):
                    ctx.ea.wait(sync)
                    ctx.ds.inc(sync.dma_start(out=ccs_in.ap(), in_=og[:, :]), 16)

                @blk.gpsimd
                def _(g):
                    ctx.ds.wait(g)
                    ctx.cc.inc(g.collective_compute(
                        "AllGather", ALU.bypass,
                        replica_groups=[[0, 1, 2, 3, 4, 5, 6, 7]],
                        ins=[ccs_in.ap()], outs=[ccs_out.ap()]), 1)

                ctx.bar()

            @blk.vector
            def _(vector):
                ctx.ev.inc(vector.memset(S_f32[:, :], 0.0), 1)

            dscS = ctx.dsc.n
            evS0 = ctx.ev.n
            with nc.sbuf_tensor(_nm("a0_sj"), [P, D], F32) as sj:
                @blk.scalar
                def _(scalar):
                    ctx.cc.wait(scalar)
                    for j in range(8):
                        if j >= 1:
                            scalar.wait_ge(ctx.ev.h, evS0 + j * NH)
                        ctx.dsc.inc(scalar.dma_start(
                            out=sj[:, :],
                            in_=ccs_out.ap()[j * P:(j + 1) * P, :]), 16)

                @blk.vector
                def _(vector):
                    for j in range(8):
                        vector.wait_ge(ctx.dsc.h, dscS + 16 * (j + 1))
                        for h in range(NH):
                            ctx.ev.inc(vector.scalar_tensor_tensor(
                                out=S_f32[:, h * P:(h + 1) * P],
                                in0=sj[:, h * P:(h + 1) * P],
                                scalar=COEFB[:, j * 16 + h:j * 16 + h + 1],
                                in1=S_f32[:, h * P:(h + 1) * P],
                                op0=ALU.mult, op1=ALU.add), 1)

                @blk.scalar
                def _(scalar):
                    ctx.ev.wait(scalar)
                    ctx.ea.inc(scalar.copy(out=S_bf[:, :], in_=S_f32[:, :]), 1)

                ctx.bar()

            with (
                nc.psum_tensor(_nm("a0_pO"), [P, D], F32) as pO,
                nc.psum_tensor(_nm("a0_pst"), [P, 512], F32) as pST,
                nc.psum_tensor(_nm("a0_px"), [P, 512], F32) as pX,
                nc.psum_tensor(_nm("a0_ps2"), [P, 512], F32) as pS2,
            ):
                qrs = [Sem(nc, _nm("s_qr0")), Sem(nc, _nm("s_qr1"))]
                krs = [Sem(nc, _nm("s_kr0")), Sem(nc, _nm("s_kr1"))]
                trb = [Sem(nc, _nm("s_tb0")), Sem(nc, _nm("s_tb1"))]
                for i in range(NT):
                    sl = i % 2
                    mm0 = ctx.mm.n
                    ea0 = ctx.ea.n
                    ev0 = ctx.ev.n

                    @blk.sync
                    def _(sync, i=i, sl=sl):
                        qrs[sl].inc(sync.dma_start(
                            out=qti[:, sl * D:(sl + 1) * D],
                            in_=qspill.ap()
                            .rearrange("(f p) t -> p f t", p=P)[:, :, i * P:(i + 1) * P]), 16)
                        krs[sl].inc(sync.dma_start(
                            out=kti[:, sl * D:(sl + 1) * D],
                            in_=kspill.ap()
                            .rearrange("(f p) t -> p f t", p=P)[:, :, i * P:(i + 1) * P]), 16)
                        trb[sl].inc(sync.dma_start_transpose(
                            out=ktok[:, sl * D:(sl + 1) * D],
                            in_=kspill.ap()[:, i * P:(i + 1) * P]), 16)

                    for gI in range(4):
                        hs0 = gI * 4

                        @blk.tensor
                        def _(tensor, i=i, sl=sl, gI=gI, hs0=hs0, mm0=mm0, ev0=ev0, ea0=ea0):
                            if gI == 0:
                                tensor.wait_ge(qrs[sl].h, qrs[sl].n)
                                tensor.wait_ge(krs[sl].h, krs[sl].n)
                            else:
                                tensor.wait_ge(ctx.ev.h, ctx.ev.n)
                            m = None
                            for hh in range(4):
                                h = hs0 + hh
                                m = tensor.matmul(pST[:, hh * P:(hh + 1) * P],
                                                  kti[:, sl * D + h * P:sl * D + (h + 1) * P],
                                                  qti[:, sl * D + h * P:sl * D + (h + 1) * P],
                                                  start=True, stop=True)
                            ctx.mm.inc(m, 1)

                        @blk.vector
                        def _(vector, i=i, gI=gI, hs0=hs0):
                            ctx.mm.wait(vector)
                            m = None
                            for hh in range(4):
                                h = hs0 + hh
                                m = vector.tensor_tensor(
                                    out=st2[:, hh * P:(hh + 1) * P],
                                    in0=pST[:, hh * P:(hh + 1) * P],
                                    in1=DT[:, h * P:(h + 1) * P], op=ALU.mult)
                            ctx.ev.inc(m, 1)

                        @blk.tensor
                        def _(tensor, i=i, sl=sl, gI=gI, hs0=hs0):
                            ctx.ev.wait(tensor)
                            ctx.ea.wait(tensor)
                            m = None
                            for hh in range(4):
                                h = hs0 + hh
                                tensor.matmul(pO[:, h * P:(h + 1) * P],
                                              st2[:, hh * P:(hh + 1) * P],
                                              V[:, i * D + h * P:i * D + (h + 1) * P],
                                              start=True, stop=True)
                                m = tensor.matmul(pX[:, hh * P:(hh + 1) * P],
                                                  qti[:, sl * D + h * P:sl * D + (h + 1) * P],
                                                  S_bf[:, h * P:(h + 1) * P],
                                                  start=True, stop=True)
                            ctx.mm.inc(m, 1)

                        @blk.scalar
                        def _(scalar, i=i, sl=sl, gI=gI, hs0=hs0):
                            ctx.mm.wait(scalar)
                            if gI == 0:
                                trb[sl].wait(scalar)
                            m = None
                            for hh in range(4):
                                h = hs0 + hh
                                scalar.activation(
                                    out=og[:, h * P:(h + 1) * P],
                                    in_=pX[:, hh * P:(hh + 1) * P], func=AF.Copy,
                                    scale=QDG[:, h:h + 1])
                                m = scalar.activation(
                                    out=ks2[:, sl * D + h * P:sl * D + (h + 1) * P],
                                    in_=ktok[:, sl * D + h * P:sl * D + (h + 1) * P],
                                    func=AF.Copy, scale=KDL[:, h:h + 1])
                            ctx.ea.inc(m, 1)

                        @blk.tensor
                        def _(tensor, i=i, sl=sl, gI=gI, hs0=hs0):
                            ctx.ea.wait(tensor)
                            ctx.ev.wait(tensor)
                            m = None
                            for hh in range(4):
                                h = hs0 + hh
                                m = tensor.matmul(pS2[:, hh * P:(hh + 1) * P],
                                                  ks2[:, sl * D + h * P:sl * D + (h + 1) * P],
                                                  V[:, i * D + h * P:i * D + (h + 1) * P],
                                                  start=True, stop=True)
                            ctx.mm.inc(m, 1)

                        @blk.vector
                        def _(vector, gI=gI, hs0=hs0):
                            ctx.mm.wait(vector)
                            m = None
                            for hh in range(4):
                                h = hs0 + hh
                                m = vector.scalar_tensor_tensor(
                                    out=S_f32[:, h * P:(h + 1) * P],
                                    in0=S_f32[:, h * P:(h + 1) * P],
                                    scalar=CD[:, h:h + 1],
                                    in1=pS2[:, hh * P:(hh + 1) * P],
                                    op0=ALU.mult, op1=ALU.add)
                            ctx.ev.inc(m, 1)

                        @blk.scalar
                        def _(scalar, gI=gI, hs0=hs0):
                            ctx.ev.wait(scalar)
                            ctx.ea.inc(scalar.copy(
                                out=S_bf[:, hs0 * P:(hs0 + 4) * P],
                                in_=S_f32[:, hs0 * P:(hs0 + 4) * P]), 1)

                    @blk.vector
                    def _(vector, i=i):
                        ctx.ea.wait(vector)
                        m = None
                        for h4 in range(4):
                            m = vector.tensor_tensor(
                                out=og[:, h4 * HF:(h4 + 1) * HF],
                                in0=og[:, h4 * HF:(h4 + 1) * HF],
                                in1=pO[:, h4 * HF:(h4 + 1) * HF], op=ALU.add)
                        ctx.ev.inc(m, 1)

                    @blk.tensor
                    def _(tensor, i=i):
                        ctx.ev.wait(tensor)
                        m = None
                        for h4 in range(4):
                            m = tensor.matmul(pO[:, h4 * HF:(h4 + 1) * HF],
                                              g1[:, i * P:(i + 1) * P],
                                              wg2[:, h4 * HF:(h4 + 1) * HF],
                                              start=True, stop=True)
                        ctx.mm.inc(m, 1)

                    @blk.scalar
                    def _(scalar, i=i):
                        ctx.mm.wait(scalar)
                        ctx.ea.inc(scalar.activation(out=gs[:, :], in_=pO[:, :],
                                                     func=AF.Sigmoid), 1)

                    @blk.vector
                    def _(vector, i=i):
                        ctx.ea.wait(vector)
                        ctx.ev.inc(vector.tensor_tensor(out=og[:, :], in0=og[:, :],
                                                        in1=gs[:, :], op=ALU.mult), 1)

                    @blk.scalar
                    def _(scalar, i=i):
                        ctx.ev.wait(scalar)
                        for h in range(NH):
                            scalar.activation(out=sqtr[:, :], in_=og[:, h * P:(h + 1) * P],
                                              func=AF.Square,
                                              accum_out=rmsb[:, h:h + 1])
                        scalar.activation(out=rmsb[:, 16:32], in_=rmsb[:, 0:16],
                                          func=AF.Ln, scale=1.0 / P,
                                          bias=ctx.bufs["EPSCOL"][:, 0:1])
                        ctx.ea.inc(scalar.activation(out=rmsb[:, 16:32],
                                                     in_=rmsb[:, 16:32],
                                                     func=AF.Exp, scale=-0.5), 1)

                    @blk.vector
                    def _(vector, i=i):
                        ctx.ea.wait(vector)
                        for h in range(NH):
                            vector.tensor_scalar(out=og[:, h * P:(h + 1) * P],
                                                 in0=og[:, h * P:(h + 1) * P],
                                                 scalar1=rmsb[:, 16 + h:17 + h],
                                                 scalar2=None, op0=ALU.mult)
                        m = None
                        for h4 in range(4):
                            m = vector.tensor_tensor(out=og[:, h4 * HF:(h4 + 1) * HF],
                                                     in0=og[:, h4 * HF:(h4 + 1) * HF],
                                                     in1=NW[:, h4 * HF:(h4 + 1) * HF],
                                                     op=ALU.mult)
                        ctx.ev.inc(m, 1)

                    @blk.scalar
                    def _(scalar, i=i):
                        ctx.ev.wait(scalar)
                        ctx.ea.inc(scalar.copy(out=og_bf[:, :], in_=og[:, :]), 1)

                    @blk.sync
                    def _(sync, i=i):
                        ctx.ea.wait(sync)
                        for f in range(NH):
                            ctx.ds.inc(sync.dma_start_transpose(
                                out=onTi[:, f * P:(f + 1) * P],
                                in_=og_bf[:, f * P:(f + 1) * P]), 16)

                    @blk.gpsimd
                    def _(g, i=i):
                        ctx.ds.wait(g)
                        ctx.dg.inc(g.dma_start(
                            out=ospill.ap().rearrange("(k p) t -> p k t", p=P)
                            [:, :, i * P:(i + 1) * P],
                            in_=onTi[:, :].rearrange("p (k c) -> p k c", c=P)), 16)

                    ctx.bar()

            onk = es.enter_context(nc.sbuf_tensor(_nm("a0_onk"), [P, 2 * TC], BF16))
            oks = [Sem(nc, _nm("s_ok0")), Sem(nc, _nm("s_ok1"))]
            for og2 in range(4):
                pieces = [W["L0_attn_Wo"][k * P:(k + 1) * P, og2 * 512:(og2 + 1) * 512]
                          for k in range(KC)]
                eaWo = ctx.ea.n
                mmWb = ctx.mm.n

                @blk.sync
                def _(sync, og2=og2, mmWb=mmWb):
                    if og2 == 0:
                        ctx.dg.wait(sync)
                    for k in range(KC):
                        sl = k % 2
                        if k >= 2:
                            sync.wait_ge(ctx.mm.h, mmWb + k - 1)
                        oks[sl].inc(sync.dma_start(
                            out=onk[:, sl * TC:(sl + 1) * TC],
                            in_=ospill.ap()[k * P:(k + 1) * P, :]), 16)

                def prewait(tensor, n, eaWo=eaWo, og2=og2):
                    sl = n % 2
                    tensor.wait_ge(oks[sl].h, oks[sl].n - 16 * (KC // 2 - 1 - n // 2))
                    if n == 0:
                        tensor.wait_ge(ctx.ea.h, eaWo)

                def onp(tensor, n, slot):
                    k = n
                    sl = k % 2
                    m = None
                    for oc in range(4):
                        for h in range(2):
                            m = tensor.matmul(
                                psW4[oc][:, h * HF:(h + 1) * HF],
                                slot[:, oc * P:(oc + 1) * P],
                                onk[:, sl * TC + h * HF:sl * TC + (h + 1) * HF],
                                start=(k == 0), stop=(k == KC - 1))
                    return m

                with (
                    nc.psum_tensor(_nm("a0_pw0"), [P, TC], F32) as pw0,
                    nc.psum_tensor(_nm("a0_pw1"), [P, TC], F32) as pw1,
                    nc.psum_tensor(_nm("a0_pw2"), [P, TC], F32) as pw2,
                    nc.psum_tensor(_nm("a0_pw3"), [P, TC], F32) as pw3,
                ):
                    psW4 = [pw0, pw1, pw2, pw3]
                    mmWo = ctx.mm.n
                    wstream(ctx, pieces, 512, onp, prewait)

                    @blk.scalar
                    def _(scalar, og2=og2, mmWo=mmWo):
                        ctx.dg.wait(scalar)
                        scalar.wait_ge(ctx.mm.h, mmWo + KC)
                        for oc in range(4):
                            ctx.ea.inc(scalar.copy(
                                out=ctx.xT[:, (og2 * 4 + oc) * TC:(og2 * 4 + oc + 1) * TC],
                                in_=psW4[oc][:]), 1)

                    ctx.bar()

            es.close()

    if os.environ.get("K_DBG_DUMP"):
        d_os = nc.dram_tensor("dbg_ospill", [D, TC], BF16, kind="ExternalOutput").ap()
        d_qs = nc.dram_tensor("dbg_qspill", [D, TC], BF16, kind="ExternalOutput").ap()
        d_cc = nc.dram_tensor("dbg_ccsin", [P, D], F32, kind="ExternalOutput").ap()
        d_cco = nc.dram_tensor("dbg_ccsout", [NCORE * P, D], F32, kind="ExternalOutput").ap()
        d_y = nc.dram_tensor("dbg_y", [P, KC * TC], F32, kind="ExternalOutput").ap()

        @blk.gpsimd
        def _(g):
            ctx.dg.inc(g.dma_start(out=d_os, in_=ospill.ap()), 16)
            ctx.dg.inc(g.dma_start(out=d_qs, in_=qspill.ap()), 16)
            ctx.dg.inc(g.dma_start(out=d_cc, in_=ccs_in.ap()), 16)
            ctx.dg.inc(g.dma_start(out=d_cco, in_=ccs_out.ap()), 16)
            ctx.dg.inc(g.dma_start(out=d_y, in_=ctx.xT[:, :]), 16)
        ctx.bar()

    emit_ln(ctx, lambda q: ctx.xT[:, q * TC:(q + 1) * TC], "residual",
            xold_dram=ctx.xspill)


EMITTERS = {"l0a": emit_l0_attn, "l0f": emit_l0_ffn, "l1a": emit_l1_attn,
            "l1f": emit_l1_ffn}


# version 05 (final)
# speedup vs baseline: 1.0843x; 1.0348x over previous
import contextlib
import os
import numpy as np
import ml_dtypes
import concourse.bass as bass
import concourse.mybir as mybir
from concourse.bass_utils import run_bass_kernel_spmd

F32 = mybir.dt.float32
BF16 = mybir.dt.bfloat16
AF = mybir.ActivationFunctionType
ALU = mybir.AluOpType
AX = mybir.AxisListType

P = 128
D = 2048
KC = D // P
TC = 1024
NT = TC // P
NCORE = 8
HF = 512
NSW = 4
EPS = 1e-5

BF = ml_dtypes.bfloat16

_uid = [0]


def _nm(s):
    _uid[0] += 1
    return f"{s}_{_uid[0]}"


class Sem:

    def __init__(self, nc, name):
        self.h = nc.alloc_semaphore(name)
        self.n = 0

    def inc(self, instr, k=1):
        instr.then_inc(self.h, k)
        self.n += k
        return instr

    def wait(self, eng, val=None):
        eng.wait_ge(self.h, self.n if val is None else val)


class Ctx:
    def __init__(self, nc, block):
        self.nc = nc
        self.block = block
        self.mm = Sem(nc, "s_mm")
        self.ea = Sem(nc, "s_ea")
        self.ev = Sem(nc, "s_ev")
        self.ep = Sem(nc, "s_ep")
        self.ds = Sem(nc, "s_ds")
        self.dsc = Sem(nc, "s_dsc")
        self.dg = Sem(nc, "s_dg")
        self.wr = [Sem(nc, f"s_wr{i}") for i in range(NSW)]
        self.wring_free = [0] * NSW
        self.xr = [Sem(nc, f"s_xr{i}") for i in range(2)]
        self.cc = Sem(nc, "s_cc")
        self.barsem = Sem(nc, "s_bar")
        self.bufs = {}
        self.carry_ready = False
        self.carry_next = True

    def fresh_dma_sems(self, tag):
        nc = self.nc
        self.ds = Sem(nc, _nm(f"ds_{tag}"))
        self.dsc = Sem(nc, _nm(f"dsc_{tag}"))
        self.dg = Sem(nc, _nm(f"dg_{tag}"))
        self.wr = [Sem(nc, _nm(f"wr{i}_{tag}")) for i in range(NSW)]
        self.cc = Sem(nc, _nm(f"cc_{tag}"))

    def vgap(self, vector):
        vector.tensor_copy(out=self.dummy[:, 8:16], in_=self.dummy[:, 0:8])
        vector.tensor_copy(out=self.dummy[:, 16:24], in_=self.dummy[:, 0:8])

    def bar(self):
        b = self.barsem
        target = b.n + 5
        blk = self.block

        @blk.sync
        def _(e):
            e.drain().then_inc(b.h, 1)
            e.wait_ge(b.h, target)

        @blk.tensor
        def _(e):
            e.drain().then_inc(b.h, 1)
            e.wait_ge(b.h, target)

        @blk.vector
        def _(e):
            e.drain().then_inc(b.h, 1)
            e.wait_ge(b.h, target)

        @blk.scalar
        def _(e):
            e.drain().then_inc(b.h, 1)
            e.wait_ge(b.h, target)

        @blk.gpsimd
        def _(e):
            e.drain().then_inc(b.h, 1)
            e.wait_ge(b.h, target)

        b.n = target


def _np_bf(x):
    return np.asarray(x, dtype=BF)



def emit_prologue(ctx, xin, consts):
    nc, blk = ctx.nc, ctx.block
    xT = ctx.xT
    with (
        nc.sbuf_tensor(_nm("xstage"), [P, 2 * D], F32) as xstage,
        nc.sbuf_tensor(_nm("pro_sq"), [P, 2 * TC], F32) as sq,
        nc.psum_tensor(_nm("ppro0"), [P, P], F32) as pt0,
        nc.psum_tensor(_nm("ppro1"), [P, P], F32) as pt1,
        nc.psum_tensor(_nm("ppro_m"), [1, TC], F32) as ps_m,
        nc.psum_tensor(_nm("ppro_s"), [1, TC], F32) as ps_s,
    ):
        pts = [pt0, pt1]
        mm0, ea0 = ctx.mm.n, ctx.ea.n

        @blk.sync
        def _(sync):
            for name in consts:
                ctx.ds.inc(sync.dma_start(out=ctx.bufs[name], in_=ctx.inl[name].ap()), 16)
            for t in range(NT):
                s = t % 2
                if t >= 2:
                    sync.wait_ge(ctx.mm.h, mm0 + KC * (t - 1))
                sync.dma_start(out=xstage[:, s * D:(s + 1) * D],
                               in_=xin[t * P:(t + 1) * P, :]).then_inc(ctx.xr[s].h, 16)
                ctx.xr[s].n += 16

        @blk.vector
        def _(vector):
            vector.memset(ctx.dummy[:, :], 0.0)

        @blk.tensor
        def _(tensor):
            ctx.ds.wait(tensor)
            for t in range(NT):
                s = t % 2
                tensor.wait_ge(ctx.xr[s].h, 16 * (t // 2 + 1))
                for q in range(KC):
                    j = t * KC + q
                    if j >= 2:
                        tensor.wait_ge(ctx.ea.h, ea0 + j - 1)
                    ctx.mm.inc(tensor.transpose(
                        pts[j % 2][:], xstage[:, s * D + q * P: s * D + (q + 1) * P],
                        ctx.bufs["ID_F32"]), 1)

        @blk.scalar
        def _(scalar):
            for t in range(NT):
                for q in range(KC):
                    j = t * KC + q
                    scalar.wait_ge(ctx.mm.h, mm0 + j + 1)
                    ctx.ea.inc(scalar.copy(
                        out=xT[:, q * TC + t * P: q * TC + (t + 1) * P],
                        in_=pts[j % 2][:]), 1)

        mmS0, eaS0 = ctx.mm.n, ctx.ea.n

        @blk.scalar
        def _(scalar):
            for q in range(KC):
                if q >= 2:
                    scalar.wait_ge(ctx.mm.h, mmS0 + (q - 1))
                ctx.ea.inc(scalar.square(out=sq[:, (q % 2) * TC:(q % 2 + 1) * TC],
                                         in_=xT[:, q * TC:(q + 1) * TC]), 1)

        @blk.tensor
        def _(tensor):
            ones = ctx.bufs["ONES1"]
            for q in range(KC):
                tensor.wait_ge(ctx.ea.h, eaS0 + q + 1)
                for h in range(2):
                    tensor.matmul(ps_m[0:1, h * HF:(h + 1) * HF], ones,
                                  xT[:, q * TC + h * HF:q * TC + (h + 1) * HF],
                                  start=(q == 0), stop=(q == KC - 1))
                for h in range(2):
                    m = tensor.matmul(ps_s[0:1, h * HF:(h + 1) * HF], ones,
                                      sq[:, (q % 2) * TC + h * HF:(q % 2) * TC + (h + 1) * HF],
                                      start=(q == 0), stop=(q == KC - 1))
                ctx.mm.inc(m, 1)

        @blk.scalar
        def _(scalar):
            scalar.wait_ge(ctx.mm.h, mmS0 + KC)
            scalar.copy(out=sq[0:1, 0:TC], in_=ps_m[0:1, :])
            ctx.ea.inc(scalar.copy(out=sq[0:1, TC:2 * TC], in_=ps_s[0:1, :]), 1)

        @blk.sync
        def _(sync):
            ctx.ea.wait(sync)
            ctx.ds.inc(sync.dma_start(out=ctx.carryd.ap(), in_=sq[0:1, 0:2 * TC]), 16)
            ctx.ds.wait(sync)

        ctx.carry_ready = True
        ctx.bar()


def emit_epilogue(ctx, xout):
    nc, blk = ctx.nc, ctx.block
    xT = ctx.xT
    with (
        nc.sbuf_tensor(_nm("ostage"), [P, 2 * D], F32) as ostage,
        nc.psum_tensor(_nm("pepi0"), [P, P], F32) as pt0,
        nc.psum_tensor(_nm("pepi1"), [P, P], F32) as pt1,
    ):
        pts = [pt0, pt1]
        mm0, ea0 = ctx.mm.n, ctx.ea.n
        st = Sem(nc, _nm("s_st"))

        @blk.tensor
        def _(tensor):
            for t in range(NT):
                for q in range(KC):
                    j = t * KC + q
                    if j >= 2:
                        tensor.wait_ge(ctx.ea.h, ea0 + j - 1)
                    ctx.mm.inc(tensor.transpose(
                        pts[j % 2][:], xT[:, q * TC + t * P: q * TC + (t + 1) * P],
                        ctx.bufs["ID_F32"]), 1)

        @blk.scalar
        def _(scalar):
            for t in range(NT):
                s = t % 2
                if t >= 2:
                    scalar.wait_ge(st.h, 16 * (t - 1))
                for q in range(KC):
                    j = t * KC + q
                    scalar.wait_ge(ctx.mm.h, mm0 + j + 1)
                    ctx.ea.inc(scalar.copy(
                        out=ostage[:, s * D + q * P: s * D + (q + 1) * P],
                        in_=pts[j % 2][:]), 1)

        @blk.sync
        def _(sync):
            for t in range(NT):
                s = t % 2
                sync.wait_ge(ctx.ea.h, ea0 + (t + 1) * KC)
                st.inc(sync.dma_start(out=xout[t * P:(t + 1) * P, :],
                                      in_=ostage[:, s * D:(s + 1) * D]), 16)
            st.wait(sync)



LN_SPLIT = 10


def emit_ln(ctx, src_q, dst_mode, xold_dram=None):
    nc, blk = ctx.nc, ctx.block
    carry_in = dst_mode == "xln" and ctx.carry_ready
    carry_out = dst_mode == "residual" and ctx.carry_next
    SPL = LN_SPLIT
    import contextlib
    with contextlib.ExitStack() as _es:
        need_sq = (not carry_in) or carry_out
        if need_sq:
            sq = _es.enter_context(nc.sbuf_tensor(_nm("ln_sq"), [P, 2 * TC], F32))
        rows = _es.enter_context(nc.sbuf_tensor(_nm("ln_rows"), [1, 4 * TC], F32))
        rstd_s = _es.enter_context(nc.sbuf_tensor(_nm("ln_rstd"), [P, TC], F32))
        mrs_s = _es.enter_context(nc.sbuf_tensor(_nm("ln_mrs"), [P, TC], F32))
        tsc = _es.enter_context(nc.sbuf_tensor(_nm("ln_t"), [P, TC], F32))
        tsc2 = _es.enter_context(nc.sbuf_tensor(_nm("ln_t2"), [P, TC], F32))
        if dst_mode == "residual":
            xoldb = _es.enter_context(nc.sbuf_tensor(_nm("ln_xold"), [P, 4 * TC], F32))
        if not carry_in or carry_out:
            ps_m = _es.enter_context(nc.psum_tensor(_nm("ln_pm"), [1, TC], F32))
            ps_s = _es.enter_context(nc.psum_tensor(_nm("ln_ps"), [1, TC], F32))
        ps_b0 = _es.enter_context(nc.psum_tensor(_nm("ln_pb0"), [P, TC], F32))
        ps_b1 = _es.enter_context(nc.psum_tensor(_nm("ln_pb1"), [P, TC], F32))
        mm0, ea0, ev0 = ctx.mm.n, ctx.ea.n, ctx.ev.n
        dsl0 = ctx.ds.n
        if carry_in:
            @blk.sync
            def _(sync):
                ctx.ds.inc(sync.dma_start(out=rows[0:1, 2 * TC:4 * TC],
                                          in_=ctx.carryd.ap()), 16)

        if not carry_in:
            @blk.scalar
            def _(scalar):
                for q in range(KC):
                    if q >= 2:
                        scalar.wait_ge(ctx.mm.h, mm0 + (q - 1))
                    ctx.ea.inc(scalar.square(out=sq[:, (q % 2) * TC:(q % 2 + 1) * TC],
                                             in_=src_q(q)), 1)

            @blk.tensor
            def _(tensor):
                ones = ctx.bufs["ONES1"]
                for q in range(KC):
                    tensor.wait_ge(ctx.ea.h, ea0 + q + 1)
                    for h in range(2):
                        tensor.matmul(ps_m[0:1, h * HF:(h + 1) * HF], ones,
                                      src_q(q)[:, h * HF:(h + 1) * HF],
                                      start=(q == 0), stop=(q == KC - 1))
                    for h in range(2):
                        m = tensor.matmul(ps_s[0:1, h * HF:(h + 1) * HF], ones,
                                          sq[:, (q % 2) * TC + h * HF:(q % 2) * TC + (h + 1) * HF],
                                          start=(q == 0), stop=(q == KC - 1))
                    ctx.mm.inc(m, 1)

        m_row = rows[0:1, 0:TC]
        var_row = rows[0:1, TC:2 * TC]
        rstd_row = rows[0:1, 2 * TC:3 * TC]
        mrs_row = rows[0:1, 3 * TC:4 * TC]
        sum_m = rstd_row if carry_in else ps_m[0:1, :]
        sum_s = mrs_row if carry_in else ps_s[0:1, :]

        @blk.vector
        def _(vector):
            if carry_in:
                vector.wait_ge(ctx.ds.h, dsl0 + 16)
            else:
                vector.wait_ge(ctx.mm.h, mm0 + KC)
            vector.tensor_scalar(out=m_row, in0=sum_m, scalar1=1.0 / D,
                                 scalar2=None, op0=ALU.mult)
            vector.tensor_scalar(out=var_row, in0=sum_s, scalar1=1.0 / D,
                                 scalar2=None, op0=ALU.mult)
            vector.tensor_tensor(out=rstd_row, in0=m_row, in1=m_row, op=ALU.mult)
            ctx.ev.inc(vector.tensor_tensor(out=var_row, in0=var_row, in1=rstd_row,
                                            op=ALU.subtract), 1)

        @blk.scalar
        def _(scalar):
            ctx.ev.wait(scalar)
            scalar.activation(out=rstd_row, in_=var_row, func=AF.Ln, bias=ctx.bufs["EPSCOL"][0:1, 0:1])
            ctx.ea.inc(scalar.activation(out=rstd_row, in_=rstd_row, func=AF.Exp,
                                         scale=-0.5), 1)

        @blk.vector
        def _(vector):
            ctx.ea.wait(vector)
            ctx.ev.inc(vector.tensor_tensor(out=mrs_row, in0=m_row, in1=rstd_row,
                                            op=ALU.mult), 1)

        @blk.tensor
        def _(tensor):
            ctx.ev.wait(tensor)
            onesrow = ctx.bufs["ONESROW"]
            for h in range(2):
                tensor.matmul(ps_b0[:, h * HF:(h + 1) * HF], onesrow,
                              rstd_row[:, h * HF:(h + 1) * HF], start=True, stop=True)
            for h in range(2):
                m = tensor.matmul(ps_b1[:, h * HF:(h + 1) * HF], onesrow,
                                  mrs_row[:, h * HF:(h + 1) * HF], start=True, stop=True)
            ctx.mm.inc(m, 1)

        @blk.scalar
        def _(scalar):
            ctx.mm.wait(scalar)
            scalar.copy(out=rstd_s[:], in_=ps_b0[:])
            ctx.ea.inc(scalar.copy(out=mrs_s[:], in_=ps_b1[:]), 1)

        evD0 = ctx.ev.n
        epG0 = ctx.ep.n
        dsc0 = ctx.dsc.n
        ds0 = ctx.ds.n
        if dst_mode == "residual":
            @blk.scalar
            def _(scalar):
                ctx.dg.wait(scalar)
                for q in range(SPL):
                    if q >= 2:
                        scalar.wait_ge(ctx.ev.h, evD0 + (q - 1))
                    ctx.dsc.inc(scalar.dma_start(
                        out=xoldb[:, (q % 2) * TC:(q % 2 + 1) * TC],
                        in_=xold_dram.ap()[:, q * TC:(q + 1) * TC]), 16)

            @blk.sync
            def _(sync):
                ctx.dg.wait(sync)
                for q in range(SPL, KC):
                    j = q - SPL
                    if j >= 2:
                        sync.wait_ge(ctx.ep.h, epG0 + (j - 1))
                    ctx.ds.inc(sync.dma_start(
                        out=xoldb[:, (2 + q % 2) * TC:(2 + q % 2 + 1) * TC],
                        in_=xold_dram.ap()[:, q * TC:(q + 1) * TC]), 16)

        @blk.vector
        def _(vector):
            ctx.ea.wait(vector)
            for q in range(SPL):
                if dst_mode == "xln":
                    vector.tensor_tensor(out=tsc[:], in0=src_q(q), in1=rstd_s[:],
                                         op=ALU.mult)
                    ctx.ev.inc(vector.tensor_tensor(
                        out=ctx.xln[:, q * TC:(q + 1) * TC],
                        in0=tsc[:], in1=mrs_s[:], op=ALU.subtract), 1)
                else:
                    vector.wait_ge(ctx.dsc.h, dsc0 + 16 * (q + 1))
                    vector.tensor_tensor(out=tsc[:], in0=src_q(q), in1=rstd_s[:],
                                         op=ALU.mult)
                    vector.tensor_tensor(out=tsc[:], in0=tsc[:], in1=mrs_s[:],
                                         op=ALU.subtract)
                    ctx.ev.inc(vector.tensor_tensor(
                        out=ctx.xT[:, q * TC:(q + 1) * TC],
                        in0=tsc[:], in1=xoldb[:, (q % 2) * TC:(q % 2 + 1) * TC],
                        op=ALU.add), 1)

        @blk.gpsimd
        def _(g):
            ctx.ea.wait(g)
            for q in range(SPL, KC):
                j = q - SPL
                if dst_mode == "xln":
                    g.tensor_tensor(out=tsc2[:], in0=src_q(q), in1=rstd_s[:],
                                    op=ALU.mult)
                    ctx.ep.inc(g.tensor_tensor(
                        out=ctx.xln[:, q * TC:(q + 1) * TC],
                        in0=tsc2[:], in1=mrs_s[:], op=ALU.subtract), 1)
                else:
                    g.wait_ge(ctx.ds.h, ds0 + 16 * (j + 1))
                    g.tensor_tensor(out=tsc2[:], in0=src_q(q), in1=rstd_s[:],
                                    op=ALU.mult)
                    g.tensor_tensor(out=tsc2[:], in0=tsc2[:], in1=mrs_s[:],
                                    op=ALU.subtract)
                    ctx.ep.inc(g.tensor_tensor(
                        out=ctx.xT[:, q * TC:(q + 1) * TC],
                        in0=tsc2[:], in1=xoldb[:, (2 + q % 2) * TC:(2 + q % 2 + 1) * TC],
                        op=ALU.add), 1)

        if carry_out:
            mmC0, eaC0 = ctx.mm.n, ctx.ea.n

            @blk.scalar
            def _(scalar):
                for q in range(KC):
                    if q >= 2:
                        scalar.wait_ge(ctx.mm.h, mmC0 + (q - 1))
                    if q < SPL:
                        scalar.wait_ge(ctx.ev.h, evD0 + q + 1)
                    else:
                        scalar.wait_ge(ctx.ep.h, epG0 + (q - SPL) + 1)
                    ctx.ea.inc(scalar.square(out=sq[:, (q % 2) * TC:(q % 2 + 1) * TC],
                                             in_=ctx.xT[:, q * TC:(q + 1) * TC]), 1)

            @blk.tensor
            def _(tensor):
                ones = ctx.bufs["ONES1"]
                for q in range(KC):
                    tensor.wait_ge(ctx.ea.h, eaC0 + q + 1)
                    for h in range(2):
                        tensor.matmul(ps_m[0:1, h * HF:(h + 1) * HF], ones,
                                      ctx.xT[:, q * TC + h * HF:q * TC + (h + 1) * HF],
                                      start=(q == 0), stop=(q == KC - 1))
                    for h in range(2):
                        m = tensor.matmul(ps_s[0:1, h * HF:(h + 1) * HF], ones,
                                          sq[:, (q % 2) * TC + h * HF:(q % 2) * TC + (h + 1) * HF],
                                          start=(q == 0), stop=(q == KC - 1))
                    ctx.mm.inc(m, 1)

            @blk.scalar
            def _(scalar):
                scalar.wait_ge(ctx.mm.h, mmC0 + KC)
                scalar.copy(out=m_row, in_=ps_m[0:1, :])
                ctx.ea.inc(scalar.copy(out=var_row, in_=ps_s[0:1, :]), 1)

            @blk.sync
            def _(sync):
                ctx.ea.wait(sync)
                ctx.ds.inc(sync.dma_start(out=ctx.carryd.ap(),
                                          in_=rows[0:1, 0:2 * TC]), 16)
                ctx.ds.wait(sync)

    if dst_mode == "residual":
        ctx.carry_ready = carry_out
    ctx.bar()


def emit_spill_x(ctx):
    blk = ctx.block

    @blk.gpsimd
    def _(g):
        ctx.dg.inc(g.dma_start(out=ctx.xspill.ap(), in_=ctx.xT[:, :]), 16)



def emit_l0_ffn(ctx, W):
    nc, blk = ctx.nc, ctx.block
    ctx.fresh_dma_sems("l0f")
    E, R = 8, 64
    DBG = int(os.environ.get("K_DBG_PHASES", "99"))

    esx = contextlib.ExitStack()
    ctx.xln = esx.enter_context(nc.sbuf_tensor(_nm("xln"), [P, KC * TC], BF16))
    xln = ctx.xln
    emit_ln(ctx, lambda q: ctx.xT[:, q * TC:(q + 1) * TC], "xln")
    emit_spill_x(ctx)

    with (
        nc.sbuf_tensor(_nm("f0_wrt"), [P, E * KC], BF16) as wrt,
        nc.sbuf_tensor(_nm("f0_rbb"), [P, E], F32) as rbb,
        nc.sbuf_tensor(_nm("f0_at"), [P, KC * 512], BF16) as at,
        nc.sbuf_tensor(_nm("f0_bt"), [P, 8 * 1024], BF16) as bt,
        nc.sbuf_tensor(_nm("f0_g0t"), [P, NT * P], BF16) as g0t,
        nc.sbuf_tensor(_nm("f0_selq"), [E, 4 * P], BF16) as selq,
        nc.sbuf_tensor(_nm("f0_hsb"), [P, 4 * TC], F32) as hsb,
        nc.sbuf_tensor(_nm("f0_hs"), [P, 4 * TC], BF16) as hs,
    ):
        if DBG < 2:
            return
        selq_np = np.zeros((E, 4 * P), dtype=np.float32)
        for qq in range(4):
            selq_np[2 * qq, qq * P:qq * P + 64] = 1.0
            selq_np[2 * qq + 1, qq * P + 64:(qq + 1) * P] = 1.0
        c_selq = nc.inline_tensor(_np_bf(selq_np), _nm("c_selq"))

        with (
            nc.sbuf_tensor(_nm("f0_stage"), [P, 4 * D], BF16) as stg,
            nc.sbuf_tensor(_nm("f0_wrst"), [E, D], F32) as wrst,
            nc.sbuf_tensor(_nm("f0_brrow"), [1, E], F32) as brrow,
            nc.psum_tensor(_nm("f0_psr0"), [P, E], F32) as ps_r0,
            nc.psum_tensor(_nm("f0_psr1"), [P, E], F32) as ps_r1,
            nc.psum_tensor(_nm("f0_psrb"), [P, E], F32) as ps_rb,
        ):
            ps_rp = [ps_r0, ps_r1]
            ds0, dg0, mm0, ea0 = ctx.ds.n, ctx.dg.n, ctx.mm.n, ctx.ea.n

            @blk.sync
            def _(sync):
                ctx.ds.inc(sync.dma_start(out=selq[:], in_=c_selq.ap()), 16)
                ctx.ds.inc(sync.dma_start(out=wrst[:], in_=W["L0_ffn_Wr"]), 16)
                ctx.ds.inc(sync.dma_start(out=brrow[0:1, :],
                                          in_=W["L0_ffn_br"].rearrange("(o e) -> o e", o=1)), 16)

            @blk.gpsimd
            def _(g):
                av = W["L0_ffn_A"].rearrange("e r d -> (e r) d")
                for a in range(4):
                    ctx.dg.inc(g.dma_start(out=stg[:, a * D:(a + 1) * D],
                                           in_=av[a * P:(a + 1) * P, :]), 16)

            @blk.tensor
            def _(tensor):
                ctx.ds.wait(tensor)
                for k in range(KC):
                    if k >= 2:
                        tensor.wait_ge(ctx.ea.h, ea0 + k - 1)
                    ctx.mm.inc(tensor.transpose(
                        ps_rp[k % 2][:],
                        wrst[:, k * P:(k + 1) * P],
                        ctx.bufs["ID_F32"][0:E, 0:E]), 1)
                m = tensor.matmul(ps_rb[:], ctx.bufs["ONESROW"],
                                  brrow[0:1, :], start=True, stop=True)
                ctx.mm.inc(m, 1)

            @blk.scalar
            def _(scalar):
                for k in range(KC):
                    scalar.wait_ge(ctx.mm.h, mm0 + k + 1)
                    ctx.ea.inc(scalar.copy(out=wrt[:, k * E:(k + 1) * E],
                                           in_=ps_rp[k % 2][:]), 1)
                scalar.wait_ge(ctx.mm.h, mm0 + KC + 1)
                ctx.ea.inc(scalar.copy(out=rbb[:], in_=ps_rb[:]), 1)

            @blk.sync
            def _(sync):
                ctx.dg.wait(sync)
                for a in range(4):
                    for k in range(KC):
                        ctx.ds.inc(sync.dma_start_transpose(
                            out=at[:, k * 512 + a * P: k * 512 + (a + 1) * P],
                            in_=stg[:, a * D + k * P: a * D + (k + 1) * P]), 16)

            ctx.bar()

            @blk.gpsimd
            def _(g):
                stgv = stg[:, 0:8192].rearrange("p (a e r) -> p a e r", a=KC, e=E, r=64)
                for e in range(E):
                    bv = W["L0_ffn_B"][e].rearrange("(a p) r -> p a r", p=P)
                    ctx.dg.inc(g.dma_start(out=stgv[:, :, e, :], in_=bv), 16)

            @blk.sync
            def _(sync):
                ctx.dg.wait(sync)
                for a in range(KC):
                    for e2 in range(4):
                        blk_i = a * 4 + e2
                        ctx.ds.inc(sync.dma_start_transpose(
                            out=bt[:, blk_i * P:(blk_i + 1) * P],
                            in_=stg[:, blk_i * P:(blk_i + 1) * P]), 16)

            ctx.bar()

        if DBG < 3:
            return
        with (
            nc.sbuf_tensor(_nm("f0_L"), [P, 3 * E], F32) as Lb,
            nc.sbuf_tensor(_nm("f0_cols"), [P, 4], F32) as cols,
            nc.sbuf_tensor(_nm("f0_gbf"), [P, P], BF16) as gbf,
            nc.psum_tensor(_nm("f0_pslg"), [P, E], F32) as ps_lg,
        ):
            L = Lb[:, 0:E]
            Ee = Lb[:, E:2 * E]
            mask = Lb[:, 2 * E:3 * E]
            nmx = cols[:, 0:1]
            zz = cols[:, 1:2]
            rz = cols[:, 2:3]
            ds_r0 = ctx.ds.n

            @blk.gpsimd
            def _(g):
                ctx.ep.inc(g.memset(gbf[:, :], 0.0), 1)

            for t in range(NT):
                @blk.tensor
                def _(tensor, t=t):
                    if t == 0:
                        ctx.ea.wait(tensor)
                    else:
                        tensor.wait_ge(ctx.ep.h, ctx.ep.n)
                    m = None
                    for k in range(KC):
                        m = tensor.matmul(ps_lg[:, :],
                                          xln[:, k * TC + t * P: k * TC + (t + 1) * P],
                                          wrt[:, k * E:(k + 1) * E],
                                          start=(k == 0), stop=(k == KC - 1))
                    ctx.mm.inc(m, 1)

                @blk.scalar
                def _(scalar, t=t):
                    ctx.mm.wait(scalar)
                    ctx.ea.inc(scalar.copy(out=L, in_=ps_lg[:, :]), 1)

                @blk.gpsimd
                def _(g, t=t):
                    ctx.ea.wait(g)
                    ctx.ep.inc(g.tensor_tensor(out=L, in0=L, in1=rbb[:], op=ALU.add), 1)

                @blk.vector
                def _(vector, t=t):
                    ctx.ep.wait(vector)
                    ctx.ev.inc(vector.reduce_max(out=nmx, in_=L, axis=AX.X,
                                                 negate=True), 1)

                @blk.scalar
                def _(scalar, t=t):
                    ctx.ev.wait(scalar)
                    ctx.ea.inc(scalar.activation(out=Ee, in_=L, func=AF.Exp, bias=nmx), 1)

                @blk.vector
                def _(vector, t=t):
                    ctx.ea.wait(vector)
                    ctx.ev.inc(vector.reduce_sum(out=zz, in_=Ee, axis=AX.X), 1)

                @blk.scalar
                def _(scalar, t=t):
                    ctx.ev.wait(scalar)
                    scalar.activation(out=rz, in_=zz, func=AF.Ln)
                    ctx.ea.inc(scalar.activation(out=rz, in_=rz, func=AF.Exp,
                                                 scale=-1.0), 1)

                @blk.gpsimd
                def _(g, t=t):
                    ctx.ea.wait(g)
                    if t >= 1:
                        g.wait_ge(ctx.ds.h, ds_r0 + 16 * t)
                    g.tensor_scalar(out=mask, in0=L, scalar1=nmx, scalar2=0.0,
                                    op0=ALU.add, op1=ALU.is_equal)
                    ctx.ep.inc(g.tensor_scalar(out=gbf[:, 0:E], in0=mask, scalar1=rz,
                                               scalar2=None, op0=ALU.mult), 1)

                @blk.sync
                def _(sync, t=t):
                    ctx.ep.wait(sync)
                    ctx.ds.inc(sync.dma_start_transpose(out=g0t[:, t * P:(t + 1) * P],
                                                        in_=gbf[:, :]), 16)

            ctx.bar()
            if os.environ.get("K_DBG_DUMP"):
                d_L = nc.dram_tensor("dbg_L", [P, 3 * E], F32, kind="ExternalOutput").ap()
                d_cols = nc.dram_tensor("dbg_cols", [P, 3], F32, kind="ExternalOutput").ap()
                d_gbf = nc.dram_tensor("dbg_gbf", [P, P], F32, kind="ExternalOutput").ap()
                with nc.sbuf_tensor(_nm("dbg2"), [P, P], F32) as dbg2:
                    @blk.vector
                    def _(v):
                        v.tensor_copy(out=dbg2[:, 0:P], in_=gbf[:, :])
                        ctx.ev.inc(v.tensor_copy(out=dbg2[:, 0:P], in_=gbf[:, :]), 1)

                    @blk.gpsimd
                    def _(g):
                        ctx.ev.wait(g)
                        ctx.dg.inc(g.dma_start(out=d_gbf, in_=dbg2[:, 0:P]), 16)
                        ctx.dg.inc(g.dma_start(out=d_L, in_=Lb[:, :]), 16)
                        ctx.dg.inc(g.dma_start(out=d_cols, in_=cols[:, 0:3]), 16)
                    ctx.bar()

        if DBG < 4:
            return
        with nc.psum_tensor(_nm("f0_ph0"), [P, TC], F32) as ph0, \
             nc.psum_tensor(_nm("f0_ph1"), [P, TC], F32) as ph1, \
             nc.psum_tensor(_nm("f0_ph2"), [P, TC], F32) as ph2, \
             nc.psum_tensor(_nm("f0_ph3"), [P, TC], F32) as ph3:
            ps_h = [ph0, ph1, ph2, ph3]
            mmh = ctx.mm.n

            @blk.tensor
            def _(tensor):
                ctx.ds.wait(tensor)
                for rc in range(4):
                    m = None
                    for k in range(KC):
                        for h in range(2):
                            m = tensor.matmul(ps_h[rc][:, h * HF:(h + 1) * HF],
                                              at[:, k * 512 + rc * P:k * 512 + (rc + 1) * P],
                                              xln[:, k * TC + h * HF:k * TC + (h + 1) * HF],
                                              start=(k == 0), stop=(k == KC - 1))
                    ctx.mm.inc(m, 1)

            @blk.scalar
            def _(scalar):
                for rc in range(4):
                    scalar.wait_ge(ctx.mm.h, mmh + rc + 1)
                    ctx.ea.inc(scalar.copy(out=hsb[:, rc * TC:(rc + 1) * TC],
                                           in_=ps_h[rc][:]), 1)

            eah = ctx.ea.n
            mmg = ctx.mm.n

            @blk.tensor
            def _(tensor):
                for rc in range(4):
                    tensor.wait_ge(ctx.ea.h, eah - 4 + rc + 1)
                    m = None
                    for h in range(2):
                        m = tensor.matmul(ps_h[rc][:, h * HF:(h + 1) * HF],
                                          selq[:, rc * P:(rc + 1) * P],
                                          g0t[0:E, h * HF:(h + 1) * HF], start=True, stop=True)
                    ctx.mm.inc(m, 1)

            @blk.vector
            def _(vector):
                for rc in range(4):
                    vector.wait_ge(ctx.mm.h, mmg + rc + 1)
                    ctx.ev.inc(vector.tensor_tensor(out=hs[:, rc * TC:(rc + 1) * TC],
                                                    in0=hsb[:, rc * TC:(rc + 1) * TC],
                                                    in1=ps_h[rc][:], op=ALU.mult), 1)

            ctx.bar()

        if DBG < 5:
            return
        with nc.psum_tensor(_nm("f0_py0"), [P, TC], F32) as py0, \
             nc.psum_tensor(_nm("f0_py1"), [P, TC], F32) as py1:
            ps_y = [py0, py1]
            mmy = ctx.mm.n
            eay = ctx.ea.n

            @blk.tensor
            def _(tensor):
                for oc in range(KC):
                    sl = oc % 2
                    if oc >= 2:
                        tensor.wait_ge(ctx.ea.h, eay + oc - 1)
                    m = None
                    for e2 in range(4):
                        blk_i = oc * 4 + e2
                        for h in range(2):
                            m = tensor.matmul(ps_y[sl][:, h * HF:(h + 1) * HF],
                                              bt[:, blk_i * P:(blk_i + 1) * P],
                                              hs[:, e2 * TC + h * HF:e2 * TC + (h + 1) * HF],
                                              start=(e2 == 0), stop=(e2 == 3))
                    ctx.mm.inc(m, 1)

            @blk.scalar
            def _(scalar):
                ctx.dg.wait(scalar)
                for oc in range(KC):
                    sl = oc % 2
                    scalar.wait_ge(ctx.mm.h, mmy + oc + 1)
                    ctx.ea.inc(scalar.activation(out=ctx.xT[:, oc * TC:(oc + 1) * TC],
                                                 in_=ps_y[sl][:], func=AF.Copy,
                                                 scale=1.0 / R), 1)

            ctx.bar()

        if os.environ.get("K_DBG_DUMP"):
            d_bt = nc.dram_tensor("dbg_bt", [P, 8 * 1024], F32, kind="ExternalOutput").ap()
            d_g0t = nc.dram_tensor("dbg_g0t", [P, NT * P], F32, kind="ExternalOutput").ap()
            d_hs = nc.dram_tensor("dbg_hs", [P, 4 * TC], F32, kind="ExternalOutput").ap()
            d_y = nc.dram_tensor("dbg_y", [P, KC * TC], F32, kind="ExternalOutput").ap()
            with nc.sbuf_tensor(_nm("dbgf"), [P, 8 * 1024], F32) as dbgf:
                @blk.vector
                def _(v):
                    ctx.ev.inc(v.tensor_copy(out=dbgf[:, 0:8 * 1024], in_=bt[:]), 1)

                @blk.gpsimd
                def _(g):
                    ctx.ev.wait(g)
                    ctx.dg.inc(g.dma_start(out=d_bt, in_=dbgf[:, 0:8 * 1024]), 16)
                ctx.bar()

                @blk.vector
                def _(v):
                    ctx.ev.inc(v.tensor_copy(out=dbgf[:, 0:NT * P], in_=g0t[:]), 1)

                @blk.gpsimd
                def _(g):
                    ctx.ev.wait(g)
                    ctx.dg.inc(g.dma_start(out=d_g0t, in_=dbgf[:, 0:NT * P]), 16)
                ctx.bar()

                @blk.vector
                def _(v):
                    ctx.ev.inc(v.tensor_copy(out=dbgf[:, 0:4 * TC], in_=hs[:]), 1)

                @blk.gpsimd
                def _(g):
                    ctx.ev.wait(g)
                    ctx.dg.inc(g.dma_start(out=d_hs, in_=dbgf[:, 0:4 * TC]), 16)
                ctx.bar()

                @blk.gpsimd
                def _(g):
                    ctx.dg.inc(g.dma_start(out=d_y, in_=ctx.xT[:, :]), 16)
                ctx.bar()

    esx.close()
    if DBG >= 6:
        emit_ln(ctx, lambda q: ctx.xT[:, q * TC:(q + 1) * TC], "residual",
                xold_dram=ctx.xspill)



BLOCK_INPUTS = {
    "l0a": ["L0_attn_log_decay", "L0_attn_Wq", "L0_attn_Wk", "L0_attn_Wv",
            "L0_attn_Wo", "L0_attn_Wg1", "L0_attn_Wg2", "L0_attn_norm_w"],
    "l0f": ["L0_ffn_A", "L0_ffn_B", "L0_ffn_Wr", "L0_ffn_br"],
    "l1a": ["L1_attn_w_hg", "L1_attn_w_out"],
    "l1f": ["L1_ffn_router_w", "L1_ffn_router_bias", "L1_ffn_Wu", "L1_ffn_Wg",
            "L1_ffn_Wd", "L1_ffn_Su", "L1_ffn_Sg", "L1_ffn_Sd"],
}

INPUT_SHAPES = {
    "L0_attn_log_decay": [16], "L0_attn_Wq": [D, D], "L0_attn_Wk": [D, D],
    "L0_attn_Wv": [D, D], "L0_attn_Wo": [D, D], "L0_attn_Wg1": [D, 128],
    "L0_attn_Wg2": [128, D], "L0_attn_norm_w": [D],
    "L0_ffn_A": [8, 64, D], "L0_ffn_B": [8, D, 64], "L0_ffn_Wr": [8, D], "L0_ffn_br": [8],
    "L1_attn_w_hg": [D, 6144], "L1_attn_w_out": [3072, D],
    "L1_ffn_router_w": [16, D], "L1_ffn_router_bias": [16],
    "L1_ffn_Wu": [16, D, 1024], "L1_ffn_Wg": [16, D, 1024], "L1_ffn_Wd": [16, 1024, D],
    "L1_ffn_Su": [1, D, D], "L1_ffn_Sg": [1, D, D], "L1_ffn_Sd": [1, D, D],
}

_PROGRAM_CACHE = {}


def build_program(blocks):
    key = tuple(blocks)
    if key in _PROGRAM_CACHE:
        return _PROGRAM_CACHE[key]
    nc = bass.Bass("TRN2", target_bir_lowering=False, detect_race_conditions=False)
    xin = nc.dram_tensor("xin", [TC, D], F32, kind="ExternalInput").ap()
    xout = nc.dram_tensor("xout", [TC, D], F32, kind="ExternalOutput").ap()
    W = {}
    names = []
    for b in blocks:
        for nmn in BLOCK_INPUTS[b]:
            W[nmn] = nc.dram_tensor(nmn, INPUT_SHAPES[nmn], F32, kind="ExternalInput").ap()
            names.append(nmn)
    if "l0a" in blocks:
        W["msel"] = nc.dram_tensor("msel", [8, 1], F32, kind="ExternalInput").ap()
        names.append("msel")
    if "l1a" in blocks:
        W["tmat"] = nc.dram_tensor("tmat", [8, 8], F32, kind="ExternalInput").ap()
        W["tbias"] = nc.dram_tensor("tbias", [8, 1], F32, kind="ExternalInput").ap()
        names += ["tmat", "tbias"]

    with nc.Block() as blk:
        ctx = Ctx(nc, blk)
        ctx.inl = {
            "ID_F32": nc.inline_tensor(np.eye(P, dtype=np.float32), "c_id_f32"),
            "ID_BF": nc.inline_tensor(_np_bf(np.eye(P)), "c_id_bf"),
            "ONES1": nc.inline_tensor(np.ones((P, 1), dtype=np.float32), "c_ones1"),
            "EPSCOL": nc.inline_tensor(np.full((P, 1), EPS, dtype=np.float32), "c_epscol"),
            "ONES1B": nc.inline_tensor(_np_bf(np.ones((P, 1))), "c_ones1b"),
            "ONESROW": nc.inline_tensor(np.ones((1, P), dtype=np.float32), "c_onesrow"),
        }
        consts = {
            "ID_F32": ([P, P], F32),
            "ID_BF": ([P, P], BF16),
            "ONES1": ([P, 1], F32),
            "EPSCOL": ([P, 1], F32),
            "ONES1B": ([P, 1], BF16),
            "ONESROW": ([1, P], F32),
        }
        for name, (shape, dtype) in consts.items():
            ctx.bufs[name] = nc.alloc_sbuf_tensor(_nm("sb_" + name), shape, dtype).ap()
        ctx.dummy = nc.alloc_sbuf_tensor("vgap_dummy", [P, 24], F32).ap()
        ctx.wbf = nc.alloc_sbuf_tensor("wbf", [P, NSW * 512], BF16).ap()
        ctx.carryd = nc.dram_tensor("ln_carry", [1, 2 * TC], F32)
        ctx.xT = nc.alloc_sbuf_tensor("xT", [P, KC * TC], F32).ap()
        ctx.xspill = nc.dram_tensor("xspill", [P, KC * TC], F32)
        emit_prologue(ctx, xin, consts)
        for i, b in enumerate(blocks):
            ctx.carry_next = i < len(blocks) - 1
            EMITTERS[b](ctx, W)
        emit_epilogue(ctx, xout)

    _PROGRAM_CACHE[key] = (nc, names)
    return _PROGRAM_CACHE[key]


def _percore_consts(core):
    r = core
    msel = np.array([[1024.0 * (r - 1 - j) if j < r else 1e9] for j in range(8)],
                    dtype=np.float32)
    tmat = np.zeros((8, 8), dtype=np.float32)
    for mrow in range(8):
        for j in range(8):
            if j < mrow < r:
                tmat[mrow, j] = 1.0
    tbias = np.array([[0.0 if j < r else -1e38] for j in range(8)], dtype=np.float32)
    return {"msel": msel, "tmat": tmat, "tbias": tbias}


def run_blocks(x_full, inputs, blocks):
    nc, names = build_program(blocks)
    in_maps = []
    for c in range(NCORE):
        m = {"xin": np.ascontiguousarray(x_full[c * TC:(c + 1) * TC])}
        pc = _percore_consts(c)
        for nmn in names:
            m[nmn] = pc[nmn] if nmn in pc else inputs[nmn]
        in_maps.append(m)
    res = run_bass_kernel_spmd(nc, in_maps, core_ids=list(range(NCORE)))
    return np.concatenate([res.results[c]["xout"] for c in range(NCORE)], axis=0)


ALL_BLOCKS = ["l0a", "l0f", "l1a", "l1f"]


def kernel(**inputs):
    inputs = {k: np.asarray(v, dtype=np.float32) for k, v in inputs.items()}
    x = np.ascontiguousarray(inputs["L0_attn_x"].reshape(8192, D))
    if os.environ.get("K_UNFUSED"):
        for blocks in (["l0a"], ["l0f"], ["l1a"], ["l1f"]):
            x = run_blocks(x, inputs, blocks)
    else:
        x = run_blocks(x, inputs, ALL_BLOCKS)
    return x.reshape(1, 8192, D)



def wstream(ctx, pieces, wtile, on_piece, pe_prewait=None, queue="gp"):
    nc, blk = ctx.nc, ctx.block
    NS = NSW
    mm0 = ctx.mm.n
    wr0 = [sm.n for sm in ctx.wr]

    def emit_dmas(eng):
        for n, ap in enumerate(pieces):
            sl = n % NS
            if ctx.wring_free[sl] > 0:
                eng.wait_ge(ctx.mm.h, ctx.wring_free[sl])
            ctx.wring_free[sl] = mm0 + n + 1
            eng.dma_start(out=ctx.wbf[:, sl * 512:sl * 512 + wtile], in_=ap).then_inc(
                ctx.wr[sl].h, 16)
            ctx.wr[sl].n += 16

    if queue == "gp":
        @blk.gpsimd
        def _(g):
            emit_dmas(g)
    else:
        @blk.sync
        def _(sy):
            emit_dmas(sy)

    @blk.tensor
    def _(tensor):
        for n in range(len(pieces)):
            sl = n % NS
            tensor.wait_ge(ctx.wr[sl].h, wr0[sl] + 16 * (n // NS + 1))
            if pe_prewait is not None:
                pe_prewait(tensor, n)
            last = on_piece(tensor, n, ctx.wbf[:, sl * 512:sl * 512 + wtile])
            ctx.mm.inc(last, 1)


def emit_l1_ffn(ctx, W):
    nc, blk = ctx.nc, ctx.block
    ctx.fresh_dma_sems("l1f")
    E = 16
    NEXP = int(os.environ.get("K_DBG_L1F_NEXP", "16"))
    NOSHARED = bool(os.environ.get("K_DBG_L1F_NOSHARED", ""))
    NOPOSTLN = bool(os.environ.get("K_DBG_L1F_NOPOSTLN", ""))
    Wu, Wg, Wd = W["L1_ffn_Wu"], W["L1_ffn_Wg"], W["L1_ffn_Wd"]
    Su, Sg, Sd = W["L1_ffn_Su"], W["L1_ffn_Sg"], W["L1_ffn_Sd"]

    esx = contextlib.ExitStack()
    ctx.xln = esx.enter_context(nc.sbuf_tensor(_nm("xln"), [P, KC * TC], BF16))
    xln = ctx.xln
    emit_ln(ctx, lambda q: ctx.xT[:, q * TC:(q + 1) * TC], "xln")
    emit_spill_x(ctx)

    onehot_np = np.zeros((E, E * P), dtype=np.float32)
    for e in range(E):
        onehot_np[e, e * P:(e + 1) * P] = 1.0
    c_oh = nc.inline_tensor(_np_bf(onehot_np), _nm("c_oh16"))

    with (
        nc.sbuf_tensor(_nm("f1_rwt"), [P, E * KC], BF16) as rwt,
        nc.sbuf_tensor(_nm("f1_rbb"), [P, E], F32) as rbb,
        nc.sbuf_tensor(_nm("f1_oh"), [E, E * P], BF16) as oh,
        nc.sbuf_tensor(_nm("f1_gt"), [P, NT * P], BF16) as gt,
        nc.sbuf_tensor(_nm("f1_usb"), [P, NT * TC], BF16) as usb,
        nc.sbuf_tensor(_nm("f1_gsb"), [P, NT * TC], BF16) as gsb,
        nc.sbuf_tensor(_nm("f1_gbs"), [P, TC], BF16) as gbs,
        nc.sbuf_tensor(_nm("f1_ssb"), [P, KC * TC], BF16) as ssb,
    ):
        with (
            nc.sbuf_tensor(_nm("f1_rwst"), [E, D], F32) as rwst,
            nc.sbuf_tensor(_nm("f1_brrow"), [1, E], F32) as brrow,
            nc.psum_tensor(_nm("f1_psr0"), [P, E], F32) as ps_r0,
            nc.psum_tensor(_nm("f1_psr1"), [P, E], F32) as ps_r1,
            nc.psum_tensor(_nm("f1_psrb"), [P, E], F32) as ps_rb,
        ):
            ps_rp = [ps_r0, ps_r1]
            mm0, ea0 = ctx.mm.n, ctx.ea.n

            @blk.sync
            def _(sync):
                ctx.ds.inc(sync.dma_start(out=oh[:], in_=c_oh.ap()), 16)
                ctx.ds.inc(sync.dma_start(out=rwst[:], in_=W["L1_ffn_router_w"]), 16)
                ctx.ds.inc(sync.dma_start(
                    out=brrow[0:1, :],
                    in_=W["L1_ffn_router_bias"].rearrange("(o e) -> o e", o=1)), 16)

            @blk.tensor
            def _(tensor):
                ctx.ds.wait(tensor)
                for k in range(KC):
                    if k >= 2:
                        tensor.wait_ge(ctx.ea.h, ea0 + k - 1)
                    ctx.mm.inc(tensor.transpose(
                        ps_rp[k % 2][:], rwst[:, k * P:(k + 1) * P],
                        ctx.bufs["ID_F32"][0:E, 0:E]), 1)
                m = tensor.matmul(ps_rb[:], ctx.bufs["ONESROW"], brrow[0:1, :],
                                  start=True, stop=True)
                ctx.mm.inc(m, 1)

            @blk.scalar
            def _(scalar):
                for k in range(KC):
                    scalar.wait_ge(ctx.mm.h, mm0 + k + 1)
                    ctx.ea.inc(scalar.copy(out=rwt[:, k * E:(k + 1) * E],
                                           in_=ps_rp[k % 2][:]), 1)
                scalar.wait_ge(ctx.mm.h, mm0 + KC + 1)
                ctx.ea.inc(scalar.copy(out=rbb[:], in_=ps_rb[:]), 1)

            ctx.bar()

        with (
            nc.sbuf_tensor(_nm("f1_L"), [P, 8 * E], F32) as Lb,
            nc.sbuf_tensor(_nm("f1_cols"), [P, 8], F32) as cols,
            nc.sbuf_tensor(_nm("f1_cbf"), [P, P], BF16) as cbf,
            nc.psum_tensor(_nm("f1_pslg"), [P, E], F32) as ps_lg,
        ):
            L = Lb[:, 0:E]
            LB = Lb[:, E:2 * E]
            m1 = Lb[:, 2 * E:3 * E]
            m2 = Lb[:, 3 * E:4 * E]
            Ex = Lb[:, 4 * E:5 * E]
            wrk = Lb[:, 5 * E:6 * E]
            w1b = Lb[:, 6 * E:7 * E]
            w2b = Lb[:, 7 * E:8 * E]
            nm1 = cols[:, 0:1]
            nm2 = cols[:, 1:2]
            nmL = cols[:, 2:3]
            aa = cols[:, 3:4]
            bb2 = cols[:, 4:5]
            rs = cols[:, 5:6]
            t1c = cols[:, 6:7]
            t2c = cols[:, 7:8]
            ds_r0 = ctx.ds.n

            @blk.gpsimd
            def _(g):
                ctx.ep.inc(g.memset(cbf[:, :], 0.0), 1)

            for t in range(NT):
                @blk.tensor
                def _(tensor, t=t):
                    if t == 0:
                        ctx.ea.wait(tensor)
                    else:
                        tensor.wait_ge(ctx.ep.h, ctx.ep.n)
                    m = None
                    for k in range(KC):
                        m = tensor.matmul(ps_lg[:, :],
                                          xln[:, k * TC + t * P:k * TC + (t + 1) * P],
                                          rwt[:, k * E:(k + 1) * E],
                                          start=(k == 0), stop=(k == KC - 1))
                    ctx.mm.inc(m, 1)

                @blk.scalar
                def _(scalar, t=t):
                    ctx.mm.wait(scalar)
                    ctx.ea.inc(scalar.copy(out=L, in_=ps_lg[:, :]), 1)

                @blk.gpsimd
                def _(g, t=t):
                    ctx.ea.wait(g)
                    ctx.ep.inc(g.tensor_tensor(out=LB, in0=L, in1=rbb[:], op=ALU.add), 1)

                @blk.vector
                def _(vector, t=t):
                    ctx.ep.wait(vector)
                    vector.reduce_max(out=nm1, in_=LB, axis=AX.X, negate=True)
                    ctx.ev.inc(vector.reduce_max(out=nmL, in_=L, axis=AX.X,
                                                 negate=True), 1)

                @blk.gpsimd
                def _(g, t=t):
                    ctx.ev.wait(g)
                    g.tensor_scalar(out=m1, in0=LB, scalar1=nm1, scalar2=0.0,
                                    op0=ALU.add, op1=ALU.is_equal)
                    g.tensor_scalar(out=wrk, in0=m1, scalar1=-1e30, scalar2=None,
                                    op0=ALU.mult)
                    ctx.ep.inc(g.tensor_tensor(out=wrk, in0=wrk, in1=LB, op=ALU.add), 1)

                @blk.vector
                def _(vector, t=t):
                    ctx.ep.wait(vector)
                    ctx.ev.inc(vector.reduce_max(out=nm2, in_=wrk, axis=AX.X,
                                                 negate=True), 1)

                @blk.scalar
                def _(scalar, t=t):
                    ctx.ev.wait(scalar)
                    ctx.ea.inc(scalar.activation(out=Ex, in_=L, func=AF.Exp,
                                                 bias=nmL), 1)

                @blk.gpsimd
                def _(g, t=t):
                    ctx.ea.wait(g)
                    g.tensor_scalar(out=m2, in0=wrk, scalar1=nm2, scalar2=0.0,
                                    op0=ALU.add, op1=ALU.is_equal)
                    g.tensor_tensor(out=w1b, in0=m1, in1=Ex, op=ALU.mult)
                    ctx.ep.inc(g.tensor_tensor(out=w2b, in0=m2, in1=Ex, op=ALU.mult), 1)

                @blk.vector
                def _(vector, t=t):
                    ctx.ep.wait(vector)
                    vector.reduce_sum(out=aa, in_=w1b, axis=AX.X)
                    ctx.ev.inc(vector.reduce_sum(out=bb2, in_=w2b, axis=AX.X), 1)

                @blk.gpsimd
                def _(g, t=t):
                    ctx.ev.wait(g)
                    ctx.ep.inc(g.tensor_tensor(out=rs, in0=aa, in1=bb2, op=ALU.add), 1)

                @blk.scalar
                def _(scalar, t=t):
                    ctx.ep.wait(scalar)
                    scalar.activation(out=rs, in_=rs, func=AF.Ln)
                    ctx.ea.inc(scalar.activation(out=rs, in_=rs, func=AF.Exp,
                                                 scale=-1.0), 1)

                @blk.gpsimd
                def _(g, t=t):
                    ctx.ea.wait(g)
                    if t >= 1:
                        g.wait_ge(ctx.ds.h, ds_r0 + 16 * t)
                    g.tensor_tensor(out=t1c, in0=aa, in1=rs, op=ALU.mult)
                    g.tensor_tensor(out=t2c, in0=bb2, in1=rs, op=ALU.mult)
                    g.tensor_scalar(out=wrk, in0=m1, scalar1=t1c, scalar2=None,
                                    op0=ALU.mult)
                    g.tensor_scalar(out=w2b, in0=m2, scalar1=t2c, scalar2=None,
                                    op0=ALU.mult)
                    ctx.ep.inc(g.tensor_tensor(out=cbf[:, 0:E], in0=w2b, in1=wrk,
                                               op=ALU.add), 1)

                @blk.sync
                def _(sync, t=t):
                    ctx.ep.wait(sync)
                    ctx.ds.inc(sync.dma_start_transpose(out=gt[:, t * P:(t + 1) * P],
                                                        in_=cbf[:, :]), 16)

            ctx.bar()
            if os.environ.get("K_DBG_DUMP"):
                d_gt = nc.dram_tensor("dbg_gt", [P, NT * P], F32, kind="ExternalOutput").ap()
                d_L = nc.dram_tensor("dbg_L", [P, 6 * E], F32, kind="ExternalOutput").ap()
                d_cols = nc.dram_tensor("dbg_cols", [P, 8], F32, kind="ExternalOutput").ap()
                with nc.sbuf_tensor(_nm("dbg3"), [P, NT * P], F32) as dbg3:
                    @blk.vector
                    def _(v):
                        ctx.ev.inc(v.tensor_copy(out=dbg3[:, :], in_=gt[:, :]), 1)

                    @blk.gpsimd
                    def _(g):
                        ctx.ev.wait(g)
                        ctx.dg.inc(g.dma_start(out=d_gt, in_=dbg3[:, :]), 16)
                        ctx.dg.inc(g.dma_start(out=d_L, in_=Lb[:, :]), 16)
                        ctx.dg.inc(g.dma_start(out=d_cols, in_=cols[:, :]), 16)
                    ctx.bar()

        with (
            nc.psum_tensor(_nm("f1_pa"), [P, TC], F32) as pA,
            nc.psum_tensor(_nm("f1_pb"), [P, TC], F32) as pB,
            nc.psum_tensor(_nm("f1_pc"), [P, TC], F32) as pC,
            nc.psum_tensor(_nm("f1_pd"), [P, TC], F32) as pD,
        ):
            ps4 = [pA, pB, pC, pD]

            def u_or_g(mat_ap, dst_sb, act_fn):
                eaE = [ctx.ea.n]

                def prewait(tensor, n):
                    if n == 0:
                        tensor.wait_ge(ctx.ea.h, eaE[0])

                def onp(tensor, n, slot):
                    og = n // KC
                    k = n % KC
                    m = None
                    for oc in range(4):
                        if k == 0 and og >= 1:
                            tensor.wait_ge(ctx.ea.h, eaE[0] + (og - 1) * 4 + oc + 1)
                        for h in range(2):
                            m = tensor.matmul(
                                ps4[oc][:, h * HF:(h + 1) * HF],
                                slot[:, oc * P:(oc + 1) * P],
                                xln[:, k * TC + h * HF:k * TC + (h + 1) * HF],
                                start=(k == 0), stop=(k == KC - 1))
                    return m

                pieces = []
                for og in range(2):
                    for k in range(KC):
                        pieces.append(mat_ap[k * P:(k + 1) * P, og * 512:(og + 1) * 512])
                mmE = ctx.mm.n
                wstream(ctx, pieces, 512, onp, prewait)

                @blk.scalar
                def _(scalar):
                    for og in range(2):
                        scalar.wait_ge(ctx.mm.h, mmE + (og + 1) * KC)
                        for oc in range(4):
                            ctx.ea.inc(scalar.activation(
                                out=dst_sb[:, (og * 4 + oc) * TC:(og * 4 + oc + 1) * TC],
                                in_=ps4[oc][:], func=act_fn), 1)

            for e in range(NEXP):
                mmG = ctx.mm.n

                @blk.tensor
                def _(tensor, e=e):
                    if e == 0:
                        ctx.ea.wait(tensor)
                    else:
                        ctx.ev.wait(tensor)
                    m = None
                    for h in range(2):
                        m = tensor.matmul(pA[:, h * HF:(h + 1) * HF],
                                          oh[:, e * P:(e + 1) * P],
                                          gt[0:E, h * HF:(h + 1) * HF],
                                          start=True, stop=True)
                    ctx.mm.inc(m, 1)

                @blk.scalar
                def _(scalar, e=e):
                    ctx.mm.wait(scalar)
                    ctx.ea.inc(scalar.copy(out=gbs[:], in_=pA[:]), 1)

                u_or_g(Wu[e], usb, AF.Silu)
                u_or_g(Wg[e], gsb, AF.Copy)

                noh = bool(os.environ.get("K_DBG_L1F_NOH"))
                evh0 = ctx.ev.n

                @blk.vector
                def _(vector, e=e):
                    ctx.ea.wait(vector)
                    if noh:
                        ctx.ev.inc(vector.tensor_copy(out=ctx.dummy[:, 3:4],
                                                      in_=ctx.dummy[:, 0:1]), 1)
                        return
                    for rc in range(NT):
                        vector.tensor_tensor(out=usb[:, rc * TC:(rc + 1) * TC],
                                             in0=usb[:, rc * TC:(rc + 1) * TC],
                                             in1=gsb[:, rc * TC:(rc + 1) * TC],
                                             op=ALU.mult)
                        ctx.ev.inc(vector.tensor_tensor(
                            out=usb[:, rc * TC:(rc + 1) * TC],
                            in0=usb[:, rc * TC:(rc + 1) * TC],
                            in1=gbs[:], op=ALU.mult), 1)

                evY0 = ctx.ev.n

                def prewaity(tensor, n, evh0=evh0, evY0=evY0):
                    og2 = n // NT
                    rk = n % NT
                    if og2 == 0:
                        tensor.wait_ge(ctx.ev.h,
                                       evY0 if noh else evh0 + rk + 1)

                def onpy(tensor, n, slot, evY0=evY0):
                    og2 = n // NT
                    rk = n % NT
                    m = None
                    for oc in range(4):
                        if rk == 0 and og2 >= 1:
                            tensor.wait_ge(ctx.ev.h, evY0 + (og2 - 1) * 4 + oc + 1)
                        for h in range(2):
                            m = tensor.matmul(
                                ps4[oc][:, h * HF:(h + 1) * HF],
                                slot[:, oc * P:(oc + 1) * P],
                                usb[:, rk * TC + h * HF:rk * TC + (h + 1) * HF],
                                start=(rk == 0), stop=(rk == NT - 1))
                    return m

                piecesy = []
                for og2 in range(4):
                    for rk in range(NT):
                        piecesy.append(Wd[e][rk * P:(rk + 1) * P,
                                             og2 * 512:(og2 + 1) * 512])
                mmY = ctx.mm.n
                wstream(ctx, piecesy, 512, onpy, prewaity)

                @blk.vector
                def _(vector, e=e, mmY=mmY):
                    for og2 in range(4):
                        vector.wait_ge(ctx.mm.h, mmY + (og2 + 1) * NT)
                        for oc in range(4):
                            dst = ctx.xT[:, (og2 * 4 + oc) * TC:(og2 * 4 + oc + 1) * TC]
                            if e == 0:
                                ctx.ev.inc(vector.tensor_copy(out=dst,
                                                              in_=ps4[oc][:]), 1)
                            else:
                                ctx.ev.inc(vector.tensor_tensor(
                                    out=dst, in0=dst, in1=ps4[oc][:], op=ALU.add), 1)

            def s_u_or_g(mat_ap, act_fn, second):
                eaE = [ctx.ea.n]
                ev_last = ctx.ev.n

                def prewait(tensor, n):
                    if n == 0:
                        tensor.wait_ge(ctx.ea.h, eaE[0])
                        if not second:
                            tensor.wait_ge(ctx.ev.h, ev_last)
                def onp(tensor, n, slot):
                    og = n // KC
                    k = n % KC
                    m = None
                    for oc in range(4):
                        if k == 0 and og >= 1:
                            tensor.wait_ge(ctx.ea.h, eaE[0] + (og - 1) * 4 + oc + 1)
                        for h in range(2):
                            m = tensor.matmul(
                                ps4[oc][:, h * HF:(h + 1) * HF],
                                slot[:, oc * P:(oc + 1) * P],
                                xln[:, k * TC + h * HF:k * TC + (h + 1) * HF],
                                start=(k == 0), stop=(k == KC - 1))
                    return m

                pieces = []
                for og in range(4):
                    for k in range(KC):
                        pieces.append(mat_ap[k * P:(k + 1) * P, og * 512:(og + 1) * 512])
                mmE = ctx.mm.n
                wstream(ctx, pieces, 512, onp, prewait)

                @blk.scalar
                def _(scalar):
                    for og in range(4):
                        scalar.wait_ge(ctx.mm.h, mmE + (og + 1) * KC)
                        for oc in range(4):
                            occ = og * 4 + oc
                            if second:
                                dst = (usb if occ < NT else gsb)[
                                    :, (occ % NT) * TC:(occ % NT + 1) * TC]
                            else:
                                dst = ssb[:, occ * TC:(occ + 1) * TC]
                            ctx.ea.inc(scalar.activation(out=dst, in_=ps4[oc][:],
                                                         func=act_fn), 1)

            if not NOSHARED:
                s_u_or_g(Su[0], AF.Silu, False)
                s_u_or_g(Sg[0], AF.Copy, True)

            evSS0 = ctx.ev.n
            if not NOSHARED:
                @blk.vector
                def _(vector):
                    ctx.ea.wait(vector)
                    for rc in range(KC):
                        src2 = (usb if rc < NT else gsb)[:, (rc % NT) * TC:(rc % NT + 1) * TC]
                        ctx.ev.inc(vector.tensor_tensor(
                            out=ssb[:, rc * TC:(rc + 1) * TC],
                            in0=ssb[:, rc * TC:(rc + 1) * TC],
                            in1=src2, op=ALU.mult), 1)

            evY0s = ctx.ev.n

            def prewaitsy(tensor, n):
                og2 = n // KC
                rk = n % KC
                if og2 == 0:
                    tensor.wait_ge(ctx.ev.h, evSS0 + rk + 1)

            def onpsy(tensor, n, slot):
                og2 = n // KC
                rk = n % KC
                m = None
                for oc in range(4):
                    if rk == 0 and og2 >= 1:
                        tensor.wait_ge(ctx.ev.h, evY0s + (og2 - 1) * 4 + oc + 1)
                    for h in range(2):
                        m = tensor.matmul(
                            ps4[oc][:, h * HF:(h + 1) * HF],
                            slot[:, oc * P:(oc + 1) * P],
                            ssb[:, rk * TC + h * HF:rk * TC + (h + 1) * HF],
                            start=(rk == 0), stop=(rk == KC - 1))
                return m

            if not NOSHARED:
                piecessy = []
                for og2 in range(4):
                    for rk in range(KC):
                        piecessy.append(Sd[0][rk * P:(rk + 1) * P, og2 * 512:(og2 + 1) * 512])
                mmSY = ctx.mm.n
                wstream(ctx, piecessy, 512, onpsy, prewaitsy)

                @blk.vector
                def _(vector, mmSY=mmSY):
                    for og2 in range(4):
                        vector.wait_ge(ctx.mm.h, mmSY + (og2 + 1) * KC)
                        for oc in range(4):
                            dst = ctx.xT[:, (og2 * 4 + oc) * TC:(og2 * 4 + oc + 1) * TC]
                            ctx.ev.inc(vector.tensor_tensor(out=dst, in0=dst,
                                                            in1=ps4[oc][:], op=ALU.add), 1)

            ctx.bar()

    emit_ln(ctx, lambda q: ctx.xT[:, q * TC:(q + 1) * TC], "residual",
            xold_dram=ctx.xspill)


def emit_l1_attn(ctx, W):
    nc, blk = ctx.nc, ctx.block
    ctx.fresh_dma_sems("l1a")
    whg = W["L1_attn_w_hg"]
    wout = W["L1_attn_w_out"]
    NQ = 24

    esx = contextlib.ExitStack()
    ctx.xln = esx.enter_context(nc.sbuf_tensor(_nm("xln"), [P, KC * TC], BF16))
    xln = ctx.xln
    emit_ln(ctx, lambda q: ctx.xT[:, q * TC:(q + 1) * TC], "xln")
    emit_spill_x(ctx)

    pspill = nc.dram_tensor(_nm("pspill"), [P, NQ * TC], BF16)
    ccin = nc.dram_tensor(_nm("cc_h_in"), [2, 3072], F32)
    ccout = nc.dram_tensor(_nm("cc_h_out"), [16, 3072], F32)

    with (
        nc.sbuf_tensor(_nm("a1_h"), [P, NQ * TC], BF16) as hbuf,
        nc.sbuf_tensor(_nm("a1_pst"), [P, TC], BF16) as pst,
        nc.sbuf_tensor(_nm("a1_pst2"), [P, 2 * TC], BF16) as pst2,
        nc.sbuf_tensor(_nm("a1_s1"), [P, TC], F32) as s1,
        nc.sbuf_tensor(_nm("a1_s2"), [P, TC], F32) as s2,
        nc.sbuf_tensor(_nm("a1_az"), [P, 2 * TC], F32) as az,
        nc.sbuf_tensor(_nm("a1_zero"), [P, TC], F32) as zeros,
        nc.sbuf_tensor(_nm("a1_stg"), [P, 48], F32) as stg,
        nc.sbuf_tensor(_nm("a1_stgT"), [48, P], F32) as stgT,
        nc.sbuf_tensor(_nm("a1_he"), [8, 3072], BF16) as he,
        nc.sbuf_tensor(_nm("a1_pt"), [8, 3072], BF16) as pt8,
        nc.sbuf_tensor(_nm("a1_w6"), [8, 3072], BF16) as w6,
        nc.sbuf_tensor(_nm("a1_tm"), [8, 8], BF16) as tm,
        nc.sbuf_tensor(_nm("a1_tb"), [8, 1], F32) as tb,
        nc.sbuf_tensor(_nm("a1_hc"), [P, NQ], F32) as hcols,
        nc.psum_tensor(_nm("a1_p0"), [P, TC], F32) as pA,
        nc.psum_tensor(_nm("a1_p1"), [P, TC], F32) as pB,
        nc.psum_tensor(_nm("a1_p2"), [P, TC], F32) as pC,
        nc.psum_tensor(_nm("a1_p3"), [P, TC], F32) as pD,
    ):
        ps4 = [pA, pB, pC, pD]
        a_sb = az[:, 0:TC]

        @blk.vector
        def _(vector):
            ctx.ev.inc(vector.memset(zeros[:, :], 0.0), 1)

        @blk.gpsimd
        def _(g):
            ctx.dg.inc(g.dma_start(out=tm[:, :], in_=W["tmat"]), 16)

        @blk.sync
        def _(sync):
            ctx.ds.inc(sync.dma_start(out=tb[:, :], in_=W["tbias"]), 16)

        dg_spill0 = ctx.dg.n
        for pg in range(12):
            q0 = 2 * pg
            pieces = []
            for k in range(KC):
                pieces.append(whg[k * P:(k + 1) * P, q0 * P:(q0 + 2) * P])
                pieces.append(whg[k * P:(k + 1) * P, (24 + q0) * P:(24 + q0 + 2) * P])
            eaB = ctx.ea.n
            evB = ctx.ev.n

            def prewait(tensor, n, eaB=eaB, evB=evB):
                if n == 0:
                    tensor.wait_ge(ctx.ea.h, eaB)
                    tensor.wait_ge(ctx.ev.h, evB)

            def onp(tensor, n, slot, pg=pg):
                k = n // 2
                hid = (n % 2 == 0)
                m = None
                for j in range(2):
                    tgt = ps4[j] if hid else ps4[2 + j]
                    for h in range(2):
                        m = tensor.matmul(
                            tgt[:, h * HF:(h + 1) * HF],
                            slot[:, j * P:(j + 1) * P],
                            xln[:, k * TC + h * HF:k * TC + (h + 1) * HF],
                            start=(k == 0), stop=(k == KC - 1))
                return m

            wstream(ctx, pieces, 256, onp, prewait)

            for j in range(2):
                q = q0 + j
                hidp, gatep = ps4[j], ps4[2 + j]
                mmB = ctx.mm.n

                @blk.scalar
                def _(scalar, q=q, hidp=hidp, gatep=gatep, mmB=mmB, j=j):
                    scalar.wait_ge(ctx.mm.h, mmB)
                    scalar.wait_ge(ctx.ev.h, ctx.ev.n)
                    scalar.activation(out=az[:, 0:TC], in_=gatep[:], func=AF.Sigmoid,
                                      scale=-1.0)
                    scalar.activation(out=az[:, TC:2 * TC], in_=gatep[:],
                                      func=AF.Sigmoid)
                    ctx.ea.inc(scalar.activation(out=s1[:], in_=hidp[:],
                                                 func=AF.Relu), 1)

                @blk.vector
                def _(vector, q=q, hidp=hidp):
                    ctx.ea.wait(vector)
                    ctx.ev.inc(vector.tensor_scalar(out=s2[:], in0=hidp[:], scalar1=0.0,
                                                    scalar2=None, op0=ALU.min), 1)

                @blk.scalar
                def _(scalar, q=q):
                    ctx.ev.wait(scalar)
                    ctx.ea.inc(scalar.activation(out=s2[:], in_=s2[:],
                                                 func=AF.Sigmoid), 1)

                @blk.vector
                def _(vector, q=q):
                    ctx.ea.wait(vector)
                    vector.tensor_tensor(out=s1[:], in0=s1[:], in1=s2[:], op=ALU.add)
                    vector.tensor_tensor(out=s2[:], in0=az[:, TC:2 * TC], in1=s1[:],
                                         op=ALU.mult)
                    ctx.vgap(vector)
                    vector.tensor_tensor_scan(out=hbuf[:, q * TC:(q + 1) * TC],
                                              data0=a_sb, data1=s2[:], initial=0.0,
                                              op0=ALU.mult, op1=ALU.add)
                    ctx.vgap(vector)
                    vector.wait_ge(ctx.dg.h, ctx.dg.n)
                    vector.tensor_tensor_scan(out=pst[:, :], data0=a_sb, data1=zeros[:],
                                              initial=1.0, op0=ALU.mult, op1=ALU.add)
                    ctx.vgap(vector)
                    vector.tensor_copy(out=stg[:, q:q + 1],
                                       in_=hbuf[:, q * TC + TC - 1:q * TC + TC])
                    ctx.ev.inc(vector.tensor_copy(out=stg[:, 24 + q:25 + q],
                                                  in_=pst[:, TC - 1:TC]), 1)

                @blk.gpsimd
                def _(g, q=q):
                    ctx.ev.wait(g)
                    ctx.dg.inc(g.dma_start(out=pspill.ap()[:, q * TC:(q + 1) * TC],
                                           in_=pst[:, :]), 16)

        mmX = ctx.mm.n

        @blk.tensor
        def _(tensor):
            ctx.ev.wait(tensor)
            ctx.mm.inc(tensor.transpose(pA[0:48, 0:P], stg[:, :],
                                        ctx.bufs["ID_F32"]), 1)

        @blk.scalar
        def _(scalar):
            ctx.mm.wait(scalar)
            ctx.ea.inc(scalar.copy(out=stgT[:, :], in_=pA[0:48, 0:P]), 1)

        @blk.sync
        def _(sync):
            ctx.ea.wait(sync)
            ctx.ds.inc(sync.dma_start(
                out=ccin.ap()[0:1, :].rearrange("o (q p) -> (o q) p", p=P),
                in_=stgT[0:24, :]), 16)
            ctx.ds.inc(sync.dma_start(
                out=ccin.ap()[1:2, :].rearrange("o (q p) -> (o q) p", p=P),
                in_=stgT[24:48, :]), 16)

        @blk.gpsimd
        def _(g):
            ctx.ds.wait(g)
            ctx.cc.inc(g.collective_compute(
                "AllGather", ALU.bypass,
                replica_groups=[[0, 1, 2, 3, 4, 5, 6, 7]],
                ins=[ccin.ap()], outs=[ccout.ap()]), 1)

        @blk.gpsimd
        def _(g):
            ctx.cc.wait(g)
            cov = ccout.ap().rearrange("(j two) c -> two j c", two=2)
            ctx.dg.inc(g.dma_start(out=he[:, :], in_=cov[0]), 16)
            ctx.dg.inc(g.dma_start(out=pt8[:, :], in_=cov[1]), 16)

        @blk.gpsimd
        def _(g):
            ctx.dg.wait(g)
            ctx.ep.inc(g.tensor_scalar(out=pt8[:, :], in0=pt8[:, :], scalar1=1e-38,
                                       scalar2=None, op0=ALU.max), 1)

        @blk.scalar
        def _(scalar):
            ctx.ep.wait(scalar)
            ctx.ea.inc(scalar.activation(out=pt8[:, :], in_=pt8[:, :], func=AF.Ln), 1)

        mmW = ctx.mm.n
        eaW0 = ctx.ea.n

        @blk.tensor
        def _(tensor):
            ctx.ea.wait(tensor)
            for sl6 in range(6):
                if sl6 >= 2:
                    tensor.wait_ge(ctx.ea.h, eaW0 + sl6 - 1)
                ctx.mm.inc(tensor.matmul(pB[0:8, (sl6 % 2) * HF:(sl6 % 2 + 1) * HF],
                                         tm[:, :], pt8[:, sl6 * HF:(sl6 + 1) * HF],
                                         start=True, stop=True), 1)

        @blk.scalar
        def _(scalar):
            for sl6 in range(6):
                scalar.wait_ge(ctx.mm.h, mmW + sl6 + 1)
                ctx.ea.inc(scalar.activation(out=w6[:, sl6 * HF:(sl6 + 1) * HF],
                                             in_=pB[0:8, (sl6 % 2) * HF:(sl6 % 2 + 1) * HF],
                                             func=AF.Exp, bias=tb[:, 0:1]), 1)

        @blk.gpsimd
        def _(g):
            ctx.ea.wait(g)
            ctx.ep.inc(g.tensor_tensor(out=w6[:, :], in0=w6[:, :], in1=he[:, :],
                                       op=ALU.mult), 1)

        mmH = ctx.mm.n

        @blk.tensor
        def _(tensor):
            ctx.ep.wait(tensor)
            for q in range(NQ):
                ctx.mm.inc(tensor.matmul(pC[:, q:q + 1],
                                         w6[:, q * P:(q + 1) * P],
                                         ctx.bufs["ONES1B"][0:8, 0:1],
                                         start=True, stop=True), 1)

        @blk.scalar
        def _(scalar):
            scalar.wait_ge(ctx.mm.h, mmH + NQ)
            ctx.ea.inc(scalar.copy(out=hcols[:, :], in_=pC[:, 0:NQ]), 1)

        dsc1 = ctx.dsc.n
        ev1 = ctx.ev.n

        @blk.scalar
        def _(scalar):
            ctx.dg.wait(scalar)
            for q in range(NQ):
                sl = q % 2
                if q >= 2:
                    scalar.wait_ge(ctx.ev.h, ev1 + q - 1)
                ctx.dsc.inc(scalar.dma_start(
                    out=pst2[:, sl * TC:(sl + 1) * TC],
                    in_=pspill.ap()[:, q * TC:(q + 1) * TC]), 16)

        @blk.vector
        def _(vector):
            ctx.ea.wait(vector)
            for q in range(NQ):
                sl = q % 2
                vector.wait_ge(ctx.dsc.h, dsc1 + 16 * (q + 1))
                ctx.ev.inc(vector.scalar_tensor_tensor(
                    out=hbuf[:, q * TC:(q + 1) * TC], in0=pst2[:, sl * TC:(sl + 1) * TC],
                    scalar=hcols[:, q:q + 1],
                    in1=hbuf[:, q * TC:(q + 1) * TC],
                    op0=ALU.mult, op1=ALU.add), 1)

        ctx.bar()

        for og2 in range(4):
            pieces = [wout[rk * P:(rk + 1) * P, og2 * 512:(og2 + 1) * 512]
                      for rk in range(NQ)]
            eaW = ctx.ea.n

            def prewait(tensor, n, eaW=eaW):
                if n == 0:
                    tensor.wait_ge(ctx.ea.h, eaW)

            def onp(tensor, n, slot):
                rk = n
                m = None
                for oc in range(4):
                    for h in range(2):
                        m = tensor.matmul(
                            ps4[oc][:, h * HF:(h + 1) * HF],
                            slot[:, oc * P:(oc + 1) * P],
                            hbuf[:, rk * TC + h * HF:rk * TC + (h + 1) * HF],
                            start=(rk == 0), stop=(rk == NQ - 1))
                return m

            mmWo = ctx.mm.n
            wstream(ctx, pieces, 512, onp, prewait)

            @blk.scalar
            def _(scalar, og2=og2, mmWo=mmWo):
                scalar.wait_ge(ctx.mm.h, mmWo + NQ)
                for oc in range(4):
                    ctx.ea.inc(scalar.copy(
                        out=ctx.xT[:, (og2 * 4 + oc) * TC:(og2 * 4 + oc + 1) * TC],
                        in_=ps4[oc][:]), 1)

        ctx.bar()

    esx.close()
    emit_ln(ctx, lambda q: ctx.xT[:, q * TC:(q + 1) * TC], "residual",
            xold_dram=ctx.xspill)


def emit_l0_attn(ctx, W):
    nc, blk = ctx.nc, ctx.block
    ctx.fresh_dma_sems("l0a")
    NH = 16

    qspill = nc.dram_tensor(_nm("qspill"), [D, TC], BF16)
    ospill = nc.dram_tensor(_nm("ospill"), [D, TC], BF16)
    kspill = nc.dram_tensor(_nm("kspill"), [D, TC], BF16)
    ccs_in = nc.dram_tensor(_nm("ccs_in"), [P, D], F32)
    ccs_out = nc.dram_tensor(_nm("ccs_out"), [NCORE * P, D], F32)

    cs_np = np.zeros((P, P), dtype=np.float32)
    for s_ in range(P):
        for c_ in range(P):
            cs_np[s_, c_] = (c_ - s_) if c_ >= s_ else 1e9
    c_cs = nc.inline_tensor(cs_np, _nm("c_cs"))
    c127_np = (127.0 - np.arange(P, dtype=np.float32))[:, None]
    c_127 = nc.inline_tensor(c127_np, _nm("c_127"))
    cqp_np = np.zeros((P, 8), dtype=np.float32)
    ckp_np = np.zeros((P, 8), dtype=np.float32)
    for i_ in range(8):
        cqp_np[:, i_] = i_ * P + np.arange(P) + 1.0
        ckp_np[:, i_] = 1023.0 - i_ * P - np.arange(P)
    c_qp = nc.inline_tensor(cqp_np, _nm("c_qp"))
    c_kp = nc.inline_tensor(ckp_np, _nm("c_kp"))
    ej_np = np.zeros((8, 8 * P), dtype=np.float32)
    for j_ in range(8):
        ej_np[j_, j_ * P:(j_ + 1) * P] = 1.0
    c_ej = nc.inline_tensor(ej_np, _nm("c_ej"))

    with (
        nc.sbuf_tensor(_nm("a0_v"), [P, NT * D], BF16) as V,
        nc.sbuf_tensor(_nm("a0_g1"), [P, TC], BF16) as g1,
        nc.sbuf_tensor(_nm("a0_wg2"), [P, D], BF16) as wg2,
        nc.sbuf_tensor(_nm("a0_ldr"), [1, 16], F32) as ldr,
        nc.sbuf_tensor(_nm("a0_msel"), [8, 1], F32) as msel,
        nc.sbuf_tensor(_nm("a0_ej"), [8, 8 * P], F32) as ejs,
        nc.sbuf_tensor(_nm("a0_nwr"), [1, D], F32) as nwr,
    ):
        with (
            nc.sbuf_tensor(_nm("xln"), [P, KC * TC], BF16) as xln,
            nc.sbuf_tensor(_nm("a0_stg"), [P, 2 * TC], BF16) as stg,
        ):
            ctx.xln = xln
            emit_ln(ctx, lambda q: ctx.xT[:, q * TC:(q + 1) * TC], "xln")
            emit_spill_x(ctx)
            esp = contextlib.ExitStack()
            ps4 = [esp.enter_context(nc.psum_tensor(_nm(f"a0_pp{i}"), [P, TC], F32))
                   for i in range(4)]
            pp0 = ps4[0]

            @blk.gpsimd
            def _(g):
                ctx.dg.inc(g.dma_start(out=wg2[:, :], in_=W["L0_attn_Wg2"]), 16)

            @blk.sync
            def _(sync):
                ctx.ds.inc(sync.dma_start(
                    out=ldr[0:1, :],
                    in_=W["L0_attn_log_decay"].rearrange("(o e) -> o e", o=1)), 16)
                ctx.ds.inc(sync.dma_start(out=msel[:, :], in_=W["msel"]), 16)
                ctx.ds.inc(sync.dma_start(out=ejs[:, :], in_=c_ej.ap()), 16)
                ctx.ds.inc(sync.dma_start(
                    out=nwr[0:1, :],
                    in_=W["L0_attn_norm_w"].rearrange("(o e) -> o e", o=1)), 16)

            def qk_proj(mat_ap, spill):
                eaP = [ctx.ea.n]
                dscP = [ctx.dsc.n]

                def prewait(tensor, n, eaP=eaP):
                    og = n // KC
                    if n == 0:
                        tensor.wait_ge(ctx.ea.h, eaP[0])
                    if n % KC == 0 and og >= 1:
                        tensor.wait_ge(ctx.ea.h, eaP[0] + og * 4)

                def onp(tensor, n, slot):
                    k = n % KC
                    m = None
                    for oc in range(4):
                        for h in range(2):
                            m = tensor.matmul(
                                ps4[oc][:, h * HF:(h + 1) * HF],
                                slot[:, oc * P:(oc + 1) * P],
                                xln[:, k * TC + h * HF:k * TC + (h + 1) * HF],
                                start=(k == 0), stop=(k == KC - 1))
                    return m

                pieces = []
                for og in range(4):
                    for k in range(KC):
                        pieces.append(mat_ap[k * P:(k + 1) * P, og * 512:(og + 1) * 512])
                mmP = ctx.mm.n
                wstream(ctx, pieces, 512, onp, prewait)

                eaQ0 = [0]

                @blk.scalar
                def _(scalar):
                    nd0 = dscP[0]
                    eaQ0[0] = ctx.ea.n
                    for og in range(4):
                        scalar.wait_ge(ctx.mm.h, mmP + (og + 1) * KC)
                        for oc in range(4):
                            hh = og * 4 + oc
                            sl = hh % 2
                            if hh >= 2:
                                scalar.wait_ge(ctx.dsc.h, nd0 + 16 * (hh - 1))
                            ctx.ea.inc(scalar.activation(
                                out=stg[:, sl * TC:(sl + 1) * TC],
                                in_=ps4[oc][:], func=AF.Silu), 1)

                @blk.sync
                def _(sync):
                    for hh in range(16):
                        sl = hh % 2
                        sync.wait_ge(ctx.ea.h, eaQ0[0] + hh + 1)
                        ctx.dsc.inc(sync.dma_start(
                            out=spill.ap()[hh * P:(hh + 1) * P, :],
                            in_=stg[:, sl * TC:(sl + 1) * TC]), 16)

            qk_proj(W["L0_attn_Wq"], qspill)
            qk_proj(W["L0_attn_Wk"], kspill)

            eaV = [ctx.ea.n]

            def prewaitv(tensor, n):
                og = n // KC
                if n == 0:
                    tensor.wait_ge(ctx.ea.h, eaV[0])
                if n % KC == 0 and og >= 1:
                    tensor.wait_ge(ctx.ea.h, eaV[0] + og * 8)

            def onpv(tensor, n, slot):
                k = n % KC
                m = None
                for t in range(NT):
                    tgt = ps4[t % 4][:, (t // 4) * HF:(t // 4 + 1) * HF]
                    m = tensor.matmul(tgt,
                                      xln[:, k * TC + t * P:k * TC + (t + 1) * P],
                                      slot[:, :],
                                      start=(k == 0), stop=(k == KC - 1))
                return m

            piecesv = []
            for og in range(4):
                for k in range(KC):
                    piecesv.append(W["L0_attn_Wv"][k * P:(k + 1) * P,
                                                   og * 512:(og + 1) * 512])
            mmV = ctx.mm.n
            wstream(ctx, piecesv, 512, onpv, prewaitv)

            @blk.scalar
            def _(scalar):
                for og in range(4):
                    scalar.wait_ge(ctx.mm.h, mmV + (og + 1) * KC)
                    for t in range(NT):
                        ctx.ea.inc(scalar.activation(
                            out=V[:, t * D + og * 512:t * D + (og + 1) * 512],
                            in_=ps4[t % 4][:, (t // 4) * HF:(t // 4 + 1) * HF],
                            func=AF.Silu), 1)

            eaG = [ctx.ea.n]

            def prewaitg(tensor, n):
                if n == 0:
                    tensor.wait_ge(ctx.ea.h, eaG[0])

            def onpg(tensor, n, slot):
                k = n
                m = None
                for h in range(2):
                    m = tensor.matmul(pp0[:, h * HF:(h + 1) * HF], slot[:, 0:P],
                                      xln[:, k * TC + h * HF:k * TC + (h + 1) * HF],
                                      start=(k == 0), stop=(k == KC - 1))
                return m

            piecesg = [W["L0_attn_Wg1"][k * P:(k + 1) * P, :] for k in range(KC)]
            mmG = ctx.mm.n
            wstream(ctx, piecesg, 128, onpg, prewaitg)

            @blk.scalar
            def _(scalar):
                scalar.wait_ge(ctx.mm.h, mmG + KC)
                ctx.ea.inc(scalar.copy(out=g1[:, :], in_=pp0[:]), 1)

            ctx.bar()
            esp.close()

        es = contextlib.ExitStack()
        csb = es.enter_context(nc.sbuf_tensor(_nm("a0_csb"), [P, P], F32))
        cpp = es.enter_context(nc.sbuf_tensor(_nm("a0_cpp"), [P, 17], F32))
        DT = es.enter_context(nc.sbuf_tensor(_nm("a0_dt"), [P, NH * P], BF16))
        ldbc = es.enter_context(nc.sbuf_tensor(_nm("a0_ldbc"), [P, 16], F32))
        KDG = es.enter_context(nc.sbuf_tensor(_nm("a0_kdg"), [P, 8 * 16], F32))
        QDG = es.enter_context(nc.sbuf_tensor(_nm("a0_qdg"), [P, 8 * 16], F32))
        KDL = es.enter_context(nc.sbuf_tensor(_nm("a0_kdl"), [P, 16], F32))
        CD = es.enter_context(nc.sbuf_tensor(_nm("a0_cd"), [P, 16], F32))
        coef8 = es.enter_context(nc.sbuf_tensor(_nm("a0_cf8"), [8, 16], F32))
        COEFB = es.enter_context(nc.sbuf_tensor(_nm("a0_coefb"), [P, 8 * 16], F32))
        NW = es.enter_context(nc.sbuf_tensor(_nm("a0_nw"), [P, D], BF16))
        S_f32 = es.enter_context(nc.sbuf_tensor(_nm("a0_sf"), [P, D], F32))
        S_bf = es.enter_context(nc.sbuf_tensor(_nm("a0_sb"), [P, D], BF16))
        og = es.enter_context(nc.sbuf_tensor(_nm("a0_og"), [P, D], F32))
        gs = es.enter_context(nc.sbuf_tensor(_nm("a0_gs"), [P, D], F32))
        ks2 = es.enter_context(nc.sbuf_tensor(_nm("a0_ks"), [P, 2 * D], BF16))
        st2 = es.enter_context(nc.sbuf_tensor(_nm("a0_st2"), [P, 512], BF16))
        rmsb = es.enter_context(nc.sbuf_tensor(_nm("a0_rms"), [P, 33], F32))
        sqtr = es.enter_context(nc.sbuf_tensor(_nm("a0_sqtr"), [P, P], F32))
        og_bf = es.enter_context(nc.sbuf_tensor(_nm("a0_ogbf"), [P, D], BF16))
        ktok = es.enter_context(nc.sbuf_tensor(_nm("a0_ktok"), [P, 2 * D], BF16))
        qti = es.enter_context(nc.sbuf_tensor(_nm("a0_qti"), [P, 2 * D], BF16))
        kti = es.enter_context(nc.sbuf_tensor(_nm("a0_kti"), [P, 2 * D], BF16))
        onTi = es.enter_context(nc.sbuf_tensor(_nm("a0_onti"), [P, D], BF16))
        if True:
            with (
                nc.psum_tensor(_nm("a0_pc0"), [P, TC], F32) as pc0,
                nc.psum_tensor(_nm("a0_pc1"), [P, TC], F32) as pc1,
            ):
                mmC = ctx.mm.n

                @blk.sync
                def _(sync):
                    ctx.ds.inc(sync.dma_start(out=csb[:, :], in_=c_cs.ap()), 16)
                    ctx.ds.inc(sync.dma_start(out=cpp[:, 0:1], in_=c_127.ap()), 16)
                    ctx.ds.inc(sync.dma_start(out=cpp[:, 1:9], in_=c_qp.ap()), 16)
                    ctx.ds.inc(sync.dma_start(out=cpp[:, 9:17], in_=c_kp.ap()), 16)

                eaC = ctx.ea.n

                @blk.tensor
                def _(tensor):
                    ctx.ds.wait(tensor)
                    ctx.mm.inc(tensor.matmul(pc0[:, 0:16], ctx.bufs["ONESROW"],
                                             ldr[0:1, :], start=True, stop=True), 1)
                    for r in range(2):
                        if r == 1:
                            tensor.wait_ge(ctx.ea.h, eaC + 2)
                        for h2 in range(2):
                            h = 2 * r + h2
                            ctx.mm.inc(tensor.matmul(pc1[:, h2 * HF:(h2 + 1) * HF],
                                                     ctx.bufs["ONESROW"],
                                                     nwr[0:1, h * HF:(h + 1) * HF],
                                                     start=True, stop=True), 1)

                @blk.scalar
                def _(scalar):
                    scalar.wait_ge(ctx.mm.h, mmC + 1)
                    ctx.ea.inc(scalar.copy(out=ldbc[:, :], in_=pc0[:, 0:16]), 1)
                    for r in range(2):
                        scalar.wait_ge(ctx.mm.h, mmC + 1 + (r + 1) * 2)
                        ctx.ea.inc(scalar.copy(out=NW[:, r * TC:(r + 1) * TC],
                                               in_=pc1[:]), 1)

                @blk.vector
                def _(vector):
                    ctx.ea.wait(vector)
                    for h in range(NH):
                        vector.tensor_scalar(out=gs[:, h * P:(h + 1) * P], in0=csb[:, :],
                                             scalar1=ldbc[:, h:h + 1], scalar2=None,
                                             op0=ALU.mult)
                    vector.tensor_scalar(out=KDL[:, :], in0=ldbc[:, :],
                                         scalar1=cpp[:, 0:1], scalar2=None, op0=ALU.mult)
                    vector.tensor_scalar(out=CD[:, :], in0=ldbc[:, :], scalar1=128.0,
                                         scalar2=None, op0=ALU.mult)
                    for i in range(8):
                        vector.tensor_scalar(out=QDG[:, i * 16:(i + 1) * 16],
                                             in0=ldbc[:, :], scalar1=cpp[:, 1 + i:2 + i],
                                             scalar2=None, op0=ALU.mult)
                        vector.tensor_scalar(out=KDG[:, i * 16:(i + 1) * 16],
                                             in0=ldbc[:, :], scalar1=cpp[:, 9 + i:10 + i],
                                             scalar2=None, op0=ALU.mult)
                    ctx.ev.inc(vector.tensor_scalar(out=coef8[:, :], in0=ldbc[0:8, :],
                                                    scalar1=msel[:, 0:1], scalar2=None,
                                                    op0=ALU.mult), 1)

                @blk.scalar
                def _(scalar):
                    ctx.ev.wait(scalar)
                    scalar.activation(out=DT[:, :], in_=gs[:, :], func=AF.Exp)
                    scalar.activation(out=KDL[:, :], in_=KDL[:, :], func=AF.Exp)
                    scalar.activation(out=CD[:, :], in_=CD[:, :], func=AF.Exp)
                    scalar.activation(out=QDG[:, :], in_=QDG[:, :], func=AF.Exp)
                    scalar.activation(out=KDG[:, :], in_=KDG[:, :], func=AF.Exp)
                    ctx.ea.inc(scalar.activation(out=coef8[:, :], in_=coef8[:, :],
                                                 func=AF.Exp), 1)

                mmE = ctx.mm.n

                @blk.tensor
                def _(tensor):
                    ctx.ea.wait(tensor)
                    eaEJ = ctx.ea.n
                    for j in range(8):
                        if j >= 2:
                            tensor.wait_ge(ctx.ea.h, eaEJ + j - 1)
                        ctx.mm.inc(tensor.matmul(
                            pc0[:, (j % 2) * 16:(j % 2) * 16 + 16],
                            ejs[:, j * P:(j + 1) * P], coef8[:, :],
                            start=True, stop=True), 1)

                @blk.scalar
                def _(scalar):
                    for j in range(8):
                        scalar.wait_ge(ctx.mm.h, mmE + j + 1)
                        ctx.ea.inc(scalar.copy(out=COEFB[:, j * 16:(j + 1) * 16],
                                               in_=pc0[:, (j % 2) * 16:(j % 2) * 16 + 16]), 1)

                ctx.bar()

            with nc.psum_tensor(_nm("a0_psA"), [P, D], F32) as psS:
                trs = [Sem(nc, _nm("s_tr0")), Sem(nc, _nm("s_tr1"))]
                eaA0 = ctx.ea.n
                mmA0 = ctx.mm.n
                for i in range(NT):
                    sl = i % 2

                    @blk.sync
                    def _(sync, i=i, sl=sl):
                        if i >= 2:
                            sync.wait_ge(ctx.ea.h, eaA0 + (i - 1) * NH)
                        trs[sl].inc(sync.dma_start_transpose(
                            out=ktok[:, sl * D:(sl + 1) * D],
                            in_=kspill.ap()[:, i * P:(i + 1) * P]), 16)

                    @blk.scalar
                    def _(scalar, i=i, sl=sl):
                        trs[sl].wait(scalar)
                        if i >= 2:
                            scalar.wait_ge(ctx.mm.h, mmA0 + i - 1)
                        for h in range(NH):
                            ctx.ea.inc(scalar.activation(
                                out=ks2[:, sl * D + h * P:sl * D + (h + 1) * P],
                                in_=ktok[:, sl * D + h * P:sl * D + (h + 1) * P],
                                func=AF.Copy, scale=KDG[:, i * 16 + h:i * 16 + h + 1]), 1)

                    @blk.tensor
                    def _(tensor, i=i, sl=sl):
                        ctx.ea.wait(tensor)
                        m = None
                        for h in range(NH):
                            m = tensor.matmul(psS[:, h * P:(h + 1) * P],
                                              ks2[:, sl * D + h * P:sl * D + (h + 1) * P],
                                              V[:, i * D + h * P:i * D + (h + 1) * P],
                                              start=(i == 0 and h % 4 == 0),
                                              stop=(i == NT - 1 and h % 4 == 3))
                        ctx.mm.inc(m, 1)

                @blk.scalar
                def _(scalar):
                    ctx.mm.wait(scalar)
                    for h in range(4):
                        ctx.ea.inc(scalar.copy(out=og[:, h * HF:(h + 1) * HF],
                                               in_=psS[:, h * HF:(h + 1) * HF]), 1)

                @blk.sync
                def _(sync):
                    ctx.ea.wait(sync)
                    ctx.ds.inc(sync.dma_start(out=ccs_in.ap(), in_=og[:, :]), 16)

                @blk.gpsimd
                def _(g):
                    ctx.ds.wait(g)
                    ctx.cc.inc(g.collective_compute(
                        "AllGather", ALU.bypass,
                        replica_groups=[[0, 1, 2, 3, 4, 5, 6, 7]],
                        ins=[ccs_in.ap()], outs=[ccs_out.ap()]), 1)

                ctx.bar()

            @blk.vector
            def _(vector):
                ctx.ev.inc(vector.memset(S_f32[:, :], 0.0), 1)

            dscS = ctx.dsc.n
            evS0 = ctx.ev.n
            with nc.sbuf_tensor(_nm("a0_sj"), [P, D], F32) as sj:
                @blk.scalar
                def _(scalar):
                    ctx.cc.wait(scalar)
                    for j in range(8):
                        if j >= 1:
                            scalar.wait_ge(ctx.ev.h, evS0 + j * NH)
                        ctx.dsc.inc(scalar.dma_start(
                            out=sj[:, :],
                            in_=ccs_out.ap()[j * P:(j + 1) * P, :]), 16)

                @blk.vector
                def _(vector):
                    for j in range(8):
                        vector.wait_ge(ctx.dsc.h, dscS + 16 * (j + 1))
                        for h in range(NH):
                            ctx.ev.inc(vector.scalar_tensor_tensor(
                                out=S_f32[:, h * P:(h + 1) * P],
                                in0=sj[:, h * P:(h + 1) * P],
                                scalar=COEFB[:, j * 16 + h:j * 16 + h + 1],
                                in1=S_f32[:, h * P:(h + 1) * P],
                                op0=ALU.mult, op1=ALU.add), 1)

                @blk.scalar
                def _(scalar):
                    ctx.ev.wait(scalar)
                    ctx.ea.inc(scalar.copy(out=S_bf[:, :], in_=S_f32[:, :]), 1)

                ctx.bar()

            with (
                nc.psum_tensor(_nm("a0_pO"), [P, D], F32) as pO,
                nc.psum_tensor(_nm("a0_pst"), [P, 512], F32) as pST,
                nc.psum_tensor(_nm("a0_px"), [P, 512], F32) as pX,
                nc.psum_tensor(_nm("a0_ps2"), [P, 512], F32) as pS2,
            ):
                qrs = [Sem(nc, _nm("s_qr0")), Sem(nc, _nm("s_qr1"))]
                krs = [Sem(nc, _nm("s_kr0")), Sem(nc, _nm("s_kr1"))]
                trb = [Sem(nc, _nm("s_tb0")), Sem(nc, _nm("s_tb1"))]
                for i in range(NT):
                    sl = i % 2
                    mm0 = ctx.mm.n
                    ea0 = ctx.ea.n
                    ev0 = ctx.ev.n

                    @blk.sync
                    def _(sync, i=i, sl=sl):
                        qrs[sl].inc(sync.dma_start(
                            out=qti[:, sl * D:(sl + 1) * D],
                            in_=qspill.ap()
                            .rearrange("(f p) t -> p f t", p=P)[:, :, i * P:(i + 1) * P]), 16)
                        krs[sl].inc(sync.dma_start(
                            out=kti[:, sl * D:(sl + 1) * D],
                            in_=kspill.ap()
                            .rearrange("(f p) t -> p f t", p=P)[:, :, i * P:(i + 1) * P]), 16)
                        trb[sl].inc(sync.dma_start_transpose(
                            out=ktok[:, sl * D:(sl + 1) * D],
                            in_=kspill.ap()[:, i * P:(i + 1) * P]), 16)

                    for gI in range(4):
                        hs0 = gI * 4

                        @blk.tensor
                        def _(tensor, i=i, sl=sl, gI=gI, hs0=hs0, mm0=mm0, ev0=ev0, ea0=ea0):
                            if gI == 0:
                                tensor.wait_ge(qrs[sl].h, qrs[sl].n)
                                tensor.wait_ge(krs[sl].h, krs[sl].n)
                            else:
                                tensor.wait_ge(ctx.ev.h, ctx.ev.n)
                            m = None
                            for hh in range(4):
                                h = hs0 + hh
                                m = tensor.matmul(pST[:, hh * P:(hh + 1) * P],
                                                  kti[:, sl * D + h * P:sl * D + (h + 1) * P],
                                                  qti[:, sl * D + h * P:sl * D + (h + 1) * P],
                                                  start=True, stop=True)
                            ctx.mm.inc(m, 1)

                        @blk.vector
                        def _(vector, i=i, gI=gI, hs0=hs0):
                            ctx.mm.wait(vector)
                            m = None
                            for hh in range(4):
                                h = hs0 + hh
                                m = vector.tensor_tensor(
                                    out=st2[:, hh * P:(hh + 1) * P],
                                    in0=pST[:, hh * P:(hh + 1) * P],
                                    in1=DT[:, h * P:(h + 1) * P], op=ALU.mult)
                            ctx.ev.inc(m, 1)

                        @blk.tensor
                        def _(tensor, i=i, sl=sl, gI=gI, hs0=hs0):
                            ctx.ev.wait(tensor)
                            ctx.ea.wait(tensor)
                            m = None
                            for hh in range(4):
                                h = hs0 + hh
                                tensor.matmul(pO[:, h * P:(h + 1) * P],
                                              st2[:, hh * P:(hh + 1) * P],
                                              V[:, i * D + h * P:i * D + (h + 1) * P],
                                              start=True, stop=True)
                                m = tensor.matmul(pX[:, hh * P:(hh + 1) * P],
                                                  qti[:, sl * D + h * P:sl * D + (h + 1) * P],
                                                  S_bf[:, h * P:(h + 1) * P],
                                                  start=True, stop=True)
                            ctx.mm.inc(m, 1)

                        @blk.scalar
                        def _(scalar, i=i, sl=sl, gI=gI, hs0=hs0):
                            ctx.mm.wait(scalar)
                            if gI == 0:
                                trb[sl].wait(scalar)
                            m = None
                            for hh in range(4):
                                h = hs0 + hh
                                scalar.activation(
                                    out=og[:, h * P:(h + 1) * P],
                                    in_=pX[:, hh * P:(hh + 1) * P], func=AF.Copy,
                                    scale=QDG[:, h:h + 1])
                                m = scalar.activation(
                                    out=ks2[:, sl * D + h * P:sl * D + (h + 1) * P],
                                    in_=ktok[:, sl * D + h * P:sl * D + (h + 1) * P],
                                    func=AF.Copy, scale=KDL[:, h:h + 1])
                            ctx.ea.inc(m, 1)

                        @blk.tensor
                        def _(tensor, i=i, sl=sl, gI=gI, hs0=hs0):
                            ctx.ea.wait(tensor)
                            ctx.ev.wait(tensor)
                            m = None
                            for hh in range(4):
                                h = hs0 + hh
                                m = tensor.matmul(pS2[:, hh * P:(hh + 1) * P],
                                                  ks2[:, sl * D + h * P:sl * D + (h + 1) * P],
                                                  V[:, i * D + h * P:i * D + (h + 1) * P],
                                                  start=True, stop=True)
                            ctx.mm.inc(m, 1)

                        @blk.vector
                        def _(vector, gI=gI, hs0=hs0):
                            ctx.mm.wait(vector)
                            m = None
                            for hh in range(4):
                                h = hs0 + hh
                                m = vector.scalar_tensor_tensor(
                                    out=S_f32[:, h * P:(h + 1) * P],
                                    in0=S_f32[:, h * P:(h + 1) * P],
                                    scalar=CD[:, h:h + 1],
                                    in1=pS2[:, hh * P:(hh + 1) * P],
                                    op0=ALU.mult, op1=ALU.add)
                            ctx.ev.inc(m, 1)

                        @blk.scalar
                        def _(scalar, gI=gI, hs0=hs0):
                            ctx.ev.wait(scalar)
                            ctx.ea.inc(scalar.copy(
                                out=S_bf[:, hs0 * P:(hs0 + 4) * P],
                                in_=S_f32[:, hs0 * P:(hs0 + 4) * P]), 1)

                    @blk.vector
                    def _(vector, i=i):
                        ctx.ea.wait(vector)
                        m = None
                        for h4 in range(4):
                            m = vector.tensor_tensor(
                                out=og[:, h4 * HF:(h4 + 1) * HF],
                                in0=og[:, h4 * HF:(h4 + 1) * HF],
                                in1=pO[:, h4 * HF:(h4 + 1) * HF], op=ALU.add)
                        ctx.ev.inc(m, 1)

                    @blk.tensor
                    def _(tensor, i=i):
                        ctx.ev.wait(tensor)
                        m = None
                        for h4 in range(4):
                            m = tensor.matmul(pO[:, h4 * HF:(h4 + 1) * HF],
                                              g1[:, i * P:(i + 1) * P],
                                              wg2[:, h4 * HF:(h4 + 1) * HF],
                                              start=True, stop=True)
                        ctx.mm.inc(m, 1)

                    @blk.scalar
                    def _(scalar, i=i):
                        ctx.mm.wait(scalar)
                        ctx.ea.inc(scalar.activation(out=gs[:, :], in_=pO[:, :],
                                                     func=AF.Sigmoid), 1)

                    @blk.vector
                    def _(vector, i=i):
                        ctx.ea.wait(vector)
                        ctx.ev.inc(vector.tensor_tensor(out=og[:, :], in0=og[:, :],
                                                        in1=gs[:, :], op=ALU.mult), 1)

                    @blk.scalar
                    def _(scalar, i=i):
                        ctx.ev.wait(scalar)
                        for h in range(NH):
                            scalar.activation(out=sqtr[:, :], in_=og[:, h * P:(h + 1) * P],
                                              func=AF.Square,
                                              accum_out=rmsb[:, h:h + 1])
                        scalar.activation(out=rmsb[:, 16:32], in_=rmsb[:, 0:16],
                                          func=AF.Ln, scale=1.0 / P,
                                          bias=ctx.bufs["EPSCOL"][:, 0:1])
                        ctx.ea.inc(scalar.activation(out=rmsb[:, 16:32],
                                                     in_=rmsb[:, 16:32],
                                                     func=AF.Exp, scale=-0.5), 1)

                    @blk.vector
                    def _(vector, i=i):
                        ctx.ea.wait(vector)
                        for h in range(NH):
                            vector.tensor_scalar(out=og[:, h * P:(h + 1) * P],
                                                 in0=og[:, h * P:(h + 1) * P],
                                                 scalar1=rmsb[:, 16 + h:17 + h],
                                                 scalar2=None, op0=ALU.mult)
                        m = None
                        for h4 in range(4):
                            m = vector.tensor_tensor(out=og[:, h4 * HF:(h4 + 1) * HF],
                                                     in0=og[:, h4 * HF:(h4 + 1) * HF],
                                                     in1=NW[:, h4 * HF:(h4 + 1) * HF],
                                                     op=ALU.mult)
                        ctx.ev.inc(m, 1)

                    @blk.scalar
                    def _(scalar, i=i):
                        ctx.ev.wait(scalar)
                        ctx.ea.inc(scalar.copy(out=og_bf[:, :], in_=og[:, :]), 1)

                    @blk.sync
                    def _(sync, i=i):
                        ctx.ea.wait(sync)
                        for f in range(NH):
                            ctx.ds.inc(sync.dma_start_transpose(
                                out=onTi[:, f * P:(f + 1) * P],
                                in_=og_bf[:, f * P:(f + 1) * P]), 16)

                    @blk.gpsimd
                    def _(g, i=i):
                        ctx.ds.wait(g)
                        ctx.dg.inc(g.dma_start(
                            out=ospill.ap().rearrange("(k p) t -> p k t", p=P)
                            [:, :, i * P:(i + 1) * P],
                            in_=onTi[:, :].rearrange("p (k c) -> p k c", c=P)), 16)

                    ctx.bar()

            onk = es.enter_context(nc.sbuf_tensor(_nm("a0_onk"), [P, 2 * TC], BF16))
            oks = [Sem(nc, _nm("s_ok0")), Sem(nc, _nm("s_ok1"))]
            for og2 in range(4):
                pieces = [W["L0_attn_Wo"][k * P:(k + 1) * P, og2 * 512:(og2 + 1) * 512]
                          for k in range(KC)]
                eaWo = ctx.ea.n
                mmWb = ctx.mm.n

                @blk.sync
                def _(sync, og2=og2, mmWb=mmWb):
                    if og2 == 0:
                        ctx.dg.wait(sync)
                    for k in range(KC):
                        sl = k % 2
                        if k >= 2:
                            sync.wait_ge(ctx.mm.h, mmWb + k - 1)
                        oks[sl].inc(sync.dma_start(
                            out=onk[:, sl * TC:(sl + 1) * TC],
                            in_=ospill.ap()[k * P:(k + 1) * P, :]), 16)

                def prewait(tensor, n, eaWo=eaWo, og2=og2):
                    sl = n % 2
                    tensor.wait_ge(oks[sl].h, oks[sl].n - 16 * (KC // 2 - 1 - n // 2))
                    if n == 0:
                        tensor.wait_ge(ctx.ea.h, eaWo)

                def onp(tensor, n, slot):
                    k = n
                    sl = k % 2
                    m = None
                    for oc in range(4):
                        for h in range(2):
                            m = tensor.matmul(
                                psW4[oc][:, h * HF:(h + 1) * HF],
                                slot[:, oc * P:(oc + 1) * P],
                                onk[:, sl * TC + h * HF:sl * TC + (h + 1) * HF],
                                start=(k == 0), stop=(k == KC - 1))
                    return m

                with (
                    nc.psum_tensor(_nm("a0_pw0"), [P, TC], F32) as pw0,
                    nc.psum_tensor(_nm("a0_pw1"), [P, TC], F32) as pw1,
                    nc.psum_tensor(_nm("a0_pw2"), [P, TC], F32) as pw2,
                    nc.psum_tensor(_nm("a0_pw3"), [P, TC], F32) as pw3,
                ):
                    psW4 = [pw0, pw1, pw2, pw3]
                    mmWo = ctx.mm.n
                    wstream(ctx, pieces, 512, onp, prewait)

                    @blk.scalar
                    def _(scalar, og2=og2, mmWo=mmWo):
                        ctx.dg.wait(scalar)
                        scalar.wait_ge(ctx.mm.h, mmWo + KC)
                        for oc in range(4):
                            ctx.ea.inc(scalar.copy(
                                out=ctx.xT[:, (og2 * 4 + oc) * TC:(og2 * 4 + oc + 1) * TC],
                                in_=psW4[oc][:]), 1)

                    ctx.bar()

            es.close()

    if os.environ.get("K_DBG_DUMP"):
        d_os = nc.dram_tensor("dbg_ospill", [D, TC], BF16, kind="ExternalOutput").ap()
        d_qs = nc.dram_tensor("dbg_qspill", [D, TC], BF16, kind="ExternalOutput").ap()
        d_cc = nc.dram_tensor("dbg_ccsin", [P, D], F32, kind="ExternalOutput").ap()
        d_cco = nc.dram_tensor("dbg_ccsout", [NCORE * P, D], F32, kind="ExternalOutput").ap()
        d_y = nc.dram_tensor("dbg_y", [P, KC * TC], F32, kind="ExternalOutput").ap()

        @blk.gpsimd
        def _(g):
            ctx.dg.inc(g.dma_start(out=d_os, in_=ospill.ap()), 16)
            ctx.dg.inc(g.dma_start(out=d_qs, in_=qspill.ap()), 16)
            ctx.dg.inc(g.dma_start(out=d_cc, in_=ccs_in.ap()), 16)
            ctx.dg.inc(g.dma_start(out=d_cco, in_=ccs_out.ap()), 16)
            ctx.dg.inc(g.dma_start(out=d_y, in_=ctx.xT[:, :]), 16)
        ctx.bar()

    emit_ln(ctx, lambda q: ctx.xT[:, q * TC:(q + 1) * TC], "residual",
            xold_dram=ctx.xspill)


EMITTERS = {"l0a": emit_l0_attn, "l0f": emit_l0_ffn, "l1a": emit_l1_attn,
            "l1f": emit_l1_ffn}
